# version 21
# baseline (speedup 1.0000x reference)
import numpy as np
import secrets
from contextlib import ExitStack

import concourse.bass as bass
import concourse.mybir as mybir
from concourse.bass_utils import run_bass_kernel_spmd
from concourse.alu_op_type import AluOpType as RALU

F32 = mybir.dt.float32
BF16 = mybir.dt.bfloat16
I32 = mybir.dt.int32
AF = mybir.ActivationFunctionType
ALU = mybir.AluOpType
AX = mybir.AxisListType


BODIES = {"pe": "tensor", "act": "scalar", "dve": "vector", "pool": "gpsimd",
          "sync": "sync"}


class Op:
    __slots__ = ("queue", "fn", "waits", "val", "tag", "step")

    def __init__(self, queue, fn, waits, val, tag, step):
        self.queue, self.fn, self.waits, self.val, self.tag, self.step = (
            queue, fn, waits, val, tag, step)


class Prog:
    def __init__(self, nc):
        self.nc = nc
        self.ops = []
        self.queues = {q: {"body": b, "step": 1} for q, b in
                       [("pe", "tensor"), ("act", "scalar"), ("dve", "vector"),
                        ("pool", "gpsimd")]}
        self.queues["syncq"] = {"body": "sync", "step": 0}
        self.count = {q: 0 for q in self.queues}
        self.last_write = {}
        self.readers = {}
        self.hwm = {}
        self.upgrades = []

    def dma_queue(self, name, body="sync"):
        assert name not in self.queues
        self.queues[name] = {"body": BODIES.get(body, body), "step": 16}
        self.count[name] = 0
        return name

    def emit(self, queue, fn, reads=(), writes=(), tag=""):
        qi = self.queues[queue]
        step = qi["step"]
        needed = {}

        def need(ev):
            q, v = ev
            if q == queue and q == "pe":
                return
            if self.queues[q]["step"] == 16 and v != self.count[q]:
                self.upgrades.append((q, v, self.count[q], tag))
                v = self.count[q]
            if self.hwm.setdefault(queue, {}).get(q, -1) >= v:
                return
            if needed.get(q, -1) < v:
                needed[q] = v

        for k in reads:
            if k in self.last_write:
                need(self.last_write[k])
        for k in writes:
            if k in self.last_write:
                need(self.last_write[k])
            for ev in self.readers.get(k, ()):
                need(ev)

        if step == 16 and self.count[queue] > 0:
            need((queue, self.count[queue]))
        waits = sorted(needed.items())
        self.count[queue] += step
        val = self.count[queue]
        self.ops.append(Op(queue, fn, waits, val, tag, step))
        h = self.hwm.setdefault(queue, {})
        for q, v in waits:
            if h.get(q, -1) < v:
                h[q] = v
        ev = (queue, val)
        for k in writes:
            self.last_write[k] = ev
            self.readers[k] = []
        for k in reads:
            self.readers.setdefault(k, []).append(ev)
        return ev

    def barrier_all(self, tag="phase_barrier"):
        totals = dict(self.count)
        for q in ("pe", "act", "dve", "pool", "syncq"):
            def fn(eng):
                return None
            waits = []
            for pq, v in totals.items():
                if pq == q or v == 0:
                    continue
                if self.hwm.setdefault(q, {}).get(pq, -1) >= v:
                    continue
                waits.append((pq, v))
                self.hwm[q][pq] = v
            self.ops.append(Op(q, fn, sorted(waits), self.count[q], tag, 0))

    def build(self, block, sems=None, es=None):
        if sems is None:
            sems = {q: es.enter_context(self.nc.semaphore("sem_" + q))
                    for q in self.queues if self.count[q] > 0 or
                    self.queues[q]["step"] != 16}
        self.sems = sems
        by_body = {}
        for op in self.ops:
            by_body.setdefault(self.queues[op.queue]["body"], []).append(op)
        for body, ops in by_body.items():
            dec = getattr(block, body)

            def make(ops=ops):
                def _(eng):
                    for op in ops:
                        for q, v in op.waits:
                            eng.wait_ge(sems[q], v)
                        inst = op.fn(eng)
                        if op.step == 0:
                            assert inst is None
                            continue
                        inc = min(op.step, 16)
                        if inst is not None:
                            inst.then_inc(sems[op.queue], inc)
                        else:
                            eng.nop().then_inc(sems[op.queue], inc)
                return _

            dec(make())



class Geo:
    def __init__(self, TH=4096, D=1024, H=16, DH=64, CH=128,
                 F0=2048, F1=4096, E=8, MACRO=1024, C0=1536, C1=768):
        self.C0, self.C1 = C0, C1
        self.TH = TH
        self.D = D
        self.H = H
        self.DH = DH
        self.CH = CH
        self.NCH = TH // CH
        self.F0 = F0
        self.F1 = F1
        self.E = E
        self.MACRO = MACRO
        self.NMAC = TH // MACRO
        self.NT = TH // 128
        self.ND = D // 128

    def col(self, h):
        return (h // 2) * self.CH + (h % 2) * (self.H // 2) * self.CH


def build_consts(g):
    H, CH, DH = g.H, g.CH, g.DH
    c = {}
    c["ident128"] = np.eye(128, dtype=np.float32)
    c["U128"] = np.triu(np.ones((128, 128), np.float32))
    c["ones_1x512"] = np.ones((1, 512), np.float32)
    c["ones_1x128"] = np.ones((1, 128), np.float32)
    c["ones_128x1"] = np.ones((128, 1), np.float32)
    c["ones_2x128"] = np.ones((2, 128), np.float32)
    c["eps_1x512"] = np.full((1, 512), 1e-6, np.float32)
    m = np.triu(np.ones((CH, CH), np.float32))
    mask = np.zeros((CH, H * CH), np.float32)
    for h in range(H):
        mask[:, g.col(h):g.col(h) + CH] = m
    c["maskU_rep"] = mask
    negI = np.zeros((H, H * CH), np.float32)
    for h in range(H):
        negI[h, g.col(h):g.col(h) + CH] = -1.0
    c["negIexp"] = negI
    c["negIexp2"] = np.concatenate([negI, negI], axis=0)
    d64 = np.zeros((H, H * DH), np.float32)
    for h in range(H):
        d64[h, h * DH:(h + 1) * DH] = 1.0
    c["delta64"] = d64
    HP = H // 2
    dP = np.zeros((HP, HP * DH + HP), np.float32)
    for jj in range(HP):
        dP[jj, jj * DH:(jj + 1) * DH] = 1.0
        dP[jj, HP * DH + jj] = 1.0
    c["deltaP"] = dP
    c["ones_16x64"] = np.ones((16, 64), np.float32)
    c["U128strict"] = np.triu(np.ones((128, 128), np.float32), 1)
    C0, C1 = g.C0, g.C1
    c["eCbase0"] = (np.arange(8, dtype=np.float32) * C0 - 1.0).reshape(1, 8)
    c["eCbase1"] = (np.arange(8, dtype=np.float32) * C1 - 1.0).reshape(1, 8)
    c["caprow0"] = np.tile((np.arange(1, 9, dtype=np.float32) * C0 - 1.0), (128, 1))
    c["caprow1"] = np.tile((np.arange(1, 9, dtype=np.float32) * C1 - 1.0), (128, 1))
    c["iota_t"] = np.arange(g.TH, dtype=np.float32).reshape(g.NT, 128).T.copy()
    return c



def load_consts(p, nc, es, g, io, want, arrs, qname="q_const"):
    q = p.dma_queue(qname, "gpsimd")
    out = {}
    ALIAS = {"identb": ("ident128", BF16), "ones_128x1b": ("ones_128x1", BF16),
             "ones_2x128b": ("ones_2x128", BF16),
             "negIexp2b": ("negIexp2", BF16), "delta64b": ("delta64", BF16),
             "eps_1x512b": ("eps_1x512", BF16)}
    for nm in want:
        base, dt = ALIAS.get(nm, (nm, F32))
        shp = list(arrs[base].shape)
        t = es.enter_context(nc.sbuf_tensor("cs_" + nm, shp, dt))
        src = io["c_" + base]
        p.emit(q, lambda e, t=t, src=src: e.dma_start(out=t[:], in_=src),
               writes=["c_" + nm], tag="const")
        out[nm] = t
    return out


def phase_front(p, nc, es, g, io, cst):
    TH, D, ND, NT, H, CH, NCH = g.TH, g.D, g.ND, g.NT, g.H, g.CH, g.NCH
    S = lambda nm, shp, dt=F32: es.enter_context(nc.sbuf_tensor(nm, shp, dt))
    PS = lambda nm, shp, dt=F32: es.enter_context(nc.psum_tensor(nm, shp, dt))

    q_ldx = p.dma_queue("f_ldx", "sync")
    q_ldw = p.dma_queue("f_ldw", "gpsimd")
    q_stx = p.dma_queue("f_stx", "sync")
    q_stq = p.dma_queue("f_stq", "sync")
    q_stk = p.dma_queue("f_stk", "sync")
    q_stv = p.dma_queue("f_stv", "sync")
    q_stt = p.dma_queue("f_stt", "gpsimd")
    q_stc = p.dma_queue("f_stc", "sync")

    wq = S("wq", [128, ND * D], BF16)
    wk = S("wk", [128, ND * D], BF16)
    wv = S("wv", [128, ND * D], BF16)
    wa = S("wa", [128, ND * H], BF16)
    for t, nm in ((wq, "Wq"), (wk, "Wk"), (wv, "Wv")):
        src = io[nm].rearrange("(dc p) f -> p dc f", p=128)
        p.emit(q_ldw, lambda e, t=t, src=src: e.dma_start(
            out=t[:].rearrange("p (dc f) -> p dc f", dc=ND), in_=src),
            writes=[t.name], tag="ldW")
    p.emit(q_ldw, lambda e: e.dma_start(
        out=wa[:].rearrange("p (dc f) -> p dc f", dc=ND),
        in_=io["Wa"].rearrange("(dc p) f -> p dc f", p=128)),
        writes=["wa"], tag="ldWa")
    ba_s = S("ba_s", [1, H])
    p.emit(q_ldw, lambda e: e.dma_start(out=ba_s[:], in_=io["ba"]),
           writes=["ba_s"])
    ba2 = S("ba2", [1, H])
    p.emit("dve", lambda e: e.tensor_scalar(out=ba2[:], in0=ba_s[:],
           scalar1=2.0, scalar2=None, op0=ALU.add), reads=["ba_s"],
           writes=["ba2"])

    pj = [PS(f"pj{i}", [128, 512]) for i in range(4)]
    ptr = PS("ptrT", [128, 128], BF16)
    ap_ = PS("ap_", [128, H])

    xnT = [S(f"xnT{j}", [128, TH], BF16) for j in range(ND)]

    def xnT_keys(trange=None):
        if trange is None:
            trange = range(NT)
        return [f"xnT{j}_{i}" for j in range(ND) for i in trange]

    xt = [S(f"xt{i}", [128, D]) for i in range(2)]
    xc = [S(f"xc{i}", [128, D]) for i in range(2)]
    xnb = [S(f"xnb{i}", [128, D], BF16) for i in range(2)]
    stat = S("stat", [128, 8])
    for i in range(NT):
        pa = i % 2
        kxt, kxc, kxnb = f"xt{pa}", f"xc{pa}", f"xnb{pa}"
        p.emit(q_ldx, lambda e, i=i, pa=pa: e.dma_start(
            out=xt[pa][:], in_=io["x"][i * 128:(i + 1) * 128, :]),
            reads=[kxc], writes=[kxt], tag=f"ldx{i}")
        p.emit("dve", lambda e, pa=pa: e.reduce_sum(
            stat[:, 0:1], xt[pa][:], axis=AX.X), reads=[kxt], writes=["stat0"])
        p.emit("dve", lambda e: e.tensor_scalar(
            out=stat[:, 1:2], in0=stat[:, 0:1], scalar1=1.0 / D, scalar2=None,
            op0=ALU.mult), reads=["stat0"], writes=["statm"])
        p.emit("dve", lambda e, pa=pa: e.tensor_scalar(
            out=xc[pa][:], in0=xt[pa][:], scalar1=stat[:, 1:2], scalar2=None,
            op0=ALU.subtract), reads=[kxt, "statm"], writes=[kxc])
        p.emit("act", lambda e, pa=pa: e.activation(
            out=xt[pa][:], in_=xc[pa][:], func=AF.Square),
            reads=[kxc], writes=[kxt], tag="sq")
        p.emit("dve", lambda e, pa=pa: e.reduce_sum(
            stat[:, 2:3], xt[pa][:], axis=AX.X), reads=[kxt], writes=["statv"])
        p.emit("dve", lambda e: e.tensor_scalar(
            out=stat[:, 5:6], in0=stat[:, 2:3], scalar1=1.0 / D, scalar2=1e-5,
            op0=ALU.mult, op1=ALU.add), reads=["statv"], writes=["statv2"])
        p.emit("act", lambda e: e.activation(
            out=stat[:, 3:4], in_=stat[:, 5:6], func=AF.Ln),
            reads=["statv2"], writes=["statl"])
        p.emit("act", lambda e: e.activation(
            out=stat[:, 4:5], in_=stat[:, 3:4], func=AF.Exp, scale=-0.5),
            reads=["statl"], writes=["statr"])
        p.emit("dve", lambda e, pa=pa: e.tensor_scalar(
            out=xc[pa][:], in0=xc[pa][:], scalar1=stat[:, 4:5], scalar2=None,
            op0=ALU.mult), reads=[kxc, "statr"], writes=[kxc])
        p.emit(q_stx, lambda e, i=i, pa=pa: e.dma_start(
            out=io["xn32"][i * 128:(i + 1) * 128, :], in_=xc[pa][:]),
            reads=[kxc], writes=["xn32w"], tag=f"stxn{i}")
        p.emit("dve", lambda e, pa=pa: e.tensor_copy(xnb[pa][:], xc[pa][:]),
               reads=[kxc], writes=[kxnb])
        for j in range(ND):
            p.emit("pe", lambda e, pa=pa, j=j: e.transpose(
                ptr[:], xnb[pa][:, j * 128:(j + 1) * 128], cst["identb"][:]),
                reads=[kxnb, "c_identb"], writes=["ptrT"], tag=f"tr{i}_{j}")
            p.emit("dve", lambda e, i=i, j=j: e.tensor_copy(
                xnT[j][:, i * 128:(i + 1) * 128], ptr[:]),
                reads=["ptrT"], writes=[f"xnT{j}_{i}"], tag=f"trc{i}_{j}")
    p.barrier_all("front_sub0")

    qsl = [S(f"qsl{i}", [128, 512], BF16) for i in range(2)]
    esc = [S(f"esc{i}", [128, 512]) for i in range(2)]
    esb = [S(f"esb{i}", [128, 512], BF16) for i in range(2)]
    NSLAB = TH // 512
    for (wsb, wname, dst) in ((wq, "wq", "QT"), (wk, "wk", "KT")):
        for fout in range(ND):
            for sg in range((NSLAB + 3) // 4):
                slabs = list(range(sg * 4, min((sg + 1) * 4, NSLAB)))
                for d in range(ND):
                    for si, sl in enumerate(slabs):
                        p.emit("pe", lambda e, d=d, sl=sl, si=si, fout=fout, wsb=wsb: e.matmul(
                            pj[si][:],
                            wsb[:, d * D + fout * 128: d * D + (fout + 1) * 128],
                            xnT[d][:, sl * 512:(sl + 1) * 512],
                            start=(d == 0), stop=(d == ND - 1)),
                            reads=[wname], writes=[f"pj{si}"], tag=f"pj{fout}_{sl}")
                for si, sl in enumerate(slabs):
                    pa = si % 2
                    p.emit("dve", lambda e, si=si, pa=pa: e.tensor_scalar(
                        out=esc[pa][:], in0=pj[si][:], scalar1=0.0, scalar2=None,
                        op0=ALU.min), reads=[f"pj{si}"], writes=[f"esc{pa}"])
                    p.emit("act", lambda e, pa=pa: e.activation(
                        out=esb[pa][:], in_=esc[pa][:], func=AF.Exp),
                        reads=[f"esc{pa}"], writes=[f"esb{pa}"])
                    p.emit("dve", lambda e, si=si, pa=pa: e.scalar_tensor_tensor(
                        out=qsl[pa][:], in0=pj[si][:], scalar=0.0, in1=esb[pa][:],
                        op0=ALU.max, op1=ALU.add),
                        reads=[f"pj{si}", f"esb{pa}"], writes=[f"qsl{pa}"])
                    p.emit(q_stq if dst == "QT" else q_stk,
                           lambda e, fout=fout, sl=sl, pa=pa, dst=dst: e.dma_start(
                        out=io[dst][fout, :, sl * 512:(sl + 1) * 512], in_=qsl[pa][:]),
                        reads=[f"qsl{pa}"], writes=[dst + "w"], tag=f"st{dst}")
    p.barrier_all("front_sub1")

    u_all = S("u_all", [128, H * NT])
    ksl = [S(f"ksl{i}", [128, D], BF16) for i in range(2)]
    vsl = [S(f"vsl{i}", [128, D], BF16) for i in range(2)]
    kesc = [S(f"kesc{i}", [128, D]) for i in range(2)]
    kesb = [S(f"kesb{i}", [128, D], BF16) for i in range(2)]
    for i in range(NT):
        pa = i % 2
        rk = xnT_keys([i])
        for d in range(ND):
            for half in range(2):
                p.emit("pe", lambda e, d=d, i=i, half=half: e.matmul(
                    pj[half][:], xnT[d][:, i * 128:(i + 1) * 128],
                    wk[:, d * D + half * 512: d * D + (half + 1) * 512],
                    start=(d == 0), stop=(d == ND - 1)),
                    reads=["wk"], writes=[f"pj{half}"], tag=f"ktm{i}")
                p.emit("pe", lambda e, d=d, i=i, half=half: e.matmul(
                    pj[2 + half][:], xnT[d][:, i * 128:(i + 1) * 128],
                    wv[:, d * D + half * 512: d * D + (half + 1) * 512],
                    start=(d == 0), stop=(d == ND - 1)),
                    reads=["wv"], writes=[f"pj{2+half}"], tag=f"vtm{i}")
            p.emit("pe", lambda e, d=d, i=i: e.matmul(
                ap_[:], xnT[d][:, i * 128:(i + 1) * 128], wa[:, d * H:(d + 1) * H],
                start=(d == 0), stop=False), reads=["wa"],
                writes=["ap_"], tag=f"atm{i}")
        p.emit("pe", lambda e: e.matmul(
            ap_[:], cst["ones_1x128"][:], ba2[:], start=False, stop=True),
            reads=["c_ones_1x128", "ba2"], writes=["ap_"], tag=f"aba{i}")
        p.emit("dve", lambda e, i=i: e.tensor_copy(
            u_all[:, i * H:(i + 1) * H], ap_[:]), reads=["ap_"],
            writes=[f"u_{i}"], tag=f"uev{i}")
        for half in range(2):
            p.emit("dve", lambda e, half=half, pa=pa: e.tensor_scalar(
                out=kesc[pa][:, half * 512:(half + 1) * 512], in0=pj[half][:],
                scalar1=0.0, scalar2=None, op0=ALU.min),
                reads=[f"pj{half}"], writes=[f"kesc{pa}_{half}"])
            p.emit("act", lambda e, half=half, pa=pa: e.activation(
                out=kesb[pa][:, half * 512:(half + 1) * 512],
                in_=kesc[pa][:, half * 512:(half + 1) * 512], func=AF.Exp),
                reads=[f"kesc{pa}_{half}"], writes=[f"kesb{pa}_{half}"])
            p.emit("dve", lambda e, half=half, pa=pa: e.scalar_tensor_tensor(
                out=ksl[pa][:, half * 512:(half + 1) * 512], in0=pj[half][:],
                scalar=0.0, in1=kesb[pa][:, half * 512:(half + 1) * 512],
                op0=ALU.max, op1=ALU.add),
                reads=[f"pj{half}", f"kesb{pa}_{half}"], writes=[f"ksl{pa}"])
            p.emit("dve", lambda e, half=half, pa=pa: e.tensor_copy(
                vsl[pa][:, half * 512:(half + 1) * 512], pj[2 + half][:]),
                reads=[f"pj{2+half}"], writes=[f"vsl{pa}"])
        p.emit(q_stk, lambda e, i=i, pa=pa: e.dma_start(
            out=io["Ktm"][i * 128:(i + 1) * 128, :], in_=ksl[pa][:]),
            reads=[f"ksl{pa}"], writes=["Ktmw"], tag="stKtm")
        p.emit(q_stv, lambda e, i=i, pa=pa: e.dma_start(
            out=io["V"][i * 128:(i + 1) * 128, :], in_=vsl[pa][:]),
            reads=[f"vsl{pa}"], writes=["Vw"], tag="stV")

    tls = S("tls", [128, H * NT])
    tls2 = S("tls2", [128, H * NT])
    ukeys = [f"u_{i}" for i in range(NT)]
    p.emit("act", lambda e: e.activation(out=tls[:], in_=u_all[:], func=AF.Abs),
           reads=ukeys, writes=["tls"])
    p.emit("act", lambda e: e.activation(out=tls[:], in_=tls[:], func=AF.Exp,
           scale=-1.0), reads=["tls"], writes=["tls"])
    p.emit("act", lambda e: e.activation(out=tls[:], in_=tls[:], func=AF.Ln,
           bias=1.0), reads=["tls"], writes=["tls"])
    p.emit("dve", lambda e: e.tensor_scalar(out=tls2[:], in0=u_all[:],
           scalar1=0.0, scalar2=None, op0=ALU.min), reads=ukeys, writes=["tls2"])
    p.emit("dve", lambda e: e.tensor_sub(u_all[:], tls2[:], tls[:]),
           reads=["tls", "tls2"] + ukeys, writes=["loga"] + ukeys)
    p.barrier_all("front_sub2")

    cumtm_ps = pj[0][:, 0:H]
    cumhm_ps = pj[1][0:H, 0:128]
    tot_ps = pj[2][0:1, 0:H]
    cumtm_all = S("cumtm_all", [128, H * NCH])
    cumhm_sb = S("cumhm_sb", [H, 128])
    cumg_sb = S("cumg_sb", [H, 128])
    chl16 = S("chl16", [H, 128], BF16)
    clo32 = S("clo32", [H, 128])
    clo16 = S("clo16", [H, 128], BF16)
    pfx = S("pfx", [H, 2])
    tot_all = S("tot_all", [1, H * NCH])
    p.emit("dve", lambda e: e.memset(pfx[:], 0.0), writes=["pfx"], tag="pfx0")
    totc_ps = pj[3][0:H, 0:1]
    for c in range(NCH):
        p.emit("pe", lambda e, c=c: e.matmul(
            cumtm_ps, cst["U128"][:], u_all[:, c * H:(c + 1) * H],
            start=True, stop=True), reads=["loga", "c_U128"],
            writes=["pj0"], tag=f"cumtm{c}")
        p.emit("pe", lambda e, c=c: e.matmul(
            cumhm_ps, u_all[:, c * H:(c + 1) * H], cst["U128"][:],
            start=True, stop=True), reads=["loga", "c_U128"],
            writes=["pj1"], tag=f"cumhm{c}")
        p.emit("pe", lambda e, c=c: e.matmul(
            tot_ps, cst["ones_128x1"][:], u_all[:, c * H:(c + 1) * H],
            start=True, stop=True), reads=["loga", "c_ones_128x1"],
            writes=["pj2"], tag=f"tot{c}")
        p.emit("dve", lambda e, c=c: e.tensor_copy(
            cumtm_all[:, c * H:(c + 1) * H], cumtm_ps),
            reads=["pj0"], writes=["cumtm_all"])
        p.emit("dve", lambda e, c=c: e.tensor_copy(cumhm_sb[:], cumhm_ps),
               reads=["pj1"], writes=["cumhm_sb"])
        p.emit("dve", lambda e, c=c: e.tensor_copy(
            tot_all[:, c * H:(c + 1) * H], tot_ps),
            reads=["pj2"], writes=["tot_all"])
        p.emit(q_stc, lambda e, c=c: e.dma_start(
            out=io["cum_nat"][:, c * 128:(c + 1) * 128], in_=cumhm_sb[:]),
            reads=["cumhm_sb"], writes=["cumnatw"], tag="stnat")
        p.emit("dve", lambda e: e.tensor_copy(chl16[:], cumhm_sb[:]),
               reads=["cumhm_sb"], writes=["chl16"], tag=f"chl{c}")
        p.emit("dve", lambda e: e.tensor_tensor(
            out=clo32[:], in0=cumhm_sb[:], in1=chl16[:], op=ALU.subtract),
            reads=["cumhm_sb", "chl16"], writes=["clo32"], tag=f"clo{c}")
        p.emit("dve", lambda e: e.tensor_copy(clo16[:], clo32[:]),
               reads=["clo32"], writes=["clo16"], tag=f"clob{c}")
        p.emit(q_stc, lambda e, c=c: e.dma_start(
            out=io["cumhl"][0, :, c * 128:(c + 1) * 128], in_=chl16[:]),
            reads=["chl16"], writes=["cumhlw"], tag="sthl")
        p.emit(q_stc, lambda e, c=c: e.dma_start(
            out=io["cumhl"][1, :, c * 128:(c + 1) * 128], in_=clo16[:]),
            reads=["clo16"], writes=["cumhlw"], tag="sthl2")
        p.emit("pe", lambda e, c=c: e.matmul(
            totc_ps, u_all[:, c * H:(c + 1) * H], cst["ones_128x1"][:],
            start=True, stop=True), reads=["loga", "c_ones_128x1"],
            writes=["pj3"], tag=f"totc{c}")
        p.emit("dve", lambda e, c=c: e.tensor_scalar(
            out=cumg_sb[:], in0=cumhm_sb[:], scalar1=pfx[:, 0:1], scalar2=None,
            op0=ALU.add), reads=["cumhm_sb", "pfx"], writes=["cumg_sb"],
            tag=f"cumg{c}")
        p.emit(q_stc, lambda e, c=c: e.dma_start(
            out=io["cumg_nat"][:, c * 128:(c + 1) * 128], in_=cumg_sb[:]),
            reads=["cumg_sb"], writes=["cumgw"], tag="stcumg")
        p.emit("dve", lambda e: e.tensor_copy(chl16[:], cumg_sb[:]),
               reads=["cumg_sb", "cumhlw"], writes=["chl16"], tag=f"ghl{c}")
        p.emit("dve", lambda e: e.tensor_tensor(
            out=clo32[:], in0=cumg_sb[:], in1=chl16[:], op=ALU.subtract),
            reads=["cumg_sb", "chl16"], writes=["clo32"], tag=f"glo{c}")
        p.emit("dve", lambda e: e.tensor_copy(clo16[:], clo32[:]),
               reads=["clo32"], writes=["clo16"], tag=f"glob{c}")
        p.emit(q_stc, lambda e, c=c: e.dma_start(
            out=io["cumghl"][0, :, c * 128:(c + 1) * 128], in_=chl16[:]),
            reads=["chl16"], writes=["cumghlw"], tag="stghl")
        p.emit(q_stc, lambda e, c=c: e.dma_start(
            out=io["cumghl"][1, :, c * 128:(c + 1) * 128], in_=clo16[:]),
            reads=["clo16"], writes=["cumghlw"], tag="stghl2")
        p.emit("dve", lambda e, c=c: e.scalar_tensor_tensor(
            out=pfx[:, 0:1], in0=totc_ps, scalar=1.0, in1=pfx[:, 0:1],
            op0=ALU.mult, op1=ALU.add), reads=["pj3", "pfx"], writes=["pfx"],
            tag=f"pfxu{c}")
    p.emit(q_stc, lambda e: e.dma_start(out=io["cumtm"], in_=cumtm_all[:]),
           reads=["cumtm_all"], writes=["cumtmw"])
    p.emit(q_stc, lambda e: e.dma_start(out=io["totals"], in_=tot_all[:]),
           reads=["tot_all"], writes=["totw"])
    for j in range(ND):
        p.emit(q_stt, lambda e, j=j: e.dma_start(
            out=io["xnT"][j, :, :], in_=xnT[j][:]),
            reads=xnT_keys(), writes=["xnTw"], tag=f"stxnT{j}")
    return dict(xnT=xnT, xnT_keys=xnT_keys)


def phase_scan(p, nc, es, g, io, cst, pref="s"):
    TH, D, ND, H, CH, NCH, DH = g.TH, g.D, g.ND, g.H, g.CH, g.NCH, g.DH
    HP = H // 2
    SCOLS = HP * DH + HP
    S = lambda nm, shp, dt=F32: es.enter_context(nc.sbuf_tensor(pref + nm, shp, dt))
    PS = lambda nm, shp, dt=F32: es.enter_context(nc.psum_tensor(pref + nm, shp, dt))

    q_lq = p.dma_queue(pref + "_lq", "sync")
    q_lk = p.dma_queue(pref + "_lk", "sync")
    q_lkm = p.dma_queue(pref + "_lkm", "sync")
    q_lv = p.dma_queue(pref + "_lv", "sync")
    q_lcf = p.dma_queue(pref + "_lcf", "gpsimd")
    q_lch = p.dma_queue(pref + "_lch", "gpsimd")
    q_ldc = p.dma_queue(pref + "_ldc", "gpsimd")
    q_st = p.dma_queue(pref + "_st", "sync")

    cumtm = S("cumtm_r", [128, H * NCH])
    totals = S("totals_r", [1, H * NCH])
    p.emit(q_ldc, lambda e: e.dma_start(out=cumtm[:], in_=io["cumtm"]),
           writes=["cumtm_r"])
    p.emit(q_ldc, lambda e: e.dma_start(out=totals[:], in_=io["totals"]),
           writes=["totals_r"])
    S32 = S("S32", [128, SCOLS])
    Sb = S("Sb", [128, SCOLS], BF16)
    p.emit("dve", lambda e: e.memset(S32[:], 0.0), writes=["S32"])
    p.emit("dve", lambda e: e.tensor_copy(Sb[:], S32[:]), reads=["S32"],
           writes=["Sb"])

    PB = PS("PB", [128, 2048])
    PP = PS("PP", [128, 2048])
    POv = PB[:, 0:1024]
    SIv = PB[:, 1024:1024 + SCOLS]

    qt = [S(f"qt{i}", [128, ND * CH], BF16) for i in range(2)]
    kt = [S(f"kt{i}", [128, ND * CH], BF16) for i in range(2)]
    ktm = [S(f"ktm{i}", [128, D], BF16) for i in range(2)]
    vtm = [S(f"vtm{i}", [128, D], BF16) for i in range(2)]
    cfl = [S(f"cfl{i}", [2, H * CH], BF16) for i in range(2)]
    chm = [S(f"chm{i}", [H, CH]) for i in range(2)]
    chmh = [S(f"chmh{i}", [2 * H, CH], BF16) for i in range(2)]
    ecum = S("ecum", [128, H * CH], BF16)
    elog = S("elog", [128, H * CH])
    Etile = S("Etile", [128, H * CH], BF16)
    Pm = S("Pm", [128, H * CH], BF16)
    PmE = S("PmE", [128, H * CH], BF16)
    qte = S("qte", [128, ND * CH], BF16)
    denr = S("denr", [1, H * CH])
    outc32 = [S(f"outc32{i}", [128, ND * CH]) for i in range(2)]
    e2vhm = S("e2vhm", [H, CH], BF16)
    e2vhm32 = S("e2vhm32", [H, CH])
    e2vb = S("e2vb", [128, H * DH], BF16)
    vh = S("vh", [128, H * DH], BF16)
    tcol = S("tcol", [H, 1])
    ecolE = S("ecolE", [HP, 1])
    ecolO = S("ecolO", [HP, 1])
    ehmE = S("ehmE", [HP, DH])
    ehmO = S("ehmO", [HP, DH])
    stmp = S("stmp", [128, SCOLS])

    cum_flat_view = io["cumhl"].rearrange("w (a b) t -> w b a t", b=2)

    for c in range(NCH):
        pa = c % 2
        kq, kk, kkm, kv, kcf, kch = (f"qt{pa}", f"kt{pa}", f"ktm{pa}",
                                     f"vtm{pa}", f"cfl{pa}", f"chm{pa}")
        p.emit(q_lq, lambda e, c=c, pa=pa: e.dma_start(
            out=qt[pa][:].rearrange("p (j t) -> p j t", j=ND),
            in_=io["QT"][:, :, c * CH:(c + 1) * CH].rearrange("j p t -> p j t")),
            writes=[kq], tag=f"ldq{c}")
        p.emit(q_lk, lambda e, c=c, pa=pa: e.dma_start(
            out=kt[pa][:].rearrange("p (j t) -> p j t", j=ND),
            in_=io["KT"][:, :, c * CH:(c + 1) * CH].rearrange("j p t -> p j t")),
            writes=[kk], tag=f"ldk{c}")
        p.emit(q_lkm, lambda e, c=c, pa=pa: e.dma_start(
            out=ktm[pa][:], in_=io["Ktm"][c * CH:(c + 1) * CH, :]),
            writes=[kkm], tag=f"ldktm{c}")
        p.emit(q_lv, lambda e, c=c, pa=pa: e.dma_start(
            out=vtm[pa][:], in_=io["V"][c * CH:(c + 1) * CH, :]),
            writes=[kv], tag=f"ldv{c}")
        for w in range(2):
            p.emit(q_lcf, lambda e, c=c, pa=pa, w=w: e.dma_start(
                out=cfl[pa][w:w + 1, :].rearrange("o (b a t) -> o b a t", b=2, a=ND),
                in_=cum_flat_view[w:w + 1, :, :, c * CH:(c + 1) * CH].squeeze(0).unsqueeze(0)),
                writes=[kcf], tag=f"ldcf{c}_{w}")
        p.emit(q_lch, lambda e, c=c, pa=pa: e.dma_start(
            out=chm[pa][:], in_=io["cum_nat"][:, c * CH:(c + 1) * CH]),
            writes=[kch], tag=f"ldch{c}")
        p.emit(q_lch, lambda e, c=c, pa=pa: e.dma_start(
            out=chmh[pa][:],
            in_=io["cumhl"][:, :, c * CH:(c + 1) * CH].rearrange(
                "w a t -> (w a) t")),
            writes=[kch + "h"], tag=f"ldchh{c}")

        for h in range(H):
            b, j = h % 2, h // 2
            p.emit("pe", lambda e, b=b, j=j, h=h, pa=pa: e.matmul(
                PP[:, g.col(h):g.col(h) + CH],
                kt[pa][b * 64:(b + 1) * 64, j * CH:(j + 1) * CH],
                qt[pa][b * 64:(b + 1) * 64, j * CH:(j + 1) * CH],
                start=True, stop=True),
                reads=[kq, kk], writes=["PP"], tag=f"pp{c}_{h}")
        for sl in range(4):
            p.emit("pe", lambda e, sl=sl, pa=pa: e.matmul(
                PB[:, sl * 512:(sl + 1) * 512], cst["ones_2x128b"][:],
                cfl[pa][0:2, sl * 512:(sl + 1) * 512], start=True, stop=True),
                reads=[kcf, "c_ones_2x128b"], writes=["PB"], tag=f"b1_{c}_{sl}")
        p.emit("act", lambda e: e.activation(out=ecum[:], in_=PB[:],
               func=AF.Exp), reads=["PB"], writes=["ecum"], tag=f"ecum{c}")
        for sl in range(4):
            p.emit("pe", lambda e, sl=sl, pa=pa: e.matmul(
                PB[:, sl * 512:(sl + 1) * 512], cst["ones_2x128b"][:],
                cfl[pa][0:2, sl * 512:(sl + 1) * 512], start=True, stop=False),
                reads=[kcf, "c_ones_2x128b"], writes=["PB"], tag=f"b1b_{c}_{sl}")
        for sl in range(4):
            p.emit("pe", lambda e, sl=sl, pa=pa: e.matmul(
                PB[:, sl * 512:(sl + 1) * 512], chmh[pa][:],
                cst["negIexp2b"][:, sl * 512:(sl + 1) * 512],
                start=False, stop=True),
                reads=[kch + "h", "c_negIexp2b"], writes=["PB"], tag=f"b2_{c}_{sl}")
        p.emit("act", lambda e: e.activation(out=elog[:], in_=PB[:],
               func=AF.Relu, scale=-1.0), reads=["PB"], writes=["elog"],
               tag=f"elog{c}")
        p.emit("act", lambda e: e.activation(out=Etile[:], in_=elog[:],
               func=AF.Exp, scale=-1.0), reads=["elog"], writes=["Etile"],
               tag=f"E{c}")
        for h in range(H):
            b, j = h % 2, h // 2
            p.emit("dve", lambda e, b=b, j=j, h=h, pa=pa: e.tensor_tensor(
                out=qte[b * 64:(b + 1) * 64, j * CH:(j + 1) * CH],
                in0=qt[pa][b * 64:(b + 1) * 64, j * CH:(j + 1) * CH],
                in1=ecum[b * 64:(b + 1) * 64, g.col(h):g.col(h) + CH],
                op=ALU.mult), reads=[kq, "ecum"], writes=["qte"],
                tag=f"qte{c}_{h}")
        p.emit("dve", lambda e: e.tensor_tensor(
            out=Pm[:], in0=PP[:], in1=cst["maskU_rep"][:], op=ALU.mult),
            reads=["PP", "c_maskU_rep"], writes=["Pm"], tag=f"pm{c}")
        p.emit("dve", lambda e: e.tensor_tensor(
            out=PmE[:], in0=Pm[:], in1=Etile[:], op=ALU.mult),
            reads=["Pm", "Etile"], writes=["PmE"], tag=f"pme{c}")
        po_banks = {}
        for h in range(H):
            po_banks.setdefault(((h // 2) // 4, h % 2), []).append(h)
        for bank, hs in po_banks.items():
            for k, h in enumerate(hs):
                b, j = h % 2, h // 2
                p.emit("pe", lambda e, b=b, j=j, h=h, st=(k == 0): e.matmul(
                    POv[b * 64:(b + 1) * 64, j * CH:(j + 1) * CH],
                    Sb[b * 64:(b + 1) * 64, j * DH:(j + 1) * DH],
                    qte[b * 64:(b + 1) * 64, j * CH:(j + 1) * CH],
                    start=st, stop=False, tile_position=(b * 64, b * 64),
                    skip_group_check=True),
                    reads=["Sb", "qte"], writes=["PB"], tag=f"oex{c}_{h}")
        for sl in range(4):
            p.emit("pe", lambda e, sl=sl: e.matmul(
                PP[0:1, sl * 512:(sl + 1) * 512], cst["ones_128x1b"][:],
                PmE[:, sl * 512:(sl + 1) * 512], start=True, stop=False),
                reads=["PmE", "c_ones_128x1b"], writes=["PP"], tag=f"dsum{c}_{sl}")
        for h in range(H):
            b, j = h % 2, h // 2
            p.emit("pe", lambda e, b=b, j=j, h=h: e.matmul(
                PP[0:1, g.col(h):g.col(h) + CH],
                Sb[b * 64:(b + 1) * 64, 512 + j:513 + j],
                qte[b * 64:(b + 1) * 64, j * CH:(j + 1) * CH],
                start=False, stop=False),
                reads=["Sb", "qte"], writes=["PP"], tag=f"dz{c}_{h}")
        for sl in range(4):
            p.emit("pe", lambda e, sl=sl: e.matmul(
                PP[0:1, sl * 512:(sl + 1) * 512],
                cst["ones_128x1b"][0:1, 0:1],
                cst["eps_1x512b"][:], start=False, stop=True),
                reads=["c_eps_1x512b", "c_ones_128x1b"], writes=["PP"],
                tag=f"deps{c}_{sl}")
        for bank, hs in po_banks.items():
            for k, h in enumerate(hs):
                b, j = h % 2, h // 2
                p.emit("pe", lambda e, b=b, j=j, h=h, pa=pa,
                       sp=(k == len(hs) - 1): e.matmul(
                    POv[b * 64:(b + 1) * 64, j * CH:(j + 1) * CH],
                    vtm[pa][:, h * DH:(h + 1) * DH],
                    PmE[:, g.col(h):g.col(h) + CH],
                    start=False, stop=sp, tile_position=(0, b * 64),
                    skip_group_check=True),
                    reads=[kv, "PmE"], writes=["PB"], tag=f"oin{c}_{h}")
        p.emit("act", lambda e, pa=pa: e.copy(denr[:], PP[0:1, :]),
               reads=["PP"], writes=["denr"], tag=f"denr{c}")
        p.emit(q_st, lambda e, c=c: e.dma_start(
            out=io["den_hbm"][c:c + 1, :], in_=denr[:]),
            reads=["denr"], writes=["denw"], tag=f"stden{c}")
        p.emit("act", lambda e, pa=pa: e.copy(outc32[pa][:], POv),
               reads=["PB"], writes=[f"outc32{pa}"], tag=f"nev{c}")
        p.emit(q_st, lambda e, c=c, pa=pa: e.dma_start(
            out=io["num_hbm"][c, :, :], in_=outc32[pa][:]),
            reads=[f"outc32{pa}"], writes=["numw"], tag=f"stnum{c}")
        p.emit("pe", lambda e, c=c: e.matmul(
            PP[0:H, 1040:1041], totals[:, c * H:(c + 1) * H],
            cst["ones_1x128"][0:1, 0:1], start=True, stop=True),
            reads=["totals_r", "c_ones_1x128"], writes=["PP"], tag=f"tcol{c}")
        p.emit("pe", lambda e, c=c: e.matmul(
            PP[0:HP, 1042:1043], totals[0:1, c * H:(c + 1) * H:2],
            cst["ones_1x128"][0:1, 0:1], start=True, stop=True),
            reads=["totals_r", "c_ones_1x128"], writes=["PP"], tag=f"tcE{c}")
        p.emit("pe", lambda e, c=c: e.matmul(
            PP[0:HP, 1044:1045], totals[0:1, c * H + 1:(c + 1) * H:2],
            cst["ones_1x128"][0:1, 0:1], start=True, stop=True),
            reads=["totals_r", "c_ones_1x128"], writes=["PP"], tag=f"tcO{c}")
        p.emit("dve", lambda e: e.tensor_copy(tcol[:], PP[0:H, 1040:1041]),
               reads=["PP"], writes=["tcolS"], tag=f"tcolc{c}")
        p.emit("act", lambda e: e.activation(out=ecolE[:], in_=PP[0:HP, 1042:1043],
               func=AF.Exp), reads=["PP"], writes=["ecolE"], tag=f"ecE{c}")
        p.emit("act", lambda e: e.activation(out=ecolO[:], in_=PP[0:HP, 1044:1045],
               func=AF.Exp), reads=["PP"], writes=["ecolO"], tag=f"ecO{c}")
        p.emit("dve", lambda e, pa=pa: e.tensor_scalar(
            out=e2vhm32[:], in0=chm[pa][:], scalar1=tcol[:, 0:1], scalar2=-1.0,
            op0=ALU.subtract, op1=ALU.mult), reads=[kch, "tcolS"],
            writes=["e2vhm32"], tag=f"e2v32{c}")
        p.emit("act", lambda e: e.activation(out=e2vhm[:], in_=e2vhm32[:],
               func=AF.Exp), reads=["e2vhm32"], writes=["e2vhm"], tag=f"e2v{c}")
        for half in range(2):
            p.emit("pe", lambda e, half=half: e.matmul(
                PP[:, half * 512:(half + 1) * 512], e2vhm[:],
                cst["delta64b"][:, half * 512:(half + 1) * 512],
                start=True, stop=True),
                reads=["e2vhm", "c_delta64b"], writes=["PP"], tag=f"e2vb{c}_{half}")
        p.emit("act", lambda e: e.copy(e2vb[:], PP[:, 0:1024]),
               reads=["PP"], writes=["e2vb"], tag=f"e2vbc{c}")
        p.emit("dve", lambda e, pa=pa: e.tensor_tensor(
            out=vh[:], in0=vtm[pa][:], in1=e2vb[:], op=ALU.mult),
            reads=[kv, "e2vb"], writes=["vh"], tag=f"vh{c}")
        for h in range(H):
            b, j = h % 2, h // 2
            p.emit("pe", lambda e, b=b, j=j, h=h, pa=pa: e.matmul(
                SIv[b * 64:(b + 1) * 64, j * DH:(j + 1) * DH],
                ktm[pa][:, h * DH:(h + 1) * DH], vh[:, h * DH:(h + 1) * DH],
                start=True, stop=True, tile_position=(0, b * 64)),
                reads=[kkm, "vh"], writes=["PB"], tag=f"si{c}_{h}")
            p.emit("pe", lambda e, b=b, j=j, h=h, pa=pa: e.matmul(
                SIv[b * 64:(b + 1) * 64, 512 + j:513 + j],
                ktm[pa][:, h * DH:(h + 1) * DH],
                e2vb[:, h * DH:h * DH + 1],
                start=True, stop=True, tile_position=(0, b * 64)),
                reads=[kkm, "e2vb"], writes=["PB"], tag=f"siz{c}_{h}")
        p.emit("dve", lambda e: e.tensor_scalar(
            out=ehmE[:], in0=cst["ones_16x64"][0:HP, :],
            scalar1=ecolE[:, 0:1], scalar2=None, op0=ALU.mult),
            reads=["ecolE", "c_ones_16x64"], writes=["ehmE"], tag=f"ehE{c}")
        p.emit("dve", lambda e: e.tensor_scalar(
            out=ehmO[:], in0=cst["ones_16x64"][0:HP, :],
            scalar1=ecolO[:, 0:1], scalar2=None, op0=ALU.mult),
            reads=["ecolO", "c_ones_16x64"], writes=["ehmO"], tag=f"ehO{c}")
        for (ehm, kehm, pb, tp) in ((ehmE, "ehmE", 0, None), (ehmO, "ehmO", 64, (0, 64))):
            p.emit("pe", lambda e, ehm=ehm, pb=pb, tp=tp: e.matmul(
                PP[pb:pb + 64, 1024:1536], ehm[:], cst["deltaP"][:, 0:512],
                start=True, stop=True, tile_position=tp),
                reads=[kehm, "c_deltaP"], writes=["PP"], tag=f"etbS{c}_{pb}")
            p.emit("pe", lambda e, ehm=ehm, pb=pb, tp=tp: e.matmul(
                PP[pb:pb + 64, 1536:1536 + HP], ehm[:], cst["deltaP"][:, 512:512 + HP],
                start=True, stop=True, tile_position=tp),
                reads=[kehm, "c_deltaP"], writes=["PP"], tag=f"etbZ{c}_{pb}")
        p.emit("dve", lambda e: e.tensor_tensor(
            out=stmp[:], in0=S32[:], in1=PP[:, 1024:1024 + SCOLS],
            op=ALU.mult), reads=["S32", "PP"], writes=["stmp"], tag=f"sdec{c}")
        p.emit("dve", lambda e: e.tensor_tensor(
            out=S32[:], in0=stmp[:], in1=SIv, op=ALU.add),
            reads=["stmp", "PB"], writes=["S32"], tag=f"supd{c}")
        p.emit("act", lambda e: e.copy(Sb[:], S32[:]),
               reads=["S32"], writes=["Sb"], tag=f"sbc{c}")
    p.emit(q_st, lambda e: e.dma_start(out=io["Sfin"], in_=S32[:]),
           reads=["S32"], writes=["Sfinw"], tag="stSfin")


def _ehT_keys(pref, lo, hi, MACRO):
    return [f"{pref}ehT_{h}" for h in range(lo // MACRO,
                                            (hi + MACRO - 1) // MACRO)]


def emit_expert_block(p, nc, g, cst, pref, m, e, NF, actfn, xT_m, kxT,
                      w1sb, w2sb, q_lw1, q_lw2, io_w1, io_w2,
                      b1e, io_b1, b2e, io_b2, ehT_m, PSA, PSB, y, wgate,
                      kwg, D=1024):
    MACRO, ND, E = g.MACRO, g.ND, g.E
    NTM = MACRO // 128
    assert NTM * 512 * 2 == MACRO // 128 * 1024
    NHC = NF // 128
    NSL = NF // 512
    kb1f, kb2f = pref + "b1e", pref + "b2e"

    for sl in range(NSL):
        b = sl % 2
        kw1 = f"{pref}w1_{b}"
        p.emit(q_lw1, lambda e_, e=e, sl=sl, b=b: e_.dma_start(
            out=w1sb[b][:].rearrange("p (dc h) -> p dc h", dc=ND),
            in_=io_w1[e].rearrange("(dc p) h -> p dc h", p=128)[:, :, sl * 512:(sl + 1) * 512]),
            writes=[kw1], tag=f"{pref}lw1_{m}_{e}_{sl}")
        for hq in range(4):
            hc = sl * 4 + hq
            pb = hc % 2
            kpsa = f"{pref}PSA{pb}"
            for d in range(ND):
                for tsl in range(MACRO // 512):
                    p.emit("pe", lambda e_, d=d, hq=hq, tsl=tsl, b=b, pb=pb: e_.matmul(
                        PSA[:, pb * 1024 + tsl * 512: pb * 1024 + (tsl + 1) * 512],
                        w1sb[b][:, d * 512 + hq * 128: d * 512 + (hq + 1) * 128],
                        xT_m[:, d * MACRO + tsl * 512: d * MACRO + (tsl + 1) * 512],
                        start=(d == 0), stop=(d == ND - 1)),
                        reads=[kw1, kxT], writes=[kpsa], tag=f"{pref}eA{m}_{e}_{hc}")
            p.emit("act", lambda e_, hc=hc, pb=pb, kpsa=kpsa: e_.activation(
                out=ehT_m[:, hc * MACRO:(hc + 1) * MACRO],
                in_=PSA[:, pb * 1024: pb * 1024 + MACRO], func=actfn,
                bias=b1e[:, e * NHC + hc: e * NHC + hc + 1]),
                reads=[kpsa, kb1f], writes=[f"{pref}ehT_{hc}"],
                tag=f"{pref}eAs{m}_{e}_{hc}")
    ehkeys = [f"{pref}ehT_{hc}" for hc in range(NHC)]
    NHG = NHC // 4
    for dhalf in range(2):
        for hg in range(NHG):
            b = hg % 2
            kw2 = f"{pref}w2_{b}"
            p.emit(q_lw2, lambda e_, e=e, hg=hg, b=b, dhalf=dhalf: e_.dma_start(
                out=w2sb[b][:].rearrange("p (hc f) -> p hc f", hc=4),
                in_=io_w2[e][hg * 512:(hg + 1) * 512,
                             dhalf * 512:(dhalf + 1) * 512]
                .rearrange("(hc p) f -> p hc f", p=128)),
                writes=[kw2], tag=f"{pref}lw2_{m}_{e}_{dhalf}_{hg}")
            for hq in range(4):
                hc = hg * 4 + hq
                for i in range(NTM):
                    PSx = PSB if i < 4 else PSA
                    kps = (pref + ("PSB" if i < 4 else "PSA") + str((i % 4) // 2))
                    off = (i % 4) * 512
                    p.emit("pe", lambda e_, hc=hc, hq=hq, i=i, b=b, PSx=PSx, off=off: e_.matmul(
                        PSx[:, off:off + 512],
                        ehT_m[:, hc * MACRO + i * 128: hc * MACRO + (i + 1) * 128],
                        w2sb[b][:, hq * 512:(hq + 1) * 512], start=(hc == 0), stop=False),
                        reads=[kw2, ehkeys[hc]], writes=[kps],
                        tag=f"{pref}eB{m}_{e}_{dhalf}_{i}")
        for i in range(NTM):
            PSx = PSB if i < 4 else PSA
            kps = (pref + ("PSB" if i < 4 else "PSA") + str((i % 4) // 2))
            off = (i % 4) * 512
            p.emit("pe", lambda e_, e=e, i=i, dhalf=dhalf, PSx=PSx, off=off: e_.matmul(
                PSx[:, off:off + 512], cst["ones_2x128b"][:],
                b2e[0:2, e * D + dhalf * 512: e * D + (dhalf + 1) * 512],
                start=False, stop=True),
                reads=[kb2f, "c_ones_2x128b"], writes=[kps],
                tag=f"{pref}eBb{m}_{e}_{dhalf}_{i}")
            p.emit("dve", lambda e_, e=e, i=i, dhalf=dhalf, PSx=PSx, off=off: e_.scalar_tensor_tensor(
                out=y[:, i * D + dhalf * 512: i * D + (dhalf + 1) * 512],
                in0=PSx[:, off:off + 512],
                scalar=wgate[:, i * g.E + e: i * g.E + e + 1],
                in1=y[:, i * D + dhalf * 512: i * D + (dhalf + 1) * 512],
                op0=ALU.mult, op1=ALU.add),
                reads=[kps, kwg, f"y_{i}"], writes=[f"y_{i}"],
                tag=f"{pref}yacc{m}_{e}_{i}")


def phase_l0back(p, nc, es, g, io, cst):
    TH, D, ND, E, F0, MACRO = g.TH, g.D, g.ND, g.E, g.F0, g.MACRO
    NM, NTM = g.NMAC, MACRO // 128
    S = lambda nm, shp, dt=F32: es.enter_context(nc.sbuf_tensor("m0" + nm, shp, dt))
    PS = lambda nm, shp, dt=F32: es.enter_context(nc.psum_tensor("m0" + nm, shp, dt))
    q_lw = p.dma_queue("m0_lw", "gpsimd")
    q_lw1 = p.dma_queue("m0_lw1", "gpsimd")
    q_lw2 = p.dma_queue("m0_lw2", "gpsimd")
    p.dma_queue("m0_lb", "gpsimd")
    q_lx = p.dma_queue("m0_lx", "sync")
    q_lxT = p.dma_queue("m0_lxT", "sync")
    q_st = p.dma_queue("m0_st", "sync")

    wo = S("wo", [128, ND * D], BF16)
    rw1 = S("rw1", [128, ND * D], BF16)
    rw2 = S("rw2", [128, ND * E], BF16)
    for t, nm in ((wo, "Wo"), (rw1, "rw1")):
        p.emit(q_lw, lambda e, t=t, nm=nm: e.dma_start(
            out=t[:].rearrange("p (dc f) -> p dc f", dc=ND),
            in_=io[nm].rearrange("(dc p) f -> p dc f", p=128)),
            writes=[t.name], tag="ld" + nm)
    p.emit(q_lw, lambda e: e.dma_start(
        out=rw2[:].rearrange("p (dc f) -> p dc f", dc=ND),
        in_=io["rw2"].rearrange("(dc p) f -> p dc f", p=128)),
        writes=["m0rw2"], tag="ldrw2")
    rb1f = S("rb1f", [128, ND]); rb2f = S("rb2f", [1, E])
    NHC0 = F0 // 128
    rb1r = S("rb1r", [ND, 128])
    p.emit(q_lw, lambda e: e.dma_start(
        out=rb1r[:], in_=io["rb1"].rearrange("o (f p) -> (o f) p", p=128)),
        writes=["m0rb1r"], tag="ldrb1")
    p.emit(q_lw, lambda e: e.dma_start(out=rb2f[:], in_=io["rb2"]),
           writes=["m0rb2f"], tag="ldrb2")
    b1e = S("b1e", [128, E * NHC0])
    b2e = S("b2e", [2, E * D], BF16)

    xnT_m = S("xnT_m", [128, ND * MACRO], BF16)
    scr = S("scr", [128, ND * MACRO], BF16)
    oT_m = scr; rT_m = scr
    ehT_m = S("ehT_m", [128, (F0 // 128) * MACRO], BF16)
    y = S("y", [128, NTM * D])
    wg = S("wg", [128, NTM * E])
    lg = S("lg", [128, 24])
    xt = [S(f"xt{i}", [128, D]) for i in range(2)]
    xnt = [S(f"xnt{i}", [128, D]) for i in range(2)]
    w1sb = [S(f"w1s{i}", [128, ND * 512], BF16) for i in range(2)]
    w2sb = [S(f"w2s{i}", [128, 2048], BF16) for i in range(2)]
    PSA = PS("PSA", [128, 2048])
    PSB = PS("PSB", [128, 2048])

    b1r = ehT_m[0:NHC0, 0:2 * E * 128].bitcast(F32)
    kb1r = _ehT_keys("m0", 0, 2 * E * 128, MACRO)
    p.emit(q_lw, lambda e: e.dma_start(
        out=b1r[:].rearrange("hc (e p) -> hc e p", e=E),
        in_=io["eb1"].rearrange("o (e hc p) -> (o hc) e p", hc=NHC0, p=128)),
        writes=["m0b1r"] + kb1r, tag="ldb1r")
    for e in range(E):
        p.emit("pe", lambda e_, e=e: e_.transpose(
            PSA[:, 0:NHC0], b1r[:, e * 128:(e + 1) * 128],
            cst["ident128"][0:NHC0, 0:NHC0]),
            reads=["m0b1r", "c_ident128"] + kb1r,
            writes=["m0PSA0"], tag=f"b1t{e}")
        p.emit("act", lambda e_, e=e: e_.copy(
            b1e[:, e * NHC0:(e + 1) * NHC0], PSA[:, 0:NHC0]),
            reads=["m0PSA0"], writes=["m0b1e"], tag=f"b1tc{e}")
    p.emit("pe", lambda e: e.transpose(
        PSA[:, 0:ND], rb1r[:], cst["ident128"][0:ND, 0:ND]),
        reads=["m0rb1r", "c_ident128"], writes=["m0PSA0"], tag="rb1t")
    p.emit("act", lambda e: e.copy(rb1f[:], PSA[:, 0:ND]),
           reads=["m0PSA0"], writes=["m0rb1f"], tag="rb1tc")
    b2st = ehT_m[0:1, 2 * D:4 * D]
    b2r = ehT_m[0:1, 4 * D:6 * D].bitcast(F32)
    b2l32 = ehT_m[0:1, 6 * D:8 * D].bitcast(F32)
    kst = _ehT_keys("m0", 2 * D, 4 * D, MACRO)
    kr = _ehT_keys("m0", 4 * D, 6 * D, MACRO)
    kl = _ehT_keys("m0", 6 * D, 8 * D, MACRO)
    for e in range(E):
        p.emit(q_lw, lambda e_, e=e: e_.dma_start(
            out=b2r[:], in_=io["eb2"][0:1, e * D:(e + 1) * D]),
            writes=["m0b2r"] + kr, tag=f"lb2r{e}")
        p.emit("dve", lambda e_: e_.tensor_copy(b2st[0:1, 0:D], b2r[:]),
               reads=["m0b2r"], writes=["m0b2hi"] + kst, tag=f"b2h{e}")
        p.emit("dve", lambda e_: e_.tensor_tensor(
            out=b2l32[:], in0=b2r[:], in1=b2st[0:1, 0:D], op=ALU.subtract),
            reads=["m0b2hi", "m0b2r"],
            writes=["m0b2l32"] + kl, tag=f"b2s{e}")
        p.emit("dve", lambda e_: e_.tensor_copy(b2st[0:1, D:2 * D], b2l32[:]),
               reads=["m0b2l32"], writes=["m0b2lo"] + kst, tag=f"b2l{e}")
        p.emit(q_lw, lambda e_, e=e: e_.dma_start(
            out=b2e[:, e * D:(e + 1) * D],
            in_=b2st[0:1, :]),
            reads=["m0b2hi", "m0b2lo"], writes=["m0b2e"], tag=f"b2p{e}")

    for m in range(NM):
        p.emit(q_lxT, lambda e, m=m: e.dma_start(
            out=xnT_m[:].rearrange("p (j t) -> p j t", j=ND),
            in_=io["xnT"][:, :, m * MACRO:(m + 1) * MACRO].rearrange("j p t -> p j t")),
            writes=["kxnTm"], tag=f"lxnT{m}")
        p.emit(q_lxT, lambda e, m=m: e.dma_start(
            out=oT_m[:].rearrange("p (j t) -> p j t", j=ND),
            in_=io["outHT"][:, :, m * MACRO:(m + 1) * MACRO].rearrange("j p t -> p j t")),
            writes=["m0scr"], tag=f"loT{m}")
        for i in range(NTM):
            pb = i % 2
            kps = f"m0PSB{pb}"
            pa = i % 2
            p.emit(q_lx, lambda e, m=m, i=i, pa=pa: e.dma_start(
                out=xt[pa][:], in_=io["x"][m * MACRO + i * 128: m * MACRO + (i + 1) * 128, :]),
                writes=[f"xt{pa}"], tag=f"lx{m}_{i}")
            p.emit(q_lx, lambda e, m=m, i=i, pa=pa: e.dma_start(
                out=xnt[pa][:], in_=io["xn32"][m * MACRO + i * 128: m * MACRO + (i + 1) * 128, :]),
                writes=[f"xnt{pa}"], tag=f"lxn{m}_{i}")
            for j in range(ND):
                for half in range(2):
                    p.emit("pe", lambda e, i=i, j=j, half=half, pb=pb: e.matmul(
                        PSB[:, pb * 1024 + half * 512: pb * 1024 + (half + 1) * 512],
                        oT_m[:, j * MACRO + i * 128: j * MACRO + (i + 1) * 128],
                        wo[:, j * D + half * 512: j * D + (half + 1) * 512],
                        start=(j == 0), stop=(j == ND - 1)),
                        reads=["m0scr", "m0wo"], writes=[kps], tag=f"wo{m}_{i}")
            p.emit("dve", lambda e, i=i, pa=pa, pb=pb: e.tensor_tensor(
                out=y[:, i * D:(i + 1) * D], in0=PSB[:, pb * 1024:(pb + 1) * 1024],
                in1=xt[pa][:], op=ALU.add),
                reads=[kps, f"xt{pa}"], writes=[f"y_{i}"], tag=f"yi{m}_{i}")
            p.emit("dve", lambda e, i=i, pa=pa: e.tensor_tensor(
                out=y[:, i * D:(i + 1) * D], in0=y[:, i * D:(i + 1) * D],
                in1=xnt[pa][:], op=ALU.add),
                reads=[f"y_{i}", f"xnt{pa}"], writes=[f"y_{i}"], tag=f"yn{m}_{i}")
        for fout in range(ND):
            pb = fout % 2
            kps = f"m0PSA{pb}"
            for d in range(ND):
                for tsl in range(MACRO // 512):
                    p.emit("pe", lambda e, fout=fout, d=d, tsl=tsl, pb=pb: e.matmul(
                        PSA[:, pb * 1024 + tsl * 512: pb * 1024 + (tsl + 1) * 512],
                        rw1[:, d * D + fout * 128: d * D + (fout + 1) * 128],
                        xnT_m[:, d * MACRO + tsl * 512: d * MACRO + (tsl + 1) * 512],
                        start=(d == 0), stop=(d == ND - 1)),
                        reads=["m0rw1", "kxnTm"], writes=[kps], tag=f"rt{m}_{fout}")
            p.emit("act", lambda e, fout=fout, pb=pb: e.activation(
                out=rT_m[:, fout * MACRO:(fout + 1) * MACRO],
                in_=PSA[:, pb * 1024: pb * 1024 + MACRO], func=AF.Silu,
                bias=rb1f[:, fout:fout + 1]),
                reads=[kps, "m0rb1f"], writes=["m0scr"], tag=f"rts{m}_{fout}")
        for i in range(NTM):
            for j in range(ND):
                p.emit("pe", lambda e, i=i, j=j: e.matmul(
                    PSB[:, 0:E], rT_m[:, j * MACRO + i * 128: j * MACRO + (i + 1) * 128],
                    rw2[:, j * E:(j + 1) * E], start=(j == 0), stop=False),
                    reads=["m0scr", "m0rw2"], writes=["m0PSB0"], tag=f"lg{m}_{i}")
            p.emit("pe", lambda e, i=i: e.matmul(
                PSB[:, 0:E], cst["ones_1x128"][:], rb2f[:],
                start=False, stop=True),
                reads=["m0rb2f", "c_ones_1x128"], writes=["m0PSB0"], tag=f"lgb{m}_{i}")
            p.emit("dve", lambda e: e.tensor_copy(lg[:, 0:E], PSB[:, 0:E]),
                   reads=["m0PSB0"], writes=["lgv"], tag=f"lgc{m}_{i}")
            p.emit("dve", lambda e: e.reduce_max(lg[:, 8:9], lg[:, 0:E], axis=AX.X),
                   reads=["lgv"], writes=["lgm1"], tag=f"gm1{m}_{i}")
            p.emit("dve", lambda e: e.tensor_scalar(
                out=lg[:, 9:10], in0=lg[:, 8:9], scalar1=-1.0, scalar2=None,
                op0=ALU.mult), reads=["lgm1"], writes=["lgnm1"], tag=f"gn{m}_{i}")
            p.emit("dve", lambda e: e.tensor_scalar(
                out=lg[:, 10:10 + E], in0=lg[:, 0:E], scalar1=lg[:, 8:9],
                scalar2=None, op0=ALU.is_ge), reads=["lgv", "lgm1"],
                writes=["lgmask1"], tag=f"gmk{m}_{i}")
            p.emit("dve", lambda e: e.scalar_tensor_tensor(
                out=lg[:, 10:10 + E], in0=lg[:, 10:10 + E], scalar=-1e30,
                in1=lg[:, 0:E], op0=ALU.mult, op1=ALU.add),
                reads=["lgmask1", "lgv"], writes=["lgl2"], tag=f"gl2{m}_{i}")
            p.emit("dve", lambda e: e.reduce_max(lg[:, 9:10], lg[:, 10:10 + E],
                   axis=AX.X), reads=["lgl2"], writes=["lgm2"], tag=f"gm2{m}_{i}")
            p.emit("dve", lambda e, i=i: e.tensor_scalar(
                out=wg[:, i * E:(i + 1) * E], in0=lg[:, 0:E], scalar1=lg[:, 9:10],
                scalar2=None, op0=ALU.is_ge), reads=["lgv", "lgm2"],
                writes=["wmask2"], tag=f"gmk2{m}_{i}")
            p.emit("dve", lambda e: e.tensor_scalar(
                out=lg[:, 10:10 + E], in0=lg[:, 0:E], scalar1=lg[:, 8:9],
                scalar2=None, op0=ALU.subtract), reads=["lgv", "lgm1"],
                writes=["lgsub"], tag=f"gsb{m}_{i}")
            p.emit("act", lambda e: e.activation(out=lg[:, 10:10 + E],
                   in_=lg[:, 10:10 + E], func=AF.Exp), reads=["lgsub"],
                   writes=["lgexp"], tag=f"gex{m}_{i}")
            p.emit("dve", lambda e, i=i: e.tensor_tensor(
                out=wg[:, i * E:(i + 1) * E], in0=wg[:, i * E:(i + 1) * E],
                in1=lg[:, 10:10 + E], op=ALU.mult),
                reads=["wmask2", "lgexp"], writes=["wgew"], tag=f"gew{m}_{i}")
            p.emit("dve", lambda e, i=i: e.reduce_sum(
                lg[:, 8:9], wg[:, i * E:(i + 1) * E], axis=AX.X),
                reads=["wgew"], writes=["lgsum"], tag=f"gsm{m}_{i}")
            p.emit("dve", lambda e: e.reciprocal(lg[:, 9:10], lg[:, 8:9]),
                   reads=["lgsum"], writes=["lgrec"], tag=f"grc{m}_{i}")
            p.emit("dve", lambda e, i=i: e.tensor_scalar(
                out=wg[:, i * E:(i + 1) * E], in0=wg[:, i * E:(i + 1) * E],
                scalar1=lg[:, 9:10], scalar2=None, op0=ALU.mult),
                reads=["wgew", "lgrec"], writes=["kwg"], tag=f"gw{m}_{i}")
        for e in range(E):
            emit_expert_block(p, nc, g, cst, "m0", m, e, F0, AF.Silu,
                              xnT_m, "kxnTm", w1sb, w2sb, q_lw1, q_lw2,
                              io["ew1"], io["ew2"], b1e, io["eb1"],
                              b2e, io["eb2"], ehT_m, PSA, PSB, y, wg, "kwg")
        for i in range(NTM):
            p.emit(q_st, lambda e, m=m, i=i: e.dma_start(
                out=io["x1"][m * MACRO + i * 128: m * MACRO + (i + 1) * 128, :],
                in_=y[:, i * D:(i + 1) * D]),
                reads=[f"y_{i}"], writes=["x1w"], tag=f"sx1_{m}_{i}")


def phase_mid(p, nc, es, g, io, cst):
    TH, D, ND, NT = g.TH, g.D, g.ND, g.NT
    S = lambda nm, shp, dt=F32: es.enter_context(nc.sbuf_tensor("md" + nm, shp, dt))
    PS = lambda nm, shp, dt=F32: es.enter_context(nc.psum_tensor("md" + nm, shp, dt))
    q_lx = p.dma_queue("md_lx", "sync")
    q_st = p.dma_queue("md_st", "sync")
    q_stg = p.dma_queue("md_stg", "sync")

    x1n = S("x1n", [128, NT * D], BF16)
    xt = [S(f"xt{i}", [128, D]) for i in range(2)]
    xc = [S(f"xc{i}", [128, D]) for i in range(2)]
    stat = S("stat", [128, 8])
    ptr = PS("ptrT", [128, 128], BF16)
    PG = PS("PG", [128, 2048])
    x1nT_sb = S("x1nT_sb", [128, ND * NT * 128], BF16)
    gev = S("gev", [128, D])

    for i in range(NT):
        pa = i % 2
        kxt, kxc = f"xt{pa}", f"xc{pa}"
        p.emit(q_lx, lambda e, i=i, pa=pa: e.dma_start(
            out=xt[pa][:], in_=io["x1"][i * 128:(i + 1) * 128, :]),
            writes=[kxt], tag=f"lx1_{i}")
        p.emit("dve", lambda e, pa=pa: e.reduce_sum(
            stat[:, 0:1], xt[pa][:], axis=AX.X), reads=[kxt], writes=["st0"])
        p.emit("dve", lambda e: e.tensor_scalar(
            out=stat[:, 1:2], in0=stat[:, 0:1], scalar1=1.0 / D, scalar2=None,
            op0=ALU.mult), reads=["st0"], writes=["stm"])
        p.emit("dve", lambda e, pa=pa: e.tensor_scalar(
            out=xc[pa][:], in0=xt[pa][:], scalar1=stat[:, 1:2], scalar2=None,
            op0=ALU.subtract), reads=[kxt, "stm"], writes=[kxc])
        p.emit("act", lambda e, pa=pa: e.activation(
            out=xt[pa][:], in_=xc[pa][:], func=AF.Square),
            reads=[kxc], writes=[kxt])
        p.emit("dve", lambda e, pa=pa: e.reduce_sum(
            stat[:, 2:3], xt[pa][:], axis=AX.X), reads=[kxt], writes=["stv"])
        p.emit("dve", lambda e: e.tensor_scalar(
            out=stat[:, 5:6], in0=stat[:, 2:3], scalar1=1.0 / D, scalar2=1e-5,
            op0=ALU.mult, op1=ALU.add), reads=["stv"], writes=["stv2"])
        p.emit("act", lambda e: e.activation(
            out=stat[:, 3:4], in_=stat[:, 5:6], func=AF.Ln),
            reads=["stv2"], writes=["stl"])
        p.emit("act", lambda e: e.activation(
            out=stat[:, 4:5], in_=stat[:, 3:4], func=AF.Exp, scale=-0.5),
            reads=["stl"], writes=["str"])
        p.emit("dve", lambda e, i=i, pa=pa: e.tensor_scalar(
            out=x1n[:, i * D:(i + 1) * D], in0=xc[pa][:], scalar1=stat[:, 4:5],
            scalar2=None, op0=ALU.mult), reads=[kxc, "str"],
            writes=[f"x1n_{i}"], tag=f"x1n{i}")
        for j in range(ND):
            p.emit("pe", lambda e, i=i, j=j: e.transpose(
                ptr[:], x1n[:, i * D + j * 128: i * D + (j + 1) * 128],
                cst["identb"][:]), reads=[f"x1n_{i}", "c_identb"],
                writes=["ptrT"], tag=f"tr{i}_{j}")
            p.emit("dve", lambda e, i=i, j=j: e.tensor_copy(
                x1nT_sb[:, (j * NT + i) * 128:(j * NT + i + 1) * 128], ptr[:]),
                reads=["ptrT"], writes=[f"x1nT_{j}_{i}"], tag=f"trc{i}_{j}")
    x1nkeys = [f"x1n_{i}" for i in range(NT)]
    for f1 in range(ND):
        pb = f1 % 2
        for i in range(NT):
            for half in range(2):
                p.emit("pe", lambda e, f1=f1, i=i, half=half, pb=pb: e.matmul(
                    PG[:, pb * 1024 + half * 512: pb * 1024 + (half + 1) * 512],
                    x1n[:, i * D + f1 * 128: i * D + (f1 + 1) * 128],
                    x1n[:, i * D + half * 512: i * D + (half + 1) * 512],
                    start=(i == 0), stop=(i == NT - 1)),
                    reads=x1nkeys, writes=["PG"], tag=f"g{f1}_{i}")
        p.emit("dve", lambda e, f1=f1, pb=pb: e.tensor_copy(
            gev[:], PG[:, pb * 1024:(pb + 1) * 1024]),
            reads=["PG"], writes=["gev"], tag=f"gev{f1}")
        p.emit(q_stg, lambda e, f1=f1: e.dma_start(
            out=io["G_half"][:, f1 * D:(f1 + 1) * D], in_=gev[:]),
            reads=["gev"], writes=["Gw"], tag=f"stG{f1}")
    tkeys = [f"x1nT_{j}_{i}" for j in range(ND) for i in range(NT)]
    for j in range(ND):
        p.emit(q_st, lambda e, j=j: e.dma_start(
            out=io["x1nT"][j, :, :],
            in_=x1nT_sb[:, j * NT * 128:(j + 1) * NT * 128]),
            reads=tkeys, writes=["x1nTw"], tag=f"strT{j}")
    if "x1nb" in io:
        p.emit(q_st, lambda e: e.dma_start(
            out=io["x1nb"].rearrange("(i p) d -> p i d", p=128),
            in_=x1n[:].rearrange("p (i d) -> p i d", i=NT)),
            reads=[f"x1n_{i}" for i in range(NT)], writes=["x1nbw"],
            tag="stx1nb")


def phase_l1back(p, nc, es, g, io, cst):
    TH, D, ND, E, F1, MACRO = g.TH, g.D, g.ND, g.E, g.F1, g.MACRO
    NM, NTM = g.NMAC, MACRO // 128
    S = lambda nm, shp, dt=F32: es.enter_context(nc.sbuf_tensor("m1" + nm, shp, dt))
    PS = lambda nm, shp, dt=F32: es.enter_context(nc.psum_tensor("m1" + nm, shp, dt))
    q_lw = p.dma_queue("m1_lw", "gpsimd")
    q_lw1 = p.dma_queue("m1_lw1", "gpsimd")
    q_lw2 = p.dma_queue("m1_lw2", "gpsimd")
    p.dma_queue("m1_lb", "gpsimd")
    q_lx = p.dma_queue("m1_lx", "sync")
    q_lxT = p.dma_queue("m1_lxT", "sync")
    q_st = p.dma_queue("m1_st", "sync")

    PSA = PS("PSA", [128, 2048])
    PSB = PS("PSB", [128, 2048])
    Msb = S("Msb", [128, ND * D], BF16)
    es_m = ExitStack()
    Sm = lambda nm, shp, dt=F32: es_m.enter_context(nc.sbuf_tensor("m1" + nm, shp, dt))
    gta = [Sm(f"gta{i}", [128, D]) for i in range(2)]
    gtb = [Sm(f"gtb{i}", [128, D]) for i in range(2)]
    gb16 = Sm("gb16", [128, ND * D], BF16)
    for f1 in range(ND):
        pa = f1 % 2
        p.emit(q_lx, lambda e, f1=f1, pa=pa: e.dma_start(
            out=gta[pa][:], in_=io["Ga"][:, f1 * D:(f1 + 1) * D]),
            reads=["Gw"], writes=[f"gta{pa}"], tag=f"lGa{f1}")
        p.emit(q_lxT, lambda e, f1=f1, pa=pa: e.dma_start(
            out=gtb[pa][:], in_=io["Gb"][:, f1 * D:(f1 + 1) * D]),
            reads=["Gbw"], writes=[f"gtb{pa}"], tag=f"lGb{f1}")
        p.emit("dve", lambda e, f1=f1, pa=pa: e.tensor_tensor(
            out=gb16[:, f1 * D:(f1 + 1) * D], in0=gta[pa][:], in1=gtb[pa][:],
            op=ALU.add), reads=[f"gta{pa}", f"gtb{pa}"], writes=["gb16"],
            tag=f"gc{f1}")
    wq1 = Sm("wq1", [128, ND * D], BF16)
    p.emit(q_lw, lambda e: e.dma_start(
        out=wq1[:].rearrange("p (dc f) -> p dc f", dc=ND),
        in_=io["W_q"].rearrange("(dc p) f -> p dc f", p=128)),
        writes=["wq1"], tag="ldWq1")
    for hc in range(ND):
        pb = hc % 2
        for oc in range(ND):
            for half in range(2):
                p.emit("pe", lambda e, hc=hc, oc=oc, half=half, pb=pb: e.matmul(
                    PSA[:, pb * 1024 + half * 512: pb * 1024 + (half + 1) * 512],
                    wq1[:, oc * D + hc * 128: oc * D + (hc + 1) * 128],
                    gb16[:, oc * D + half * 512: oc * D + (half + 1) * 512],
                    start=(oc == 0), stop=(oc == ND - 1)),
                    reads=["wq1", "gb16"], writes=[f"m1PSA{hc % 2}"], tag=f"M{hc}_{oc}")
        p.emit("dve", lambda e, hc=hc, pb=pb: e.tensor_copy(
            Msb[:, hc * D:(hc + 1) * D], PSA[:, pb * 1024:(pb + 1) * 1024]),
            reads=[f"m1PSA{pb}"], writes=["Msb"], tag=f"Mev{hc}")
    p.barrier_all("m1_Mdone")
    es_m.close()
    gwsb = S("gwsb", [E, D], BF16)
    gwT = S("gwT", [128, ND * E], BF16)
    NHC1 = F1 // 128
    b1e = S("b1e", [128, E * NHC1])
    b2e = S("b2e", [2, E * D], BF16)
    xT_m = S("xT_m", [128, ND * MACRO], BF16)
    ehT_m = S("ehT_m", [128, (F1 // 128) * MACRO], BF16)
    y = S("y", [128, NTM * D])
    wg = S("wg", [128, NTM * E])
    lg = S("lg", [128, 24])
    xt = [S(f"xt{i}", [128, D]) for i in range(2)]
    w1sb = [S(f"w1s{i}", [128, ND * 512], BF16) for i in range(2)]
    w2sb = [S(f"w2s{i}", [128, 2048], BF16) for i in range(2)]
    gwp = PSA[:, 0:4].bitcast(BF16)
    p.emit(q_lw, lambda e: e.dma_start(out=gwsb[:], in_=io["gate_w"]),
           writes=["gwsb"], tag="ldgw")
    for j in range(ND):
        p.emit("pe", lambda e, j=j: e.transpose(
            gwp, gwsb[:, j * 128:(j + 1) * 128], cst["identb"][0:E, 0:E]),
            reads=["gwsb", "c_identb"], writes=["m1PSA0"], tag=f"gwt{j}")
        p.emit("dve", lambda e, j=j: e.tensor_copy(
            gwT[:, j * E:(j + 1) * E], gwp), reads=["m1PSA0"],
            writes=["gwT"], tag=f"gwc{j}")
    b1r = ehT_m[0:NHC1, 0:2 * E * 128].bitcast(F32)
    kb1r = _ehT_keys("m1", 0, 2 * E * 128, MACRO)
    p.emit(q_lw, lambda e: e.dma_start(
        out=b1r[:].rearrange("hc (e p) -> hc e p", e=E),
        in_=io["b1"].rearrange("o (e hc p) -> (o hc) e p", hc=NHC1, p=128)),
        writes=["m1b1r"] + kb1r, tag="ldb1r")
    for e in range(E):
        p.emit("pe", lambda e_, e=e: e_.transpose(
            PSA[:, 0:NHC1], b1r[:, e * 128:(e + 1) * 128],
            cst["ident128"][0:NHC1, 0:NHC1]),
            reads=["m1b1r", "c_ident128"] + kb1r,
            writes=["m1PSA0"], tag=f"b1t{e}")
        p.emit("act", lambda e_, e=e: e_.copy(
            b1e[:, e * NHC1:(e + 1) * NHC1], PSA[:, 0:NHC1]),
            reads=["m1PSA0"], writes=["m1b1e"], tag=f"b1tc{e}")
    b2st = ehT_m[0:1, 2 * D:4 * D]
    b2r = ehT_m[0:1, 4 * D:6 * D].bitcast(F32)
    b2l32 = ehT_m[0:1, 6 * D:8 * D].bitcast(F32)
    kst = _ehT_keys("m1", 2 * D, 4 * D, MACRO)
    kr = _ehT_keys("m1", 4 * D, 6 * D, MACRO)
    kl = _ehT_keys("m1", 6 * D, 8 * D, MACRO)
    for e in range(E):
        p.emit(q_lw, lambda e_, e=e: e_.dma_start(
            out=b2r[:], in_=io["b2"][0:1, e * D:(e + 1) * D]),
            writes=["m1b2r"] + kr, tag=f"lb2r{e}")
        p.emit("dve", lambda e_: e_.tensor_copy(b2st[0:1, 0:D], b2r[:]),
               reads=["m1b2r"], writes=["m1b2hi"] + kst, tag=f"b2h{e}")
        p.emit("dve", lambda e_: e_.tensor_tensor(
            out=b2l32[:], in0=b2r[:], in1=b2st[0:1, 0:D], op=ALU.subtract),
            reads=["m1b2hi", "m1b2r"],
            writes=["m1b2l32"] + kl, tag=f"b2s{e}")
        p.emit("dve", lambda e_: e_.tensor_copy(b2st[0:1, D:2 * D], b2l32[:]),
               reads=["m1b2l32"], writes=["m1b2lo"] + kst, tag=f"b2l{e}")
        p.emit(q_lw, lambda e_, e=e: e_.dma_start(
            out=b2e[:, e * D:(e + 1) * D],
            in_=b2st[0:1, :]),
            reads=["m1b2hi", "m1b2lo"], writes=["m1b2e"], tag=f"b2p{e}")
    for m in range(NM):
        p.emit(q_lxT, lambda e, m=m: e.dma_start(
            out=xT_m[:].rearrange("p (j t) -> p j t", j=ND),
            in_=io["x1nT"][:, :, m * MACRO:(m + 1) * MACRO].rearrange("j p t -> p j t")),
            writes=["kxTm"], tag=f"lxT{m}")
        for i in range(NTM):
            pb = i % 2
            kpsb = f"m1PSB{pb}"
            pa = i % 2
            p.emit(q_lx, lambda e, m=m, i=i, pa=pa: e.dma_start(
                out=xt[pa][:], in_=io["x1"][m * MACRO + i * 128: m * MACRO + (i + 1) * 128, :]),
                writes=[f"xt{pa}"], tag=f"lx1b{m}_{i}")
            for j in range(ND):
                for half in range(2):
                    p.emit("pe", lambda e, i=i, j=j, half=half, pb=pb: e.matmul(
                        PSB[:, pb * 1024 + half * 512: pb * 1024 + (half + 1) * 512],
                        xT_m[:, j * MACRO + i * 128: j * MACRO + (i + 1) * 128],
                        Msb[:, j * D + half * 512: j * D + (half + 1) * 512],
                        start=(j == 0), stop=(j == ND - 1)),
                        reads=["kxTm", "Msb"], writes=[kpsb], tag=f"at{m}_{i}")
            p.emit("dve", lambda e, i=i, pa=pa, pb=pb: e.tensor_tensor(
                out=y[:, i * D:(i + 1) * D], in0=PSB[:, pb * 1024:(pb + 1) * 1024],
                in1=xt[pa][:], op=ALU.add),
                reads=[kpsb, f"xt{pa}"], writes=[f"y_{i}"], tag=f"yi1{m}_{i}")
        for i in range(NTM):
            for j in range(ND):
                p.emit("pe", lambda e, i=i, j=j: e.matmul(
                    PSA[:, 0:E], xT_m[:, j * MACRO + i * 128: j * MACRO + (i + 1) * 128],
                    gwT[:, j * E:(j + 1) * E], start=(j == 0), stop=(j == ND - 1)),
                    reads=["kxTm", "gwT"], writes=["m1PSA0"], tag=f"gl{m}_{i}")
            p.emit("dve", lambda e: e.tensor_copy(lg[:, 0:E], PSA[:, 0:E]),
                   reads=["m1PSA0"], writes=["lgv"], tag=f"glc{m}_{i}")
            p.emit("dve", lambda e: e.reduce_max(lg[:, 8:9], lg[:, 0:E], axis=AX.X),
                   reads=["lgv"], writes=["lgm1"], tag=f"g1m{m}_{i}")
            p.emit("dve", lambda e, i=i: e.tensor_scalar(
                out=wg[:, i * E:(i + 1) * E], in0=lg[:, 0:E], scalar1=lg[:, 8:9],
                scalar2=None, op0=ALU.is_ge), reads=["lgv", "lgm1"],
                writes=["wmask"], tag=f"g1k{m}_{i}")
            p.emit("dve", lambda e: e.tensor_scalar(
                out=lg[:, 10:10 + E], in0=lg[:, 0:E], scalar1=lg[:, 8:9],
                scalar2=None, op0=ALU.subtract), reads=["lgv", "lgm1"],
                writes=["lgsub"], tag=f"g1s{m}_{i}")
            p.emit("act", lambda e: e.activation(out=lg[:, 10:10 + E],
                   in_=lg[:, 10:10 + E], func=AF.Exp), reads=["lgsub"],
                   writes=["lgexp"], tag=f"g1e{m}_{i}")
            p.emit("dve", lambda e: e.reduce_sum(lg[:, 8:9], lg[:, 10:10 + E],
                   axis=AX.X), reads=["lgexp"], writes=["lgsum"], tag=f"g1t{m}_{i}")
            p.emit("dve", lambda e: e.reciprocal(lg[:, 9:10], lg[:, 8:9]),
                   reads=["lgsum"], writes=["lgrec"], tag=f"g1r{m}_{i}")
            p.emit("dve", lambda e, i=i: e.tensor_scalar(
                out=wg[:, i * E:(i + 1) * E], in0=wg[:, i * E:(i + 1) * E],
                scalar1=lg[:, 9:10], scalar2=None, op0=ALU.mult),
                reads=["wmask", "lgrec"], writes=["kwg"], tag=f"g1w{m}_{i}")
        for e in range(E):
            emit_expert_block(p, nc, g, cst, "m1", m, e, F1, AF.Relu,
                              xT_m, "kxTm", w1sb, w2sb, q_lw1, q_lw2,
                              io["w1"], io["w2"], b1e, io["b1"],
                              b2e, io["b2"], ehT_m, PSA, PSB, y, wg, "kwg")
        for i in range(NTM):
            p.emit(q_st, lambda e, m=m, i=i: e.dma_start(
                out=io["out"][m * MACRO + i * 128: m * MACRO + (i + 1) * 128, :],
                in_=y[:, i * D:(i + 1) * D]),
                reads=[f"y_{i}"], writes=["outw"], tag=f"sout{m}_{i}")



GEO = Geo()
CONST_WANT = ["identb", "ident128", "U128", "ones_1x128", "ones_128x1",
              "ones_1x512", "maskU_rep", "negIexp", "eps_1x512", "delta64",
              "deltaP", "ones_16x64", "ones_128x1b", "ones_2x128b"]
CONST_BASE = ["identb", "ident128", "U128", "ones_1x128", "ones_128x1",
              "ones_1x512", "eps_1x512", "ones_128x1b", "ones_2x128b",
              "U128strict", "eCbase0", "eCbase1", "caprow0", "caprow1"]
CONST_SCAN = ["maskU_rep", "negIexp2b", "delta64b", "deltaP", "ones_16x64",
              "eps_1x512b"]


def _declare(nc, specs):
    io = {}
    for nm, shp, dt, kind in specs:
        io[nm] = nc.dram_tensor(nm, shp, dt, kind=kind).ap()
    return io


def _const_specs(consts):
    return [("c_" + nm, list(a.shape), F32, "ExternalInput")
            for nm, a in consts.items()]


def _build_launch(phase_fns, specs, consts, want=CONST_WANT):
    nc = bass.Bass("TRN2", target_bir_lowering=False)
    io = _declare(nc, specs + _const_specs(consts))
    es = ExitStack()
    with es:
        p = Prog(nc)
        cst = load_consts(p, nc, es, GEO, io, want, consts)
        for fn in phase_fns:
            with ExitStack() as es_p:
                fn(p, nc, es_p, GEO, io, cst)
                p.barrier_all("phase_end")
        block = es.enter_context(nc.Block())
        p.build(block, es=es)
    return nc


def kernel_unfused(L0_attn_x, L0_attn_Wq, L0_attn_Wk, L0_attn_Wv, L0_attn_Wo,
           L0_attn_Wa, L0_attn_ba, L0_ffn_rw1, L0_ffn_rb1, L0_ffn_rw2,
           L0_ffn_rb2, L0_ffn_ew1, L0_ffn_eb1, L0_ffn_ew2, L0_ffn_eb2,
           L1_attn_W_q, L1_ffn_gate_w, L1_ffn_w1, L1_ffn_b1, L1_ffn_w2,
           L1_ffn_b2):
    g = GEO
    TH, D, ND, E, F0, F1, H, NCH = (g.TH, g.D, g.ND, g.E, g.F0, g.F1, g.H,
                                    g.NCH)
    f32 = lambda a: np.ascontiguousarray(np.asarray(a, dtype=np.float32))
    x_full = f32(L0_attn_x)
    consts = build_consts(g)
    shards = []
    for c in range(8):
        b, half = c // 2, c % 2
        shards.append(f32(x_full[b, half * TH:(half + 1) * TH, :]))
    flat = lambda a: f32(a).reshape(1, -1)
    Wq, Wk, Wv, Wo = map(f32, (L0_attn_Wq, L0_attn_Wk, L0_attn_Wv, L0_attn_Wo))
    Wa = f32(L0_attn_Wa)
    ba = flat(L0_attn_ba)
    rw1, rw2 = f32(L0_ffn_rw1), f32(L0_ffn_rw2)
    rb1, rb2 = flat(L0_ffn_rb1), flat(L0_ffn_rb2)
    ew1, ew2 = f32(L0_ffn_ew1), f32(L0_ffn_ew2)
    eb1, eb2 = flat(L0_ffn_eb1), flat(L0_ffn_eb2)
    Wq1 = f32(L1_attn_W_q)
    gate_w = f32(L1_ffn_gate_w)
    w1, w2 = f32(L1_ffn_w1), f32(L1_ffn_w2)
    b1, b2 = flat(L1_ffn_b1), flat(L1_ffn_b2)
    cmaps = {"c_" + nm: a for nm, a in consts.items()}

    EI, EO = "ExternalInput", "ExternalOutput"
    specsA = [("x", [TH, D], F32, EI), ("Wq", [D, D], F32, EI),
              ("Wk", [D, D], F32, EI), ("Wv", [D, D], F32, EI),
              ("Wa", [D, H], F32, EI), ("ba", [1, H], F32, EI),
              ("xn32", [TH, D], F32, EO), ("xnT", [ND, 128, TH], BF16, EO),
              ("QT", [ND, 128, TH], BF16, EO), ("KT", [ND, 128, TH], BF16, EO),
              ("Ktm", [TH, D], BF16, EO), ("V", [TH, D], BF16, EO),
              ("cum_nat", [H, TH], F32, EO),
              ("cumtm", [128, H * NCH], F32, EO),
              ("totals", [1, H * NCH], F32, EO)]
    ncA = _build_launch([phase_front], specsA, consts)
    imA = [dict(x=shards[c], Wq=Wq, Wk=Wk, Wv=Wv, Wa=Wa, ba=ba, **cmaps)
           for c in range(8)]
    resA = run_bass_kernel_spmd(ncA, imA, core_ids=list(range(8))).results

    specsB = [("QT", [ND, 128, TH], BF16, EI), ("KT", [ND, 128, TH], BF16, EI),
              ("Ktm", [TH, D], BF16, EI), ("V", [TH, D], BF16, EI),
              ("cum_nat", [H, TH], F32, EI),
              ("cumtm", [128, H * NCH], F32, EI),
              ("totals", [1, H * NCH], F32, EI),
              ("S0", [128, 520], F32, EI),
              ("outHT", [ND, 128, TH], BF16, EO),
              ("Sfin", [128, 520], F32, EO)]
    ncB = _build_launch([phase_scan], specsB, consts)
    z520 = np.zeros((128, 520), np.float32)
    def imB(c, s0):
        r = resA[c]
        return dict(QT=r["QT"], KT=r["KT"], Ktm=r["Ktm"], V=r["V"],
                    cum_nat=r["cum_nat"], cumtm=r["cumtm"],
                    totals=r["totals"], S0=s0, **cmaps)
    resB1 = run_bass_kernel_spmd(
        ncB, [imB(c, z520) for c in (0, 2, 4, 6)], core_ids=[0, 2, 4, 6]).results
    resB2 = run_bass_kernel_spmd(
        ncB, [imB(c, resB1[i]["Sfin"]) for i, c in enumerate((1, 3, 5, 7))],
        core_ids=[1, 3, 5, 7]).results
    outHT = [None] * 8
    Sfin = [None] * 8
    for i, c in enumerate((0, 2, 4, 6)):
        outHT[c] = resB1[i]["outHT"]
    for i, c in enumerate((1, 3, 5, 7)):
        outHT[c] = resB2[i]["outHT"]

    specsC = [("x", [TH, D], F32, EI), ("xn32", [TH, D], F32, EI),
              ("xnT", [ND, 128, TH], BF16, EI),
              ("outHT", [ND, 128, TH], BF16, EI),
              ("Wo", [D, D], F32, EI), ("rw1", [D, D], F32, EI),
              ("rw2", [D, E], F32, EI), ("rb1", [1, D], F32, EI),
              ("rb2", [1, E], F32, EI), ("ew1", [E, D, F0], F32, EI),
              ("eb1", [1, E * F0], F32, EI), ("ew2", [E, F0, D], F32, EI),
              ("eb2", [1, E * D], F32, EI), ("x1", [TH, D], F32, EO)]
    ncC = _build_launch([phase_l0back], specsC, consts)
    imC = [dict(x=shards[c], xn32=resA[c]["xn32"], xnT=resA[c]["xnT"],
                outHT=outHT[c], Wo=Wo, rw1=rw1, rw2=rw2, rb1=rb1, rb2=rb2,
                ew1=ew1, eb1=eb1, ew2=ew2, eb2=eb2, **cmaps)
           for c in range(8)]
    resC = run_bass_kernel_spmd(ncC, imC, core_ids=list(range(8))).results

    specsD = [("x1", [TH, D], F32, EI),
              ("x1nT", [ND, 128, TH], BF16, EO),
              ("G_half", [128, ND * D], F32, EO)]
    ncD = _build_launch([phase_mid], specsD, consts)
    imD = [dict(x1=resC[c]["x1"], **cmaps) for c in range(8)]
    resD = run_bass_kernel_spmd(ncD, imD, core_ids=list(range(8))).results

    specsE = [("x1", [TH, D], F32, EI), ("x1nT", [ND, 128, TH], BF16, EI),
              ("Ga", [128, ND * D], F32, EI), ("Gb", [128, ND * D], F32, EI),
              ("W_q", [D, D], F32, EI), ("gate_w", [E, D], F32, EI),
              ("w1", [E, D, F1], F32, EI), ("b1", [1, E * F1], F32, EI),
              ("w2", [E, F1, D], F32, EI), ("b2", [1, E * D], F32, EI),
              ("out", [TH, D], F32, EO)]
    ncE = _build_launch([phase_l1back], specsE, consts)
    imE = [dict(x1=resC[c]["x1"], x1nT=resD[c]["x1nT"],
                Ga=resD[c]["G_half"], Gb=resD[c ^ 1]["G_half"],
                W_q=Wq1, gate_w=gate_w, w1=w1, b1=b1, w2=w2, b2=b2, **cmaps)
           for c in range(8)]
    resE = run_bass_kernel_spmd(ncE, imE, core_ids=list(range(8))).results

    out = np.empty((4, 8192, 1024), np.float32)
    for c in range(8):
        b, half = c // 2, c % 2
        out[b, half * TH:(half + 1) * TH, :] = resE[c]["out"]
    return out


def make_exchange(p, nc, es, g, io):
    D, ND = g.D, g.ND
    SH_S = nc.dram_tensor("SH_S", [128, 520], F32, kind="Internal",
                          addr_space="Shared").ap()
    SH_G = nc.dram_tensor("SH_G", [2, 128, ND * D], F32, kind="Internal",
                          addr_space="Shared").ap()
    SH_F = nc.dram_tensor("SH_F", [4, 16], I32, kind="Internal",
                          addr_space="Shared").ap()
    tok = es.enter_context(nc.sbuf_tensor("ex_tok", [1, 16], I32))
    fb = es.enter_context(nc.sbuf_tensor("ex_fb", [1, 16], I32))
    sbounce = es.enter_context(nc.sbuf_tensor("ex_sb", [128, 520], F32))
    gbounce = es.enter_context(nc.sbuf_tensor("ex_gb", [128, 512], F32))
    exd1 = es.enter_context(nc.semaphore("exd1"))
    exp1 = es.enter_context(nc.semaphore("exp1"))
    exd2 = es.enter_context(nc.semaphore("exd2"))
    exd3 = es.enter_context(nc.semaphore("exd3"))
    exp3 = es.enter_context(nc.semaphore("exp3"))

    def poll(e, flag_row, psem, sfx=""):
        with e.register("nd" + sfx) as nd, e.register("f" + sfx) as f, e.register("tk" + sfx) as tk, e.register("cnt" + sfx) as cnt:
            e.reg_load(tk, tok[0:1, 0:1])
            e.reg_mov(nd, 1)
            e.reg_mov(cnt, 0)
            with e.While(nd):
                e.reg_alu(cnt, cnt, 16, RALU.add)
                e.dma_start(out=fb[:], in_=SH_F[flag_row:flag_row + 1, :]).then_inc(psem, 16)
                e.wait_ge(psem, cnt)
                e.reg_load(f, fb[0:1, 0:1])
                e.reg_alu(nd, f, tk, RALU.subtract)

    def m1_fn(e):
        e.dma_start(out=tok[:], in_=io["token"]).then_inc(exd1, 16)
        e.wait_ge(exd1, 16)
        par = e.partition_id() % 2
        with e.If(par == 0):
            e.memset(sbounce[:], 0.0)
            e.dma_start(out=io["S0"], in_=sbounce[:]).then_inc(exd1, 16)
            e.wait_ge(exd1, 32)
        with e.Else():
            poll(e, 0, exp1, "m1")
            e.dma_start(out=sbounce[:], in_=SH_S).then_inc(exd1, 16)
            e.wait_ge(exd1, 32)
            e.dma_start(out=io["S0"], in_=sbounce[:]).then_inc(exd1, 16)
            e.wait_ge(exd1, 48)
        return e.engine_nop()

    def m2_fn(e):
        e.dma_start(out=tok[:], in_=io["token"]).then_inc(exd2, 16)
        e.wait_ge(exd2, 16)
        par = e.partition_id() % 2
        with e.If(par == 0):
            e.dma_start(out=sbounce[:], in_=io["Sfin"]).then_inc(exd2, 16)
            e.wait_ge(exd2, 32)
            e.dma_start(out=SH_S, in_=sbounce[:]).then_inc(exd2, 16)
            e.wait_ge(exd2, 48)
            e.dma_start(out=SH_F[0:1, :], in_=tok[:]).then_inc(exd2, 16)
            e.wait_ge(exd2, 64)
        return e.engine_nop()

    def gx_fn(e):
        par = e.partition_id() % 2
        NCK = (ND * D) // 512
        for slot in range(2):
            with (e.If(par == 0) if slot == 0 else e.Else()):
                v = 0
                for k in range(NCK):
                    e.dma_start(out=gbounce[:],
                                in_=io["G_half"][:, k * 512:(k + 1) * 512]).then_inc(exd3, 16)
                    v += 16
                    e.wait_ge(exd3, v)
                    e.dma_start(out=SH_G[slot, :, k * 512:(k + 1) * 512],
                                in_=gbounce[:]).then_inc(exd3, 16)
                    v += 16
                    e.wait_ge(exd3, v)
                e.dma_start(out=SH_F[1 + slot:2 + slot, :], in_=tok[:]).then_inc(exd3, 16)
                v += 16
                e.wait_ge(exd3, v)
                poll(e, 2 - slot, exp3, f"g{slot}")
                for k in range(NCK):
                    e.dma_start(out=gbounce[:],
                                in_=SH_G[1 - slot, :, k * 512:(k + 1) * 512]).then_inc(exd3, 16)
                    v += 16
                    e.wait_ge(exd3, v)
                    e.dma_start(out=io["Gb"][:, k * 512:(k + 1) * 512],
                                in_=gbounce[:]).then_inc(exd3, 16)
                    v += 16
                    e.wait_ge(exd3, v)
        return e.engine_nop()

    def m1(p_):
        p_.emit("pool", m1_fn, reads=["ex"], writes=["S0w", "ex"], tag="m1x")

    def m2(p_):
        p_.emit("pool", m2_fn, reads=["Sfinw", "ex"], writes=["ex"], tag="m2x")

    def gx(p_):
        p_.emit("pool", gx_fn, reads=["Gw", "ex"], writes=["Gbw", "ex"], tag="gxx")

    return m1, m2, gx


def kernel_fused(inp):
    g = GEO
    TH, D, ND, E, F0, F1, H, NCH = (g.TH, g.D, g.ND, g.E, g.F0, g.F1, g.H,
                                    g.NCH)
    f32 = lambda a: np.ascontiguousarray(np.asarray(a, dtype=np.float32))
    flat = lambda a: f32(a).reshape(1, -1)
    consts = build_consts(g)
    x_full = f32(inp["L0_attn_x"])
    shards = [f32(x_full[c // 2, (c % 2) * TH:(c % 2 + 1) * TH, :])
              for c in range(8)]
    token = np.full((1, 16), secrets.randbelow(2**31 - 2) + 1, np.int32)

    EI, EO, IN = "ExternalInput", "ExternalOutput", "Internal"
    specs = [
        ("x", [TH, D], F32, EI), ("Wq", [D, D], F32, EI),
        ("Wk", [D, D], F32, EI), ("Wv", [D, D], F32, EI),
        ("Wa", [D, H], F32, EI), ("ba", [1, H], F32, EI),
        ("Wo", [D, D], F32, EI), ("rw1", [D, D], F32, EI),
        ("rw2", [D, E], F32, EI), ("rb1", [1, D], F32, EI),
        ("rb2", [1, E], F32, EI), ("ew1", [E, D, F0], F32, EI),
        ("eb1", [1, E * F0], F32, EI), ("ew2", [E, F0, D], F32, EI),
        ("eb2", [1, E * D], F32, EI), ("W_q", [D, D], F32, EI),
        ("gate_w", [E, D], F32, EI), ("w1", [E, D, F1], F32, EI),
        ("b1", [1, E * F1], F32, EI), ("w2", [E, F1, D], F32, EI),
        ("b2", [1, E * D], F32, EI), ("token", [1, 16], I32, EI),
        ("xn32", [TH, D], F32, IN), ("xnT", [ND, 128, TH], BF16, IN),
        ("QT", [ND, 128, TH], BF16, IN), ("KT", [ND, 128, TH], BF16, IN),
        ("Ktm", [TH, D], BF16, IN), ("V", [TH, D], BF16, IN),
        ("cum_nat", [H, TH], F32, IN), ("cumtm", [128, H * NCH], F32, IN),
        ("totals", [1, H * NCH], F32, IN), ("S0", [128, 520], F32, IN),
        ("cumg_nat", [H, TH], F32, IN),
        ("cumhl", [2, H, TH], BF16, IN), ("cumghl", [2, H, TH], BF16, IN),
        ("num_hbm", [NCH, 128, D], F32, IN), ("den_hbm", [NCH, H * 128], F32, IN),
        ("outHT", [ND, 128, TH], BF16, IN), ("Sfin", [128, 520], F32, IN),
        ("x1", [TH, D], F32, IN), ("x1nT", [ND, 128, TH], BF16, IN),
        ("G_half", [128, ND * D], F32, IN), ("Gb", [128, ND * D], F32, IN),
        ("x1nb", [TH, D], BF16, IN),
        ("pX1", [E * g.C1, D], BF16, IN), ("pW1", [E * g.C1, 16], F32, IN),
        ("A1", [TH, D], BF16, IN), ("tot1", [TH // 128, E], F32, IN),
        ("offs1", [TH // 128, E], F32, IN),
        ("out", [TH, D], F32, EO),
    ]
    nc = bass.Bass("TRN2", target_bir_lowering=False)
    io = _declare(nc, specs + _const_specs(consts))
    io["Ga"] = io["G_half"]
    es = ExitStack()
    with es:
        p = Prog(nc)
        cst = load_consts(p, nc, es, g, io, CONST_BASE, consts)
        m1, m2, gx = make_exchange(p, nc, es, g, io)
        for fn, marker in ((phase_front, None), (phase_scan, None),
                           (None, m2), (None, m1), (phase_scan2, None),
                           (phase_l0back, None), (phase_mid, None),
                           (None, gx), (phase_l1back, None)):
            if marker is not None:
                marker(p)
                continue
            with ExitStack() as es_p:
                if fn is phase_scan:
                    cst2 = dict(cst)
                    cst2.update(load_consts(p, nc, es_p, g, io, CONST_SCAN,
                                            consts, qname="q_const_scan"))
                    fn(p, nc, es_p, g, io, cst2)
                else:
                    fn(p, nc, es_p, g, io, cst)
                p.barrier_all("phase_end")
        p.barrier_all("final")
        block = es.enter_context(nc.Block())
        p.build(block, es=es)

    im = []
    for c in range(8):
        im.append(dict(
            x=shards[c], Wq=f32(inp["L0_attn_Wq"]), Wk=f32(inp["L0_attn_Wk"]),
            Wv=f32(inp["L0_attn_Wv"]), Wa=f32(inp["L0_attn_Wa"]),
            ba=flat(inp["L0_attn_ba"]), Wo=f32(inp["L0_attn_Wo"]),
            rw1=f32(inp["L0_ffn_rw1"]), rw2=f32(inp["L0_ffn_rw2"]),
            rb1=flat(inp["L0_ffn_rb1"]), rb2=flat(inp["L0_ffn_rb2"]),
            ew1=f32(inp["L0_ffn_ew1"]), eb1=flat(inp["L0_ffn_eb1"]),
            ew2=f32(inp["L0_ffn_ew2"]), eb2=flat(inp["L0_ffn_eb2"]),
            W_q=f32(inp["L1_attn_W_q"]), gate_w=f32(inp["L1_ffn_gate_w"]),
            w1=f32(inp["L1_ffn_w1"]), b1=flat(inp["L1_ffn_b1"]),
            w2=f32(inp["L1_ffn_w2"]), b2=flat(inp["L1_ffn_b2"]),
            token=token,
            **{"c_" + nm: a for nm, a in consts.items()}))
    res = run_bass_kernel_spmd(nc, im, core_ids=list(range(8))).results
    out = np.empty((4, 8192, 1024), np.float32)
    for c in range(8):
        out[c // 2, (c % 2) * TH:(c % 2 + 1) * TH, :] = res[c]["out"]
    return out


def kernel(**inputs):
    return kernel_fused(inputs)



def emit_pool_init(p, nc, g, pref, io_pW, bigtile, EC, io_pX, zt):
    q = pref + "_pwi"
    NB = (EC * 16 + 128 * 256 - 1) // (128 * 256)
    rows_per = 128 * 256 // 16
    for b in range(NB):
        r0 = b * rows_per
        r1 = min(EC, r0 + rows_per)
        p.emit(q, lambda e, r0=r0, r1=r1: e.dma_start(
            out=io_pW[r0:r1, :],
            in_=bigtile[:].rearrange("p (q w) -> p q w", w=16)[:, 0:(r1 - r0) // 128, :]),
            reads=["bigtile"], writes=["pWi"], tag=f"{pref}pwi{b}")
    for b in range(EC // 128):
        p.emit(q, lambda e, b=b: e.dma_start(
            out=io_pX[b * 128:(b + 1) * 128, :], in_=zt[:]),
            reads=["ztz"], writes=["pXi"], tag=f"{pref}pxi{b}")


def emit_slot_scatter(p, nc, g, cst, pref, i, rank, masks, kmask, wsel, kwsel,
                      offs_flat, koffs, xnb_t, kxnb, iota_sb, PSA, slot_s,
                      wid, ux, io_pX, io_pW, C, EC, layer, id_base=0):
    E, NT = g.E, g.NT
    kps = pref + "PSA0"
    ecb = "eCbase0" if layer == 0 else "eCbase1"
    cap = "caprow0" if layer == 0 else "caprow1"
    p.emit("pe", lambda e, i=i: e.matmul(
        PSA[:, 0:E], cst["U128"][:], masks[:, i * E:(i + 1) * E],
        start=True, stop=False), reads=[kmask, "c_U128"], writes=[kps],
        tag=f"{pref}sp{i}_{rank}")
    p.emit("pe", lambda e, i=i: e.matmul(
        PSA[:, 0:E], cst["ones_1x128"][:], offs_flat[0:1, i * E:(i + 1) * E],
        start=False, stop=False), reads=[koffs, "c_ones_1x128"], writes=[kps],
        tag=f"{pref}so{i}_{rank}")
    p.emit("pe", lambda e: e.matmul(
        PSA[:, 0:E], cst["ones_1x128"][:], cst[ecb][:],
        start=False, stop=True), reads=["c_" + ecb, "c_ones_1x128"],
        writes=[kps], tag=f"{pref}se{i}_{rank}")
    p.emit("dve", lambda e: e.tensor_copy(slot_s[:, 0:E], PSA[:, 0:E]),
           reads=[kps], writes=["slotf"], tag=f"{pref}sc{i}_{rank}")
    p.emit("dve", lambda e: e.tensor_tensor(
        out=slot_s[:, E:2 * E], in0=slot_s[:, 0:E], in1=cst[cap][:],
        op=ALU.is_le),
        reads=["slotf", "c_" + cap], writes=["slotle"], tag=f"{pref}sl{i}_{rank}")
    p.emit("dve", lambda e, i=i: e.tensor_tensor(
        out=slot_s[:, E:2 * E], in0=slot_s[:, E:2 * E],
        in1=masks[:, i * E:(i + 1) * E], op=ALU.mult),
        reads=["slotle", kmask], writes=["slotok"], tag=f"{pref}sm{i}_{rank}")
    p.emit("dve", lambda e: e.tensor_scalar(
        out=slot_s[:, 2 * E:3 * E], in0=slot_s[:, E:2 * E], scalar1=-1e9,
        scalar2=1e9, op0=ALU.mult, op1=ALU.add),
        reads=["slotok"], writes=["slotbig"], tag=f"{pref}sb{i}_{rank}")
    p.emit("dve", lambda e: e.tensor_tensor(
        out=slot_s[:, 0:E], in0=slot_s[:, 0:E], in1=slot_s[:, E:2 * E],
        op=ALU.mult), reads=["slotf", "slotok"], writes=["slotf"],
        tag=f"{pref}sf{i}_{rank}")
    p.emit("dve", lambda e: e.tensor_tensor(
        out=slot_s[:, 0:E], in0=slot_s[:, 0:E], in1=slot_s[:, 2 * E:3 * E],
        op=ALU.add), reads=["slotf", "slotbig"], writes=["slotf"],
        tag=f"{pref}sf2{i}_{rank}")
    p.emit("dve", lambda e: e.tensor_reduce(
        slot_s[:, 3 * E:3 * E + 1], slot_s[:, 0:E], axis=AX.X, op=ALU.min),
        reads=["slotf"], writes=["slotmin"], tag=f"{pref}sr{i}_{rank}")
    p.emit("dve", lambda e: e.tensor_copy(ux[:, 0:1], slot_s[:, 3 * E:3 * E + 1]),
           reads=["slotmin"], writes=["uxw"], tag=f"{pref}su{i}_{rank}")
    p.emit("dve", lambda e, i=i: e.tensor_copy(
        wid[:, 0:1], wsel[:, i:i + 1]), reads=[kwsel], writes=["widw"],
        tag=f"{pref}sw{i}_{rank}")
    p.emit("dve", lambda e, i=i, id_base=id_base: e.tensor_scalar(
        out=wid[:, 1:2], in0=iota_sb[:, i:i + 1], scalar1=float(id_base),
        scalar2=None, op0=ALU.add), reads=["c_iota"], writes=["widw"],
        tag=f"{pref}si{i}_{rank}")
    qx = pref + "_scx"
    qw = pref + "_scw"
    p.emit(qx, lambda e: e.indirect_dma_start(
        out=io_pX, out_offset=bass.IndirectOffsetOnAxis(ap=ux[:, 0:1], axis=0),
        in_=xnb_t[:], in_offset=None, bounds_check=EC - 1, oob_is_err=False),
        reads=["uxw", kxnb, "pXi"], writes=["pX"], tag=f"{pref}scx{i}_{rank}")
    p.emit(qw, lambda e: e.indirect_dma_start(
        out=io_pW, out_offset=bass.IndirectOffsetOnAxis(ap=ux[:, 0:1], axis=0),
        in_=wid[:, 0:16], in_offset=None, bounds_check=EC - 1, oob_is_err=False),
        reads=["uxw", "widw", "pWi"], writes=["pW"], tag=f"{pref}scw{i}_{rank}")


def emit_sparse_expert(p, nc, g, cst, pref, e, NF, C, actfn, io_pX, io_pW,
                       io_w1, io_w2, io_b1, io_b2, io_AB,
                       xg, xgT, pw_sb, ehT_m, ebuf, b1e, b2e, ux2,
                       w1sb, w2sb, PSA, PSB, ptr, D=1024):
    ND, E = g.ND, g.E
    NTS = C // 128
    NHC = NF // 128
    q_lp = pref + "_lp"
    q_lw1 = pref + "_lw1"
    q_lw2 = pref + "_lw2"
    q_lb = pref + "_lb"
    q_sb = pref + "_sb"
    kxg, kxgT, kpw = pref + "xg", pref + "xgT", pref + "pw"
    kb1f, kb2f = pref + "b1e", pref + "b2e"
    p.emit(q_lp, lambda e_, e=e: e_.dma_start(
        out=xg[:].rearrange("p (q d) -> p q d", q=NTS),
        in_=io_pX[e * C:(e + 1) * C, :].rearrange("(q p) d -> p q d", p=128)),
        reads=["pX"], writes=[kxg], tag=f"{pref}lpx{e}")
    p.emit(q_lp, lambda e_, e=e: e_.dma_start(
        out=pw_sb[:].rearrange("p (q w) -> p q w", q=NTS),
        in_=io_pW[e * C:(e + 1) * C, :].rearrange("(q p) w -> p q w", p=128)),
        reads=["pW"], writes=[kpw], tag=f"{pref}lpw{e}")
    p.emit(q_lb, lambda e_, e=e: e_.dma_start(
        out=b1e[:], in_=io_b1[0:1, e * NF:(e + 1) * NF]),
        writes=[kb1f], tag=f"{pref}lb1s{e}")
    p.emit(q_lb, lambda e_, e=e: e_.dma_start(
        out=b2e[:], in_=io_b2[0:1, e * D:(e + 1) * D]),
        writes=[kb2f], tag=f"{pref}lb2s{e}")
    kptr = pref + "PSB0"
    for q in range(NTS):
        for j in range(ND):
            p.emit("pe", lambda e_, q=q, j=j: e_.transpose(
                ptr, xg[:, q * D + j * 128:(q * D) + (j + 1) * 128],
                cst["identb"][:]), reads=[kxg, "c_identb"], writes=[kptr],
                tag=f"{pref}tp{e}_{q}_{j}")
            p.emit("dve", lambda e_, q=q, j=j: e_.tensor_copy(
                xgT[:, j * C + q * 128: j * C + (q + 1) * 128], ptr),
                reads=[kptr], writes=[kxgT], tag=f"{pref}tpc{e}_{q}_{j}")
    tslices = []
    off = 0
    while off < C:
        w = min(512, C - off)
        tslices.append((off, w))
        off += w
    NSL = NF // 512
    for sl in range(NSL):
        b = sl % 2
        kw1 = f"{pref}w1_{b}"
        p.emit(q_lw1, lambda e_, e=e, sl=sl, b=b: e_.dma_start(
            out=w1sb[b][:].rearrange("p (dc h) -> p dc h", dc=ND),
            in_=io_w1[e].rearrange("(dc p) h -> p dc h", p=128)[:, :, sl * 512:(sl + 1) * 512]),
            writes=[kw1], tag=f"{pref}slw1_{e}_{sl}")
        for hq in range(4):
            hc = sl * 4 + hq
            pb = hc % 2
            kpsa = f"{pref}PSA{pb}"
            for d in range(ND):
                for (toff, tw) in tslices:
                    p.emit("pe", lambda e_, d=d, hq=hq, toff=toff, tw=tw, b=b, pb=pb: e_.matmul(
                        PSA[:, pb * 1024 + toff: pb * 1024 + toff + tw],
                        w1sb[b][:, d * 512 + hq * 128: d * 512 + (hq + 1) * 128],
                        xgT[:, d * C + toff: d * C + toff + tw],
                        start=(d == 0), stop=False),
                        reads=[kw1, kxgT], writes=[kpsa], tag=f"{pref}sA{e}_{hc}")
            for (toff, tw) in tslices:
                p.emit("pe", lambda e_, e=e, hc=hc, toff=toff, tw=tw, pb=pb: e_.matmul(
                    PSA[:, pb * 1024 + toff: pb * 1024 + toff + tw],
                    b1e[0:1, hc * 128:(hc + 1) * 128],
                    cst["ones_1x512"][:, 0:tw], start=False, stop=True),
                    reads=[kb1f, "c_ones_1x512"], writes=[kpsa],
                    tag=f"{pref}sAb{e}_{hc}")
            p.emit("act", lambda e_, hc=hc, pb=pb, kpsa=kpsa: e_.activation(
                out=ehT_m[:, hc * C:(hc + 1) * C],
                in_=PSA[:, pb * 1024: pb * 1024 + C], func=actfn),
                reads=[kpsa], writes=[f"{pref}ehT_{hc}"], tag=f"{pref}sAs{e}_{hc}")
    assert NTS <= 8
    ehkeys = [f"{pref}ehT_{hc}" for hc in range(NHC)]
    NHG = NHC // 4
    for dhalf in range(2):
        for hg in range(NHG):
            b = hg % 2
            kw2 = f"{pref}w2_{b}"
            p.emit(q_lw2, lambda e_, e=e, hg=hg, b=b, dhalf=dhalf: e_.dma_start(
                out=w2sb[b][:].rearrange("p (hc f) -> p hc f", hc=4),
                in_=io_w2[e][hg * 512:(hg + 1) * 512,
                             dhalf * 512:(dhalf + 1) * 512]
                .rearrange("(hc p) f -> p hc f", p=128)),
                writes=[kw2], tag=f"{pref}slw2_{e}_{dhalf}_{hg}")
            for hq in range(4):
                hc = hg * 4 + hq
                for i in range(NTS):
                    PSx = PSA if i < 4 else PSB
                    kps = (pref + ("PSA" if i < 4 else "PSB") + str((i % 4) // 2))
                    off = (i % 4) * 512
                    p.emit("pe", lambda e_, hc=hc, hq=hq, i=i, b=b, PSx=PSx, off=off: e_.matmul(
                        PSx[:, off:off + 512],
                        ehT_m[:, hc * C + i * 128: hc * C + (i + 1) * 128],
                        w2sb[b][:, hq * 512:(hq + 1) * 512],
                        start=(hc == 0), stop=False),
                        reads=[kw2, ehkeys[hc]], writes=[kps],
                        tag=f"{pref}sB{e}_{dhalf}_{i}")
        for i in range(NTS):
            PSx = PSA if i < 4 else PSB
            kps = (pref + ("PSA" if i < 4 else "PSB") + str((i % 4) // 2))
            off = (i % 4) * 512
            p.emit("pe", lambda e_, e=e, i=i, dhalf=dhalf, PSx=PSx, off=off: e_.matmul(
                PSx[:, off:off + 512], cst["ones_1x128"][:],
                b2e[0:1, dhalf * 512:(dhalf + 1) * 512],
                start=False, stop=True),
                reads=[kb2f, "c_ones_1x128"], writes=[kps],
                tag=f"{pref}sBb{e}_{dhalf}_{i}")
            p.emit("dve", lambda e_, i=i, dhalf=dhalf, PSx=PSx, off=off: e_.tensor_scalar(
                out=ebuf[:, i * D + dhalf * 512: i * D + (dhalf + 1) * 512],
                in0=PSx[:, off:off + 512], scalar1=pw_sb[:, i * 16:i * 16 + 1],
                scalar2=None, op0=ALU.mult),
                reads=[kps, kpw], writes=[f"{pref}eb_{i}_{dhalf}"],
                tag=f"{pref}sY{e}_{dhalf}_{i}")
    for i in range(NTS):
        p.emit("dve", lambda e_, i=i: e_.tensor_copy(
            ux2[:, 0:1], pw_sb[:, i * 16 + 1: i * 16 + 2]),
            reads=[kpw], writes=["ux2w"], tag=f"{pref}su2{e}_{i}")
        p.emit(q_sb, lambda e_, i=i: e_.indirect_dma_start(
            out=io_AB, out_offset=bass.IndirectOffsetOnAxis(ap=ux2[:, 0:1], axis=0),
            in_=ebuf[:, i * D:(i + 1) * D], in_offset=None,
            bounds_check=io_AB.shape[0] - 1, oob_is_err=False),
            reads=["ux2w", f"{pref}eb_{i}_0", f"{pref}eb_{i}_1"],
            writes=["ABw"], tag=f"{pref}scb{e}_{i}")


def phase_l1back_sparse(p, nc, es, g, io, cst):
    TH, D, ND, E, F1, MACRO = g.TH, g.D, g.ND, g.E, g.F1, g.MACRO
    C, NT = g.C1, g.NT
    EC = E * C
    NM, NTM = g.NMAC, MACRO // 128
    S = lambda nm, shp, dt=F32: es.enter_context(nc.sbuf_tensor("m1" + nm, shp, dt))
    PS = lambda nm, shp, dt=F32: es.enter_context(nc.psum_tensor("m1" + nm, shp, dt))
    q_lw = p.dma_queue("m1_lw", "gpsimd")
    q_lw1 = p.dma_queue("m1_lw1", "gpsimd")
    q_lw2 = p.dma_queue("m1_lw2", "gpsimd")
    p.dma_queue("m1_lb", "gpsimd")
    p.dma_queue("m1_lp", "sync")
    p.dma_queue("m1_pwi", "gpsimd")
    p.dma_queue("m1_scx", "gpsimd")
    p.dma_queue("m1_scw", "gpsimd")
    p.dma_queue("m1_sb", "gpsimd")
    q_lx = p.dma_queue("m1_lx", "sync")
    q_lxT = p.dma_queue("m1_lxT", "sync")
    q_st = p.dma_queue("m1_st", "sync")
    q_tot = p.dma_queue("m1_tot", "sync")

    PSA = PS("PSA", [128, 2048])
    PSB = PS("PSB", [128, 2048])
    ptr = PSB[:, 0:64].bitcast(BF16)
    Msb = S("Msb", [128, ND * D], BF16)
    es_m = ExitStack()
    Sm = lambda nm, shp, dt=F32: es_m.enter_context(nc.sbuf_tensor("m1" + nm, shp, dt))
    gta = [Sm(f"gta{i}", [128, D]) for i in range(2)]
    gtb = [Sm(f"gtb{i}", [128, D]) for i in range(2)]
    gb16 = Sm("gb16", [128, ND * D], BF16)
    for f1 in range(ND):
        pa = f1 % 2
        p.emit(q_lx, lambda e, f1=f1, pa=pa: e.dma_start(
            out=gta[pa][:], in_=io["Ga"][:, f1 * D:(f1 + 1) * D]),
            reads=["Gw"], writes=[f"gta{pa}"], tag=f"lGa{f1}")
        p.emit(q_lxT, lambda e, f1=f1, pa=pa: e.dma_start(
            out=gtb[pa][:], in_=io["Gb"][:, f1 * D:(f1 + 1) * D]),
            reads=["Gbw"], writes=[f"gtb{pa}"], tag=f"lGb{f1}")
        p.emit("dve", lambda e, f1=f1, pa=pa: e.tensor_tensor(
            out=gb16[:, f1 * D:(f1 + 1) * D], in0=gta[pa][:], in1=gtb[pa][:],
            op=ALU.add), reads=[f"gta{pa}", f"gtb{pa}"], writes=["gb16"],
            tag=f"gc{f1}")
    wq1 = Sm("wq1", [128, ND * D], BF16)
    p.emit(q_lw, lambda e: e.dma_start(
        out=wq1[:].rearrange("p (dc f) -> p dc f", dc=ND),
        in_=io["W_q"].rearrange("(dc p) f -> p dc f", p=128)),
        writes=["wq1"], tag="ldWq1")
    for hc in range(ND):
        pb = hc % 2
        for oc in range(ND):
            for half in range(2):
                p.emit("pe", lambda e, hc=hc, oc=oc, half=half, pb=pb: e.matmul(
                    PSA[:, pb * 1024 + half * 512: pb * 1024 + (half + 1) * 512],
                    wq1[:, oc * D + hc * 128: oc * D + (hc + 1) * 128],
                    gb16[:, oc * D + half * 512: oc * D + (half + 1) * 512],
                    start=(oc == 0), stop=(oc == ND - 1)),
                    reads=["wq1", "gb16"], writes=[f"m1PSA{hc % 2}"], tag=f"M{hc}_{oc}")
        p.emit("dve", lambda e, hc=hc, pb=pb: e.tensor_copy(
            Msb[:, hc * D:(hc + 1) * D], PSA[:, pb * 1024:(pb + 1) * 1024]),
            reads=[f"m1PSA{pb}"], writes=["Msb"], tag=f"Mev{hc}")
    p.barrier_all("m1_Mdone")
    es_m.close()
    gwsb = S("gwsb", [E, D], BF16)
    gwT = S("gwT", [128, ND * E], BF16)
    b1e = S("b1e", [1, F1]); b2e = S("b2e", [1, D])
    xT_m = S("xT_m", [128, ND * MACRO], BF16)
    masks = S("masks", [128, NT * E])
    wsel = S("wsel", [128, NT])
    lg = S("lg", [128, 24])
    xt = [S(f"xt{i}", [128, D]) for i in range(2)]
    yt = [S(f"yt{i}", [128, D]) for i in range(2)]
    iota_sb = S("iota_sb", [128, NT])
    totrow = S("totrow", [1, E])
    offsf = S("offsf", [1, NT * E])
    offs32 = S("offs32", [NT, E])
    slot_s = S("slot_s", [128, 32])
    wid = S("wid", [128, 16])
    ux = S("ux", [128, 1], mybir.dt.uint32)
    ux2 = S("ux2", [128, 1], mybir.dt.uint32)
    bigtile = S("bigtile", [128, 256])
    xnb_t = [S(f"xnb{i}", [128, D], BF16) for i in range(2)]
    xg = S("xg", [128, (C // 128) * D], BF16)
    xgT = S("xgT", [128, ND * C], BF16)
    pw_sb = S("pw_sb", [128, (C // 128) * 16])
    ehT_m = S("ehT_m", [128, (F1 // 128) * C], BF16)
    ebuf = S("ebuf", [128, (C // 128) * D], BF16)
    w1sb = [S(f"w1s{i}", [128, ND * 512], BF16) for i in range(2)]
    w2sb = [S(f"w2s{i}", [128, 2048], BF16) for i in range(2)]
    p.emit(q_lw, lambda e: e.dma_start(out=iota_sb[:], in_=io["c_iota_t"]),
           writes=["c_iota"], tag="ldiota")
    p.emit("dve", lambda e: e.memset(bigtile[:], 1e9), writes=["bigtile"],
           tag="bigt")
    p.emit("dve", lambda e: e.memset(wid[:], 0.0), writes=["widw"], tag="widz")
    ztb = S("ztb", [128, D], BF16)
    p.emit("dve", lambda e: e.memset(ztb[:], 0.0), writes=["ztz"], tag="ztz")
    emit_pool_init(p, nc, g, "m1", io["pW1"], bigtile, EC, io["pX1"], ztb)
    gwp = PSA[:, 0:4].bitcast(BF16)
    p.emit(q_lw, lambda e: e.dma_start(out=gwsb[:], in_=io["gate_w"]),
           writes=["gwsb"], tag="ldgw")
    for j in range(ND):
        p.emit("pe", lambda e, j=j: e.transpose(
            gwp, gwsb[:, j * 128:(j + 1) * 128], cst["identb"][0:E, 0:E]),
            reads=["gwsb", "c_identb"], writes=["m1PSA0"], tag=f"gwt{j}")
        p.emit("dve", lambda e, j=j: e.tensor_copy(
            gwT[:, j * E:(j + 1) * E], gwp), reads=["m1PSA0"],
            writes=["gwT"], tag=f"gwc{j}")
    for m in range(NM):
        p.emit(q_lxT, lambda e, m=m: e.dma_start(
            out=xT_m[:].rearrange("p (j t) -> p j t", j=ND),
            in_=io["x1nT"][:, :, m * MACRO:(m + 1) * MACRO].rearrange("j p t -> p j t")),
            writes=["kxTm"], tag=f"lxT{m}")
        for i in range(NTM):
            gi = m * NTM + i
            pb = i % 2
            kpsb = f"m1PSB{pb}"
            pa = i % 2
            p.emit(q_lx, lambda e, m=m, i=i, pa=pa: e.dma_start(
                out=xt[pa][:], in_=io["x1"][m * MACRO + i * 128: m * MACRO + (i + 1) * 128, :]),
                writes=[f"xt{pa}"], tag=f"lx1b{m}_{i}")
            for j in range(ND):
                for half in range(2):
                    p.emit("pe", lambda e, i=i, j=j, half=half, pb=pb: e.matmul(
                        PSB[:, pb * 1024 + half * 512: pb * 1024 + (half + 1) * 512],
                        xT_m[:, j * MACRO + i * 128: j * MACRO + (i + 1) * 128],
                        Msb[:, j * D + half * 512: j * D + (half + 1) * 512],
                        start=(j == 0), stop=(j == ND - 1)),
                        reads=["kxTm", "Msb"], writes=[kpsb], tag=f"at{m}_{i}")
            p.emit("dve", lambda e, i=i, pa=pa, pb=pb: e.tensor_tensor(
                out=yt[pa][:], in0=PSB[:, pb * 1024:(pb + 1) * 1024],
                in1=xt[pa][:], op=ALU.add),
                reads=[kpsb, f"xt{pa}"], writes=[f"yt{pa}"], tag=f"yi1{m}_{i}")
            p.emit(q_st, lambda e, m=m, i=i, pa=pa: e.dma_start(
                out=io["out"][m * MACRO + i * 128: m * MACRO + (i + 1) * 128, :],
                in_=yt[pa][:]),
                reads=[f"yt{pa}"], writes=["outw"], tag=f"sbase{m}_{i}")
            for j in range(ND):
                p.emit("pe", lambda e, i=i, j=j: e.matmul(
                    PSA[:, 0:E], xT_m[:, j * MACRO + i * 128: j * MACRO + (i + 1) * 128],
                    gwT[:, j * E:(j + 1) * E], start=(j == 0), stop=(j == ND - 1)),
                    reads=["kxTm", "gwT"], writes=["m1PSA0"], tag=f"gl{m}_{i}")
            p.emit("dve", lambda e: e.tensor_copy(lg[:, 0:E], PSA[:, 0:E]),
                   reads=["m1PSA0"], writes=["lgv"], tag=f"glc{m}_{i}")
            p.emit("dve", lambda e: e.reduce_max(lg[:, 8:9], lg[:, 0:E], axis=AX.X),
                   reads=["lgv"], writes=["lgm1"], tag=f"g1m{m}_{i}")
            p.emit("dve", lambda e, gi=gi: e.tensor_scalar(
                out=masks[:, gi * E:(gi + 1) * E], in0=lg[:, 0:E], scalar1=lg[:, 8:9],
                scalar2=None, op0=ALU.is_ge), reads=["lgv", "lgm1"],
                writes=["kmask"], tag=f"g1k{m}_{i}")
            p.emit("dve", lambda e: e.tensor_scalar(
                out=lg[:, 10:10 + E], in0=lg[:, 0:E], scalar1=lg[:, 8:9],
                scalar2=None, op0=ALU.subtract), reads=["lgv", "lgm1"],
                writes=["lgsub"], tag=f"g1s{m}_{i}")
            p.emit("act", lambda e: e.activation(out=lg[:, 10:10 + E],
                   in_=lg[:, 10:10 + E], func=AF.Exp), reads=["lgsub"],
                   writes=["lgexp"], tag=f"g1e{m}_{i}")
            p.emit("dve", lambda e: e.reduce_sum(lg[:, 8:9], lg[:, 10:10 + E],
                   axis=AX.X), reads=["lgexp"], writes=["lgsum"], tag=f"g1t{m}_{i}")
            p.emit("dve", lambda e, gi=gi: e.reciprocal(wsel[:, gi:gi + 1], lg[:, 8:9]),
                   reads=["lgsum"], writes=["kwsel"], tag=f"g1r{m}_{i}")
            p.emit("pe", lambda e, gi=gi: e.matmul(
                PSA[0:1, 16:16 + E], cst["ones_128x1"][:],
                masks[:, gi * E:(gi + 1) * E], start=True, stop=True),
                reads=["kmask", "c_ones_128x1"], writes=["m1PSA0"],
                tag=f"tot{m}_{i}")
            p.emit("dve", lambda e: e.tensor_copy(totrow[:], PSA[0:1, 16:16 + E]),
                   reads=["m1PSA0"], writes=["totrow"], tag=f"totc{m}_{i}")
            p.emit(q_tot, lambda e, gi=gi: e.dma_start(
                out=io["tot1"][gi:gi + 1, :], in_=totrow[:]),
                reads=["totrow"], writes=["totw"], tag=f"tots{m}_{i}")
    p.emit(q_tot, lambda e: e.dma_start(out=offs32[:], in_=io["tot1"]),
           reads=["totw"], writes=["offs32"], tag="ldtot")
    p.emit("pe", lambda e: e.matmul(
        PSA[0:NT, 0:E], cst["U128strict"][0:NT, 0:NT], offs32[:],
        start=True, stop=True), reads=["offs32", "c_U128strict"],
        writes=["m1PSA0"], tag="offmm")
    p.emit("dve", lambda e: e.tensor_copy(offs32[:], PSA[0:NT, 0:E]),
           reads=["m1PSA0"], writes=["offs32"], tag="offev")
    p.emit(q_tot, lambda e: e.dma_start(out=io["offs1"], in_=offs32[:]),
           reads=["offs32"], writes=["offsw"], tag="stoffs")
    p.emit(q_tot, lambda e: e.dma_start(
        out=offsf[:], in_=io["offs1"].rearrange("a b -> (a b)").unsqueeze(0)),
        reads=["offsw"], writes=["koffs"], tag="ldoffs")
    for i in range(NT):
        pa = i % 2
        p.emit(q_lx, lambda e, i=i, pa=pa: e.dma_start(
            out=xnb_t[pa][:], in_=io["x1nb"][i * 128:(i + 1) * 128, :]),
            reads=["x1nbw"], writes=[f"kxnb{pa}"], tag=f"lxnb{i}")
        emit_slot_scatter(p, nc, g, cst, "m1", i, 0, masks, "kmask", wsel,
                          "kwsel", offsf, "koffs", xnb_t[pa], f"kxnb{pa}",
                          iota_sb, PSA, slot_s, wid, ux, io["pX1"], io["pW1"],
                          C, EC, layer=1)
    for e in range(E):
        emit_sparse_expert(p, nc, g, cst, "m1", e, F1, C, AF.Relu,
                           io["pX1"], io["pW1"], io["w1"], io["w2"],
                           io["b1"], io["b2"], io["A1"],
                           xg, xgT, pw_sb, ehT_m, ebuf, b1e, b2e, ux2,
                           w1sb, w2sb, PSA, PSB, ptr)
    for i in range(NT):
        pa = i % 2
        p.emit(q_lx, lambda e, i=i, pa=pa: e.dma_start(
            out=xnb_t[pa][:], in_=io["A1"][i * 128:(i + 1) * 128, :]),
            reads=["ABw", f"kxnb{pa}"], writes=[f"kxnb{pa}"], tag=f"lA{i}")
        p.emit("dve", lambda e, pa=pa: e.tensor_copy(yt[pa][:], xnb_t[pa][:]),
               reads=[f"kxnb{pa}"], writes=[f"yt{pa}"], tag=f"cA{i}")
        p.emit(q_lxT, lambda e, i=i, pa=pa: e.dma_start(
            out=xt[pa][:], in_=io["out"][i * 128:(i + 1) * 128, :]),
            reads=["outw", f"xt{pa}"], writes=[f"xt{pa}"], tag=f"lO{i}")
        p.emit("dve", lambda e, pa=pa: e.tensor_tensor(
            out=xt[pa][:], in0=xt[pa][:], in1=yt[pa][:], op=ALU.add),
            reads=[f"yt{pa}", f"xt{pa}"], writes=[f"xt{pa}"], tag=f"fadd{i}")
        p.emit(q_st, lambda e, i=i, pa=pa: e.dma_start(
            out=io["out"][i * 128:(i + 1) * 128, :], in_=xt[pa][:]),
            reads=[f"xt{pa}"], writes=["outw2"], tag=f"sfin{i}")


def phase_scan2(p, nc, es, g, io, cst):
    TH, D, ND, H, CH, NCH, DH = g.TH, g.D, g.ND, g.H, g.CH, g.NCH, g.DH
    SCOLS = (H // 2) * DH + H // 2
    S = lambda nm, shp, dt=F32: es.enter_context(nc.sbuf_tensor("s2" + nm, shp, dt))
    PS = lambda nm, shp, dt=F32: es.enter_context(nc.psum_tensor("s2" + nm, shp, dt))
    q_lq = p.dma_queue("s2_lq", "sync")
    q_ln = p.dma_queue("s2_ln", "sync")
    q_ld = p.dma_queue("s2_ld", "gpsimd")
    q_lcf = p.dma_queue("s2_lcf", "gpsimd")
    q_st = p.dma_queue("s2_st", "sync")

    PB = PS("PB", [128, 2048])
    PP = PS("PP", [128, 2048])
    POv = PB[:, 0:1024]

    S32 = S("S32", [128, SCOLS])
    Sb = S("Sb", [128, SCOLS], BF16)
    p.emit(q_ld, lambda e: e.dma_start(out=S32[:], in_=io["S0"]),
           reads=["S0w"], writes=["s2S32"])
    p.emit("dve", lambda e: e.tensor_copy(Sb[:], S32[:]), reads=["s2S32"],
           writes=["s2Sb"])

    qt = [S(f"qt{i}", [128, ND * CH], BF16) for i in range(2)]
    num = [S(f"num{i}", [128, ND * CH]) for i in range(2)]
    den = [S(f"den{i}", [1, H * CH]) for i in range(2)]
    cfl = [S(f"cfl{i}", [2, H * CH], BF16) for i in range(2)]
    ecum = S("ecum", [128, H * CH], BF16)
    qte = S("qte", [128, ND * CH], BF16)
    dent = S("dent", [1, H * CH])
    recl = S("recl", [128, H * CH])
    recd = S("recd", [128, H * CH])
    ntot = S("ntot", [128, ND * CH])
    outc = [S(f"outc{i}", [128, ND * CH], BF16) for i in range(2)]
    cumg_flat = io["cumghl"].rearrange("w (a b) t -> w b a t", b=2)

    for c in range(NCH):
        pa = c % 2
        kq, kn, kd, kcf = f"s2qt{pa}", f"s2num{pa}", f"s2den{pa}", f"s2cfl{pa}"
        p.emit(q_lq, lambda e, c=c, pa=pa: e.dma_start(
            out=qt[pa][:].rearrange("p (j t) -> p j t", j=ND),
            in_=io["QT"][:, :, c * CH:(c + 1) * CH].rearrange("j p t -> p j t")),
            writes=[kq], tag=f"s2lq{c}")
        p.emit(q_ln, lambda e, c=c, pa=pa: e.dma_start(
            out=num[pa][:], in_=io["num_hbm"][c, :, :]),
            reads=["numw"], writes=[kn], tag=f"s2ln{c}")
        p.emit(q_ld, lambda e, c=c, pa=pa: e.dma_start(
            out=den[pa][:], in_=io["den_hbm"][c:c + 1, :]),
            reads=["denw"], writes=[kd], tag=f"s2ld{c}")
        for w in range(2):
            p.emit(q_lcf, lambda e, c=c, pa=pa, w=w: e.dma_start(
                out=cfl[pa][w:w + 1, :].rearrange("o (b a t) -> o b a t", b=2, a=ND),
                in_=cumg_flat[w:w + 1, :, :, c * CH:(c + 1) * CH].squeeze(0).unsqueeze(0)),
                reads=["cumghlw"], writes=[kcf], tag=f"s2lcf{c}_{w}")
        for sl in range(4):
            p.emit("pe", lambda e, sl=sl, pa=pa: e.matmul(
                PB[:, sl * 512:(sl + 1) * 512], cst["ones_2x128b"][:],
                cfl[pa][0:2, sl * 512:(sl + 1) * 512], start=True, stop=True),
                reads=[kcf, "c_ones_2x128b"], writes=["s2PB"], tag=f"s2b1_{c}_{sl}")
        p.emit("act", lambda e: e.activation(out=ecum[:], in_=PB[:],
               func=AF.Exp), reads=["s2PB"], writes=["s2ecum"], tag=f"s2ec{c}")
        for h in range(H):
            b, j = h % 2, h // 2
            p.emit("dve", lambda e, b=b, j=j, h=h, pa=pa: e.tensor_tensor(
                out=qte[b * 64:(b + 1) * 64, j * CH:(j + 1) * CH],
                in0=qt[pa][b * 64:(b + 1) * 64, j * CH:(j + 1) * CH],
                in1=ecum[b * 64:(b + 1) * 64, g.col(h):g.col(h) + CH],
                op=ALU.mult), reads=[kq, "s2ecum"], writes=["s2qte"],
                tag=f"s2qte{c}_{h}")
        for h in range(H):
            b, j = h % 2, h // 2
            p.emit("pe", lambda e, b=b, j=j, h=h: e.matmul(
                POv[b * 64:(b + 1) * 64, j * CH:(j + 1) * CH],
                Sb[b * 64:(b + 1) * 64, j * DH:(j + 1) * DH],
                qte[b * 64:(b + 1) * 64, j * CH:(j + 1) * CH],
                start=True, stop=True, tile_position=(b * 64, b * 64)),
                reads=["s2Sb", "s2qte"], writes=["s2PB"], tag=f"s2ox{c}_{h}")
        by_bank = {}
        for h in range(H):
            by_bank.setdefault(g.col(h) // 512, []).append(h)
        for bank, hs in by_bank.items():
            for k, h in enumerate(hs):
                b, j = h % 2, h // 2
                p.emit("pe", lambda e, b=b, j=j, h=h, st=(k == 0),
                       sp=(k == len(hs) - 1): e.matmul(
                    PP[0:1, g.col(h):g.col(h) + CH],
                    Sb[b * 64:(b + 1) * 64, 512 + j:513 + j],
                    qte[b * 64:(b + 1) * 64, j * CH:(j + 1) * CH],
                    start=st, stop=sp),
                    reads=["s2Sb", "s2qte"], writes=["s2PP"], tag=f"s2dz{c}_{h}")
        p.emit("dve", lambda e, pa=pa: e.tensor_tensor(
            out=dent[:], in0=PP[0:1, :], in1=den[pa][:], op=ALU.add),
            reads=["s2PP", kd], writes=["s2dent"], tag=f"s2dt{c}")
        for sl in range(4):
            p.emit("pe", lambda e, sl=sl: e.matmul(
                PP[:, sl * 512:(sl + 1) * 512], cst["ones_1x128"][:],
                dent[:, sl * 512:(sl + 1) * 512], start=True, stop=True),
                reads=["s2dent", "c_ones_1x128"], writes=["s2PP"],
                tag=f"s2rb{c}_{sl}")
        p.emit("act", lambda e: e.activation(out=recl[:], in_=PP[:],
               func=AF.Ln), reads=["s2PP"], writes=["s2recl"], tag=f"s2ln{c}")
        p.emit("act", lambda e: e.activation(out=recd[:], in_=recl[:],
               func=AF.Exp, scale=-1.0), reads=["s2recl"], writes=["s2recd"],
               tag=f"s2rc{c}")
        p.emit("dve", lambda e, pa=pa: e.tensor_tensor(
            out=ntot[:], in0=POv, in1=num[pa][:], op=ALU.add),
            reads=["s2PB", kn], writes=["s2ntot"], tag=f"s2nt{c}")
        for h in range(H):
            b, j = h % 2, h // 2
            p.emit("dve", lambda e, b=b, j=j, h=h, pa=pa: e.tensor_tensor(
                out=outc[pa][b * 64:(b + 1) * 64, j * CH:(j + 1) * CH],
                in0=ntot[b * 64:(b + 1) * 64, j * CH:(j + 1) * CH],
                in1=recd[b * 64:(b + 1) * 64, g.col(h):g.col(h) + CH],
                op=ALU.mult), reads=["s2ntot", "s2recd"],
                writes=[f"s2outc{pa}"], tag=f"s2dv{c}_{h}")
        p.emit(q_st, lambda e, c=c, pa=pa: e.dma_start(
            out=io["outHT"][:, :, c * CH:(c + 1) * CH].rearrange("j p t -> p j t"),
            in_=outc[pa][:].rearrange("p (j t) -> p j t", j=ND)),
            reads=[f"s2outc{pa}"], writes=["outHTw"], tag=f"s2st{c}")


# version 22
# speedup vs baseline: 1.2170x; 1.2170x over previous
import numpy as np
import secrets
from contextlib import ExitStack

import concourse.bass as bass
import concourse.mybir as mybir
from concourse.bass_utils import run_bass_kernel_spmd
from concourse.alu_op_type import AluOpType as RALU

F32 = mybir.dt.float32
BF16 = mybir.dt.bfloat16
I32 = mybir.dt.int32
AF = mybir.ActivationFunctionType
ALU = mybir.AluOpType
AX = mybir.AxisListType


BODIES = {"pe": "tensor", "act": "scalar", "dve": "vector", "pool": "gpsimd",
          "sync": "sync"}


class Op:
    __slots__ = ("queue", "fn", "waits", "val", "tag", "step")

    def __init__(self, queue, fn, waits, val, tag, step):
        self.queue, self.fn, self.waits, self.val, self.tag, self.step = (
            queue, fn, waits, val, tag, step)


class Prog:
    def __init__(self, nc):
        self.nc = nc
        self.ops = []
        self.queues = {q: {"body": b, "step": 1} for q, b in
                       [("pe", "tensor"), ("act", "scalar"), ("dve", "vector"),
                        ("pool", "gpsimd")]}
        self.queues["syncq"] = {"body": "sync", "step": 0}
        self.count = {q: 0 for q in self.queues}
        self.last_write = {}
        self.readers = {}
        self.hwm = {}
        self.upgrades = []

    def dma_queue(self, name, body="sync"):
        assert name not in self.queues
        self.queues[name] = {"body": BODIES.get(body, body), "step": 16}
        self.count[name] = 0
        return name

    def emit(self, queue, fn, reads=(), writes=(), tag=""):
        qi = self.queues[queue]
        step = qi["step"]
        needed = {}

        def need(ev):
            q, v = ev
            if q == queue and q == "pe":
                return
            if self.queues[q]["step"] == 16 and v != self.count[q]:
                self.upgrades.append((q, v, self.count[q], tag))
                v = self.count[q]
            if self.hwm.setdefault(queue, {}).get(q, -1) >= v:
                return
            if needed.get(q, -1) < v:
                needed[q] = v

        for k in reads:
            if k in self.last_write:
                need(self.last_write[k])
        for k in writes:
            if k in self.last_write:
                need(self.last_write[k])
            for ev in self.readers.get(k, ()):
                need(ev)

        if step == 16 and self.count[queue] > 0:
            need((queue, self.count[queue]))
        waits = sorted(needed.items())
        self.count[queue] += step
        val = self.count[queue]
        self.ops.append(Op(queue, fn, waits, val, tag, step))
        h = self.hwm.setdefault(queue, {})
        for q, v in waits:
            if h.get(q, -1) < v:
                h[q] = v
        ev = (queue, val)
        for k in writes:
            self.last_write[k] = ev
            self.readers[k] = []
        for k in reads:
            self.readers.setdefault(k, []).append(ev)
        return ev

    def barrier_all(self, tag="phase_barrier"):
        totals = dict(self.count)
        for q in ("pe", "act", "dve", "pool", "syncq"):
            def fn(eng):
                return None
            waits = []
            for pq, v in totals.items():
                if pq == q or v == 0:
                    continue
                if self.hwm.setdefault(q, {}).get(pq, -1) >= v:
                    continue
                waits.append((pq, v))
                self.hwm[q][pq] = v
            self.ops.append(Op(q, fn, sorted(waits), self.count[q], tag, 0))

    def build(self, block, sems=None, es=None):
        if sems is None:
            sems = {q: es.enter_context(self.nc.semaphore("sem_" + q))
                    for q in self.queues if self.count[q] > 0 or
                    self.queues[q]["step"] != 16}
        self.sems = sems
        by_body = {}
        for op in self.ops:
            by_body.setdefault(self.queues[op.queue]["body"], []).append(op)
        for body, ops in by_body.items():
            dec = getattr(block, body)

            def make(ops=ops):
                def _(eng):
                    for op in ops:
                        for q, v in op.waits:
                            eng.wait_ge(sems[q], v)
                        inst = op.fn(eng)
                        if op.step == 0:
                            assert inst is None
                            continue
                        inc = min(op.step, 16)
                        if inst is not None:
                            inst.then_inc(sems[op.queue], inc)
                        else:
                            eng.nop().then_inc(sems[op.queue], inc)
                return _

            dec(make())



class Geo:
    def __init__(self, TH=4096, D=1024, H=16, DH=64, CH=128,
                 F0=2048, F1=4096, E=8, MACRO=1024, C0=1536, C1=768):
        self.C0, self.C1 = C0, C1
        self.TH = TH
        self.D = D
        self.H = H
        self.DH = DH
        self.CH = CH
        self.NCH = TH // CH
        self.F0 = F0
        self.F1 = F1
        self.E = E
        self.MACRO = MACRO
        self.NMAC = TH // MACRO
        self.NT = TH // 128
        self.ND = D // 128

    def col(self, h):
        return (h // 2) * self.CH + (h % 2) * (self.H // 2) * self.CH


def build_consts(g):
    H, CH, DH = g.H, g.CH, g.DH
    c = {}
    c["ident128"] = np.eye(128, dtype=np.float32)
    c["U128"] = np.triu(np.ones((128, 128), np.float32))
    c["ones_1x512"] = np.ones((1, 512), np.float32)
    c["ones_1x128"] = np.ones((1, 128), np.float32)
    c["ones_128x1"] = np.ones((128, 1), np.float32)
    c["ones_2x128"] = np.ones((2, 128), np.float32)
    c["eps_1x512"] = np.full((1, 512), 1e-6, np.float32)
    m = np.triu(np.ones((CH, CH), np.float32))
    mask = np.zeros((CH, H * CH), np.float32)
    for h in range(H):
        mask[:, g.col(h):g.col(h) + CH] = m
    c["maskU_rep"] = mask
    negI = np.zeros((H, H * CH), np.float32)
    for h in range(H):
        negI[h, g.col(h):g.col(h) + CH] = -1.0
    c["negIexp"] = negI
    c["negIexp2"] = np.concatenate([negI, negI], axis=0)
    d64 = np.zeros((H, H * DH), np.float32)
    for h in range(H):
        d64[h, h * DH:(h + 1) * DH] = 1.0
    c["delta64"] = d64
    HP = H // 2
    dP = np.zeros((HP, HP * DH + HP), np.float32)
    for jj in range(HP):
        dP[jj, jj * DH:(jj + 1) * DH] = 1.0
        dP[jj, HP * DH + jj] = 1.0
    c["deltaP"] = dP
    c["ones_16x64"] = np.ones((16, 64), np.float32)
    c["U128strict"] = np.triu(np.ones((128, 128), np.float32), 1)
    C0, C1 = g.C0, g.C1
    c["eCbase0"] = (np.arange(8, dtype=np.float32) * C0 - 1.0).reshape(1, 8)
    c["eCbase1"] = (np.arange(8, dtype=np.float32) * C1 - 1.0).reshape(1, 8)
    c["caprow0"] = np.tile((np.arange(1, 9, dtype=np.float32) * C0 - 1.0), (128, 1))
    c["caprow1"] = np.tile((np.arange(1, 9, dtype=np.float32) * C1 - 1.0), (128, 1))
    c["iota_t"] = np.arange(g.TH, dtype=np.float32).reshape(g.NT, 128).T.copy()
    return c



def load_consts(p, nc, es, g, io, want, arrs, qname="q_const"):
    q = p.dma_queue(qname, "gpsimd")
    out = {}
    ALIAS = {"identb": ("ident128", BF16), "ones_128x1b": ("ones_128x1", BF16),
             "ones_2x128b": ("ones_2x128", BF16),
             "negIexp2b": ("negIexp2", BF16), "delta64b": ("delta64", BF16),
             "eps_1x512b": ("eps_1x512", BF16)}
    for nm in want:
        base, dt = ALIAS.get(nm, (nm, F32))
        shp = list(arrs[base].shape)
        t = es.enter_context(nc.sbuf_tensor("cs_" + nm, shp, dt))
        src = io["c_" + base]
        p.emit(q, lambda e, t=t, src=src: e.dma_start(out=t[:], in_=src),
               writes=["c_" + nm], tag="const")
        out[nm] = t
    return out


def phase_front(p, nc, es, g, io, cst):
    TH, D, ND, NT, H, CH, NCH = g.TH, g.D, g.ND, g.NT, g.H, g.CH, g.NCH
    S = lambda nm, shp, dt=F32: es.enter_context(nc.sbuf_tensor(nm, shp, dt))
    PS = lambda nm, shp, dt=F32: es.enter_context(nc.psum_tensor(nm, shp, dt))

    q_ldx = p.dma_queue("f_ldx", "sync")
    q_ldw = p.dma_queue("f_ldw", "gpsimd")
    q_stx = p.dma_queue("f_stx", "sync")
    q_stq = p.dma_queue("f_stq", "sync")
    q_stk = p.dma_queue("f_stk", "sync")
    q_stv = p.dma_queue("f_stv", "sync")
    q_stt = p.dma_queue("f_stt", "gpsimd")
    q_stc = p.dma_queue("f_stc", "sync")

    wq = S("wq", [128, ND * D], BF16)
    wk = S("wk", [128, ND * D], BF16)
    wv = S("wv", [128, ND * D], BF16)
    wa = S("wa", [128, ND * H], BF16)
    for t, nm in ((wq, "Wq"), (wk, "Wk"), (wv, "Wv")):
        src = io[nm].rearrange("(dc p) f -> p dc f", p=128)
        p.emit(q_ldw, lambda e, t=t, src=src: e.dma_start(
            out=t[:].rearrange("p (dc f) -> p dc f", dc=ND), in_=src),
            writes=[t.name], tag="ldW")
    p.emit(q_ldw, lambda e: e.dma_start(
        out=wa[:].rearrange("p (dc f) -> p dc f", dc=ND),
        in_=io["Wa"].rearrange("(dc p) f -> p dc f", p=128)),
        writes=["wa"], tag="ldWa")
    ba_s = S("ba_s", [1, H])
    p.emit(q_ldw, lambda e: e.dma_start(out=ba_s[:], in_=io["ba"]),
           writes=["ba_s"])
    ba2 = S("ba2", [1, H])
    p.emit("dve", lambda e: e.tensor_scalar(out=ba2[:], in0=ba_s[:],
           scalar1=2.0, scalar2=None, op0=ALU.add), reads=["ba_s"],
           writes=["ba2"])

    pj = [PS(f"pj{i}", [128, 512]) for i in range(4)]
    ptr = PS("ptrT", [128, 128], BF16)
    ap_ = PS("ap_", [128, H])

    xnT = [S(f"xnT{j}", [128, TH], BF16) for j in range(ND)]

    def xnT_keys(trange=None):
        if trange is None:
            trange = range(NT)
        return [f"xnT{j}_{i}" for j in range(ND) for i in trange]

    xt = [S(f"xt{i}", [128, D]) for i in range(2)]
    xc = [S(f"xc{i}", [128, D]) for i in range(2)]
    xnb = [S(f"xnb{i}", [128, D], BF16) for i in range(2)]
    stat = S("stat", [128, 8])
    for i in range(NT):
        pa = i % 2
        kxt, kxc, kxnb = f"xt{pa}", f"xc{pa}", f"xnb{pa}"
        p.emit(q_ldx, lambda e, i=i, pa=pa: e.dma_start(
            out=xt[pa][:], in_=io["x"][i * 128:(i + 1) * 128, :]),
            reads=[kxc], writes=[kxt], tag=f"ldx{i}")
        p.emit("dve", lambda e, pa=pa: e.reduce_sum(
            stat[:, 0:1], xt[pa][:], axis=AX.X), reads=[kxt], writes=["stat0"])
        p.emit("dve", lambda e: e.tensor_scalar(
            out=stat[:, 1:2], in0=stat[:, 0:1], scalar1=1.0 / D, scalar2=None,
            op0=ALU.mult), reads=["stat0"], writes=["statm"])
        p.emit("dve", lambda e, pa=pa: e.tensor_scalar(
            out=xc[pa][:], in0=xt[pa][:], scalar1=stat[:, 1:2], scalar2=None,
            op0=ALU.subtract), reads=[kxt, "statm"], writes=[kxc])
        p.emit("act", lambda e, pa=pa: e.activation(
            out=xt[pa][:], in_=xc[pa][:], func=AF.Square),
            reads=[kxc], writes=[kxt], tag="sq")
        p.emit("dve", lambda e, pa=pa: e.reduce_sum(
            stat[:, 2:3], xt[pa][:], axis=AX.X), reads=[kxt], writes=["statv"])
        p.emit("dve", lambda e: e.tensor_scalar(
            out=stat[:, 5:6], in0=stat[:, 2:3], scalar1=1.0 / D, scalar2=1e-5,
            op0=ALU.mult, op1=ALU.add), reads=["statv"], writes=["statv2"])
        p.emit("act", lambda e: e.activation(
            out=stat[:, 3:4], in_=stat[:, 5:6], func=AF.Ln),
            reads=["statv2"], writes=["statl"])
        p.emit("act", lambda e: e.activation(
            out=stat[:, 4:5], in_=stat[:, 3:4], func=AF.Exp, scale=-0.5),
            reads=["statl"], writes=["statr"])
        p.emit("dve", lambda e, pa=pa: e.tensor_scalar(
            out=xc[pa][:], in0=xc[pa][:], scalar1=stat[:, 4:5], scalar2=None,
            op0=ALU.mult), reads=[kxc, "statr"], writes=[kxc])
        p.emit(q_stx, lambda e, i=i, pa=pa: e.dma_start(
            out=io["xn32"][i * 128:(i + 1) * 128, :], in_=xc[pa][:]),
            reads=[kxc], writes=["xn32w"], tag=f"stxn{i}")
        p.emit("dve", lambda e, pa=pa: e.tensor_copy(xnb[pa][:], xc[pa][:]),
               reads=[kxc], writes=[kxnb])
        for j in range(ND):
            p.emit("pe", lambda e, pa=pa, j=j: e.transpose(
                ptr[:], xnb[pa][:, j * 128:(j + 1) * 128], cst["identb"][:]),
                reads=[kxnb, "c_identb"], writes=["ptrT"], tag=f"tr{i}_{j}")
            p.emit("dve", lambda e, i=i, j=j: e.tensor_copy(
                xnT[j][:, i * 128:(i + 1) * 128], ptr[:]),
                reads=["ptrT"], writes=[f"xnT{j}_{i}"], tag=f"trc{i}_{j}")

    u_all = S("u_all", [128, H * NT])
    ksl = [S(f"ksl{i}", [128, D], BF16) for i in range(2)]
    vsl = [S(f"vsl{i}", [128, D], BF16) for i in range(2)]
    kesc = [S(f"kesc{i}", [128, D]) for i in range(2)]
    kesb = [S(f"kesb{i}", [128, D], BF16) for i in range(2)]
    for i in range(NT):
        pa = i % 2
        rk = xnT_keys([i])
        for d in range(ND):
            kx = f"xnT{d}_{i}"
            for half in range(2):
                p.emit("pe", lambda e, d=d, i=i, half=half: e.matmul(
                    pj[half][:], xnT[d][:, i * 128:(i + 1) * 128],
                    wk[:, d * D + half * 512: d * D + (half + 1) * 512],
                    start=(d == 0), stop=(d == ND - 1)),
                    reads=["wk", kx], writes=[f"pj{half}"], tag=f"ktm{i}")
                p.emit("pe", lambda e, d=d, i=i, half=half: e.matmul(
                    pj[2 + half][:], xnT[d][:, i * 128:(i + 1) * 128],
                    wv[:, d * D + half * 512: d * D + (half + 1) * 512],
                    start=(d == 0), stop=(d == ND - 1)),
                    reads=["wv", kx], writes=[f"pj{2+half}"], tag=f"vtm{i}")
            p.emit("pe", lambda e, d=d, i=i: e.matmul(
                ap_[:], xnT[d][:, i * 128:(i + 1) * 128], wa[:, d * H:(d + 1) * H],
                start=(d == 0), stop=False), reads=["wa", kx],
                writes=["ap_"], tag=f"atm{i}")
        p.emit("pe", lambda e: e.matmul(
            ap_[:], cst["ones_1x128"][:], ba2[:], start=False, stop=True),
            reads=["c_ones_1x128", "ba2"], writes=["ap_"], tag=f"aba{i}")
        p.emit("dve", lambda e, i=i: e.tensor_copy(
            u_all[:, i * H:(i + 1) * H], ap_[:]), reads=["ap_"],
            writes=[f"u_{i}"], tag=f"uev{i}")
        for half in range(2):
            p.emit("dve", lambda e, half=half, pa=pa: e.tensor_scalar(
                out=kesc[pa][:, half * 512:(half + 1) * 512], in0=pj[half][:],
                scalar1=0.0, scalar2=None, op0=ALU.min),
                reads=[f"pj{half}"], writes=[f"kesc{pa}_{half}"])
            p.emit("act", lambda e, half=half, pa=pa: e.activation(
                out=kesb[pa][:, half * 512:(half + 1) * 512],
                in_=kesc[pa][:, half * 512:(half + 1) * 512], func=AF.Exp),
                reads=[f"kesc{pa}_{half}"], writes=[f"kesb{pa}_{half}"])
            p.emit("dve", lambda e, half=half, pa=pa: e.scalar_tensor_tensor(
                out=ksl[pa][:, half * 512:(half + 1) * 512], in0=pj[half][:],
                scalar=0.0, in1=kesb[pa][:, half * 512:(half + 1) * 512],
                op0=ALU.max, op1=ALU.add),
                reads=[f"pj{half}", f"kesb{pa}_{half}"], writes=[f"ksl{pa}"])
            p.emit("dve", lambda e, half=half, pa=pa: e.tensor_copy(
                vsl[pa][:, half * 512:(half + 1) * 512], pj[2 + half][:]),
                reads=[f"pj{2+half}"], writes=[f"vsl{pa}"])
        p.emit(q_stk, lambda e, i=i, pa=pa: e.dma_start(
            out=io["Ktm"][i * 128:(i + 1) * 128, :], in_=ksl[pa][:]),
            reads=[f"ksl{pa}"], writes=["Ktmw"], tag="stKtm")
        p.emit(q_stv, lambda e, i=i, pa=pa: e.dma_start(
            out=io["V"][i * 128:(i + 1) * 128, :], in_=vsl[pa][:]),
            reads=[f"vsl{pa}"], writes=["Vw"], tag="stV")

    qsl = [S(f"qsl{i}", [128, 512], BF16) for i in range(2)]
    esc = [S(f"esc{i}", [128, 512]) for i in range(2)]
    esb = [S(f"esb{i}", [128, 512], BF16) for i in range(2)]
    NSLAB = TH // 512
    for (wsb, wname, dst) in ((wq, "wq", "QT"), (wk, "wk", "KT")):
        for fout in range(ND):
            for sg in range((NSLAB + 3) // 4):
                slabs = list(range(sg * 4, min((sg + 1) * 4, NSLAB)))
                for d in range(ND):
                    for si, sl in enumerate(slabs):
                        p.emit("pe", lambda e, d=d, sl=sl, si=si, fout=fout, wsb=wsb: e.matmul(
                            pj[si][:],
                            wsb[:, d * D + fout * 128: d * D + (fout + 1) * 128],
                            xnT[d][:, sl * 512:(sl + 1) * 512],
                            start=(d == 0), stop=(d == ND - 1)),
                            reads=[wname] + [f"xnT{d}_{4 * sl + t}"
                                             for t in range(4)],
                            writes=[f"pj{si}"], tag=f"pj{fout}_{sl}")
                for si, sl in enumerate(slabs):
                    pa = si % 2
                    p.emit("dve", lambda e, si=si, pa=pa: e.tensor_scalar(
                        out=esc[pa][:], in0=pj[si][:], scalar1=0.0, scalar2=None,
                        op0=ALU.min), reads=[f"pj{si}"], writes=[f"esc{pa}"])
                    p.emit("act", lambda e, pa=pa: e.activation(
                        out=esb[pa][:], in_=esc[pa][:], func=AF.Exp),
                        reads=[f"esc{pa}"], writes=[f"esb{pa}"])
                    p.emit("dve", lambda e, si=si, pa=pa: e.scalar_tensor_tensor(
                        out=qsl[pa][:], in0=pj[si][:], scalar=0.0, in1=esb[pa][:],
                        op0=ALU.max, op1=ALU.add),
                        reads=[f"pj{si}", f"esb{pa}"], writes=[f"qsl{pa}"])
                    p.emit(q_stq if dst == "QT" else q_stk,
                           lambda e, fout=fout, sl=sl, pa=pa, dst=dst: e.dma_start(
                        out=io[dst][fout, :, sl * 512:(sl + 1) * 512], in_=qsl[pa][:]),
                        reads=[f"qsl{pa}"], writes=[dst + "w"], tag=f"st{dst}")
    p.barrier_all("front_sub1")
    tls = S("tls", [128, H * NT])
    tls2 = S("tls2", [128, H * NT])
    ukeys = [f"u_{i}" for i in range(NT)]
    p.emit("act", lambda e: e.activation(out=tls[:], in_=u_all[:], func=AF.Abs),
           reads=ukeys, writes=["tls"])
    p.emit("act", lambda e: e.activation(out=tls[:], in_=tls[:], func=AF.Exp,
           scale=-1.0), reads=["tls"], writes=["tls"])
    p.emit("act", lambda e: e.activation(out=tls[:], in_=tls[:], func=AF.Ln,
           bias=1.0), reads=["tls"], writes=["tls"])
    p.emit("dve", lambda e: e.tensor_scalar(out=tls2[:], in0=u_all[:],
           scalar1=0.0, scalar2=None, op0=ALU.min), reads=ukeys, writes=["tls2"])
    p.emit("dve", lambda e: e.tensor_sub(u_all[:], tls2[:], tls[:]),
           reads=["tls", "tls2"] + ukeys, writes=["loga"] + ukeys)
    p.barrier_all("front_sub2")

    cumtm_ps = pj[0][:, 0:H]
    cumhm_ps = pj[1][0:H, 0:128]
    tot_ps = pj[2][0:1, 0:H]
    cumtm_all = S("cumtm_all", [128, H * NCH])
    cumhm_sb = S("cumhm_sb", [H, 128])
    cumg_sb = S("cumg_sb", [H, 128])
    chl16 = S("chl16", [H, 128], BF16)
    clo32 = S("clo32", [H, 128])
    clo16 = S("clo16", [H, 128], BF16)
    pfx = S("pfx", [H, 2])
    tot_all = S("tot_all", [1, H * NCH])
    p.emit("dve", lambda e: e.memset(pfx[:], 0.0), writes=["pfx"], tag="pfx0")
    totc_ps = pj[3][0:H, 0:1]
    for c in range(NCH):
        p.emit("pe", lambda e, c=c: e.matmul(
            cumtm_ps, cst["U128"][:], u_all[:, c * H:(c + 1) * H],
            start=True, stop=True), reads=["loga", "c_U128"],
            writes=["pj0"], tag=f"cumtm{c}")
        p.emit("pe", lambda e, c=c: e.matmul(
            cumhm_ps, u_all[:, c * H:(c + 1) * H], cst["U128"][:],
            start=True, stop=True), reads=["loga", "c_U128"],
            writes=["pj1"], tag=f"cumhm{c}")
        p.emit("pe", lambda e, c=c: e.matmul(
            tot_ps, cst["ones_128x1"][:], u_all[:, c * H:(c + 1) * H],
            start=True, stop=True), reads=["loga", "c_ones_128x1"],
            writes=["pj2"], tag=f"tot{c}")
        p.emit("dve", lambda e, c=c: e.tensor_copy(
            cumtm_all[:, c * H:(c + 1) * H], cumtm_ps),
            reads=["pj0"], writes=["cumtm_all"])
        p.emit("dve", lambda e, c=c: e.tensor_copy(cumhm_sb[:], cumhm_ps),
               reads=["pj1"], writes=["cumhm_sb"])
        p.emit("dve", lambda e, c=c: e.tensor_copy(
            tot_all[:, c * H:(c + 1) * H], tot_ps),
            reads=["pj2"], writes=["tot_all"])
        p.emit(q_stc, lambda e, c=c: e.dma_start(
            out=io["cum_nat"][:, c * 128:(c + 1) * 128], in_=cumhm_sb[:]),
            reads=["cumhm_sb"], writes=["cumnatw"], tag="stnat")
        p.emit("dve", lambda e: e.tensor_copy(chl16[:], cumhm_sb[:]),
               reads=["cumhm_sb"], writes=["chl16"], tag=f"chl{c}")
        p.emit("dve", lambda e: e.tensor_tensor(
            out=clo32[:], in0=cumhm_sb[:], in1=chl16[:], op=ALU.subtract),
            reads=["cumhm_sb", "chl16"], writes=["clo32"], tag=f"clo{c}")
        p.emit("dve", lambda e: e.tensor_copy(clo16[:], clo32[:]),
               reads=["clo32"], writes=["clo16"], tag=f"clob{c}")
        p.emit(q_stc, lambda e, c=c: e.dma_start(
            out=io["cumhl"][0, :, c * 128:(c + 1) * 128], in_=chl16[:]),
            reads=["chl16"], writes=["cumhlw"], tag="sthl")
        p.emit(q_stc, lambda e, c=c: e.dma_start(
            out=io["cumhl"][1, :, c * 128:(c + 1) * 128], in_=clo16[:]),
            reads=["clo16"], writes=["cumhlw"], tag="sthl2")
        p.emit("pe", lambda e, c=c: e.matmul(
            totc_ps, u_all[:, c * H:(c + 1) * H], cst["ones_128x1"][:],
            start=True, stop=True), reads=["loga", "c_ones_128x1"],
            writes=["pj3"], tag=f"totc{c}")
        p.emit("dve", lambda e, c=c: e.tensor_scalar(
            out=cumg_sb[:], in0=cumhm_sb[:], scalar1=pfx[:, 0:1], scalar2=None,
            op0=ALU.add), reads=["cumhm_sb", "pfx"], writes=["cumg_sb"],
            tag=f"cumg{c}")
        p.emit(q_stc, lambda e, c=c: e.dma_start(
            out=io["cumg_nat"][:, c * 128:(c + 1) * 128], in_=cumg_sb[:]),
            reads=["cumg_sb"], writes=["cumgw"], tag="stcumg")
        p.emit("dve", lambda e: e.tensor_copy(chl16[:], cumg_sb[:]),
               reads=["cumg_sb", "cumhlw"], writes=["chl16"], tag=f"ghl{c}")
        p.emit("dve", lambda e: e.tensor_tensor(
            out=clo32[:], in0=cumg_sb[:], in1=chl16[:], op=ALU.subtract),
            reads=["cumg_sb", "chl16"], writes=["clo32"], tag=f"glo{c}")
        p.emit("dve", lambda e: e.tensor_copy(clo16[:], clo32[:]),
               reads=["clo32"], writes=["clo16"], tag=f"glob{c}")
        p.emit(q_stc, lambda e, c=c: e.dma_start(
            out=io["cumghl"][0, :, c * 128:(c + 1) * 128], in_=chl16[:]),
            reads=["chl16"], writes=["cumghlw"], tag="stghl")
        p.emit(q_stc, lambda e, c=c: e.dma_start(
            out=io["cumghl"][1, :, c * 128:(c + 1) * 128], in_=clo16[:]),
            reads=["clo16"], writes=["cumghlw"], tag="stghl2")
        p.emit("dve", lambda e, c=c: e.scalar_tensor_tensor(
            out=pfx[:, 0:1], in0=totc_ps, scalar=1.0, in1=pfx[:, 0:1],
            op0=ALU.mult, op1=ALU.add), reads=["pj3", "pfx"], writes=["pfx"],
            tag=f"pfxu{c}")
    p.emit(q_stc, lambda e: e.dma_start(out=io["cumtm"], in_=cumtm_all[:]),
           reads=["cumtm_all"], writes=["cumtmw"])
    p.emit(q_stc, lambda e: e.dma_start(out=io["totals"], in_=tot_all[:]),
           reads=["tot_all"], writes=["totw"])
    for j in range(ND):
        p.emit(q_stt, lambda e, j=j: e.dma_start(
            out=io["xnT"][j, :, :], in_=xnT[j][:]),
            reads=xnT_keys(), writes=["xnTw"], tag=f"stxnT{j}")
    return dict(xnT=xnT, xnT_keys=xnT_keys)


def phase_scan(p, nc, es, g, io, cst, pref="s"):
    TH, D, ND, H, CH, NCH, DH = g.TH, g.D, g.ND, g.H, g.CH, g.NCH, g.DH
    HP = H // 2
    SCOLS = HP * DH + HP
    S = lambda nm, shp, dt=F32: es.enter_context(nc.sbuf_tensor(pref + nm, shp, dt))
    PS = lambda nm, shp, dt=F32: es.enter_context(nc.psum_tensor(pref + nm, shp, dt))

    q_lq = p.dma_queue(pref + "_lq", "sync")
    q_lk = p.dma_queue(pref + "_lk", "sync")
    q_lkm = p.dma_queue(pref + "_lkm", "sync")
    q_lv = p.dma_queue(pref + "_lv", "sync")
    q_lcf = p.dma_queue(pref + "_lcf", "gpsimd")
    q_lch = p.dma_queue(pref + "_lch", "gpsimd")
    q_ldc = p.dma_queue(pref + "_ldc", "gpsimd")
    q_st = p.dma_queue(pref + "_st", "sync")

    cumtm = S("cumtm_r", [128, H * NCH])
    totals = S("totals_r", [1, H * NCH])
    p.emit(q_ldc, lambda e: e.dma_start(out=cumtm[:], in_=io["cumtm"]),
           writes=["cumtm_r"])
    p.emit(q_ldc, lambda e: e.dma_start(out=totals[:], in_=io["totals"]),
           writes=["totals_r"])
    S32 = S("S32", [128, SCOLS])
    Sb = S("Sb", [128, SCOLS], BF16)
    p.emit("dve", lambda e: e.memset(S32[:], 0.0), writes=["S32"])
    p.emit("dve", lambda e: e.tensor_copy(Sb[:], S32[:]), reads=["S32"],
           writes=["Sb"])

    PB = PS("PB", [128, 2048])
    PP = PS("PP", [128, 2048])
    POv = PB[:, 0:1024]
    SIv = PB[:, 1024:1024 + SCOLS]

    qt = [S(f"qt{i}", [128, ND * CH], BF16) for i in range(2)]
    kt = [S(f"kt{i}", [128, ND * CH], BF16) for i in range(2)]
    ktm = [S(f"ktm{i}", [128, D], BF16) for i in range(2)]
    vtm = [S(f"vtm{i}", [128, D], BF16) for i in range(2)]
    cfl = [S(f"cfl{i}", [2, H * CH], BF16) for i in range(2)]
    chm = [S(f"chm{i}", [H, CH]) for i in range(2)]
    chmh = [S(f"chmh{i}", [2 * H, CH], BF16) for i in range(2)]
    ecum = S("ecum", [128, H * CH], BF16)
    elog = S("elog", [128, H * CH])
    Etile = S("Etile", [128, H * CH], BF16)
    Pm = S("Pm", [128, H * CH], BF16)
    PmE = S("PmE", [128, H * CH], BF16)
    qte = S("qte", [128, ND * CH], BF16)
    denr = S("denr", [1, H * CH])
    outc32 = [S(f"outc32{i}", [128, ND * CH]) for i in range(2)]
    e2vhm = S("e2vhm", [H, CH], BF16)
    e2vhm32 = S("e2vhm32", [H, CH])
    e2vb = S("e2vb", [128, H * DH], BF16)
    vh = S("vh", [128, H * DH], BF16)
    tcol = S("tcol", [H, 1])
    ecolE = S("ecolE", [HP, 1])
    ecolO = S("ecolO", [HP, 1])
    ehmE = S("ehmE", [HP, DH])
    ehmO = S("ehmO", [HP, DH])
    stmp = S("stmp", [128, SCOLS])

    cum_flat_view = io["cumhl"].rearrange("w (a b) t -> w b a t", b=2)

    for c in range(NCH):
        pa = c % 2
        kq, kk, kkm, kv, kcf, kch = (f"qt{pa}", f"kt{pa}", f"ktm{pa}",
                                     f"vtm{pa}", f"cfl{pa}", f"chm{pa}")
        p.emit(q_lq, lambda e, c=c, pa=pa: e.dma_start(
            out=qt[pa][:].rearrange("p (j t) -> p j t", j=ND),
            in_=io["QT"][:, :, c * CH:(c + 1) * CH].rearrange("j p t -> p j t")),
            writes=[kq], tag=f"ldq{c}")
        p.emit(q_lk, lambda e, c=c, pa=pa: e.dma_start(
            out=kt[pa][:].rearrange("p (j t) -> p j t", j=ND),
            in_=io["KT"][:, :, c * CH:(c + 1) * CH].rearrange("j p t -> p j t")),
            writes=[kk], tag=f"ldk{c}")
        p.emit(q_lkm, lambda e, c=c, pa=pa: e.dma_start(
            out=ktm[pa][:], in_=io["Ktm"][c * CH:(c + 1) * CH, :]),
            writes=[kkm], tag=f"ldktm{c}")
        p.emit(q_lv, lambda e, c=c, pa=pa: e.dma_start(
            out=vtm[pa][:], in_=io["V"][c * CH:(c + 1) * CH, :]),
            writes=[kv], tag=f"ldv{c}")
        for w in range(2):
            p.emit(q_lcf, lambda e, c=c, pa=pa, w=w: e.dma_start(
                out=cfl[pa][w:w + 1, :].rearrange("o (b a t) -> o b a t", b=2, a=ND),
                in_=cum_flat_view[w:w + 1, :, :, c * CH:(c + 1) * CH].squeeze(0).unsqueeze(0)),
                writes=[kcf], tag=f"ldcf{c}_{w}")
        p.emit(q_lch, lambda e, c=c, pa=pa: e.dma_start(
            out=chm[pa][:], in_=io["cum_nat"][:, c * CH:(c + 1) * CH]),
            writes=[kch], tag=f"ldch{c}")
        p.emit(q_lch, lambda e, c=c, pa=pa: e.dma_start(
            out=chmh[pa][:],
            in_=io["cumhl"][:, :, c * CH:(c + 1) * CH].rearrange(
                "w a t -> (w a) t")),
            writes=[kch + "h"], tag=f"ldchh{c}")

        for h in range(H):
            b, j = h % 2, h // 2
            p.emit("pe", lambda e, b=b, j=j, h=h, pa=pa: e.matmul(
                PP[:, g.col(h):g.col(h) + CH],
                kt[pa][b * 64:(b + 1) * 64, j * CH:(j + 1) * CH],
                qt[pa][b * 64:(b + 1) * 64, j * CH:(j + 1) * CH],
                start=True, stop=True),
                reads=[kq, kk], writes=["PP"], tag=f"pp{c}_{h}")
        for sl in range(4):
            p.emit("pe", lambda e, sl=sl, pa=pa: e.matmul(
                PB[:, sl * 512:(sl + 1) * 512], cst["ones_2x128b"][:],
                cfl[pa][0:2, sl * 512:(sl + 1) * 512], start=True, stop=True),
                reads=[kcf, "c_ones_2x128b"], writes=["PB"], tag=f"b1_{c}_{sl}")
        p.emit("act", lambda e: e.activation(out=ecum[:], in_=PB[:],
               func=AF.Exp), reads=["PB"], writes=["ecum"], tag=f"ecum{c}")
        for sl in range(4):
            p.emit("pe", lambda e, sl=sl, pa=pa: e.matmul(
                PB[:, sl * 512:(sl + 1) * 512], cst["ones_2x128b"][:],
                cfl[pa][0:2, sl * 512:(sl + 1) * 512], start=True, stop=False),
                reads=[kcf, "c_ones_2x128b"], writes=["PB"], tag=f"b1b_{c}_{sl}")
        for sl in range(4):
            p.emit("pe", lambda e, sl=sl, pa=pa: e.matmul(
                PB[:, sl * 512:(sl + 1) * 512], chmh[pa][:],
                cst["negIexp2b"][:, sl * 512:(sl + 1) * 512],
                start=False, stop=True),
                reads=[kch + "h", "c_negIexp2b"], writes=["PB"], tag=f"b2_{c}_{sl}")
        p.emit("act", lambda e: e.activation(out=elog[:], in_=PB[:],
               func=AF.Relu, scale=-1.0), reads=["PB"], writes=["elog"],
               tag=f"elog{c}")
        p.emit("act", lambda e: e.activation(out=Etile[:], in_=elog[:],
               func=AF.Exp, scale=-1.0), reads=["elog"], writes=["Etile"],
               tag=f"E{c}")
        for h in range(H):
            b, j = h % 2, h // 2
            p.emit("dve", lambda e, b=b, j=j, h=h, pa=pa: e.tensor_tensor(
                out=qte[b * 64:(b + 1) * 64, j * CH:(j + 1) * CH],
                in0=qt[pa][b * 64:(b + 1) * 64, j * CH:(j + 1) * CH],
                in1=ecum[b * 64:(b + 1) * 64, g.col(h):g.col(h) + CH],
                op=ALU.mult), reads=[kq, "ecum"], writes=["qte"],
                tag=f"qte{c}_{h}")
        p.emit("dve", lambda e: e.tensor_tensor(
            out=Pm[:], in0=PP[:], in1=cst["maskU_rep"][:], op=ALU.mult),
            reads=["PP", "c_maskU_rep"], writes=["Pm"], tag=f"pm{c}")
        p.emit("dve", lambda e: e.tensor_tensor(
            out=PmE[:], in0=Pm[:], in1=Etile[:], op=ALU.mult),
            reads=["Pm", "Etile"], writes=["PmE"], tag=f"pme{c}")
        po_banks = {}
        for h in range(H):
            po_banks.setdefault(((h // 2) // 4, h % 2), []).append(h)
        for bank, hs in po_banks.items():
            for k, h in enumerate(hs):
                b, j = h % 2, h // 2
                p.emit("pe", lambda e, b=b, j=j, h=h, st=(k == 0): e.matmul(
                    POv[b * 64:(b + 1) * 64, j * CH:(j + 1) * CH],
                    Sb[b * 64:(b + 1) * 64, j * DH:(j + 1) * DH],
                    qte[b * 64:(b + 1) * 64, j * CH:(j + 1) * CH],
                    start=st, stop=False, tile_position=(b * 64, b * 64),
                    skip_group_check=True),
                    reads=["Sb", "qte"], writes=["PB"], tag=f"oex{c}_{h}")
        for sl in range(4):
            p.emit("pe", lambda e, sl=sl: e.matmul(
                PP[0:1, sl * 512:(sl + 1) * 512], cst["ones_128x1b"][:],
                PmE[:, sl * 512:(sl + 1) * 512], start=True, stop=False),
                reads=["PmE", "c_ones_128x1b"], writes=["PP"], tag=f"dsum{c}_{sl}")
        for h in range(H):
            b, j = h % 2, h // 2
            p.emit("pe", lambda e, b=b, j=j, h=h: e.matmul(
                PP[0:1, g.col(h):g.col(h) + CH],
                Sb[b * 64:(b + 1) * 64, 512 + j:513 + j],
                qte[b * 64:(b + 1) * 64, j * CH:(j + 1) * CH],
                start=False, stop=False),
                reads=["Sb", "qte"], writes=["PP"], tag=f"dz{c}_{h}")
        for sl in range(4):
            p.emit("pe", lambda e, sl=sl: e.matmul(
                PP[0:1, sl * 512:(sl + 1) * 512],
                cst["ones_128x1b"][0:1, 0:1],
                cst["eps_1x512b"][:], start=False, stop=True),
                reads=["c_eps_1x512b", "c_ones_128x1b"], writes=["PP"],
                tag=f"deps{c}_{sl}")
        for bank, hs in po_banks.items():
            for k, h in enumerate(hs):
                b, j = h % 2, h // 2
                p.emit("pe", lambda e, b=b, j=j, h=h, pa=pa,
                       sp=(k == len(hs) - 1): e.matmul(
                    POv[b * 64:(b + 1) * 64, j * CH:(j + 1) * CH],
                    vtm[pa][:, h * DH:(h + 1) * DH],
                    PmE[:, g.col(h):g.col(h) + CH],
                    start=False, stop=sp, tile_position=(0, b * 64),
                    skip_group_check=True),
                    reads=[kv, "PmE"], writes=["PB"], tag=f"oin{c}_{h}")
        p.emit("act", lambda e, pa=pa: e.copy(denr[:], PP[0:1, :]),
               reads=["PP"], writes=["denr"], tag=f"denr{c}")
        p.emit(q_st, lambda e, c=c: e.dma_start(
            out=io["den_hbm"][c:c + 1, :], in_=denr[:]),
            reads=["denr"], writes=["denw"], tag=f"stden{c}")
        p.emit("act", lambda e, pa=pa: e.copy(outc32[pa][:], POv),
               reads=["PB"], writes=[f"outc32{pa}"], tag=f"nev{c}")
        p.emit(q_st, lambda e, c=c, pa=pa: e.dma_start(
            out=io["num_hbm"][c, :, :], in_=outc32[pa][:]),
            reads=[f"outc32{pa}"], writes=["numw"], tag=f"stnum{c}")
        p.emit("pe", lambda e, c=c: e.matmul(
            PP[0:H, 1040:1041], totals[:, c * H:(c + 1) * H],
            cst["ones_1x128"][0:1, 0:1], start=True, stop=True),
            reads=["totals_r", "c_ones_1x128"], writes=["PP"], tag=f"tcol{c}")
        p.emit("pe", lambda e, c=c: e.matmul(
            PP[0:HP, 1042:1043], totals[0:1, c * H:(c + 1) * H:2],
            cst["ones_1x128"][0:1, 0:1], start=True, stop=True),
            reads=["totals_r", "c_ones_1x128"], writes=["PP"], tag=f"tcE{c}")
        p.emit("pe", lambda e, c=c: e.matmul(
            PP[0:HP, 1044:1045], totals[0:1, c * H + 1:(c + 1) * H:2],
            cst["ones_1x128"][0:1, 0:1], start=True, stop=True),
            reads=["totals_r", "c_ones_1x128"], writes=["PP"], tag=f"tcO{c}")
        p.emit("dve", lambda e: e.tensor_copy(tcol[:], PP[0:H, 1040:1041]),
               reads=["PP"], writes=["tcolS"], tag=f"tcolc{c}")
        p.emit("act", lambda e: e.activation(out=ecolE[:], in_=PP[0:HP, 1042:1043],
               func=AF.Exp), reads=["PP"], writes=["ecolE"], tag=f"ecE{c}")
        p.emit("act", lambda e: e.activation(out=ecolO[:], in_=PP[0:HP, 1044:1045],
               func=AF.Exp), reads=["PP"], writes=["ecolO"], tag=f"ecO{c}")
        p.emit("dve", lambda e, pa=pa: e.tensor_scalar(
            out=e2vhm32[:], in0=chm[pa][:], scalar1=tcol[:, 0:1], scalar2=-1.0,
            op0=ALU.subtract, op1=ALU.mult), reads=[kch, "tcolS"],
            writes=["e2vhm32"], tag=f"e2v32{c}")
        p.emit("act", lambda e: e.activation(out=e2vhm[:], in_=e2vhm32[:],
               func=AF.Exp), reads=["e2vhm32"], writes=["e2vhm"], tag=f"e2v{c}")
        for half in range(2):
            p.emit("pe", lambda e, half=half: e.matmul(
                PP[:, half * 512:(half + 1) * 512], e2vhm[:],
                cst["delta64b"][:, half * 512:(half + 1) * 512],
                start=True, stop=True),
                reads=["e2vhm", "c_delta64b"], writes=["PP"], tag=f"e2vb{c}_{half}")
        p.emit("act", lambda e: e.copy(e2vb[:], PP[:, 0:1024]),
               reads=["PP"], writes=["e2vb"], tag=f"e2vbc{c}")
        p.emit("dve", lambda e, pa=pa: e.tensor_tensor(
            out=vh[:], in0=vtm[pa][:], in1=e2vb[:], op=ALU.mult),
            reads=[kv, "e2vb"], writes=["vh"], tag=f"vh{c}")
        for h in range(H):
            b, j = h % 2, h // 2
            p.emit("pe", lambda e, b=b, j=j, h=h, pa=pa: e.matmul(
                SIv[b * 64:(b + 1) * 64, j * DH:(j + 1) * DH],
                ktm[pa][:, h * DH:(h + 1) * DH], vh[:, h * DH:(h + 1) * DH],
                start=True, stop=True, tile_position=(0, b * 64)),
                reads=[kkm, "vh"], writes=["PB"], tag=f"si{c}_{h}")
            p.emit("pe", lambda e, b=b, j=j, h=h, pa=pa: e.matmul(
                SIv[b * 64:(b + 1) * 64, 512 + j:513 + j],
                ktm[pa][:, h * DH:(h + 1) * DH],
                e2vb[:, h * DH:h * DH + 1],
                start=True, stop=True, tile_position=(0, b * 64)),
                reads=[kkm, "e2vb"], writes=["PB"], tag=f"siz{c}_{h}")
        p.emit("dve", lambda e: e.tensor_scalar(
            out=ehmE[:], in0=cst["ones_16x64"][0:HP, :],
            scalar1=ecolE[:, 0:1], scalar2=None, op0=ALU.mult),
            reads=["ecolE", "c_ones_16x64"], writes=["ehmE"], tag=f"ehE{c}")
        p.emit("dve", lambda e: e.tensor_scalar(
            out=ehmO[:], in0=cst["ones_16x64"][0:HP, :],
            scalar1=ecolO[:, 0:1], scalar2=None, op0=ALU.mult),
            reads=["ecolO", "c_ones_16x64"], writes=["ehmO"], tag=f"ehO{c}")
        for (ehm, kehm, pb, tp) in ((ehmE, "ehmE", 0, None), (ehmO, "ehmO", 64, (0, 64))):
            p.emit("pe", lambda e, ehm=ehm, pb=pb, tp=tp: e.matmul(
                PP[pb:pb + 64, 1024:1536], ehm[:], cst["deltaP"][:, 0:512],
                start=True, stop=True, tile_position=tp),
                reads=[kehm, "c_deltaP"], writes=["PP"], tag=f"etbS{c}_{pb}")
            p.emit("pe", lambda e, ehm=ehm, pb=pb, tp=tp: e.matmul(
                PP[pb:pb + 64, 1536:1536 + HP], ehm[:], cst["deltaP"][:, 512:512 + HP],
                start=True, stop=True, tile_position=tp),
                reads=[kehm, "c_deltaP"], writes=["PP"], tag=f"etbZ{c}_{pb}")
        p.emit("dve", lambda e: e.tensor_tensor(
            out=stmp[:], in0=S32[:], in1=PP[:, 1024:1024 + SCOLS],
            op=ALU.mult), reads=["S32", "PP"], writes=["stmp"], tag=f"sdec{c}")
        p.emit("dve", lambda e: e.tensor_tensor(
            out=S32[:], in0=stmp[:], in1=SIv, op=ALU.add),
            reads=["stmp", "PB"], writes=["S32"], tag=f"supd{c}")
        p.emit("act", lambda e: e.copy(Sb[:], S32[:]),
               reads=["S32"], writes=["Sb"], tag=f"sbc{c}")
    p.emit(q_st, lambda e: e.dma_start(out=io["Sfin"], in_=S32[:]),
           reads=["S32"], writes=["Sfinw"], tag="stSfin")


def _ehT_keys(pref, lo, hi, MACRO):
    return [f"{pref}ehT_{h}" for h in range(lo // MACRO,
                                            (hi + MACRO - 1) // MACRO)]


def emit_expert_block(p, nc, g, cst, pref, m, e, NF, actfn, xT_m, kxT,
                      w1sb, w2sb, q_lw1, q_lw2, io_w1, io_w2,
                      b1e, io_b1, b2e, io_b2, ehT_m, PSA, PSB, y, wgate,
                      kwg, D=1024):
    MACRO, ND, E = g.MACRO, g.ND, g.E
    NTM = MACRO // 128
    assert NTM * 512 * 2 == MACRO // 128 * 1024
    NHC = NF // 128
    NSL = NF // 512
    kb1f, kb2f = pref + "b1e", pref + "b2e"

    for sl in range(NSL):
        b = sl % 2
        kw1 = f"{pref}w1_{b}"
        p.emit(q_lw1, lambda e_, e=e, sl=sl, b=b: e_.dma_start(
            out=w1sb[b][:].rearrange("p (dc h) -> p dc h", dc=ND),
            in_=io_w1[e].rearrange("(dc p) h -> p dc h", p=128)[:, :, sl * 512:(sl + 1) * 512]),
            writes=[kw1], tag=f"{pref}lw1_{m}_{e}_{sl}")
        for hq in range(4):
            hc = sl * 4 + hq
            pb = hc % 2
            kpsa = f"{pref}PSA{pb}"
            for d in range(ND):
                for tsl in range(MACRO // 512):
                    p.emit("pe", lambda e_, d=d, hq=hq, tsl=tsl, b=b, pb=pb: e_.matmul(
                        PSA[:, pb * 1024 + tsl * 512: pb * 1024 + (tsl + 1) * 512],
                        w1sb[b][:, d * 512 + hq * 128: d * 512 + (hq + 1) * 128],
                        xT_m[:, d * MACRO + tsl * 512: d * MACRO + (tsl + 1) * 512],
                        start=(d == 0), stop=(d == ND - 1)),
                        reads=[kw1, kxT], writes=[kpsa], tag=f"{pref}eA{m}_{e}_{hc}")
            p.emit("act", lambda e_, hc=hc, pb=pb, kpsa=kpsa: e_.activation(
                out=ehT_m[:, hc * MACRO:(hc + 1) * MACRO],
                in_=PSA[:, pb * 1024: pb * 1024 + MACRO], func=actfn,
                bias=b1e[:, e * NHC + hc: e * NHC + hc + 1]),
                reads=[kpsa, kb1f], writes=[f"{pref}ehT_{hc}"],
                tag=f"{pref}eAs{m}_{e}_{hc}")
    ehkeys = [f"{pref}ehT_{hc}" for hc in range(NHC)]
    NHG = NHC // 4
    for dhalf in range(2):
        for hg in range(NHG):
            b = hg % 2
            kw2 = f"{pref}w2_{b}"
            p.emit(q_lw2, lambda e_, e=e, hg=hg, b=b, dhalf=dhalf: e_.dma_start(
                out=w2sb[b][:].rearrange("p (hc f) -> p hc f", hc=4),
                in_=io_w2[e][hg * 512:(hg + 1) * 512,
                             dhalf * 512:(dhalf + 1) * 512]
                .rearrange("(hc p) f -> p hc f", p=128)),
                writes=[kw2], tag=f"{pref}lw2_{m}_{e}_{dhalf}_{hg}")
            for hq in range(4):
                hc = hg * 4 + hq
                for i in range(NTM):
                    PSx = PSB if i < 4 else PSA
                    kps = (pref + ("PSB" if i < 4 else "PSA") + str((i % 4) // 2))
                    off = (i % 4) * 512
                    p.emit("pe", lambda e_, hc=hc, hq=hq, i=i, b=b, PSx=PSx, off=off: e_.matmul(
                        PSx[:, off:off + 512],
                        ehT_m[:, hc * MACRO + i * 128: hc * MACRO + (i + 1) * 128],
                        w2sb[b][:, hq * 512:(hq + 1) * 512], start=(hc == 0), stop=False),
                        reads=[kw2, ehkeys[hc]], writes=[kps],
                        tag=f"{pref}eB{m}_{e}_{dhalf}_{i}")
        for i in range(NTM):
            PSx = PSB if i < 4 else PSA
            kps = (pref + ("PSB" if i < 4 else "PSA") + str((i % 4) // 2))
            off = (i % 4) * 512
            p.emit("pe", lambda e_, e=e, i=i, dhalf=dhalf, PSx=PSx, off=off: e_.matmul(
                PSx[:, off:off + 512], cst["ones_2x128b"][:],
                b2e[0:2, e * D + dhalf * 512: e * D + (dhalf + 1) * 512],
                start=False, stop=True),
                reads=[kb2f, "c_ones_2x128b"], writes=[kps],
                tag=f"{pref}eBb{m}_{e}_{dhalf}_{i}")
            p.emit("dve", lambda e_, e=e, i=i, dhalf=dhalf, PSx=PSx, off=off: e_.scalar_tensor_tensor(
                out=y[:, i * D + dhalf * 512: i * D + (dhalf + 1) * 512],
                in0=PSx[:, off:off + 512],
                scalar=wgate[:, i * g.E + e: i * g.E + e + 1],
                in1=y[:, i * D + dhalf * 512: i * D + (dhalf + 1) * 512],
                op0=ALU.mult, op1=ALU.add),
                reads=[kps, kwg, f"y_{i}"], writes=[f"y_{i}"],
                tag=f"{pref}yacc{m}_{e}_{i}")


def phase_l0back(p, nc, es, g, io, cst):
    TH, D, ND, E, F0, MACRO = g.TH, g.D, g.ND, g.E, g.F0, g.MACRO
    NM, NTM = g.NMAC, MACRO // 128
    S = lambda nm, shp, dt=F32: es.enter_context(nc.sbuf_tensor("m0" + nm, shp, dt))
    PS = lambda nm, shp, dt=F32: es.enter_context(nc.psum_tensor("m0" + nm, shp, dt))
    q_lw = p.dma_queue("m0_lw", "gpsimd")
    q_lw1 = p.dma_queue("m0_lw1", "gpsimd")
    q_lw2 = p.dma_queue("m0_lw2", "gpsimd")
    p.dma_queue("m0_lb", "gpsimd")
    q_lx = p.dma_queue("m0_lx", "sync")
    q_lxT = p.dma_queue("m0_lxT", "sync")
    q_st = p.dma_queue("m0_st", "sync")

    wo = S("wo", [128, ND * D], BF16)
    rw1 = S("rw1", [128, ND * D], BF16)
    rw2 = S("rw2", [128, ND * E], BF16)
    for t, nm in ((wo, "Wo"), (rw1, "rw1")):
        p.emit(q_lw, lambda e, t=t, nm=nm: e.dma_start(
            out=t[:].rearrange("p (dc f) -> p dc f", dc=ND),
            in_=io[nm].rearrange("(dc p) f -> p dc f", p=128)),
            writes=[t.name], tag="ld" + nm)
    p.emit(q_lw, lambda e: e.dma_start(
        out=rw2[:].rearrange("p (dc f) -> p dc f", dc=ND),
        in_=io["rw2"].rearrange("(dc p) f -> p dc f", p=128)),
        writes=["m0rw2"], tag="ldrw2")
    rb1f = S("rb1f", [128, ND]); rb2f = S("rb2f", [1, E])
    NHC0 = F0 // 128
    rb1r = S("rb1r", [ND, 128])
    p.emit(q_lw, lambda e: e.dma_start(
        out=rb1r[:], in_=io["rb1"].rearrange("o (f p) -> (o f) p", p=128)),
        writes=["m0rb1r"], tag="ldrb1")
    p.emit(q_lw, lambda e: e.dma_start(out=rb2f[:], in_=io["rb2"]),
           writes=["m0rb2f"], tag="ldrb2")
    b1e = S("b1e", [128, E * NHC0])
    b2e = S("b2e", [2, E * D], BF16)

    xnT_m = S("xnT_m", [128, ND * MACRO], BF16)
    scr = S("scr", [128, ND * MACRO], BF16)
    oT_m = scr; rT_m = scr
    ehT_m = S("ehT_m", [128, (F0 // 128) * MACRO], BF16)
    y = S("y", [128, NTM * D])
    wg = S("wg", [128, NTM * E])
    lg = S("lg", [128, 24])
    xt = [S(f"xt{i}", [128, D]) for i in range(2)]
    xnt = [S(f"xnt{i}", [128, D]) for i in range(2)]
    w1sb = [S(f"w1s{i}", [128, ND * 512], BF16) for i in range(2)]
    w2sb = [S(f"w2s{i}", [128, 2048], BF16) for i in range(2)]
    PSA = PS("PSA", [128, 2048])
    PSB = PS("PSB", [128, 2048])

    b1r = ehT_m[0:NHC0, 0:2 * E * 128].bitcast(F32)
    kb1r = _ehT_keys("m0", 0, 2 * E * 128, MACRO)
    p.emit(q_lw, lambda e: e.dma_start(
        out=b1r[:].rearrange("hc (e p) -> hc e p", e=E),
        in_=io["eb1"].rearrange("o (e hc p) -> (o hc) e p", hc=NHC0, p=128)),
        writes=["m0b1r"] + kb1r, tag="ldb1r")
    for e in range(E):
        p.emit("pe", lambda e_, e=e: e_.transpose(
            PSA[:, 0:NHC0], b1r[:, e * 128:(e + 1) * 128],
            cst["ident128"][0:NHC0, 0:NHC0]),
            reads=["m0b1r", "c_ident128"] + kb1r,
            writes=["m0PSA0"], tag=f"b1t{e}")
        p.emit("act", lambda e_, e=e: e_.copy(
            b1e[:, e * NHC0:(e + 1) * NHC0], PSA[:, 0:NHC0]),
            reads=["m0PSA0"], writes=["m0b1e"], tag=f"b1tc{e}")
    p.emit("pe", lambda e: e.transpose(
        PSA[:, 0:ND], rb1r[:], cst["ident128"][0:ND, 0:ND]),
        reads=["m0rb1r", "c_ident128"], writes=["m0PSA0"], tag="rb1t")
    p.emit("act", lambda e: e.copy(rb1f[:], PSA[:, 0:ND]),
           reads=["m0PSA0"], writes=["m0rb1f"], tag="rb1tc")
    b2st = ehT_m[0:1, 2 * D:4 * D]
    b2r = ehT_m[0:1, 4 * D:6 * D].bitcast(F32)
    b2l32 = ehT_m[0:1, 6 * D:8 * D].bitcast(F32)
    kst = _ehT_keys("m0", 2 * D, 4 * D, MACRO)
    kr = _ehT_keys("m0", 4 * D, 6 * D, MACRO)
    kl = _ehT_keys("m0", 6 * D, 8 * D, MACRO)
    for e in range(E):
        p.emit(q_lw, lambda e_, e=e: e_.dma_start(
            out=b2r[:], in_=io["eb2"][0:1, e * D:(e + 1) * D]),
            writes=["m0b2r"] + kr, tag=f"lb2r{e}")
        p.emit("dve", lambda e_: e_.tensor_copy(b2st[0:1, 0:D], b2r[:]),
               reads=["m0b2r"], writes=["m0b2hi"] + kst, tag=f"b2h{e}")
        p.emit("dve", lambda e_: e_.tensor_tensor(
            out=b2l32[:], in0=b2r[:], in1=b2st[0:1, 0:D], op=ALU.subtract),
            reads=["m0b2hi", "m0b2r"],
            writes=["m0b2l32"] + kl, tag=f"b2s{e}")
        p.emit("dve", lambda e_: e_.tensor_copy(b2st[0:1, D:2 * D], b2l32[:]),
               reads=["m0b2l32"], writes=["m0b2lo"] + kst, tag=f"b2l{e}")
        p.emit(q_lw, lambda e_, e=e: e_.dma_start(
            out=b2e[:, e * D:(e + 1) * D],
            in_=b2st[0:1, :]),
            reads=["m0b2hi", "m0b2lo"], writes=["m0b2e"], tag=f"b2p{e}")

    for m in range(NM):
        p.emit(q_lxT, lambda e, m=m: e.dma_start(
            out=xnT_m[:].rearrange("p (j t) -> p j t", j=ND),
            in_=io["xnT"][:, :, m * MACRO:(m + 1) * MACRO].rearrange("j p t -> p j t")),
            writes=["kxnTm"], tag=f"lxnT{m}")
        p.emit(q_lxT, lambda e, m=m: e.dma_start(
            out=oT_m[:].rearrange("p (j t) -> p j t", j=ND),
            in_=io["outHT"][:, :, m * MACRO:(m + 1) * MACRO].rearrange("j p t -> p j t")),
            writes=["m0scr"], tag=f"loT{m}")
        for i in range(NTM):
            pb = i % 2
            kps = f"m0PSB{pb}"
            pa = i % 2
            p.emit(q_lx, lambda e, m=m, i=i, pa=pa: e.dma_start(
                out=xt[pa][:], in_=io["x"][m * MACRO + i * 128: m * MACRO + (i + 1) * 128, :]),
                writes=[f"xt{pa}"], tag=f"lx{m}_{i}")
            p.emit(q_lx, lambda e, m=m, i=i, pa=pa: e.dma_start(
                out=xnt[pa][:], in_=io["xn32"][m * MACRO + i * 128: m * MACRO + (i + 1) * 128, :]),
                writes=[f"xnt{pa}"], tag=f"lxn{m}_{i}")
            for j in range(ND):
                for half in range(2):
                    p.emit("pe", lambda e, i=i, j=j, half=half, pb=pb: e.matmul(
                        PSB[:, pb * 1024 + half * 512: pb * 1024 + (half + 1) * 512],
                        oT_m[:, j * MACRO + i * 128: j * MACRO + (i + 1) * 128],
                        wo[:, j * D + half * 512: j * D + (half + 1) * 512],
                        start=(j == 0), stop=(j == ND - 1)),
                        reads=["m0scr", "m0wo"], writes=[kps], tag=f"wo{m}_{i}")
            p.emit("dve", lambda e, i=i, pa=pa, pb=pb: e.tensor_tensor(
                out=y[:, i * D:(i + 1) * D], in0=PSB[:, pb * 1024:(pb + 1) * 1024],
                in1=xt[pa][:], op=ALU.add),
                reads=[kps, f"xt{pa}"], writes=[f"y_{i}"], tag=f"yi{m}_{i}")
            p.emit("dve", lambda e, i=i, pa=pa: e.tensor_tensor(
                out=y[:, i * D:(i + 1) * D], in0=y[:, i * D:(i + 1) * D],
                in1=xnt[pa][:], op=ALU.add),
                reads=[f"y_{i}", f"xnt{pa}"], writes=[f"y_{i}"], tag=f"yn{m}_{i}")
        for fout in range(ND):
            pb = fout % 2
            kps = f"m0PSA{pb}"
            for d in range(ND):
                for tsl in range(MACRO // 512):
                    p.emit("pe", lambda e, fout=fout, d=d, tsl=tsl, pb=pb: e.matmul(
                        PSA[:, pb * 1024 + tsl * 512: pb * 1024 + (tsl + 1) * 512],
                        rw1[:, d * D + fout * 128: d * D + (fout + 1) * 128],
                        xnT_m[:, d * MACRO + tsl * 512: d * MACRO + (tsl + 1) * 512],
                        start=(d == 0), stop=(d == ND - 1)),
                        reads=["m0rw1", "kxnTm"], writes=[kps], tag=f"rt{m}_{fout}")
            p.emit("act", lambda e, fout=fout, pb=pb: e.activation(
                out=rT_m[:, fout * MACRO:(fout + 1) * MACRO],
                in_=PSA[:, pb * 1024: pb * 1024 + MACRO], func=AF.Silu,
                bias=rb1f[:, fout:fout + 1]),
                reads=[kps, "m0rb1f"], writes=["m0scr"], tag=f"rts{m}_{fout}")
        for i in range(NTM):
            for j in range(ND):
                p.emit("pe", lambda e, i=i, j=j: e.matmul(
                    PSB[:, 0:E], rT_m[:, j * MACRO + i * 128: j * MACRO + (i + 1) * 128],
                    rw2[:, j * E:(j + 1) * E], start=(j == 0), stop=False),
                    reads=["m0scr", "m0rw2"], writes=["m0PSB0"], tag=f"lg{m}_{i}")
            p.emit("pe", lambda e, i=i: e.matmul(
                PSB[:, 0:E], cst["ones_1x128"][:], rb2f[:],
                start=False, stop=True),
                reads=["m0rb2f", "c_ones_1x128"], writes=["m0PSB0"], tag=f"lgb{m}_{i}")
            p.emit("dve", lambda e: e.tensor_copy(lg[:, 0:E], PSB[:, 0:E]),
                   reads=["m0PSB0"], writes=["lgv"], tag=f"lgc{m}_{i}")
            p.emit("dve", lambda e: e.reduce_max(lg[:, 8:9], lg[:, 0:E], axis=AX.X),
                   reads=["lgv"], writes=["lgm1"], tag=f"gm1{m}_{i}")
            p.emit("dve", lambda e: e.tensor_scalar(
                out=lg[:, 9:10], in0=lg[:, 8:9], scalar1=-1.0, scalar2=None,
                op0=ALU.mult), reads=["lgm1"], writes=["lgnm1"], tag=f"gn{m}_{i}")
            p.emit("dve", lambda e: e.tensor_scalar(
                out=lg[:, 10:10 + E], in0=lg[:, 0:E], scalar1=lg[:, 8:9],
                scalar2=None, op0=ALU.is_ge), reads=["lgv", "lgm1"],
                writes=["lgmask1"], tag=f"gmk{m}_{i}")
            p.emit("dve", lambda e: e.scalar_tensor_tensor(
                out=lg[:, 10:10 + E], in0=lg[:, 10:10 + E], scalar=-1e30,
                in1=lg[:, 0:E], op0=ALU.mult, op1=ALU.add),
                reads=["lgmask1", "lgv"], writes=["lgl2"], tag=f"gl2{m}_{i}")
            p.emit("dve", lambda e: e.reduce_max(lg[:, 9:10], lg[:, 10:10 + E],
                   axis=AX.X), reads=["lgl2"], writes=["lgm2"], tag=f"gm2{m}_{i}")
            p.emit("dve", lambda e, i=i: e.tensor_scalar(
                out=wg[:, i * E:(i + 1) * E], in0=lg[:, 0:E], scalar1=lg[:, 9:10],
                scalar2=None, op0=ALU.is_ge), reads=["lgv", "lgm2"],
                writes=["wmask2"], tag=f"gmk2{m}_{i}")
            p.emit("dve", lambda e: e.tensor_scalar(
                out=lg[:, 10:10 + E], in0=lg[:, 0:E], scalar1=lg[:, 8:9],
                scalar2=None, op0=ALU.subtract), reads=["lgv", "lgm1"],
                writes=["lgsub"], tag=f"gsb{m}_{i}")
            p.emit("act", lambda e: e.activation(out=lg[:, 10:10 + E],
                   in_=lg[:, 10:10 + E], func=AF.Exp), reads=["lgsub"],
                   writes=["lgexp"], tag=f"gex{m}_{i}")
            p.emit("dve", lambda e, i=i: e.tensor_tensor(
                out=wg[:, i * E:(i + 1) * E], in0=wg[:, i * E:(i + 1) * E],
                in1=lg[:, 10:10 + E], op=ALU.mult),
                reads=["wmask2", "lgexp"], writes=["wgew"], tag=f"gew{m}_{i}")
            p.emit("dve", lambda e, i=i: e.reduce_sum(
                lg[:, 8:9], wg[:, i * E:(i + 1) * E], axis=AX.X),
                reads=["wgew"], writes=["lgsum"], tag=f"gsm{m}_{i}")
            p.emit("dve", lambda e: e.reciprocal(lg[:, 9:10], lg[:, 8:9]),
                   reads=["lgsum"], writes=["lgrec"], tag=f"grc{m}_{i}")
            p.emit("dve", lambda e, i=i: e.tensor_scalar(
                out=wg[:, i * E:(i + 1) * E], in0=wg[:, i * E:(i + 1) * E],
                scalar1=lg[:, 9:10], scalar2=None, op0=ALU.mult),
                reads=["wgew", "lgrec"], writes=["kwg"], tag=f"gw{m}_{i}")
        for e in range(E):
            emit_expert_block(p, nc, g, cst, "m0", m, e, F0, AF.Silu,
                              xnT_m, "kxnTm", w1sb, w2sb, q_lw1, q_lw2,
                              io["ew1"], io["ew2"], b1e, io["eb1"],
                              b2e, io["eb2"], ehT_m, PSA, PSB, y, wg, "kwg")
        for i in range(NTM):
            p.emit(q_st, lambda e, m=m, i=i: e.dma_start(
                out=io["x1"][m * MACRO + i * 128: m * MACRO + (i + 1) * 128, :],
                in_=y[:, i * D:(i + 1) * D]),
                reads=[f"y_{i}"], writes=["x1w"], tag=f"sx1_{m}_{i}")


def phase_mid(p, nc, es, g, io, cst):
    TH, D, ND, NT = g.TH, g.D, g.ND, g.NT
    S = lambda nm, shp, dt=F32: es.enter_context(nc.sbuf_tensor("md" + nm, shp, dt))
    PS = lambda nm, shp, dt=F32: es.enter_context(nc.psum_tensor("md" + nm, shp, dt))
    q_lx = p.dma_queue("md_lx", "sync")
    q_st = p.dma_queue("md_st", "sync")
    q_stg = p.dma_queue("md_stg", "sync")

    x1n = S("x1n", [128, NT * D], BF16)
    xt = [S(f"xt{i}", [128, D]) for i in range(2)]
    xc = [S(f"xc{i}", [128, D]) for i in range(2)]
    stat = S("stat", [128, 8])
    ptr = PS("ptrT", [128, 128], BF16)
    PG = PS("PG", [128, 2048])
    x1nT_sb = S("x1nT_sb", [128, ND * NT * 128], BF16)
    gev = S("gev", [128, D])

    for i in range(NT):
        pa = i % 2
        kxt, kxc = f"xt{pa}", f"xc{pa}"
        p.emit(q_lx, lambda e, i=i, pa=pa: e.dma_start(
            out=xt[pa][:], in_=io["x1"][i * 128:(i + 1) * 128, :]),
            writes=[kxt], tag=f"lx1_{i}")
        p.emit("dve", lambda e, pa=pa: e.reduce_sum(
            stat[:, 0:1], xt[pa][:], axis=AX.X), reads=[kxt], writes=["st0"])
        p.emit("dve", lambda e: e.tensor_scalar(
            out=stat[:, 1:2], in0=stat[:, 0:1], scalar1=1.0 / D, scalar2=None,
            op0=ALU.mult), reads=["st0"], writes=["stm"])
        p.emit("dve", lambda e, pa=pa: e.tensor_scalar(
            out=xc[pa][:], in0=xt[pa][:], scalar1=stat[:, 1:2], scalar2=None,
            op0=ALU.subtract), reads=[kxt, "stm"], writes=[kxc])
        p.emit("act", lambda e, pa=pa: e.activation(
            out=xt[pa][:], in_=xc[pa][:], func=AF.Square),
            reads=[kxc], writes=[kxt])
        p.emit("dve", lambda e, pa=pa: e.reduce_sum(
            stat[:, 2:3], xt[pa][:], axis=AX.X), reads=[kxt], writes=["stv"])
        p.emit("dve", lambda e: e.tensor_scalar(
            out=stat[:, 5:6], in0=stat[:, 2:3], scalar1=1.0 / D, scalar2=1e-5,
            op0=ALU.mult, op1=ALU.add), reads=["stv"], writes=["stv2"])
        p.emit("act", lambda e: e.activation(
            out=stat[:, 3:4], in_=stat[:, 5:6], func=AF.Ln),
            reads=["stv2"], writes=["stl"])
        p.emit("act", lambda e: e.activation(
            out=stat[:, 4:5], in_=stat[:, 3:4], func=AF.Exp, scale=-0.5),
            reads=["stl"], writes=["str"])
        p.emit("dve", lambda e, i=i, pa=pa: e.tensor_scalar(
            out=x1n[:, i * D:(i + 1) * D], in0=xc[pa][:], scalar1=stat[:, 4:5],
            scalar2=None, op0=ALU.mult), reads=[kxc, "str"],
            writes=[f"x1n_{i}"], tag=f"x1n{i}")
        for j in range(ND):
            p.emit("pe", lambda e, i=i, j=j: e.transpose(
                ptr[:], x1n[:, i * D + j * 128: i * D + (j + 1) * 128],
                cst["identb"][:]), reads=[f"x1n_{i}", "c_identb"],
                writes=["ptrT"], tag=f"tr{i}_{j}")
            p.emit("dve", lambda e, i=i, j=j: e.tensor_copy(
                x1nT_sb[:, (j * NT + i) * 128:(j * NT + i + 1) * 128], ptr[:]),
                reads=["ptrT"], writes=[f"x1nT_{j}_{i}"], tag=f"trc{i}_{j}")
    x1nkeys = [f"x1n_{i}" for i in range(NT)]
    for f1 in range(ND):
        pb = f1 % 2
        for i in range(NT):
            for half in range(2):
                p.emit("pe", lambda e, f1=f1, i=i, half=half, pb=pb: e.matmul(
                    PG[:, pb * 1024 + half * 512: pb * 1024 + (half + 1) * 512],
                    x1n[:, i * D + f1 * 128: i * D + (f1 + 1) * 128],
                    x1n[:, i * D + half * 512: i * D + (half + 1) * 512],
                    start=(i == 0), stop=(i == NT - 1)),
                    reads=x1nkeys, writes=["PG"], tag=f"g{f1}_{i}")
        p.emit("dve", lambda e, f1=f1, pb=pb: e.tensor_copy(
            gev[:], PG[:, pb * 1024:(pb + 1) * 1024]),
            reads=["PG"], writes=["gev"], tag=f"gev{f1}")
        p.emit(q_stg, lambda e, f1=f1: e.dma_start(
            out=io["G_half"][:, f1 * D:(f1 + 1) * D], in_=gev[:]),
            reads=["gev"], writes=["Gw"], tag=f"stG{f1}")
    tkeys = [f"x1nT_{j}_{i}" for j in range(ND) for i in range(NT)]
    for j in range(ND):
        p.emit(q_st, lambda e, j=j: e.dma_start(
            out=io["x1nT"][j, :, :],
            in_=x1nT_sb[:, j * NT * 128:(j + 1) * NT * 128]),
            reads=tkeys, writes=["x1nTw"], tag=f"strT{j}")
    if "x1nb" in io:
        p.emit(q_st, lambda e: e.dma_start(
            out=io["x1nb"].rearrange("(i p) d -> p i d", p=128),
            in_=x1n[:].rearrange("p (i d) -> p i d", i=NT)),
            reads=[f"x1n_{i}" for i in range(NT)], writes=["x1nbw"],
            tag="stx1nb")


def phase_l1back(p, nc, es, g, io, cst):
    TH, D, ND, E, F1, MACRO = g.TH, g.D, g.ND, g.E, g.F1, g.MACRO
    NM, NTM = g.NMAC, MACRO // 128
    S = lambda nm, shp, dt=F32: es.enter_context(nc.sbuf_tensor("m1" + nm, shp, dt))
    PS = lambda nm, shp, dt=F32: es.enter_context(nc.psum_tensor("m1" + nm, shp, dt))
    q_lw = p.dma_queue("m1_lw", "gpsimd")
    q_lw1 = p.dma_queue("m1_lw1", "gpsimd")
    q_lw2 = p.dma_queue("m1_lw2", "gpsimd")
    p.dma_queue("m1_lb", "gpsimd")
    q_lx = p.dma_queue("m1_lx", "sync")
    q_lxT = p.dma_queue("m1_lxT", "sync")
    q_st = p.dma_queue("m1_st", "sync")

    PSA = PS("PSA", [128, 2048])
    PSB = PS("PSB", [128, 2048])
    Msb = S("Msb", [128, ND * D], BF16)
    es_m = ExitStack()
    Sm = lambda nm, shp, dt=F32: es_m.enter_context(nc.sbuf_tensor("m1" + nm, shp, dt))
    gta = [Sm(f"gta{i}", [128, D]) for i in range(2)]
    gtb = [Sm(f"gtb{i}", [128, D]) for i in range(2)]
    gb16 = Sm("gb16", [128, ND * D], BF16)
    for f1 in range(ND):
        pa = f1 % 2
        p.emit(q_lx, lambda e, f1=f1, pa=pa: e.dma_start(
            out=gta[pa][:], in_=io["Ga"][:, f1 * D:(f1 + 1) * D]),
            reads=["Gw"], writes=[f"gta{pa}"], tag=f"lGa{f1}")
        p.emit(q_lxT, lambda e, f1=f1, pa=pa: e.dma_start(
            out=gtb[pa][:], in_=io["Gb"][:, f1 * D:(f1 + 1) * D]),
            reads=["Gbw"], writes=[f"gtb{pa}"], tag=f"lGb{f1}")
        p.emit("dve", lambda e, f1=f1, pa=pa: e.tensor_tensor(
            out=gb16[:, f1 * D:(f1 + 1) * D], in0=gta[pa][:], in1=gtb[pa][:],
            op=ALU.add), reads=[f"gta{pa}", f"gtb{pa}"], writes=["gb16"],
            tag=f"gc{f1}")
    wq1 = Sm("wq1", [128, ND * D], BF16)
    p.emit(q_lw, lambda e: e.dma_start(
        out=wq1[:].rearrange("p (dc f) -> p dc f", dc=ND),
        in_=io["W_q"].rearrange("(dc p) f -> p dc f", p=128)),
        writes=["wq1"], tag="ldWq1")
    for hc in range(ND):
        pb = hc % 2
        for oc in range(ND):
            for half in range(2):
                p.emit("pe", lambda e, hc=hc, oc=oc, half=half, pb=pb: e.matmul(
                    PSA[:, pb * 1024 + half * 512: pb * 1024 + (half + 1) * 512],
                    wq1[:, oc * D + hc * 128: oc * D + (hc + 1) * 128],
                    gb16[:, oc * D + half * 512: oc * D + (half + 1) * 512],
                    start=(oc == 0), stop=(oc == ND - 1)),
                    reads=["wq1", "gb16"], writes=[f"m1PSA{hc % 2}"], tag=f"M{hc}_{oc}")
        p.emit("dve", lambda e, hc=hc, pb=pb: e.tensor_copy(
            Msb[:, hc * D:(hc + 1) * D], PSA[:, pb * 1024:(pb + 1) * 1024]),
            reads=[f"m1PSA{pb}"], writes=["Msb"], tag=f"Mev{hc}")
    p.barrier_all("m1_Mdone")
    es_m.close()
    gwsb = S("gwsb", [E, D], BF16)
    gwT = S("gwT", [128, ND * E], BF16)
    NHC1 = F1 // 128
    b1e = S("b1e", [128, E * NHC1])
    b2e = S("b2e", [2, E * D], BF16)
    xT_m = S("xT_m", [128, ND * MACRO], BF16)
    ehT_m = S("ehT_m", [128, (F1 // 128) * MACRO], BF16)
    y = S("y", [128, NTM * D])
    wg = S("wg", [128, NTM * E])
    lg = S("lg", [128, 24])
    xt = [S(f"xt{i}", [128, D]) for i in range(2)]
    w1sb = [S(f"w1s{i}", [128, ND * 512], BF16) for i in range(2)]
    w2sb = [S(f"w2s{i}", [128, 2048], BF16) for i in range(2)]
    gwp = PSA[:, 0:4].bitcast(BF16)
    p.emit(q_lw, lambda e: e.dma_start(out=gwsb[:], in_=io["gate_w"]),
           writes=["gwsb"], tag="ldgw")
    for j in range(ND):
        p.emit("pe", lambda e, j=j: e.transpose(
            gwp, gwsb[:, j * 128:(j + 1) * 128], cst["identb"][0:E, 0:E]),
            reads=["gwsb", "c_identb"], writes=["m1PSA0"], tag=f"gwt{j}")
        p.emit("dve", lambda e, j=j: e.tensor_copy(
            gwT[:, j * E:(j + 1) * E], gwp), reads=["m1PSA0"],
            writes=["gwT"], tag=f"gwc{j}")
    b1r = ehT_m[0:NHC1, 0:2 * E * 128].bitcast(F32)
    kb1r = _ehT_keys("m1", 0, 2 * E * 128, MACRO)
    p.emit(q_lw, lambda e: e.dma_start(
        out=b1r[:].rearrange("hc (e p) -> hc e p", e=E),
        in_=io["b1"].rearrange("o (e hc p) -> (o hc) e p", hc=NHC1, p=128)),
        writes=["m1b1r"] + kb1r, tag="ldb1r")
    for e in range(E):
        p.emit("pe", lambda e_, e=e: e_.transpose(
            PSA[:, 0:NHC1], b1r[:, e * 128:(e + 1) * 128],
            cst["ident128"][0:NHC1, 0:NHC1]),
            reads=["m1b1r", "c_ident128"] + kb1r,
            writes=["m1PSA0"], tag=f"b1t{e}")
        p.emit("act", lambda e_, e=e: e_.copy(
            b1e[:, e * NHC1:(e + 1) * NHC1], PSA[:, 0:NHC1]),
            reads=["m1PSA0"], writes=["m1b1e"], tag=f"b1tc{e}")
    b2st = ehT_m[0:1, 2 * D:4 * D]
    b2r = ehT_m[0:1, 4 * D:6 * D].bitcast(F32)
    b2l32 = ehT_m[0:1, 6 * D:8 * D].bitcast(F32)
    kst = _ehT_keys("m1", 2 * D, 4 * D, MACRO)
    kr = _ehT_keys("m1", 4 * D, 6 * D, MACRO)
    kl = _ehT_keys("m1", 6 * D, 8 * D, MACRO)
    for e in range(E):
        p.emit(q_lw, lambda e_, e=e: e_.dma_start(
            out=b2r[:], in_=io["b2"][0:1, e * D:(e + 1) * D]),
            writes=["m1b2r"] + kr, tag=f"lb2r{e}")
        p.emit("dve", lambda e_: e_.tensor_copy(b2st[0:1, 0:D], b2r[:]),
               reads=["m1b2r"], writes=["m1b2hi"] + kst, tag=f"b2h{e}")
        p.emit("dve", lambda e_: e_.tensor_tensor(
            out=b2l32[:], in0=b2r[:], in1=b2st[0:1, 0:D], op=ALU.subtract),
            reads=["m1b2hi", "m1b2r"],
            writes=["m1b2l32"] + kl, tag=f"b2s{e}")
        p.emit("dve", lambda e_: e_.tensor_copy(b2st[0:1, D:2 * D], b2l32[:]),
               reads=["m1b2l32"], writes=["m1b2lo"] + kst, tag=f"b2l{e}")
        p.emit(q_lw, lambda e_, e=e: e_.dma_start(
            out=b2e[:, e * D:(e + 1) * D],
            in_=b2st[0:1, :]),
            reads=["m1b2hi", "m1b2lo"], writes=["m1b2e"], tag=f"b2p{e}")
    for m in range(NM):
        p.emit(q_lxT, lambda e, m=m: e.dma_start(
            out=xT_m[:].rearrange("p (j t) -> p j t", j=ND),
            in_=io["x1nT"][:, :, m * MACRO:(m + 1) * MACRO].rearrange("j p t -> p j t")),
            writes=["kxTm"], tag=f"lxT{m}")
        for i in range(NTM):
            pb = i % 2
            kpsb = f"m1PSB{pb}"
            pa = i % 2
            p.emit(q_lx, lambda e, m=m, i=i, pa=pa: e.dma_start(
                out=xt[pa][:], in_=io["x1"][m * MACRO + i * 128: m * MACRO + (i + 1) * 128, :]),
                writes=[f"xt{pa}"], tag=f"lx1b{m}_{i}")
            for j in range(ND):
                for half in range(2):
                    p.emit("pe", lambda e, i=i, j=j, half=half, pb=pb: e.matmul(
                        PSB[:, pb * 1024 + half * 512: pb * 1024 + (half + 1) * 512],
                        xT_m[:, j * MACRO + i * 128: j * MACRO + (i + 1) * 128],
                        Msb[:, j * D + half * 512: j * D + (half + 1) * 512],
                        start=(j == 0), stop=(j == ND - 1)),
                        reads=["kxTm", "Msb"], writes=[kpsb], tag=f"at{m}_{i}")
            p.emit("dve", lambda e, i=i, pa=pa, pb=pb: e.tensor_tensor(
                out=y[:, i * D:(i + 1) * D], in0=PSB[:, pb * 1024:(pb + 1) * 1024],
                in1=xt[pa][:], op=ALU.add),
                reads=[kpsb, f"xt{pa}"], writes=[f"y_{i}"], tag=f"yi1{m}_{i}")
        for i in range(NTM):
            for j in range(ND):
                p.emit("pe", lambda e, i=i, j=j: e.matmul(
                    PSA[:, 0:E], xT_m[:, j * MACRO + i * 128: j * MACRO + (i + 1) * 128],
                    gwT[:, j * E:(j + 1) * E], start=(j == 0), stop=(j == ND - 1)),
                    reads=["kxTm", "gwT"], writes=["m1PSA0"], tag=f"gl{m}_{i}")
            p.emit("dve", lambda e: e.tensor_copy(lg[:, 0:E], PSA[:, 0:E]),
                   reads=["m1PSA0"], writes=["lgv"], tag=f"glc{m}_{i}")
            p.emit("dve", lambda e: e.reduce_max(lg[:, 8:9], lg[:, 0:E], axis=AX.X),
                   reads=["lgv"], writes=["lgm1"], tag=f"g1m{m}_{i}")
            p.emit("dve", lambda e, i=i: e.tensor_scalar(
                out=wg[:, i * E:(i + 1) * E], in0=lg[:, 0:E], scalar1=lg[:, 8:9],
                scalar2=None, op0=ALU.is_ge), reads=["lgv", "lgm1"],
                writes=["wmask"], tag=f"g1k{m}_{i}")
            p.emit("dve", lambda e: e.tensor_scalar(
                out=lg[:, 10:10 + E], in0=lg[:, 0:E], scalar1=lg[:, 8:9],
                scalar2=None, op0=ALU.subtract), reads=["lgv", "lgm1"],
                writes=["lgsub"], tag=f"g1s{m}_{i}")
            p.emit("act", lambda e: e.activation(out=lg[:, 10:10 + E],
                   in_=lg[:, 10:10 + E], func=AF.Exp), reads=["lgsub"],
                   writes=["lgexp"], tag=f"g1e{m}_{i}")
            p.emit("dve", lambda e: e.reduce_sum(lg[:, 8:9], lg[:, 10:10 + E],
                   axis=AX.X), reads=["lgexp"], writes=["lgsum"], tag=f"g1t{m}_{i}")
            p.emit("dve", lambda e: e.reciprocal(lg[:, 9:10], lg[:, 8:9]),
                   reads=["lgsum"], writes=["lgrec"], tag=f"g1r{m}_{i}")
            p.emit("dve", lambda e, i=i: e.tensor_scalar(
                out=wg[:, i * E:(i + 1) * E], in0=wg[:, i * E:(i + 1) * E],
                scalar1=lg[:, 9:10], scalar2=None, op0=ALU.mult),
                reads=["wmask", "lgrec"], writes=["kwg"], tag=f"g1w{m}_{i}")
        for e in range(E):
            emit_expert_block(p, nc, g, cst, "m1", m, e, F1, AF.Relu,
                              xT_m, "kxTm", w1sb, w2sb, q_lw1, q_lw2,
                              io["w1"], io["w2"], b1e, io["b1"],
                              b2e, io["b2"], ehT_m, PSA, PSB, y, wg, "kwg")
        for i in range(NTM):
            p.emit(q_st, lambda e, m=m, i=i: e.dma_start(
                out=io["out"][m * MACRO + i * 128: m * MACRO + (i + 1) * 128, :],
                in_=y[:, i * D:(i + 1) * D]),
                reads=[f"y_{i}"], writes=["outw"], tag=f"sout{m}_{i}")



GEO = Geo()
CONST_WANT = ["identb", "ident128", "U128", "ones_1x128", "ones_128x1",
              "ones_1x512", "maskU_rep", "negIexp", "eps_1x512", "delta64",
              "deltaP", "ones_16x64", "ones_128x1b", "ones_2x128b"]
CONST_BASE = ["identb", "ident128", "U128", "ones_1x128", "ones_128x1",
              "ones_1x512", "eps_1x512", "ones_128x1b", "ones_2x128b",
              "U128strict", "eCbase0", "eCbase1", "caprow0", "caprow1"]
CONST_SCAN = ["maskU_rep", "negIexp2b", "delta64b", "deltaP", "ones_16x64",
              "eps_1x512b"]


def _declare(nc, specs):
    io = {}
    for nm, shp, dt, kind in specs:
        io[nm] = nc.dram_tensor(nm, shp, dt, kind=kind).ap()
    return io


def _const_specs(consts):
    return [("c_" + nm, list(a.shape), F32, "ExternalInput")
            for nm, a in consts.items()]


def _build_launch(phase_fns, specs, consts, want=CONST_WANT):
    nc = bass.Bass("TRN2", target_bir_lowering=False)
    io = _declare(nc, specs + _const_specs(consts))
    es = ExitStack()
    with es:
        p = Prog(nc)
        cst = load_consts(p, nc, es, GEO, io, want, consts)
        for fn in phase_fns:
            with ExitStack() as es_p:
                fn(p, nc, es_p, GEO, io, cst)
                p.barrier_all("phase_end")
        block = es.enter_context(nc.Block())
        p.build(block, es=es)
    return nc


def kernel_unfused(L0_attn_x, L0_attn_Wq, L0_attn_Wk, L0_attn_Wv, L0_attn_Wo,
           L0_attn_Wa, L0_attn_ba, L0_ffn_rw1, L0_ffn_rb1, L0_ffn_rw2,
           L0_ffn_rb2, L0_ffn_ew1, L0_ffn_eb1, L0_ffn_ew2, L0_ffn_eb2,
           L1_attn_W_q, L1_ffn_gate_w, L1_ffn_w1, L1_ffn_b1, L1_ffn_w2,
           L1_ffn_b2):
    g = GEO
    TH, D, ND, E, F0, F1, H, NCH = (g.TH, g.D, g.ND, g.E, g.F0, g.F1, g.H,
                                    g.NCH)
    f32 = lambda a: np.ascontiguousarray(np.asarray(a, dtype=np.float32))
    x_full = f32(L0_attn_x)
    consts = build_consts(g)
    shards = []
    for c in range(8):
        b, half = c // 2, c % 2
        shards.append(f32(x_full[b, half * TH:(half + 1) * TH, :]))
    flat = lambda a: f32(a).reshape(1, -1)
    Wq, Wk, Wv, Wo = map(f32, (L0_attn_Wq, L0_attn_Wk, L0_attn_Wv, L0_attn_Wo))
    Wa = f32(L0_attn_Wa)
    ba = flat(L0_attn_ba)
    rw1, rw2 = f32(L0_ffn_rw1), f32(L0_ffn_rw2)
    rb1, rb2 = flat(L0_ffn_rb1), flat(L0_ffn_rb2)
    ew1, ew2 = f32(L0_ffn_ew1), f32(L0_ffn_ew2)
    eb1, eb2 = flat(L0_ffn_eb1), flat(L0_ffn_eb2)
    Wq1 = f32(L1_attn_W_q)
    gate_w = f32(L1_ffn_gate_w)
    w1, w2 = f32(L1_ffn_w1), f32(L1_ffn_w2)
    b1, b2 = flat(L1_ffn_b1), flat(L1_ffn_b2)
    cmaps = {"c_" + nm: a for nm, a in consts.items()}

    EI, EO = "ExternalInput", "ExternalOutput"
    specsA = [("x", [TH, D], F32, EI), ("Wq", [D, D], F32, EI),
              ("Wk", [D, D], F32, EI), ("Wv", [D, D], F32, EI),
              ("Wa", [D, H], F32, EI), ("ba", [1, H], F32, EI),
              ("xn32", [TH, D], F32, EO), ("xnT", [ND, 128, TH], BF16, EO),
              ("QT", [ND, 128, TH], BF16, EO), ("KT", [ND, 128, TH], BF16, EO),
              ("Ktm", [TH, D], BF16, EO), ("V", [TH, D], BF16, EO),
              ("cum_nat", [H, TH], F32, EO),
              ("cumtm", [128, H * NCH], F32, EO),
              ("totals", [1, H * NCH], F32, EO)]
    ncA = _build_launch([phase_front], specsA, consts)
    imA = [dict(x=shards[c], Wq=Wq, Wk=Wk, Wv=Wv, Wa=Wa, ba=ba, **cmaps)
           for c in range(8)]
    resA = run_bass_kernel_spmd(ncA, imA, core_ids=list(range(8))).results

    specsB = [("QT", [ND, 128, TH], BF16, EI), ("KT", [ND, 128, TH], BF16, EI),
              ("Ktm", [TH, D], BF16, EI), ("V", [TH, D], BF16, EI),
              ("cum_nat", [H, TH], F32, EI),
              ("cumtm", [128, H * NCH], F32, EI),
              ("totals", [1, H * NCH], F32, EI),
              ("S0", [128, 520], F32, EI),
              ("outHT", [ND, 128, TH], BF16, EO),
              ("Sfin", [128, 520], F32, EO)]
    ncB = _build_launch([phase_scan], specsB, consts)
    z520 = np.zeros((128, 520), np.float32)
    def imB(c, s0):
        r = resA[c]
        return dict(QT=r["QT"], KT=r["KT"], Ktm=r["Ktm"], V=r["V"],
                    cum_nat=r["cum_nat"], cumtm=r["cumtm"],
                    totals=r["totals"], S0=s0, **cmaps)
    resB1 = run_bass_kernel_spmd(
        ncB, [imB(c, z520) for c in (0, 2, 4, 6)], core_ids=[0, 2, 4, 6]).results
    resB2 = run_bass_kernel_spmd(
        ncB, [imB(c, resB1[i]["Sfin"]) for i, c in enumerate((1, 3, 5, 7))],
        core_ids=[1, 3, 5, 7]).results
    outHT = [None] * 8
    Sfin = [None] * 8
    for i, c in enumerate((0, 2, 4, 6)):
        outHT[c] = resB1[i]["outHT"]
    for i, c in enumerate((1, 3, 5, 7)):
        outHT[c] = resB2[i]["outHT"]

    specsC = [("x", [TH, D], F32, EI), ("xn32", [TH, D], F32, EI),
              ("xnT", [ND, 128, TH], BF16, EI),
              ("outHT", [ND, 128, TH], BF16, EI),
              ("Wo", [D, D], F32, EI), ("rw1", [D, D], F32, EI),
              ("rw2", [D, E], F32, EI), ("rb1", [1, D], F32, EI),
              ("rb2", [1, E], F32, EI), ("ew1", [E, D, F0], F32, EI),
              ("eb1", [1, E * F0], F32, EI), ("ew2", [E, F0, D], F32, EI),
              ("eb2", [1, E * D], F32, EI), ("x1", [TH, D], F32, EO)]
    ncC = _build_launch([phase_l0back], specsC, consts)
    imC = [dict(x=shards[c], xn32=resA[c]["xn32"], xnT=resA[c]["xnT"],
                outHT=outHT[c], Wo=Wo, rw1=rw1, rw2=rw2, rb1=rb1, rb2=rb2,
                ew1=ew1, eb1=eb1, ew2=ew2, eb2=eb2, **cmaps)
           for c in range(8)]
    resC = run_bass_kernel_spmd(ncC, imC, core_ids=list(range(8))).results

    specsD = [("x1", [TH, D], F32, EI),
              ("x1nT", [ND, 128, TH], BF16, EO),
              ("G_half", [128, ND * D], F32, EO)]
    ncD = _build_launch([phase_mid], specsD, consts)
    imD = [dict(x1=resC[c]["x1"], **cmaps) for c in range(8)]
    resD = run_bass_kernel_spmd(ncD, imD, core_ids=list(range(8))).results

    specsE = [("x1", [TH, D], F32, EI), ("x1nT", [ND, 128, TH], BF16, EI),
              ("Ga", [128, ND * D], F32, EI), ("Gb", [128, ND * D], F32, EI),
              ("W_q", [D, D], F32, EI), ("gate_w", [E, D], F32, EI),
              ("w1", [E, D, F1], F32, EI), ("b1", [1, E * F1], F32, EI),
              ("w2", [E, F1, D], F32, EI), ("b2", [1, E * D], F32, EI),
              ("out", [TH, D], F32, EO)]
    ncE = _build_launch([phase_l1back], specsE, consts)
    imE = [dict(x1=resC[c]["x1"], x1nT=resD[c]["x1nT"],
                Ga=resD[c]["G_half"], Gb=resD[c ^ 1]["G_half"],
                W_q=Wq1, gate_w=gate_w, w1=w1, b1=b1, w2=w2, b2=b2, **cmaps)
           for c in range(8)]
    resE = run_bass_kernel_spmd(ncE, imE, core_ids=list(range(8))).results

    out = np.empty((4, 8192, 1024), np.float32)
    for c in range(8):
        b, half = c // 2, c % 2
        out[b, half * TH:(half + 1) * TH, :] = resE[c]["out"]
    return out


def make_exchange(p, nc, es, g, io):
    D, ND = g.D, g.ND
    SH_S = nc.dram_tensor("SH_S", [128, 520], F32, kind="Internal",
                          addr_space="Shared").ap()
    SH_G = nc.dram_tensor("SH_G", [2, 128, ND * D], F32, kind="Internal",
                          addr_space="Shared").ap()
    SH_F = nc.dram_tensor("SH_F", [4, 16], I32, kind="Internal",
                          addr_space="Shared").ap()
    tok = es.enter_context(nc.sbuf_tensor("ex_tok", [1, 16], I32))
    fb = es.enter_context(nc.sbuf_tensor("ex_fb", [1, 16], I32))
    sbounce = es.enter_context(nc.sbuf_tensor("ex_sb", [128, 520], F32))
    gbounce = es.enter_context(nc.sbuf_tensor("ex_gb", [128, 512], F32))
    exd1 = es.enter_context(nc.semaphore("exd1"))
    exp1 = es.enter_context(nc.semaphore("exp1"))
    exd2 = es.enter_context(nc.semaphore("exd2"))
    exd3 = es.enter_context(nc.semaphore("exd3"))
    exp3 = es.enter_context(nc.semaphore("exp3"))

    def poll(e, flag_row, psem, sfx=""):
        with e.register("nd" + sfx) as nd, e.register("f" + sfx) as f, e.register("tk" + sfx) as tk, e.register("cnt" + sfx) as cnt:
            e.reg_load(tk, tok[0:1, 0:1])
            e.reg_mov(nd, 1)
            e.reg_mov(cnt, 0)
            with e.While(nd):
                e.reg_alu(cnt, cnt, 16, RALU.add)
                e.dma_start(out=fb[:], in_=SH_F[flag_row:flag_row + 1, :]).then_inc(psem, 16)
                e.wait_ge(psem, cnt)
                e.reg_load(f, fb[0:1, 0:1])
                e.reg_alu(nd, f, tk, RALU.subtract)

    def m1_fn(e):
        e.dma_start(out=tok[:], in_=io["token"]).then_inc(exd1, 16)
        e.wait_ge(exd1, 16)
        par = e.partition_id() % 2
        with e.If(par == 0):
            e.memset(sbounce[:], 0.0)
            e.dma_start(out=io["S0"], in_=sbounce[:]).then_inc(exd1, 16)
            e.wait_ge(exd1, 32)
        with e.Else():
            poll(e, 0, exp1, "m1")
            e.dma_start(out=sbounce[:], in_=SH_S).then_inc(exd1, 16)
            e.wait_ge(exd1, 32)
            e.dma_start(out=io["S0"], in_=sbounce[:]).then_inc(exd1, 16)
            e.wait_ge(exd1, 48)
        return e.engine_nop()

    def m2_fn(e):
        e.dma_start(out=tok[:], in_=io["token"]).then_inc(exd2, 16)
        e.wait_ge(exd2, 16)
        par = e.partition_id() % 2
        with e.If(par == 0):
            e.dma_start(out=sbounce[:], in_=io["Sfin"]).then_inc(exd2, 16)
            e.wait_ge(exd2, 32)
            e.dma_start(out=SH_S, in_=sbounce[:]).then_inc(exd2, 16)
            e.wait_ge(exd2, 48)
            e.dma_start(out=SH_F[0:1, :], in_=tok[:]).then_inc(exd2, 16)
            e.wait_ge(exd2, 64)
        return e.engine_nop()

    def gx_fn(e):
        par = e.partition_id() % 2
        NCK = (ND * D) // 512
        for slot in range(2):
            with (e.If(par == 0) if slot == 0 else e.Else()):
                v = 0
                for k in range(NCK):
                    e.dma_start(out=gbounce[:],
                                in_=io["G_half"][:, k * 512:(k + 1) * 512]).then_inc(exd3, 16)
                    v += 16
                    e.wait_ge(exd3, v)
                    e.dma_start(out=SH_G[slot, :, k * 512:(k + 1) * 512],
                                in_=gbounce[:]).then_inc(exd3, 16)
                    v += 16
                    e.wait_ge(exd3, v)
                e.dma_start(out=SH_F[1 + slot:2 + slot, :], in_=tok[:]).then_inc(exd3, 16)
                v += 16
                e.wait_ge(exd3, v)
                poll(e, 2 - slot, exp3, f"g{slot}")
                for k in range(NCK):
                    e.dma_start(out=gbounce[:],
                                in_=SH_G[1 - slot, :, k * 512:(k + 1) * 512]).then_inc(exd3, 16)
                    v += 16
                    e.wait_ge(exd3, v)
                    e.dma_start(out=io["Gb"][:, k * 512:(k + 1) * 512],
                                in_=gbounce[:]).then_inc(exd3, 16)
                    v += 16
                    e.wait_ge(exd3, v)
        return e.engine_nop()

    def m1(p_):
        p_.emit("pool", m1_fn, reads=["ex"], writes=["S0w", "ex"], tag="m1x")

    def m2(p_):
        p_.emit("pool", m2_fn, reads=["Sfinw", "ex"], writes=["ex"], tag="m2x")

    def gx(p_):
        p_.emit("pool", gx_fn, reads=["Gw", "ex"], writes=["Gbw", "ex"], tag="gxx")

    return m1, m2, gx


def kernel_fused(inp):
    g = GEO
    TH, D, ND, E, F0, F1, H, NCH = (g.TH, g.D, g.ND, g.E, g.F0, g.F1, g.H,
                                    g.NCH)
    f32 = lambda a: np.ascontiguousarray(np.asarray(a, dtype=np.float32))
    flat = lambda a: f32(a).reshape(1, -1)
    consts = build_consts(g)
    x_full = f32(inp["L0_attn_x"])
    shards = [f32(x_full[c // 2, (c % 2) * TH:(c % 2 + 1) * TH, :])
              for c in range(8)]
    token = np.full((1, 16), secrets.randbelow(2**31 - 2) + 1, np.int32)

    EI, EO, IN = "ExternalInput", "ExternalOutput", "Internal"
    specs = [
        ("x", [TH, D], F32, EI), ("Wq", [D, D], F32, EI),
        ("Wk", [D, D], F32, EI), ("Wv", [D, D], F32, EI),
        ("Wa", [D, H], F32, EI), ("ba", [1, H], F32, EI),
        ("Wo", [D, D], F32, EI), ("rw1", [D, D], F32, EI),
        ("rw2", [D, E], F32, EI), ("rb1", [1, D], F32, EI),
        ("rb2", [1, E], F32, EI), ("ew1", [E, D, F0], F32, EI),
        ("eb1", [1, E * F0], F32, EI), ("ew2", [E, F0, D], F32, EI),
        ("eb2", [1, E * D], F32, EI), ("W_q", [D, D], F32, EI),
        ("gate_w", [E, D], F32, EI), ("w1", [E, D, F1], F32, EI),
        ("b1", [1, E * F1], F32, EI), ("w2", [E, F1, D], F32, EI),
        ("b2", [1, E * D], F32, EI), ("token", [1, 16], I32, EI),
        ("xn32", [TH, D], F32, IN), ("xnT", [ND, 128, TH], BF16, IN),
        ("QT", [ND, 128, TH], BF16, IN), ("KT", [ND, 128, TH], BF16, IN),
        ("Ktm", [TH, D], BF16, IN), ("V", [TH, D], BF16, IN),
        ("cum_nat", [H, TH], F32, IN), ("cumtm", [128, H * NCH], F32, IN),
        ("totals", [1, H * NCH], F32, IN), ("S0", [128, 520], F32, IN),
        ("cumg_nat", [H, TH], F32, IN),
        ("cumhl", [2, H, TH], BF16, IN), ("cumghl", [2, H, TH], BF16, IN),
        ("num_hbm", [NCH, 128, D], F32, IN), ("den_hbm", [NCH, H * 128], F32, IN),
        ("outHT", [ND, 128, TH], BF16, IN), ("Sfin", [128, 520], F32, IN),
        ("x1", [TH, D], F32, IN), ("x1nT", [ND, 128, TH], BF16, IN),
        ("G_half", [128, ND * D], F32, IN), ("Gb", [128, ND * D], F32, IN),
        ("x1nb", [TH, D], BF16, IN),
        ("pX1", [E * g.C1, D], BF16, IN), ("pW1", [E * g.C1, 16], F32, IN),
        ("A1", [TH, D], BF16, IN), ("tot1", [TH // 128, E], F32, IN),
        ("offs1", [TH // 128, E], F32, IN),
        ("out", [TH, D], F32, EO),
    ]
    nc = bass.Bass("TRN2", target_bir_lowering=False)
    io = _declare(nc, specs + _const_specs(consts))
    io["Ga"] = io["G_half"]
    es = ExitStack()
    with es:
        p = Prog(nc)
        cst = load_consts(p, nc, es, g, io, CONST_BASE, consts)
        m1, m2, gx = make_exchange(p, nc, es, g, io)
        for fn, marker in ((phase_front, None), (phase_scan, None),
                           (None, m2), (None, m1), (phase_scan2, None),
                           (phase_l0back, None), (phase_mid, None),
                           (None, gx), (phase_l1back, None)):
            if marker is not None:
                marker(p)
                continue
            with ExitStack() as es_p:
                if fn is phase_scan:
                    cst2 = dict(cst)
                    cst2.update(load_consts(p, nc, es_p, g, io, CONST_SCAN,
                                            consts, qname="q_const_scan"))
                    fn(p, nc, es_p, g, io, cst2)
                else:
                    fn(p, nc, es_p, g, io, cst)
                p.barrier_all("phase_end")
        p.barrier_all("final")
        block = es.enter_context(nc.Block())
        p.build(block, es=es)

    im = []
    for c in range(8):
        im.append(dict(
            x=shards[c], Wq=f32(inp["L0_attn_Wq"]), Wk=f32(inp["L0_attn_Wk"]),
            Wv=f32(inp["L0_attn_Wv"]), Wa=f32(inp["L0_attn_Wa"]),
            ba=flat(inp["L0_attn_ba"]), Wo=f32(inp["L0_attn_Wo"]),
            rw1=f32(inp["L0_ffn_rw1"]), rw2=f32(inp["L0_ffn_rw2"]),
            rb1=flat(inp["L0_ffn_rb1"]), rb2=flat(inp["L0_ffn_rb2"]),
            ew1=f32(inp["L0_ffn_ew1"]), eb1=flat(inp["L0_ffn_eb1"]),
            ew2=f32(inp["L0_ffn_ew2"]), eb2=flat(inp["L0_ffn_eb2"]),
            W_q=f32(inp["L1_attn_W_q"]), gate_w=f32(inp["L1_ffn_gate_w"]),
            w1=f32(inp["L1_ffn_w1"]), b1=flat(inp["L1_ffn_b1"]),
            w2=f32(inp["L1_ffn_w2"]), b2=flat(inp["L1_ffn_b2"]),
            token=token,
            **{"c_" + nm: a for nm, a in consts.items()}))
    res = run_bass_kernel_spmd(nc, im, core_ids=list(range(8))).results
    out = np.empty((4, 8192, 1024), np.float32)
    for c in range(8):
        out[c // 2, (c % 2) * TH:(c % 2 + 1) * TH, :] = res[c]["out"]
    return out


def kernel(**inputs):
    return kernel_fused(inputs)



def emit_pool_init(p, nc, g, pref, io_pW, bigtile, EC, io_pX, zt):
    q = pref + "_pwi"
    NB = (EC * 16 + 128 * 256 - 1) // (128 * 256)
    rows_per = 128 * 256 // 16
    for b in range(NB):
        r0 = b * rows_per
        r1 = min(EC, r0 + rows_per)
        p.emit(q, lambda e, r0=r0, r1=r1: e.dma_start(
            out=io_pW[r0:r1, :],
            in_=bigtile[:].rearrange("p (q w) -> p q w", w=16)[:, 0:(r1 - r0) // 128, :]),
            reads=["bigtile"], writes=["pWi"], tag=f"{pref}pwi{b}")
    for b in range(EC // 128):
        p.emit(q, lambda e, b=b: e.dma_start(
            out=io_pX[b * 128:(b + 1) * 128, :], in_=zt[:]),
            reads=["ztz"], writes=["pXi"], tag=f"{pref}pxi{b}")


def emit_slot_scatter(p, nc, g, cst, pref, i, rank, masks, kmask, wsel, kwsel,
                      offs_flat, koffs, xnb_t, kxnb, iota_sb, PSA, slot_s,
                      wid, ux, io_pX, io_pW, C, EC, layer, id_base=0):
    E, NT = g.E, g.NT
    kps = pref + "PSA0"
    ecb = "eCbase0" if layer == 0 else "eCbase1"
    cap = "caprow0" if layer == 0 else "caprow1"
    p.emit("pe", lambda e, i=i: e.matmul(
        PSA[:, 0:E], cst["U128"][:], masks[:, i * E:(i + 1) * E],
        start=True, stop=False), reads=[kmask, "c_U128"], writes=[kps],
        tag=f"{pref}sp{i}_{rank}")
    p.emit("pe", lambda e, i=i: e.matmul(
        PSA[:, 0:E], cst["ones_1x128"][:], offs_flat[0:1, i * E:(i + 1) * E],
        start=False, stop=False), reads=[koffs, "c_ones_1x128"], writes=[kps],
        tag=f"{pref}so{i}_{rank}")
    p.emit("pe", lambda e: e.matmul(
        PSA[:, 0:E], cst["ones_1x128"][:], cst[ecb][:],
        start=False, stop=True), reads=["c_" + ecb, "c_ones_1x128"],
        writes=[kps], tag=f"{pref}se{i}_{rank}")
    p.emit("dve", lambda e: e.tensor_copy(slot_s[:, 0:E], PSA[:, 0:E]),
           reads=[kps], writes=["slotf"], tag=f"{pref}sc{i}_{rank}")
    p.emit("dve", lambda e: e.tensor_tensor(
        out=slot_s[:, E:2 * E], in0=slot_s[:, 0:E], in1=cst[cap][:],
        op=ALU.is_le),
        reads=["slotf", "c_" + cap], writes=["slotle"], tag=f"{pref}sl{i}_{rank}")
    p.emit("dve", lambda e, i=i: e.tensor_tensor(
        out=slot_s[:, E:2 * E], in0=slot_s[:, E:2 * E],
        in1=masks[:, i * E:(i + 1) * E], op=ALU.mult),
        reads=["slotle", kmask], writes=["slotok"], tag=f"{pref}sm{i}_{rank}")
    p.emit("dve", lambda e: e.tensor_scalar(
        out=slot_s[:, 2 * E:3 * E], in0=slot_s[:, E:2 * E], scalar1=-1e9,
        scalar2=1e9, op0=ALU.mult, op1=ALU.add),
        reads=["slotok"], writes=["slotbig"], tag=f"{pref}sb{i}_{rank}")
    p.emit("dve", lambda e: e.tensor_tensor(
        out=slot_s[:, 0:E], in0=slot_s[:, 0:E], in1=slot_s[:, E:2 * E],
        op=ALU.mult), reads=["slotf", "slotok"], writes=["slotf"],
        tag=f"{pref}sf{i}_{rank}")
    p.emit("dve", lambda e: e.tensor_tensor(
        out=slot_s[:, 0:E], in0=slot_s[:, 0:E], in1=slot_s[:, 2 * E:3 * E],
        op=ALU.add), reads=["slotf", "slotbig"], writes=["slotf"],
        tag=f"{pref}sf2{i}_{rank}")
    p.emit("dve", lambda e: e.tensor_reduce(
        slot_s[:, 3 * E:3 * E + 1], slot_s[:, 0:E], axis=AX.X, op=ALU.min),
        reads=["slotf"], writes=["slotmin"], tag=f"{pref}sr{i}_{rank}")
    p.emit("dve", lambda e: e.tensor_copy(ux[:, 0:1], slot_s[:, 3 * E:3 * E + 1]),
           reads=["slotmin"], writes=["uxw"], tag=f"{pref}su{i}_{rank}")
    p.emit("dve", lambda e, i=i: e.tensor_copy(
        wid[:, 0:1], wsel[:, i:i + 1]), reads=[kwsel], writes=["widw"],
        tag=f"{pref}sw{i}_{rank}")
    p.emit("dve", lambda e, i=i, id_base=id_base: e.tensor_scalar(
        out=wid[:, 1:2], in0=iota_sb[:, i:i + 1], scalar1=float(id_base),
        scalar2=None, op0=ALU.add), reads=["c_iota"], writes=["widw"],
        tag=f"{pref}si{i}_{rank}")
    qx = pref + "_scx"
    qw = pref + "_scw"
    p.emit(qx, lambda e: e.indirect_dma_start(
        out=io_pX, out_offset=bass.IndirectOffsetOnAxis(ap=ux[:, 0:1], axis=0),
        in_=xnb_t[:], in_offset=None, bounds_check=EC - 1, oob_is_err=False),
        reads=["uxw", kxnb, "pXi"], writes=["pX"], tag=f"{pref}scx{i}_{rank}")
    p.emit(qw, lambda e: e.indirect_dma_start(
        out=io_pW, out_offset=bass.IndirectOffsetOnAxis(ap=ux[:, 0:1], axis=0),
        in_=wid[:, 0:16], in_offset=None, bounds_check=EC - 1, oob_is_err=False),
        reads=["uxw", "widw", "pWi"], writes=["pW"], tag=f"{pref}scw{i}_{rank}")


def emit_sparse_expert(p, nc, g, cst, pref, e, NF, C, actfn, io_pX, io_pW,
                       io_w1, io_w2, io_b1, io_b2, io_AB,
                       xg, xgT, pw_sb, ehT_m, ebuf, b1e, b2e, ux2,
                       w1sb, w2sb, PSA, PSB, ptr, D=1024):
    ND, E = g.ND, g.E
    NTS = C // 128
    NHC = NF // 128
    q_lp = pref + "_lp"
    q_lw1 = pref + "_lw1"
    q_lw2 = pref + "_lw2"
    q_lb = pref + "_lb"
    q_sb = pref + "_sb"
    kxg, kxgT, kpw = pref + "xg", pref + "xgT", pref + "pw"
    kb1f, kb2f = pref + "b1e", pref + "b2e"
    p.emit(q_lp, lambda e_, e=e: e_.dma_start(
        out=xg[:].rearrange("p (q d) -> p q d", q=NTS),
        in_=io_pX[e * C:(e + 1) * C, :].rearrange("(q p) d -> p q d", p=128)),
        reads=["pX"], writes=[kxg], tag=f"{pref}lpx{e}")
    p.emit(q_lp, lambda e_, e=e: e_.dma_start(
        out=pw_sb[:].rearrange("p (q w) -> p q w", q=NTS),
        in_=io_pW[e * C:(e + 1) * C, :].rearrange("(q p) w -> p q w", p=128)),
        reads=["pW"], writes=[kpw], tag=f"{pref}lpw{e}")
    p.emit(q_lb, lambda e_, e=e: e_.dma_start(
        out=b1e[:], in_=io_b1[0:1, e * NF:(e + 1) * NF]),
        writes=[kb1f], tag=f"{pref}lb1s{e}")
    p.emit(q_lb, lambda e_, e=e: e_.dma_start(
        out=b2e[:], in_=io_b2[0:1, e * D:(e + 1) * D]),
        writes=[kb2f], tag=f"{pref}lb2s{e}")
    kptr = pref + "PSB0"
    for q in range(NTS):
        for j in range(ND):
            p.emit("pe", lambda e_, q=q, j=j: e_.transpose(
                ptr, xg[:, q * D + j * 128:(q * D) + (j + 1) * 128],
                cst["identb"][:]), reads=[kxg, "c_identb"], writes=[kptr],
                tag=f"{pref}tp{e}_{q}_{j}")
            p.emit("dve", lambda e_, q=q, j=j: e_.tensor_copy(
                xgT[:, j * C + q * 128: j * C + (q + 1) * 128], ptr),
                reads=[kptr], writes=[kxgT], tag=f"{pref}tpc{e}_{q}_{j}")
    tslices = []
    off = 0
    while off < C:
        w = min(512, C - off)
        tslices.append((off, w))
        off += w
    NSL = NF // 512
    for sl in range(NSL):
        b = sl % 2
        kw1 = f"{pref}w1_{b}"
        p.emit(q_lw1, lambda e_, e=e, sl=sl, b=b: e_.dma_start(
            out=w1sb[b][:].rearrange("p (dc h) -> p dc h", dc=ND),
            in_=io_w1[e].rearrange("(dc p) h -> p dc h", p=128)[:, :, sl * 512:(sl + 1) * 512]),
            writes=[kw1], tag=f"{pref}slw1_{e}_{sl}")
        for hq in range(4):
            hc = sl * 4 + hq
            pb = hc % 2
            kpsa = f"{pref}PSA{pb}"
            for d in range(ND):
                for (toff, tw) in tslices:
                    p.emit("pe", lambda e_, d=d, hq=hq, toff=toff, tw=tw, b=b, pb=pb: e_.matmul(
                        PSA[:, pb * 1024 + toff: pb * 1024 + toff + tw],
                        w1sb[b][:, d * 512 + hq * 128: d * 512 + (hq + 1) * 128],
                        xgT[:, d * C + toff: d * C + toff + tw],
                        start=(d == 0), stop=False),
                        reads=[kw1, kxgT], writes=[kpsa], tag=f"{pref}sA{e}_{hc}")
            for (toff, tw) in tslices:
                p.emit("pe", lambda e_, e=e, hc=hc, toff=toff, tw=tw, pb=pb: e_.matmul(
                    PSA[:, pb * 1024 + toff: pb * 1024 + toff + tw],
                    b1e[0:1, hc * 128:(hc + 1) * 128],
                    cst["ones_1x512"][:, 0:tw], start=False, stop=True),
                    reads=[kb1f, "c_ones_1x512"], writes=[kpsa],
                    tag=f"{pref}sAb{e}_{hc}")
            p.emit("act", lambda e_, hc=hc, pb=pb, kpsa=kpsa: e_.activation(
                out=ehT_m[:, hc * C:(hc + 1) * C],
                in_=PSA[:, pb * 1024: pb * 1024 + C], func=actfn),
                reads=[kpsa], writes=[f"{pref}ehT_{hc}"], tag=f"{pref}sAs{e}_{hc}")
    assert NTS <= 8
    ehkeys = [f"{pref}ehT_{hc}" for hc in range(NHC)]
    NHG = NHC // 4
    for dhalf in range(2):
        for hg in range(NHG):
            b = hg % 2
            kw2 = f"{pref}w2_{b}"
            p.emit(q_lw2, lambda e_, e=e, hg=hg, b=b, dhalf=dhalf: e_.dma_start(
                out=w2sb[b][:].rearrange("p (hc f) -> p hc f", hc=4),
                in_=io_w2[e][hg * 512:(hg + 1) * 512,
                             dhalf * 512:(dhalf + 1) * 512]
                .rearrange("(hc p) f -> p hc f", p=128)),
                writes=[kw2], tag=f"{pref}slw2_{e}_{dhalf}_{hg}")
            for hq in range(4):
                hc = hg * 4 + hq
                for i in range(NTS):
                    PSx = PSA if i < 4 else PSB
                    kps = (pref + ("PSA" if i < 4 else "PSB") + str((i % 4) // 2))
                    off = (i % 4) * 512
                    p.emit("pe", lambda e_, hc=hc, hq=hq, i=i, b=b, PSx=PSx, off=off: e_.matmul(
                        PSx[:, off:off + 512],
                        ehT_m[:, hc * C + i * 128: hc * C + (i + 1) * 128],
                        w2sb[b][:, hq * 512:(hq + 1) * 512],
                        start=(hc == 0), stop=False),
                        reads=[kw2, ehkeys[hc]], writes=[kps],
                        tag=f"{pref}sB{e}_{dhalf}_{i}")
        for i in range(NTS):
            PSx = PSA if i < 4 else PSB
            kps = (pref + ("PSA" if i < 4 else "PSB") + str((i % 4) // 2))
            off = (i % 4) * 512
            p.emit("pe", lambda e_, e=e, i=i, dhalf=dhalf, PSx=PSx, off=off: e_.matmul(
                PSx[:, off:off + 512], cst["ones_1x128"][:],
                b2e[0:1, dhalf * 512:(dhalf + 1) * 512],
                start=False, stop=True),
                reads=[kb2f, "c_ones_1x128"], writes=[kps],
                tag=f"{pref}sBb{e}_{dhalf}_{i}")
            p.emit("dve", lambda e_, i=i, dhalf=dhalf, PSx=PSx, off=off: e_.tensor_scalar(
                out=ebuf[:, i * D + dhalf * 512: i * D + (dhalf + 1) * 512],
                in0=PSx[:, off:off + 512], scalar1=pw_sb[:, i * 16:i * 16 + 1],
                scalar2=None, op0=ALU.mult),
                reads=[kps, kpw], writes=[f"{pref}eb_{i}_{dhalf}"],
                tag=f"{pref}sY{e}_{dhalf}_{i}")
    for i in range(NTS):
        p.emit("dve", lambda e_, i=i: e_.tensor_copy(
            ux2[:, 0:1], pw_sb[:, i * 16 + 1: i * 16 + 2]),
            reads=[kpw], writes=["ux2w"], tag=f"{pref}su2{e}_{i}")
        p.emit(q_sb, lambda e_, i=i: e_.indirect_dma_start(
            out=io_AB, out_offset=bass.IndirectOffsetOnAxis(ap=ux2[:, 0:1], axis=0),
            in_=ebuf[:, i * D:(i + 1) * D], in_offset=None,
            bounds_check=io_AB.shape[0] - 1, oob_is_err=False),
            reads=["ux2w", f"{pref}eb_{i}_0", f"{pref}eb_{i}_1"],
            writes=["ABw"], tag=f"{pref}scb{e}_{i}")


def phase_l1back_sparse(p, nc, es, g, io, cst):
    TH, D, ND, E, F1, MACRO = g.TH, g.D, g.ND, g.E, g.F1, g.MACRO
    C, NT = g.C1, g.NT
    EC = E * C
    NM, NTM = g.NMAC, MACRO // 128
    S = lambda nm, shp, dt=F32: es.enter_context(nc.sbuf_tensor("m1" + nm, shp, dt))
    PS = lambda nm, shp, dt=F32: es.enter_context(nc.psum_tensor("m1" + nm, shp, dt))
    q_lw = p.dma_queue("m1_lw", "gpsimd")
    q_lw1 = p.dma_queue("m1_lw1", "gpsimd")
    q_lw2 = p.dma_queue("m1_lw2", "gpsimd")
    p.dma_queue("m1_lb", "gpsimd")
    p.dma_queue("m1_lp", "sync")
    p.dma_queue("m1_pwi", "gpsimd")
    p.dma_queue("m1_scx", "gpsimd")
    p.dma_queue("m1_scw", "gpsimd")
    p.dma_queue("m1_sb", "gpsimd")
    q_lx = p.dma_queue("m1_lx", "sync")
    q_lxT = p.dma_queue("m1_lxT", "sync")
    q_st = p.dma_queue("m1_st", "sync")
    q_tot = p.dma_queue("m1_tot", "sync")

    PSA = PS("PSA", [128, 2048])
    PSB = PS("PSB", [128, 2048])
    ptr = PSB[:, 0:64].bitcast(BF16)
    Msb = S("Msb", [128, ND * D], BF16)
    es_m = ExitStack()
    Sm = lambda nm, shp, dt=F32: es_m.enter_context(nc.sbuf_tensor("m1" + nm, shp, dt))
    gta = [Sm(f"gta{i}", [128, D]) for i in range(2)]
    gtb = [Sm(f"gtb{i}", [128, D]) for i in range(2)]
    gb16 = Sm("gb16", [128, ND * D], BF16)
    for f1 in range(ND):
        pa = f1 % 2
        p.emit(q_lx, lambda e, f1=f1, pa=pa: e.dma_start(
            out=gta[pa][:], in_=io["Ga"][:, f1 * D:(f1 + 1) * D]),
            reads=["Gw"], writes=[f"gta{pa}"], tag=f"lGa{f1}")
        p.emit(q_lxT, lambda e, f1=f1, pa=pa: e.dma_start(
            out=gtb[pa][:], in_=io["Gb"][:, f1 * D:(f1 + 1) * D]),
            reads=["Gbw"], writes=[f"gtb{pa}"], tag=f"lGb{f1}")
        p.emit("dve", lambda e, f1=f1, pa=pa: e.tensor_tensor(
            out=gb16[:, f1 * D:(f1 + 1) * D], in0=gta[pa][:], in1=gtb[pa][:],
            op=ALU.add), reads=[f"gta{pa}", f"gtb{pa}"], writes=["gb16"],
            tag=f"gc{f1}")
    wq1 = Sm("wq1", [128, ND * D], BF16)
    p.emit(q_lw, lambda e: e.dma_start(
        out=wq1[:].rearrange("p (dc f) -> p dc f", dc=ND),
        in_=io["W_q"].rearrange("(dc p) f -> p dc f", p=128)),
        writes=["wq1"], tag="ldWq1")
    for hc in range(ND):
        pb = hc % 2
        for oc in range(ND):
            for half in range(2):
                p.emit("pe", lambda e, hc=hc, oc=oc, half=half, pb=pb: e.matmul(
                    PSA[:, pb * 1024 + half * 512: pb * 1024 + (half + 1) * 512],
                    wq1[:, oc * D + hc * 128: oc * D + (hc + 1) * 128],
                    gb16[:, oc * D + half * 512: oc * D + (half + 1) * 512],
                    start=(oc == 0), stop=(oc == ND - 1)),
                    reads=["wq1", "gb16"], writes=[f"m1PSA{hc % 2}"], tag=f"M{hc}_{oc}")
        p.emit("dve", lambda e, hc=hc, pb=pb: e.tensor_copy(
            Msb[:, hc * D:(hc + 1) * D], PSA[:, pb * 1024:(pb + 1) * 1024]),
            reads=[f"m1PSA{pb}"], writes=["Msb"], tag=f"Mev{hc}")
    p.barrier_all("m1_Mdone")
    es_m.close()
    gwsb = S("gwsb", [E, D], BF16)
    gwT = S("gwT", [128, ND * E], BF16)
    b1e = S("b1e", [1, F1]); b2e = S("b2e", [1, D])
    xT_m = S("xT_m", [128, ND * MACRO], BF16)
    masks = S("masks", [128, NT * E])
    wsel = S("wsel", [128, NT])
    lg = S("lg", [128, 24])
    xt = [S(f"xt{i}", [128, D]) for i in range(2)]
    yt = [S(f"yt{i}", [128, D]) for i in range(2)]
    iota_sb = S("iota_sb", [128, NT])
    totrow = S("totrow", [1, E])
    offsf = S("offsf", [1, NT * E])
    offs32 = S("offs32", [NT, E])
    slot_s = S("slot_s", [128, 32])
    wid = S("wid", [128, 16])
    ux = S("ux", [128, 1], mybir.dt.uint32)
    ux2 = S("ux2", [128, 1], mybir.dt.uint32)
    bigtile = S("bigtile", [128, 256])
    xnb_t = [S(f"xnb{i}", [128, D], BF16) for i in range(2)]
    xg = S("xg", [128, (C // 128) * D], BF16)
    xgT = S("xgT", [128, ND * C], BF16)
    pw_sb = S("pw_sb", [128, (C // 128) * 16])
    ehT_m = S("ehT_m", [128, (F1 // 128) * C], BF16)
    ebuf = S("ebuf", [128, (C // 128) * D], BF16)
    w1sb = [S(f"w1s{i}", [128, ND * 512], BF16) for i in range(2)]
    w2sb = [S(f"w2s{i}", [128, 2048], BF16) for i in range(2)]
    p.emit(q_lw, lambda e: e.dma_start(out=iota_sb[:], in_=io["c_iota_t"]),
           writes=["c_iota"], tag="ldiota")
    p.emit("dve", lambda e: e.memset(bigtile[:], 1e9), writes=["bigtile"],
           tag="bigt")
    p.emit("dve", lambda e: e.memset(wid[:], 0.0), writes=["widw"], tag="widz")
    ztb = S("ztb", [128, D], BF16)
    p.emit("dve", lambda e: e.memset(ztb[:], 0.0), writes=["ztz"], tag="ztz")
    emit_pool_init(p, nc, g, "m1", io["pW1"], bigtile, EC, io["pX1"], ztb)
    gwp = PSA[:, 0:4].bitcast(BF16)
    p.emit(q_lw, lambda e: e.dma_start(out=gwsb[:], in_=io["gate_w"]),
           writes=["gwsb"], tag="ldgw")
    for j in range(ND):
        p.emit("pe", lambda e, j=j: e.transpose(
            gwp, gwsb[:, j * 128:(j + 1) * 128], cst["identb"][0:E, 0:E]),
            reads=["gwsb", "c_identb"], writes=["m1PSA0"], tag=f"gwt{j}")
        p.emit("dve", lambda e, j=j: e.tensor_copy(
            gwT[:, j * E:(j + 1) * E], gwp), reads=["m1PSA0"],
            writes=["gwT"], tag=f"gwc{j}")
    for m in range(NM):
        p.emit(q_lxT, lambda e, m=m: e.dma_start(
            out=xT_m[:].rearrange("p (j t) -> p j t", j=ND),
            in_=io["x1nT"][:, :, m * MACRO:(m + 1) * MACRO].rearrange("j p t -> p j t")),
            writes=["kxTm"], tag=f"lxT{m}")
        for i in range(NTM):
            gi = m * NTM + i
            pb = i % 2
            kpsb = f"m1PSB{pb}"
            pa = i % 2
            p.emit(q_lx, lambda e, m=m, i=i, pa=pa: e.dma_start(
                out=xt[pa][:], in_=io["x1"][m * MACRO + i * 128: m * MACRO + (i + 1) * 128, :]),
                writes=[f"xt{pa}"], tag=f"lx1b{m}_{i}")
            for j in range(ND):
                for half in range(2):
                    p.emit("pe", lambda e, i=i, j=j, half=half, pb=pb: e.matmul(
                        PSB[:, pb * 1024 + half * 512: pb * 1024 + (half + 1) * 512],
                        xT_m[:, j * MACRO + i * 128: j * MACRO + (i + 1) * 128],
                        Msb[:, j * D + half * 512: j * D + (half + 1) * 512],
                        start=(j == 0), stop=(j == ND - 1)),
                        reads=["kxTm", "Msb"], writes=[kpsb], tag=f"at{m}_{i}")
            p.emit("dve", lambda e, i=i, pa=pa, pb=pb: e.tensor_tensor(
                out=yt[pa][:], in0=PSB[:, pb * 1024:(pb + 1) * 1024],
                in1=xt[pa][:], op=ALU.add),
                reads=[kpsb, f"xt{pa}"], writes=[f"yt{pa}"], tag=f"yi1{m}_{i}")
            p.emit(q_st, lambda e, m=m, i=i, pa=pa: e.dma_start(
                out=io["out"][m * MACRO + i * 128: m * MACRO + (i + 1) * 128, :],
                in_=yt[pa][:]),
                reads=[f"yt{pa}"], writes=["outw"], tag=f"sbase{m}_{i}")
            for j in range(ND):
                p.emit("pe", lambda e, i=i, j=j: e.matmul(
                    PSA[:, 0:E], xT_m[:, j * MACRO + i * 128: j * MACRO + (i + 1) * 128],
                    gwT[:, j * E:(j + 1) * E], start=(j == 0), stop=(j == ND - 1)),
                    reads=["kxTm", "gwT"], writes=["m1PSA0"], tag=f"gl{m}_{i}")
            p.emit("dve", lambda e: e.tensor_copy(lg[:, 0:E], PSA[:, 0:E]),
                   reads=["m1PSA0"], writes=["lgv"], tag=f"glc{m}_{i}")
            p.emit("dve", lambda e: e.reduce_max(lg[:, 8:9], lg[:, 0:E], axis=AX.X),
                   reads=["lgv"], writes=["lgm1"], tag=f"g1m{m}_{i}")
            p.emit("dve", lambda e, gi=gi: e.tensor_scalar(
                out=masks[:, gi * E:(gi + 1) * E], in0=lg[:, 0:E], scalar1=lg[:, 8:9],
                scalar2=None, op0=ALU.is_ge), reads=["lgv", "lgm1"],
                writes=["kmask"], tag=f"g1k{m}_{i}")
            p.emit("dve", lambda e: e.tensor_scalar(
                out=lg[:, 10:10 + E], in0=lg[:, 0:E], scalar1=lg[:, 8:9],
                scalar2=None, op0=ALU.subtract), reads=["lgv", "lgm1"],
                writes=["lgsub"], tag=f"g1s{m}_{i}")
            p.emit("act", lambda e: e.activation(out=lg[:, 10:10 + E],
                   in_=lg[:, 10:10 + E], func=AF.Exp), reads=["lgsub"],
                   writes=["lgexp"], tag=f"g1e{m}_{i}")
            p.emit("dve", lambda e: e.reduce_sum(lg[:, 8:9], lg[:, 10:10 + E],
                   axis=AX.X), reads=["lgexp"], writes=["lgsum"], tag=f"g1t{m}_{i}")
            p.emit("dve", lambda e, gi=gi: e.reciprocal(wsel[:, gi:gi + 1], lg[:, 8:9]),
                   reads=["lgsum"], writes=["kwsel"], tag=f"g1r{m}_{i}")
            p.emit("pe", lambda e, gi=gi: e.matmul(
                PSA[0:1, 16:16 + E], cst["ones_128x1"][:],
                masks[:, gi * E:(gi + 1) * E], start=True, stop=True),
                reads=["kmask", "c_ones_128x1"], writes=["m1PSA0"],
                tag=f"tot{m}_{i}")
            p.emit("dve", lambda e: e.tensor_copy(totrow[:], PSA[0:1, 16:16 + E]),
                   reads=["m1PSA0"], writes=["totrow"], tag=f"totc{m}_{i}")
            p.emit(q_tot, lambda e, gi=gi: e.dma_start(
                out=io["tot1"][gi:gi + 1, :], in_=totrow[:]),
                reads=["totrow"], writes=["totw"], tag=f"tots{m}_{i}")
    p.emit(q_tot, lambda e: e.dma_start(out=offs32[:], in_=io["tot1"]),
           reads=["totw"], writes=["offs32"], tag="ldtot")
    p.emit("pe", lambda e: e.matmul(
        PSA[0:NT, 0:E], cst["U128strict"][0:NT, 0:NT], offs32[:],
        start=True, stop=True), reads=["offs32", "c_U128strict"],
        writes=["m1PSA0"], tag="offmm")
    p.emit("dve", lambda e: e.tensor_copy(offs32[:], PSA[0:NT, 0:E]),
           reads=["m1PSA0"], writes=["offs32"], tag="offev")
    p.emit(q_tot, lambda e: e.dma_start(out=io["offs1"], in_=offs32[:]),
           reads=["offs32"], writes=["offsw"], tag="stoffs")
    p.emit(q_tot, lambda e: e.dma_start(
        out=offsf[:], in_=io["offs1"].rearrange("a b -> (a b)").unsqueeze(0)),
        reads=["offsw"], writes=["koffs"], tag="ldoffs")
    for i in range(NT):
        pa = i % 2
        p.emit(q_lx, lambda e, i=i, pa=pa: e.dma_start(
            out=xnb_t[pa][:], in_=io["x1nb"][i * 128:(i + 1) * 128, :]),
            reads=["x1nbw"], writes=[f"kxnb{pa}"], tag=f"lxnb{i}")
        emit_slot_scatter(p, nc, g, cst, "m1", i, 0, masks, "kmask", wsel,
                          "kwsel", offsf, "koffs", xnb_t[pa], f"kxnb{pa}",
                          iota_sb, PSA, slot_s, wid, ux, io["pX1"], io["pW1"],
                          C, EC, layer=1)
    for e in range(E):
        emit_sparse_expert(p, nc, g, cst, "m1", e, F1, C, AF.Relu,
                           io["pX1"], io["pW1"], io["w1"], io["w2"],
                           io["b1"], io["b2"], io["A1"],
                           xg, xgT, pw_sb, ehT_m, ebuf, b1e, b2e, ux2,
                           w1sb, w2sb, PSA, PSB, ptr)
    for i in range(NT):
        pa = i % 2
        p.emit(q_lx, lambda e, i=i, pa=pa: e.dma_start(
            out=xnb_t[pa][:], in_=io["A1"][i * 128:(i + 1) * 128, :]),
            reads=["ABw", f"kxnb{pa}"], writes=[f"kxnb{pa}"], tag=f"lA{i}")
        p.emit("dve", lambda e, pa=pa: e.tensor_copy(yt[pa][:], xnb_t[pa][:]),
               reads=[f"kxnb{pa}"], writes=[f"yt{pa}"], tag=f"cA{i}")
        p.emit(q_lxT, lambda e, i=i, pa=pa: e.dma_start(
            out=xt[pa][:], in_=io["out"][i * 128:(i + 1) * 128, :]),
            reads=["outw", f"xt{pa}"], writes=[f"xt{pa}"], tag=f"lO{i}")
        p.emit("dve", lambda e, pa=pa: e.tensor_tensor(
            out=xt[pa][:], in0=xt[pa][:], in1=yt[pa][:], op=ALU.add),
            reads=[f"yt{pa}", f"xt{pa}"], writes=[f"xt{pa}"], tag=f"fadd{i}")
        p.emit(q_st, lambda e, i=i, pa=pa: e.dma_start(
            out=io["out"][i * 128:(i + 1) * 128, :], in_=xt[pa][:]),
            reads=[f"xt{pa}"], writes=["outw2"], tag=f"sfin{i}")


def phase_scan2(p, nc, es, g, io, cst):
    TH, D, ND, H, CH, NCH, DH = g.TH, g.D, g.ND, g.H, g.CH, g.NCH, g.DH
    SCOLS = (H // 2) * DH + H // 2
    S = lambda nm, shp, dt=F32: es.enter_context(nc.sbuf_tensor("s2" + nm, shp, dt))
    PS = lambda nm, shp, dt=F32: es.enter_context(nc.psum_tensor("s2" + nm, shp, dt))
    q_lq = p.dma_queue("s2_lq", "sync")
    q_ln = p.dma_queue("s2_ln", "sync")
    q_ld = p.dma_queue("s2_ld", "gpsimd")
    q_lcf = p.dma_queue("s2_lcf", "gpsimd")
    q_st = p.dma_queue("s2_st", "sync")

    PB = PS("PB", [128, 2048])
    PP = PS("PP", [128, 2048])
    POv = PB[:, 0:1024]

    S32 = S("S32", [128, SCOLS])
    Sb = S("Sb", [128, SCOLS], BF16)
    p.emit(q_ld, lambda e: e.dma_start(out=S32[:], in_=io["S0"]),
           reads=["S0w"], writes=["s2S32"])
    p.emit("dve", lambda e: e.tensor_copy(Sb[:], S32[:]), reads=["s2S32"],
           writes=["s2Sb"])

    qt = [S(f"qt{i}", [128, ND * CH], BF16) for i in range(2)]
    num = [S(f"num{i}", [128, ND * CH]) for i in range(2)]
    den = [S(f"den{i}", [1, H * CH]) for i in range(2)]
    cfl = [S(f"cfl{i}", [2, H * CH], BF16) for i in range(2)]
    ecum = S("ecum", [128, H * CH], BF16)
    qte = S("qte", [128, ND * CH], BF16)
    dent = S("dent", [1, H * CH])
    recl = S("recl", [128, H * CH])
    recd = S("recd", [128, H * CH])
    ntot = S("ntot", [128, ND * CH])
    outc = [S(f"outc{i}", [128, ND * CH], BF16) for i in range(2)]
    cumg_flat = io["cumghl"].rearrange("w (a b) t -> w b a t", b=2)

    for c in range(NCH):
        pa = c % 2
        kq, kn, kd, kcf = f"s2qt{pa}", f"s2num{pa}", f"s2den{pa}", f"s2cfl{pa}"
        p.emit(q_lq, lambda e, c=c, pa=pa: e.dma_start(
            out=qt[pa][:].rearrange("p (j t) -> p j t", j=ND),
            in_=io["QT"][:, :, c * CH:(c + 1) * CH].rearrange("j p t -> p j t")),
            writes=[kq], tag=f"s2lq{c}")
        p.emit(q_ln, lambda e, c=c, pa=pa: e.dma_start(
            out=num[pa][:], in_=io["num_hbm"][c, :, :]),
            reads=["numw"], writes=[kn], tag=f"s2ln{c}")
        p.emit(q_ld, lambda e, c=c, pa=pa: e.dma_start(
            out=den[pa][:], in_=io["den_hbm"][c:c + 1, :]),
            reads=["denw"], writes=[kd], tag=f"s2ld{c}")
        for w in range(2):
            p.emit(q_lcf, lambda e, c=c, pa=pa, w=w: e.dma_start(
                out=cfl[pa][w:w + 1, :].rearrange("o (b a t) -> o b a t", b=2, a=ND),
                in_=cumg_flat[w:w + 1, :, :, c * CH:(c + 1) * CH].squeeze(0).unsqueeze(0)),
                reads=["cumghlw"], writes=[kcf], tag=f"s2lcf{c}_{w}")
        for sl in range(4):
            p.emit("pe", lambda e, sl=sl, pa=pa: e.matmul(
                PB[:, sl * 512:(sl + 1) * 512], cst["ones_2x128b"][:],
                cfl[pa][0:2, sl * 512:(sl + 1) * 512], start=True, stop=True),
                reads=[kcf, "c_ones_2x128b"], writes=["s2PB"], tag=f"s2b1_{c}_{sl}")
        p.emit("act", lambda e: e.activation(out=ecum[:], in_=PB[:],
               func=AF.Exp), reads=["s2PB"], writes=["s2ecum"], tag=f"s2ec{c}")
        for h in range(H):
            b, j = h % 2, h // 2
            p.emit("dve", lambda e, b=b, j=j, h=h, pa=pa: e.tensor_tensor(
                out=qte[b * 64:(b + 1) * 64, j * CH:(j + 1) * CH],
                in0=qt[pa][b * 64:(b + 1) * 64, j * CH:(j + 1) * CH],
                in1=ecum[b * 64:(b + 1) * 64, g.col(h):g.col(h) + CH],
                op=ALU.mult), reads=[kq, "s2ecum"], writes=["s2qte"],
                tag=f"s2qte{c}_{h}")
        for h in range(H):
            b, j = h % 2, h // 2
            p.emit("pe", lambda e, b=b, j=j, h=h: e.matmul(
                POv[b * 64:(b + 1) * 64, j * CH:(j + 1) * CH],
                Sb[b * 64:(b + 1) * 64, j * DH:(j + 1) * DH],
                qte[b * 64:(b + 1) * 64, j * CH:(j + 1) * CH],
                start=True, stop=True, tile_position=(b * 64, b * 64)),
                reads=["s2Sb", "s2qte"], writes=["s2PB"], tag=f"s2ox{c}_{h}")
        by_bank = {}
        for h in range(H):
            by_bank.setdefault(g.col(h) // 512, []).append(h)
        for bank, hs in by_bank.items():
            for k, h in enumerate(hs):
                b, j = h % 2, h // 2
                p.emit("pe", lambda e, b=b, j=j, h=h, st=(k == 0),
                       sp=(k == len(hs) - 1): e.matmul(
                    PP[0:1, g.col(h):g.col(h) + CH],
                    Sb[b * 64:(b + 1) * 64, 512 + j:513 + j],
                    qte[b * 64:(b + 1) * 64, j * CH:(j + 1) * CH],
                    start=st, stop=sp),
                    reads=["s2Sb", "s2qte"], writes=["s2PP"], tag=f"s2dz{c}_{h}")
        p.emit("dve", lambda e, pa=pa: e.tensor_tensor(
            out=dent[:], in0=PP[0:1, :], in1=den[pa][:], op=ALU.add),
            reads=["s2PP", kd], writes=["s2dent"], tag=f"s2dt{c}")
        for sl in range(4):
            p.emit("pe", lambda e, sl=sl: e.matmul(
                PP[:, sl * 512:(sl + 1) * 512], cst["ones_1x128"][:],
                dent[:, sl * 512:(sl + 1) * 512], start=True, stop=True),
                reads=["s2dent", "c_ones_1x128"], writes=["s2PP"],
                tag=f"s2rb{c}_{sl}")
        p.emit("act", lambda e: e.activation(out=recl[:], in_=PP[:],
               func=AF.Ln), reads=["s2PP"], writes=["s2recl"], tag=f"s2ln{c}")
        p.emit("act", lambda e: e.activation(out=recd[:], in_=recl[:],
               func=AF.Exp, scale=-1.0), reads=["s2recl"], writes=["s2recd"],
               tag=f"s2rc{c}")
        p.emit("dve", lambda e, pa=pa: e.tensor_tensor(
            out=ntot[:], in0=POv, in1=num[pa][:], op=ALU.add),
            reads=["s2PB", kn], writes=["s2ntot"], tag=f"s2nt{c}")
        for h in range(H):
            b, j = h % 2, h // 2
            p.emit("dve", lambda e, b=b, j=j, h=h, pa=pa: e.tensor_tensor(
                out=outc[pa][b * 64:(b + 1) * 64, j * CH:(j + 1) * CH],
                in0=ntot[b * 64:(b + 1) * 64, j * CH:(j + 1) * CH],
                in1=recd[b * 64:(b + 1) * 64, g.col(h):g.col(h) + CH],
                op=ALU.mult), reads=["s2ntot", "s2recd"],
                writes=[f"s2outc{pa}"], tag=f"s2dv{c}_{h}")
        p.emit(q_st, lambda e, c=c, pa=pa: e.dma_start(
            out=io["outHT"][:, :, c * CH:(c + 1) * CH].rearrange("j p t -> p j t"),
            in_=outc[pa][:].rearrange("p (j t) -> p j t", j=ND)),
            reads=[f"s2outc{pa}"], writes=["outHTw"], tag=f"s2st{c}")


# version 23 (final)
# speedup vs baseline: 1.2383x; 1.0174x over previous
import numpy as np
import secrets
from contextlib import ExitStack

import concourse.bass as bass
import concourse.mybir as mybir
from concourse.bass_utils import run_bass_kernel_spmd
from concourse.alu_op_type import AluOpType as RALU

F32 = mybir.dt.float32
BF16 = mybir.dt.bfloat16
I32 = mybir.dt.int32
AF = mybir.ActivationFunctionType
ALU = mybir.AluOpType
AX = mybir.AxisListType


BODIES = {"pe": "tensor", "act": "scalar", "dve": "vector", "pool": "gpsimd",
          "sync": "sync"}


class Op:
    __slots__ = ("queue", "fn", "waits", "val", "tag", "step")

    def __init__(self, queue, fn, waits, val, tag, step):
        self.queue, self.fn, self.waits, self.val, self.tag, self.step = (
            queue, fn, waits, val, tag, step)


class Prog:
    def __init__(self, nc):
        self.nc = nc
        self.ops = []
        self.queues = {q: {"body": b, "step": 1} for q, b in
                       [("pe", "tensor"), ("act", "scalar"), ("dve", "vector"),
                        ("pool", "gpsimd")]}
        self.queues["syncq"] = {"body": "sync", "step": 0}
        self.count = {q: 0 for q in self.queues}
        self.last_write = {}
        self.readers = {}
        self.hwm = {}
        self.upgrades = []

    def dma_queue(self, name, body="sync"):
        assert name not in self.queues
        self.queues[name] = {"body": BODIES.get(body, body), "step": 16}
        self.count[name] = 0
        return name

    def emit(self, queue, fn, reads=(), writes=(), tag=""):
        qi = self.queues[queue]
        step = qi["step"]
        needed = {}

        def need(ev):
            q, v = ev
            if q == queue and q == "pe":
                return
            if self.queues[q]["step"] == 16 and v != self.count[q]:
                self.upgrades.append((q, v, self.count[q], tag))
                v = self.count[q]
            if self.hwm.setdefault(queue, {}).get(q, -1) >= v:
                return
            if needed.get(q, -1) < v:
                needed[q] = v

        for k in reads:
            if k in self.last_write:
                need(self.last_write[k])
        for k in writes:
            if k in self.last_write:
                need(self.last_write[k])
            for ev in self.readers.get(k, ()):
                need(ev)

        if step == 16 and self.count[queue] > 0:
            need((queue, self.count[queue]))
        waits = sorted(needed.items())
        self.count[queue] += step
        val = self.count[queue]
        self.ops.append(Op(queue, fn, waits, val, tag, step))
        h = self.hwm.setdefault(queue, {})
        for q, v in waits:
            if h.get(q, -1) < v:
                h[q] = v
        ev = (queue, val)
        for k in writes:
            self.last_write[k] = ev
            self.readers[k] = []
        for k in reads:
            self.readers.setdefault(k, []).append(ev)
        return ev

    def barrier_all(self, tag="phase_barrier"):
        totals = dict(self.count)
        for q in ("pe", "act", "dve", "pool", "syncq"):
            def fn(eng):
                return None
            waits = []
            for pq, v in totals.items():
                if pq == q or v == 0:
                    continue
                if self.hwm.setdefault(q, {}).get(pq, -1) >= v:
                    continue
                waits.append((pq, v))
                self.hwm[q][pq] = v
            self.ops.append(Op(q, fn, sorted(waits), self.count[q], tag, 0))

    def build(self, block, sems=None, es=None):
        if sems is None:
            sems = {q: es.enter_context(self.nc.semaphore("sem_" + q))
                    for q in self.queues if self.count[q] > 0 or
                    self.queues[q]["step"] != 16}
        self.sems = sems
        by_body = {}
        for op in self.ops:
            by_body.setdefault(self.queues[op.queue]["body"], []).append(op)
        for body, ops in by_body.items():
            dec = getattr(block, body)

            def make(ops=ops):
                def _(eng):
                    for op in ops:
                        for q, v in op.waits:
                            eng.wait_ge(sems[q], v)
                        inst = op.fn(eng)
                        if op.step == 0:
                            assert inst is None
                            continue
                        inc = min(op.step, 16)
                        if inst is not None:
                            inst.then_inc(sems[op.queue], inc)
                        else:
                            eng.nop().then_inc(sems[op.queue], inc)
                return _

            dec(make())



class Geo:
    def __init__(self, TH=4096, D=1024, H=16, DH=64, CH=128,
                 F0=2048, F1=4096, E=8, MACRO=1024, C0=1536, C1=768):
        self.C0, self.C1 = C0, C1
        self.TH = TH
        self.D = D
        self.H = H
        self.DH = DH
        self.CH = CH
        self.NCH = TH // CH
        self.F0 = F0
        self.F1 = F1
        self.E = E
        self.MACRO = MACRO
        self.NMAC = TH // MACRO
        self.NT = TH // 128
        self.ND = D // 128

    def col(self, h):
        return (h // 2) * self.CH + (h % 2) * (self.H // 2) * self.CH


def build_consts(g):
    H, CH, DH = g.H, g.CH, g.DH
    c = {}
    c["ident128"] = np.eye(128, dtype=np.float32)
    c["U128"] = np.triu(np.ones((128, 128), np.float32))
    c["ones_1x512"] = np.ones((1, 512), np.float32)
    c["ones_1x128"] = np.ones((1, 128), np.float32)
    c["ones_128x1"] = np.ones((128, 1), np.float32)
    c["ones_2x128"] = np.ones((2, 128), np.float32)
    c["eps_1x512"] = np.full((1, 512), 1e-6, np.float32)
    m = np.triu(np.ones((CH, CH), np.float32))
    mask = np.zeros((CH, H * CH), np.float32)
    for h in range(H):
        mask[:, g.col(h):g.col(h) + CH] = m
    c["maskU_rep"] = mask
    negI = np.zeros((H, H * CH), np.float32)
    for h in range(H):
        negI[h, g.col(h):g.col(h) + CH] = -1.0
    c["negIexp"] = negI
    c["negIexp2"] = np.concatenate([negI, negI], axis=0)
    d64 = np.zeros((H, H * DH), np.float32)
    for h in range(H):
        d64[h, h * DH:(h + 1) * DH] = 1.0
    c["delta64"] = d64
    HP = H // 2
    dP = np.zeros((HP, HP * DH + HP), np.float32)
    for jj in range(HP):
        dP[jj, jj * DH:(jj + 1) * DH] = 1.0
        dP[jj, HP * DH + jj] = 1.0
    c["deltaP"] = dP
    c["ones_16x64"] = np.ones((16, 64), np.float32)
    c["U128strict"] = np.triu(np.ones((128, 128), np.float32), 1)
    C0, C1 = g.C0, g.C1
    c["eCbase0"] = (np.arange(8, dtype=np.float32) * C0 - 1.0).reshape(1, 8)
    c["eCbase1"] = (np.arange(8, dtype=np.float32) * C1 - 1.0).reshape(1, 8)
    c["caprow0"] = np.tile((np.arange(1, 9, dtype=np.float32) * C0 - 1.0), (128, 1))
    c["caprow1"] = np.tile((np.arange(1, 9, dtype=np.float32) * C1 - 1.0), (128, 1))
    c["iota_t"] = np.arange(g.TH, dtype=np.float32).reshape(g.NT, 128).T.copy()
    return c



def load_consts(p, nc, es, g, io, want, arrs, qname="q_const"):
    q = p.dma_queue(qname, "gpsimd")
    out = {}
    ALIAS = {"identb": ("ident128", BF16), "ones_128x1b": ("ones_128x1", BF16),
             "ones_2x128b": ("ones_2x128", BF16),
             "negIexp2b": ("negIexp2", BF16), "delta64b": ("delta64", BF16),
             "eps_1x512b": ("eps_1x512", BF16)}
    for nm in want:
        base, dt = ALIAS.get(nm, (nm, F32))
        shp = list(arrs[base].shape)
        t = es.enter_context(nc.sbuf_tensor("cs_" + nm, shp, dt))
        src = io["c_" + base]
        p.emit(q, lambda e, t=t, src=src: e.dma_start(out=t[:], in_=src),
               writes=["c_" + nm], tag="const")
        out[nm] = t
    return out


def phase_front(p, nc, es, g, io, cst):
    TH, D, ND, NT, H, CH, NCH = g.TH, g.D, g.ND, g.NT, g.H, g.CH, g.NCH
    S = lambda nm, shp, dt=F32: es.enter_context(nc.sbuf_tensor(nm, shp, dt))
    PS = lambda nm, shp, dt=F32: es.enter_context(nc.psum_tensor(nm, shp, dt))

    q_ldx = p.dma_queue("f_ldx", "sync")
    q_ldw = p.dma_queue("f_ldw", "gpsimd")
    q_stx = p.dma_queue("f_stx", "sync")
    q_stq = p.dma_queue("f_stq", "sync")
    q_stk = p.dma_queue("f_stk", "sync")
    q_stv = p.dma_queue("f_stv", "sync")
    q_stt = p.dma_queue("f_stt", "gpsimd")
    q_stc = p.dma_queue("f_stc", "sync")

    wq = S("wq", [128, ND * D], BF16)
    wk = S("wk", [128, ND * D], BF16)
    wv = S("wv", [128, ND * D], BF16)
    wa = S("wa", [128, ND * H], BF16)
    for t, nm in ((wq, "Wq"), (wk, "Wk"), (wv, "Wv")):
        src = io[nm].rearrange("(dc p) f -> p dc f", p=128)
        p.emit(q_ldw, lambda e, t=t, src=src: e.dma_start(
            out=t[:].rearrange("p (dc f) -> p dc f", dc=ND), in_=src),
            writes=[t.name], tag="ldW")
    p.emit(q_ldw, lambda e: e.dma_start(
        out=wa[:].rearrange("p (dc f) -> p dc f", dc=ND),
        in_=io["Wa"].rearrange("(dc p) f -> p dc f", p=128)),
        writes=["wa"], tag="ldWa")
    ba_s = S("ba_s", [1, H])
    p.emit(q_ldw, lambda e: e.dma_start(out=ba_s[:], in_=io["ba"]),
           writes=["ba_s"])
    ba2 = S("ba2", [1, H])
    p.emit("dve", lambda e: e.tensor_scalar(out=ba2[:], in0=ba_s[:],
           scalar1=2.0, scalar2=None, op0=ALU.add), reads=["ba_s"],
           writes=["ba2"])

    pj = [PS(f"pj{i}", [128, 512]) for i in range(4)]
    ptr = [PS(f"ptrT{i}", [128, 128], BF16) for i in range(2)]
    ap_ = PS("ap_", [128, H])

    xnT = [S(f"xnT{j}", [128, TH], BF16) for j in range(ND)]

    def xnT_keys(trange=None):
        if trange is None:
            trange = range(NT)
        return [f"xnT{j}_{i}" for j in range(ND) for i in trange]

    xt = [S(f"xt{i}", [128, D]) for i in range(2)]
    xc = [S(f"xc{i}", [128, D]) for i in range(2)]
    xnb = [S(f"xnb{i}", [128, D], BF16) for i in range(2)]
    stat = [S(f"stat{i}", [128, 8]) for i in range(2)]
    for i in range(NT):
        pa = i % 2
        st = stat[pa]
        kxt, kxc, kxnb = f"xt{pa}", f"xc{pa}", f"xnb{pa}"
        k0, km, kv, kv2, kl, kr = (f"stat0{pa}", f"statm{pa}", f"statv{pa}",
                                   f"statv2{pa}", f"statl{pa}", f"statr{pa}")
        p.emit(q_ldx, lambda e, i=i, pa=pa: e.dma_start(
            out=xt[pa][:], in_=io["x"][i * 128:(i + 1) * 128, :]),
            reads=[kxc], writes=[kxt], tag=f"ldx{i}")
        p.emit("dve", lambda e, pa=pa, st=st: e.reduce_sum(
            st[:, 0:1], xt[pa][:], axis=AX.X), reads=[kxt], writes=[k0])
        p.emit("dve", lambda e, st=st: e.tensor_scalar(
            out=st[:, 1:2], in0=st[:, 0:1], scalar1=1.0 / D, scalar2=None,
            op0=ALU.mult), reads=[k0], writes=[km])
        p.emit("dve", lambda e, pa=pa, st=st: e.tensor_scalar(
            out=xc[pa][:], in0=xt[pa][:], scalar1=st[:, 1:2], scalar2=None,
            op0=ALU.subtract), reads=[kxt, km], writes=[kxc])
        p.emit("act", lambda e, pa=pa: e.activation(
            out=xt[pa][:], in_=xc[pa][:], func=AF.Square),
            reads=[kxc], writes=[kxt], tag="sq")
        p.emit("dve", lambda e, pa=pa, st=st: e.reduce_sum(
            st[:, 2:3], xt[pa][:], axis=AX.X), reads=[kxt], writes=[kv])
        p.emit("dve", lambda e, st=st: e.tensor_scalar(
            out=st[:, 5:6], in0=st[:, 2:3], scalar1=1.0 / D, scalar2=1e-5,
            op0=ALU.mult, op1=ALU.add), reads=[kv], writes=[kv2])
        p.emit("act", lambda e, st=st: e.activation(
            out=st[:, 3:4], in_=st[:, 5:6], func=AF.Ln),
            reads=[kv2], writes=[kl])
        p.emit("act", lambda e, st=st: e.activation(
            out=st[:, 4:5], in_=st[:, 3:4], func=AF.Exp, scale=-0.5),
            reads=[kl], writes=[kr])
        p.emit("dve", lambda e, pa=pa, st=st: e.tensor_scalar(
            out=xc[pa][:], in0=xc[pa][:], scalar1=st[:, 4:5], scalar2=None,
            op0=ALU.mult), reads=[kxc, kr], writes=[kxc])
        p.emit(q_stx, lambda e, i=i, pa=pa: e.dma_start(
            out=io["xn32"][i * 128:(i + 1) * 128, :], in_=xc[pa][:]),
            reads=[kxc], writes=["xn32w"], tag=f"stxn{i}")
        p.emit("dve", lambda e, pa=pa: e.tensor_copy(xnb[pa][:], xc[pa][:]),
               reads=[kxc], writes=[kxnb])
        for j in range(ND):
            pb = j % 2
            p.emit("pe", lambda e, pa=pa, j=j, pb=pb: e.transpose(
                ptr[pb][:], xnb[pa][:, j * 128:(j + 1) * 128], cst["identb"][:]),
                reads=[kxnb, "c_identb"], writes=[f"ptrT{pb}"], tag=f"tr{i}_{j}")
            p.emit("dve", lambda e, i=i, j=j, pb=pb: e.tensor_copy(
                xnT[j][:, i * 128:(i + 1) * 128], ptr[pb][:]),
                reads=[f"ptrT{pb}"], writes=[f"xnT{j}_{i}"], tag=f"trc{i}_{j}")

    u_all = S("u_all", [128, H * NT])
    ksl = [S(f"ksl{i}", [128, D], BF16) for i in range(2)]
    vsl = [S(f"vsl{i}", [128, D], BF16) for i in range(2)]
    kesc = [S(f"kesc{i}", [128, D]) for i in range(2)]
    kesb = [S(f"kesb{i}", [128, D], BF16) for i in range(2)]
    for i in range(NT):
        pa = i % 2
        rk = xnT_keys([i])
        for d in range(ND):
            kx = f"xnT{d}_{i}"
            for half in range(2):
                p.emit("pe", lambda e, d=d, i=i, half=half: e.matmul(
                    pj[half][:], xnT[d][:, i * 128:(i + 1) * 128],
                    wk[:, d * D + half * 512: d * D + (half + 1) * 512],
                    start=(d == 0), stop=(d == ND - 1)),
                    reads=["wk", kx], writes=[f"pj{half}"], tag=f"ktm{i}")
                p.emit("pe", lambda e, d=d, i=i, half=half: e.matmul(
                    pj[2 + half][:], xnT[d][:, i * 128:(i + 1) * 128],
                    wv[:, d * D + half * 512: d * D + (half + 1) * 512],
                    start=(d == 0), stop=(d == ND - 1)),
                    reads=["wv", kx], writes=[f"pj{2+half}"], tag=f"vtm{i}")
            p.emit("pe", lambda e, d=d, i=i: e.matmul(
                ap_[:], xnT[d][:, i * 128:(i + 1) * 128], wa[:, d * H:(d + 1) * H],
                start=(d == 0), stop=False), reads=["wa", kx],
                writes=["ap_"], tag=f"atm{i}")
        p.emit("pe", lambda e: e.matmul(
            ap_[:], cst["ones_1x128"][:], ba2[:], start=False, stop=True),
            reads=["c_ones_1x128", "ba2"], writes=["ap_"], tag=f"aba{i}")
        p.emit("dve", lambda e, i=i: e.tensor_copy(
            u_all[:, i * H:(i + 1) * H], ap_[:]), reads=["ap_"],
            writes=[f"u_{i}"], tag=f"uev{i}")
        for half in range(2):
            p.emit("dve", lambda e, half=half, pa=pa: e.tensor_scalar(
                out=kesc[pa][:, half * 512:(half + 1) * 512], in0=pj[half][:],
                scalar1=0.0, scalar2=None, op0=ALU.min),
                reads=[f"pj{half}"], writes=[f"kesc{pa}_{half}"])
            p.emit("act", lambda e, half=half, pa=pa: e.activation(
                out=kesb[pa][:, half * 512:(half + 1) * 512],
                in_=kesc[pa][:, half * 512:(half + 1) * 512], func=AF.Exp),
                reads=[f"kesc{pa}_{half}"], writes=[f"kesb{pa}_{half}"])
            p.emit("dve", lambda e, half=half, pa=pa: e.scalar_tensor_tensor(
                out=ksl[pa][:, half * 512:(half + 1) * 512], in0=pj[half][:],
                scalar=0.0, in1=kesb[pa][:, half * 512:(half + 1) * 512],
                op0=ALU.max, op1=ALU.add),
                reads=[f"pj{half}", f"kesb{pa}_{half}"], writes=[f"ksl{pa}"])
            p.emit("dve", lambda e, half=half, pa=pa: e.tensor_copy(
                vsl[pa][:, half * 512:(half + 1) * 512], pj[2 + half][:]),
                reads=[f"pj{2+half}"], writes=[f"vsl{pa}"])
        p.emit(q_stk, lambda e, i=i, pa=pa: e.dma_start(
            out=io["Ktm"][i * 128:(i + 1) * 128, :], in_=ksl[pa][:]),
            reads=[f"ksl{pa}"], writes=["Ktmw"], tag="stKtm")
        p.emit(q_stv, lambda e, i=i, pa=pa: e.dma_start(
            out=io["V"][i * 128:(i + 1) * 128, :], in_=vsl[pa][:]),
            reads=[f"vsl{pa}"], writes=["Vw"], tag="stV")

    qsl = [S(f"qsl{i}", [128, 512], BF16) for i in range(2)]
    esc = [S(f"esc{i}", [128, 512]) for i in range(2)]
    esb = [S(f"esb{i}", [128, 512], BF16) for i in range(2)]
    NSLAB = TH // 512
    for (wsb, wname, dst) in ((wq, "wq", "QT"), (wk, "wk", "KT")):
        for fout in range(ND):
            for sg in range((NSLAB + 3) // 4):
                slabs = list(range(sg * 4, min((sg + 1) * 4, NSLAB)))
                for d in range(ND):
                    for si, sl in enumerate(slabs):
                        p.emit("pe", lambda e, d=d, sl=sl, si=si, fout=fout, wsb=wsb: e.matmul(
                            pj[si][:],
                            wsb[:, d * D + fout * 128: d * D + (fout + 1) * 128],
                            xnT[d][:, sl * 512:(sl + 1) * 512],
                            start=(d == 0), stop=(d == ND - 1)),
                            reads=[wname] + [f"xnT{d}_{4 * sl + t}"
                                             for t in range(4)],
                            writes=[f"pj{si}"], tag=f"pj{fout}_{sl}")
                for si, sl in enumerate(slabs):
                    pa = si % 2
                    p.emit("dve", lambda e, si=si, pa=pa: e.tensor_scalar(
                        out=esc[pa][:], in0=pj[si][:], scalar1=0.0, scalar2=None,
                        op0=ALU.min), reads=[f"pj{si}"], writes=[f"esc{pa}"])
                    p.emit("act", lambda e, pa=pa: e.activation(
                        out=esb[pa][:], in_=esc[pa][:], func=AF.Exp),
                        reads=[f"esc{pa}"], writes=[f"esb{pa}"])
                    p.emit("dve", lambda e, si=si, pa=pa: e.scalar_tensor_tensor(
                        out=qsl[pa][:], in0=pj[si][:], scalar=0.0, in1=esb[pa][:],
                        op0=ALU.max, op1=ALU.add),
                        reads=[f"pj{si}", f"esb{pa}"], writes=[f"qsl{pa}"])
                    p.emit(q_stq if dst == "QT" else q_stk,
                           lambda e, fout=fout, sl=sl, pa=pa, dst=dst: e.dma_start(
                        out=io[dst][fout, :, sl * 512:(sl + 1) * 512], in_=qsl[pa][:]),
                        reads=[f"qsl{pa}"], writes=[dst + "w"], tag=f"st{dst}")
    p.barrier_all("front_sub1")
    tls = S("tls", [128, H * NT])
    tls2 = S("tls2", [128, H * NT])
    ukeys = [f"u_{i}" for i in range(NT)]
    p.emit("act", lambda e: e.activation(out=tls[:], in_=u_all[:], func=AF.Abs),
           reads=ukeys, writes=["tls"])
    p.emit("act", lambda e: e.activation(out=tls[:], in_=tls[:], func=AF.Exp,
           scale=-1.0), reads=["tls"], writes=["tls"])
    p.emit("act", lambda e: e.activation(out=tls[:], in_=tls[:], func=AF.Ln,
           bias=1.0), reads=["tls"], writes=["tls"])
    p.emit("dve", lambda e: e.tensor_scalar(out=tls2[:], in0=u_all[:],
           scalar1=0.0, scalar2=None, op0=ALU.min), reads=ukeys, writes=["tls2"])
    p.emit("dve", lambda e: e.tensor_sub(u_all[:], tls2[:], tls[:]),
           reads=["tls", "tls2"] + ukeys, writes=["loga"] + ukeys)
    p.barrier_all("front_sub2")

    cumtm_ps = pj[0][:, 0:H]
    cumhm_ps = pj[1][0:H, 0:128]
    tot_ps = pj[2][0:1, 0:H]
    cumtm_all = S("cumtm_all", [128, H * NCH])
    cumhm_sb = S("cumhm_sb", [H, 128])
    cumg_sb = S("cumg_sb", [H, 128])
    chl16 = S("chl16", [H, 128], BF16)
    clo32 = S("clo32", [H, 128])
    clo16 = S("clo16", [H, 128], BF16)
    pfx = S("pfx", [H, 2])
    tot_all = S("tot_all", [1, H * NCH])
    p.emit("dve", lambda e: e.memset(pfx[:], 0.0), writes=["pfx"], tag="pfx0")
    totc_ps = pj[3][0:H, 0:1]
    for c in range(NCH):
        p.emit("pe", lambda e, c=c: e.matmul(
            cumtm_ps, cst["U128"][:], u_all[:, c * H:(c + 1) * H],
            start=True, stop=True), reads=["loga", "c_U128"],
            writes=["pj0"], tag=f"cumtm{c}")
        p.emit("pe", lambda e, c=c: e.matmul(
            cumhm_ps, u_all[:, c * H:(c + 1) * H], cst["U128"][:],
            start=True, stop=True), reads=["loga", "c_U128"],
            writes=["pj1"], tag=f"cumhm{c}")
        p.emit("pe", lambda e, c=c: e.matmul(
            tot_ps, cst["ones_128x1"][:], u_all[:, c * H:(c + 1) * H],
            start=True, stop=True), reads=["loga", "c_ones_128x1"],
            writes=["pj2"], tag=f"tot{c}")
        p.emit("dve", lambda e, c=c: e.tensor_copy(
            cumtm_all[:, c * H:(c + 1) * H], cumtm_ps),
            reads=["pj0"], writes=["cumtm_all"])
        p.emit("dve", lambda e, c=c: e.tensor_copy(cumhm_sb[:], cumhm_ps),
               reads=["pj1"], writes=["cumhm_sb"])
        p.emit("dve", lambda e, c=c: e.tensor_copy(
            tot_all[:, c * H:(c + 1) * H], tot_ps),
            reads=["pj2"], writes=["tot_all"])
        p.emit(q_stc, lambda e, c=c: e.dma_start(
            out=io["cum_nat"][:, c * 128:(c + 1) * 128], in_=cumhm_sb[:]),
            reads=["cumhm_sb"], writes=["cumnatw"], tag="stnat")
        p.emit("dve", lambda e: e.tensor_copy(chl16[:], cumhm_sb[:]),
               reads=["cumhm_sb"], writes=["chl16"], tag=f"chl{c}")
        p.emit("dve", lambda e: e.tensor_tensor(
            out=clo32[:], in0=cumhm_sb[:], in1=chl16[:], op=ALU.subtract),
            reads=["cumhm_sb", "chl16"], writes=["clo32"], tag=f"clo{c}")
        p.emit("dve", lambda e: e.tensor_copy(clo16[:], clo32[:]),
               reads=["clo32"], writes=["clo16"], tag=f"clob{c}")
        p.emit(q_stc, lambda e, c=c: e.dma_start(
            out=io["cumhl"][0, :, c * 128:(c + 1) * 128], in_=chl16[:]),
            reads=["chl16"], writes=["cumhlw"], tag="sthl")
        p.emit(q_stc, lambda e, c=c: e.dma_start(
            out=io["cumhl"][1, :, c * 128:(c + 1) * 128], in_=clo16[:]),
            reads=["clo16"], writes=["cumhlw"], tag="sthl2")
        p.emit("pe", lambda e, c=c: e.matmul(
            totc_ps, u_all[:, c * H:(c + 1) * H], cst["ones_128x1"][:],
            start=True, stop=True), reads=["loga", "c_ones_128x1"],
            writes=["pj3"], tag=f"totc{c}")
        p.emit("dve", lambda e, c=c: e.tensor_scalar(
            out=cumg_sb[:], in0=cumhm_sb[:], scalar1=pfx[:, 0:1], scalar2=None,
            op0=ALU.add), reads=["cumhm_sb", "pfx"], writes=["cumg_sb"],
            tag=f"cumg{c}")
        p.emit(q_stc, lambda e, c=c: e.dma_start(
            out=io["cumg_nat"][:, c * 128:(c + 1) * 128], in_=cumg_sb[:]),
            reads=["cumg_sb"], writes=["cumgw"], tag="stcumg")
        p.emit("dve", lambda e: e.tensor_copy(chl16[:], cumg_sb[:]),
               reads=["cumg_sb", "cumhlw"], writes=["chl16"], tag=f"ghl{c}")
        p.emit("dve", lambda e: e.tensor_tensor(
            out=clo32[:], in0=cumg_sb[:], in1=chl16[:], op=ALU.subtract),
            reads=["cumg_sb", "chl16"], writes=["clo32"], tag=f"glo{c}")
        p.emit("dve", lambda e: e.tensor_copy(clo16[:], clo32[:]),
               reads=["clo32"], writes=["clo16"], tag=f"glob{c}")
        p.emit(q_stc, lambda e, c=c: e.dma_start(
            out=io["cumghl"][0, :, c * 128:(c + 1) * 128], in_=chl16[:]),
            reads=["chl16"], writes=["cumghlw"], tag="stghl")
        p.emit(q_stc, lambda e, c=c: e.dma_start(
            out=io["cumghl"][1, :, c * 128:(c + 1) * 128], in_=clo16[:]),
            reads=["clo16"], writes=["cumghlw"], tag="stghl2")
        p.emit("dve", lambda e, c=c: e.scalar_tensor_tensor(
            out=pfx[:, 0:1], in0=totc_ps, scalar=1.0, in1=pfx[:, 0:1],
            op0=ALU.mult, op1=ALU.add), reads=["pj3", "pfx"], writes=["pfx"],
            tag=f"pfxu{c}")
    p.emit(q_stc, lambda e: e.dma_start(out=io["cumtm"], in_=cumtm_all[:]),
           reads=["cumtm_all"], writes=["cumtmw"])
    p.emit(q_stc, lambda e: e.dma_start(out=io["totals"], in_=tot_all[:]),
           reads=["tot_all"], writes=["totw"])
    for j in range(ND):
        p.emit(q_stt, lambda e, j=j: e.dma_start(
            out=io["xnT"][j, :, :], in_=xnT[j][:]),
            reads=xnT_keys(), writes=["xnTw"], tag=f"stxnT{j}")
    return dict(xnT=xnT, xnT_keys=xnT_keys)


def phase_scan(p, nc, es, g, io, cst, pref="s"):
    TH, D, ND, H, CH, NCH, DH = g.TH, g.D, g.ND, g.H, g.CH, g.NCH, g.DH
    HP = H // 2
    SCOLS = HP * DH + HP
    S = lambda nm, shp, dt=F32: es.enter_context(nc.sbuf_tensor(pref + nm, shp, dt))
    PS = lambda nm, shp, dt=F32: es.enter_context(nc.psum_tensor(pref + nm, shp, dt))

    q_lq = p.dma_queue(pref + "_lq", "sync")
    q_lk = p.dma_queue(pref + "_lk", "sync")
    q_lkm = p.dma_queue(pref + "_lkm", "sync")
    q_lv = p.dma_queue(pref + "_lv", "sync")
    q_lcf = p.dma_queue(pref + "_lcf", "gpsimd")
    q_lch = p.dma_queue(pref + "_lch", "gpsimd")
    q_ldc = p.dma_queue(pref + "_ldc", "gpsimd")
    q_st = p.dma_queue(pref + "_st", "sync")

    cumtm = S("cumtm_r", [128, H * NCH])
    totals = S("totals_r", [1, H * NCH])
    p.emit(q_ldc, lambda e: e.dma_start(out=cumtm[:], in_=io["cumtm"]),
           writes=["cumtm_r"])
    p.emit(q_ldc, lambda e: e.dma_start(out=totals[:], in_=io["totals"]),
           writes=["totals_r"])
    S32 = S("S32", [128, SCOLS])
    Sb = S("Sb", [128, SCOLS], BF16)
    p.emit("dve", lambda e: e.memset(S32[:], 0.0), writes=["S32"])
    p.emit("dve", lambda e: e.tensor_copy(Sb[:], S32[:]), reads=["S32"],
           writes=["Sb"])

    PB = PS("PB", [128, 2048])
    PP = PS("PP", [128, 2048])
    POv = PB[:, 0:1024]
    SIv = PB[:, 1024:1024 + SCOLS]

    qt = [S(f"qt{i}", [128, ND * CH], BF16) for i in range(2)]
    kt = [S(f"kt{i}", [128, ND * CH], BF16) for i in range(2)]
    ktm = [S(f"ktm{i}", [128, D], BF16) for i in range(2)]
    vtm = [S(f"vtm{i}", [128, D], BF16) for i in range(2)]
    cfl = [S(f"cfl{i}", [2, H * CH], BF16) for i in range(2)]
    chm = [S(f"chm{i}", [H, CH]) for i in range(2)]
    chmh = [S(f"chmh{i}", [2 * H, CH], BF16) for i in range(2)]
    ecum = S("ecum", [128, H * CH], BF16)
    elog = S("elog", [128, H * CH])
    Etile = S("Etile", [128, H * CH], BF16)
    Pm = S("Pm", [128, H * CH], BF16)
    PmE = S("PmE", [128, H * CH], BF16)
    qte = S("qte", [128, ND * CH], BF16)
    denr = S("denr", [1, H * CH])
    outc32 = [S(f"outc32{i}", [128, ND * CH]) for i in range(2)]
    e2vhm = S("e2vhm", [H, CH], BF16)
    e2vhm32 = S("e2vhm32", [H, CH])
    e2vb = S("e2vb", [128, H * DH], BF16)
    vh = S("vh", [128, H * DH], BF16)
    tcol = S("tcol", [H, 1])
    ecolE = S("ecolE", [HP, 1])
    ecolO = S("ecolO", [HP, 1])
    ehmE = S("ehmE", [HP, DH])
    ehmO = S("ehmO", [HP, DH])
    stmp = S("stmp", [128, SCOLS])

    cum_flat_view = io["cumhl"].rearrange("w (a b) t -> w b a t", b=2)

    for c in range(NCH):
        pa = c % 2
        kq, kk, kkm, kv, kcf, kch = (f"qt{pa}", f"kt{pa}", f"ktm{pa}",
                                     f"vtm{pa}", f"cfl{pa}", f"chm{pa}")
        p.emit(q_lq, lambda e, c=c, pa=pa: e.dma_start(
            out=qt[pa][:].rearrange("p (j t) -> p j t", j=ND),
            in_=io["QT"][:, :, c * CH:(c + 1) * CH].rearrange("j p t -> p j t")),
            writes=[kq], tag=f"ldq{c}")
        p.emit(q_lk, lambda e, c=c, pa=pa: e.dma_start(
            out=kt[pa][:].rearrange("p (j t) -> p j t", j=ND),
            in_=io["KT"][:, :, c * CH:(c + 1) * CH].rearrange("j p t -> p j t")),
            writes=[kk], tag=f"ldk{c}")
        p.emit(q_lkm, lambda e, c=c, pa=pa: e.dma_start(
            out=ktm[pa][:], in_=io["Ktm"][c * CH:(c + 1) * CH, :]),
            writes=[kkm], tag=f"ldktm{c}")
        p.emit(q_lv, lambda e, c=c, pa=pa: e.dma_start(
            out=vtm[pa][:], in_=io["V"][c * CH:(c + 1) * CH, :]),
            writes=[kv], tag=f"ldv{c}")
        for w in range(2):
            p.emit(q_lcf, lambda e, c=c, pa=pa, w=w: e.dma_start(
                out=cfl[pa][w:w + 1, :].rearrange("o (b a t) -> o b a t", b=2, a=ND),
                in_=cum_flat_view[w:w + 1, :, :, c * CH:(c + 1) * CH].squeeze(0).unsqueeze(0)),
                writes=[kcf], tag=f"ldcf{c}_{w}")
        p.emit(q_lch, lambda e, c=c, pa=pa: e.dma_start(
            out=chm[pa][:], in_=io["cum_nat"][:, c * CH:(c + 1) * CH]),
            writes=[kch], tag=f"ldch{c}")
        p.emit(q_lch, lambda e, c=c, pa=pa: e.dma_start(
            out=chmh[pa][:],
            in_=io["cumhl"][:, :, c * CH:(c + 1) * CH].rearrange(
                "w a t -> (w a) t")),
            writes=[kch + "h"], tag=f"ldchh{c}")

        for h in range(H):
            b, j = h % 2, h // 2
            p.emit("pe", lambda e, b=b, j=j, h=h, pa=pa: e.matmul(
                PP[:, g.col(h):g.col(h) + CH],
                kt[pa][b * 64:(b + 1) * 64, j * CH:(j + 1) * CH],
                qt[pa][b * 64:(b + 1) * 64, j * CH:(j + 1) * CH],
                start=True, stop=True),
                reads=[kq, kk], writes=["PP"], tag=f"pp{c}_{h}")
        for sl in range(4):
            p.emit("pe", lambda e, sl=sl, pa=pa: e.matmul(
                PB[:, sl * 512:(sl + 1) * 512], cst["ones_2x128b"][:],
                cfl[pa][0:2, sl * 512:(sl + 1) * 512], start=True, stop=True),
                reads=[kcf, "c_ones_2x128b"], writes=["PB"], tag=f"b1_{c}_{sl}")
        p.emit("act", lambda e: e.activation(out=ecum[:], in_=PB[:],
               func=AF.Exp), reads=["PB"], writes=["ecum"], tag=f"ecum{c}")
        for sl in range(4):
            p.emit("pe", lambda e, sl=sl, pa=pa: e.matmul(
                PB[:, sl * 512:(sl + 1) * 512], cst["ones_2x128b"][:],
                cfl[pa][0:2, sl * 512:(sl + 1) * 512], start=True, stop=False),
                reads=[kcf, "c_ones_2x128b"], writes=["PB"], tag=f"b1b_{c}_{sl}")
        for sl in range(4):
            p.emit("pe", lambda e, sl=sl, pa=pa: e.matmul(
                PB[:, sl * 512:(sl + 1) * 512], chmh[pa][:],
                cst["negIexp2b"][:, sl * 512:(sl + 1) * 512],
                start=False, stop=True),
                reads=[kch + "h", "c_negIexp2b"], writes=["PB"], tag=f"b2_{c}_{sl}")
        p.emit("act", lambda e: e.activation(out=elog[:], in_=PB[:],
               func=AF.Relu, scale=-1.0), reads=["PB"], writes=["elog"],
               tag=f"elog{c}")
        p.emit("act", lambda e: e.activation(out=Etile[:], in_=elog[:],
               func=AF.Exp, scale=-1.0), reads=["elog"], writes=["Etile"],
               tag=f"E{c}")
        for b in range(2):
            p.emit("dve", lambda e, b=b, pa=pa: e.tensor_tensor(
                out=qte[b * 64:(b + 1) * 64, :],
                in0=qt[pa][b * 64:(b + 1) * 64, :],
                in1=ecum[b * 64:(b + 1) * 64, b * 1024:(b + 1) * 1024],
                op=ALU.mult), reads=[kq, "ecum"], writes=["qte"],
                tag=f"qte{c}_{b}")
        p.emit("dve", lambda e: e.tensor_tensor(
            out=Pm[:], in0=PP[:], in1=cst["maskU_rep"][:], op=ALU.mult),
            reads=["PP", "c_maskU_rep"], writes=["Pm"], tag=f"pm{c}")
        p.emit("dve", lambda e: e.tensor_tensor(
            out=PmE[:], in0=Pm[:], in1=Etile[:], op=ALU.mult),
            reads=["Pm", "Etile"], writes=["PmE"], tag=f"pme{c}")
        po_banks = {}
        for h in range(H):
            po_banks.setdefault(((h // 2) // 4, h % 2), []).append(h)
        for bank, hs in po_banks.items():
            for k, h in enumerate(hs):
                b, j = h % 2, h // 2
                p.emit("pe", lambda e, b=b, j=j, h=h, st=(k == 0): e.matmul(
                    POv[b * 64:(b + 1) * 64, j * CH:(j + 1) * CH],
                    Sb[b * 64:(b + 1) * 64, j * DH:(j + 1) * DH],
                    qte[b * 64:(b + 1) * 64, j * CH:(j + 1) * CH],
                    start=st, stop=False, tile_position=(b * 64, b * 64),
                    skip_group_check=True),
                    reads=["Sb", "qte"], writes=["PB"], tag=f"oex{c}_{h}")
        for sl in range(4):
            p.emit("pe", lambda e, sl=sl: e.matmul(
                PP[0:1, sl * 512:(sl + 1) * 512], cst["ones_128x1b"][:],
                PmE[:, sl * 512:(sl + 1) * 512], start=True, stop=False),
                reads=["PmE", "c_ones_128x1b"], writes=["PP"], tag=f"dsum{c}_{sl}")
        for h in range(H):
            b, j = h % 2, h // 2
            p.emit("pe", lambda e, b=b, j=j, h=h: e.matmul(
                PP[0:1, g.col(h):g.col(h) + CH],
                Sb[b * 64:(b + 1) * 64, 512 + j:513 + j],
                qte[b * 64:(b + 1) * 64, j * CH:(j + 1) * CH],
                start=False, stop=False),
                reads=["Sb", "qte"], writes=["PP"], tag=f"dz{c}_{h}")
        for sl in range(4):
            p.emit("pe", lambda e, sl=sl: e.matmul(
                PP[0:1, sl * 512:(sl + 1) * 512],
                cst["ones_128x1b"][0:1, 0:1],
                cst["eps_1x512b"][:], start=False, stop=True),
                reads=["c_eps_1x512b", "c_ones_128x1b"], writes=["PP"],
                tag=f"deps{c}_{sl}")
        for bank, hs in po_banks.items():
            for k, h in enumerate(hs):
                b, j = h % 2, h // 2
                p.emit("pe", lambda e, b=b, j=j, h=h, pa=pa,
                       sp=(k == len(hs) - 1): e.matmul(
                    POv[b * 64:(b + 1) * 64, j * CH:(j + 1) * CH],
                    vtm[pa][:, h * DH:(h + 1) * DH],
                    PmE[:, g.col(h):g.col(h) + CH],
                    start=False, stop=sp, tile_position=(0, b * 64),
                    skip_group_check=True),
                    reads=[kv, "PmE"], writes=["PB"], tag=f"oin{c}_{h}")
        p.emit("act", lambda e, pa=pa: e.copy(denr[:], PP[0:1, :]),
               reads=["PP"], writes=["denr"], tag=f"denr{c}")
        p.emit(q_st, lambda e, c=c: e.dma_start(
            out=io["den_hbm"][c:c + 1, :], in_=denr[:]),
            reads=["denr"], writes=["denw"], tag=f"stden{c}")
        p.emit("act", lambda e, pa=pa: e.copy(outc32[pa][:], POv),
               reads=["PB"], writes=[f"outc32{pa}"], tag=f"nev{c}")
        p.emit(q_st, lambda e, c=c, pa=pa: e.dma_start(
            out=io["num_hbm"][c, :, :], in_=outc32[pa][:]),
            reads=[f"outc32{pa}"], writes=["numw"], tag=f"stnum{c}")
        p.emit("pe", lambda e, c=c: e.matmul(
            PP[0:H, 1040:1041], totals[:, c * H:(c + 1) * H],
            cst["ones_1x128"][0:1, 0:1], start=True, stop=True),
            reads=["totals_r", "c_ones_1x128"], writes=["PP"], tag=f"tcol{c}")
        p.emit("pe", lambda e, c=c: e.matmul(
            PP[0:HP, 1042:1043], totals[0:1, c * H:(c + 1) * H:2],
            cst["ones_1x128"][0:1, 0:1], start=True, stop=True),
            reads=["totals_r", "c_ones_1x128"], writes=["PP"], tag=f"tcE{c}")
        p.emit("pe", lambda e, c=c: e.matmul(
            PP[0:HP, 1044:1045], totals[0:1, c * H + 1:(c + 1) * H:2],
            cst["ones_1x128"][0:1, 0:1], start=True, stop=True),
            reads=["totals_r", "c_ones_1x128"], writes=["PP"], tag=f"tcO{c}")
        p.emit("dve", lambda e: e.tensor_copy(tcol[:], PP[0:H, 1040:1041]),
               reads=["PP"], writes=["tcolS"], tag=f"tcolc{c}")
        p.emit("act", lambda e: e.activation(out=ecolE[:], in_=PP[0:HP, 1042:1043],
               func=AF.Exp), reads=["PP"], writes=["ecolE"], tag=f"ecE{c}")
        p.emit("act", lambda e: e.activation(out=ecolO[:], in_=PP[0:HP, 1044:1045],
               func=AF.Exp), reads=["PP"], writes=["ecolO"], tag=f"ecO{c}")
        p.emit("dve", lambda e, pa=pa: e.tensor_scalar(
            out=e2vhm32[:], in0=chm[pa][:], scalar1=tcol[:, 0:1], scalar2=-1.0,
            op0=ALU.subtract, op1=ALU.mult), reads=[kch, "tcolS"],
            writes=["e2vhm32"], tag=f"e2v32{c}")
        p.emit("act", lambda e: e.activation(out=e2vhm[:], in_=e2vhm32[:],
               func=AF.Exp), reads=["e2vhm32"], writes=["e2vhm"], tag=f"e2v{c}")
        for half in range(2):
            p.emit("pe", lambda e, half=half: e.matmul(
                PP[:, half * 512:(half + 1) * 512], e2vhm[:],
                cst["delta64b"][:, half * 512:(half + 1) * 512],
                start=True, stop=True),
                reads=["e2vhm", "c_delta64b"], writes=["PP"], tag=f"e2vb{c}_{half}")
        p.emit("act", lambda e: e.copy(e2vb[:], PP[:, 0:1024]),
               reads=["PP"], writes=["e2vb"], tag=f"e2vbc{c}")
        p.emit("dve", lambda e, pa=pa: e.tensor_tensor(
            out=vh[:], in0=vtm[pa][:], in1=e2vb[:], op=ALU.mult),
            reads=[kv, "e2vb"], writes=["vh"], tag=f"vh{c}")
        for h in range(H):
            b, j = h % 2, h // 2
            p.emit("pe", lambda e, b=b, j=j, h=h, pa=pa: e.matmul(
                SIv[b * 64:(b + 1) * 64, j * DH:(j + 1) * DH],
                ktm[pa][:, h * DH:(h + 1) * DH], vh[:, h * DH:(h + 1) * DH],
                start=True, stop=True, tile_position=(0, b * 64)),
                reads=[kkm, "vh"], writes=["PB"], tag=f"si{c}_{h}")
            p.emit("pe", lambda e, b=b, j=j, h=h, pa=pa: e.matmul(
                SIv[b * 64:(b + 1) * 64, 512 + j:513 + j],
                ktm[pa][:, h * DH:(h + 1) * DH],
                e2vb[:, h * DH:h * DH + 1],
                start=True, stop=True, tile_position=(0, b * 64)),
                reads=[kkm, "e2vb"], writes=["PB"], tag=f"siz{c}_{h}")
        p.emit("dve", lambda e: e.tensor_scalar(
            out=ehmE[:], in0=cst["ones_16x64"][0:HP, :],
            scalar1=ecolE[:, 0:1], scalar2=None, op0=ALU.mult),
            reads=["ecolE", "c_ones_16x64"], writes=["ehmE"], tag=f"ehE{c}")
        p.emit("dve", lambda e: e.tensor_scalar(
            out=ehmO[:], in0=cst["ones_16x64"][0:HP, :],
            scalar1=ecolO[:, 0:1], scalar2=None, op0=ALU.mult),
            reads=["ecolO", "c_ones_16x64"], writes=["ehmO"], tag=f"ehO{c}")
        for (ehm, kehm, pb, tp) in ((ehmE, "ehmE", 0, None), (ehmO, "ehmO", 64, (0, 64))):
            p.emit("pe", lambda e, ehm=ehm, pb=pb, tp=tp: e.matmul(
                PP[pb:pb + 64, 1024:1536], ehm[:], cst["deltaP"][:, 0:512],
                start=True, stop=True, tile_position=tp),
                reads=[kehm, "c_deltaP"], writes=["PP"], tag=f"etbS{c}_{pb}")
            p.emit("pe", lambda e, ehm=ehm, pb=pb, tp=tp: e.matmul(
                PP[pb:pb + 64, 1536:1536 + HP], ehm[:], cst["deltaP"][:, 512:512 + HP],
                start=True, stop=True, tile_position=tp),
                reads=[kehm, "c_deltaP"], writes=["PP"], tag=f"etbZ{c}_{pb}")
        p.emit("dve", lambda e: e.tensor_tensor(
            out=stmp[:], in0=S32[:], in1=PP[:, 1024:1024 + SCOLS],
            op=ALU.mult), reads=["S32", "PP"], writes=["stmp"], tag=f"sdec{c}")
        p.emit("dve", lambda e: e.tensor_tensor(
            out=S32[:], in0=stmp[:], in1=SIv, op=ALU.add),
            reads=["stmp", "PB"], writes=["S32"], tag=f"supd{c}")
        p.emit("act", lambda e: e.copy(Sb[:], S32[:]),
               reads=["S32"], writes=["Sb"], tag=f"sbc{c}")
    p.emit(q_st, lambda e: e.dma_start(out=io["Sfin"], in_=S32[:]),
           reads=["S32"], writes=["Sfinw"], tag="stSfin")


def _ehT_keys(pref, lo, hi, MACRO):
    return [f"{pref}ehT_{h}" for h in range(lo // MACRO,
                                            (hi + MACRO - 1) // MACRO)]


def emit_expert_block(p, nc, g, cst, pref, m, e, NF, actfn, xT_m, kxT,
                      w1sb, w2sb, q_lw1, q_lw2, io_w1, io_w2,
                      b1e, io_b1, b2e, io_b2, ehT_m, PSA, PSB, y, wgate,
                      kwg, D=1024):
    MACRO, ND, E = g.MACRO, g.ND, g.E
    NTM = MACRO // 128
    assert NTM * 512 * 2 == MACRO // 128 * 1024
    NHC = NF // 128
    NSL = NF // 512
    kb1f, kb2f = pref + "b1e", pref + "b2e"

    for sl in range(NSL):
        b = sl % 2
        kw1 = f"{pref}w1_{b}"
        p.emit(q_lw1, lambda e_, e=e, sl=sl, b=b: e_.dma_start(
            out=w1sb[b][:].rearrange("p (dc h) -> p dc h", dc=ND),
            in_=io_w1[e].rearrange("(dc p) h -> p dc h", p=128)[:, :, sl * 512:(sl + 1) * 512]),
            writes=[kw1], tag=f"{pref}lw1_{m}_{e}_{sl}")
        for hq in range(4):
            hc = sl * 4 + hq
            pb = hc % 2
            kpsa = f"{pref}PSA{pb}"
            for d in range(ND):
                for tsl in range(MACRO // 512):
                    p.emit("pe", lambda e_, d=d, hq=hq, tsl=tsl, b=b, pb=pb: e_.matmul(
                        PSA[:, pb * 1024 + tsl * 512: pb * 1024 + (tsl + 1) * 512],
                        w1sb[b][:, d * 512 + hq * 128: d * 512 + (hq + 1) * 128],
                        xT_m[:, d * MACRO + tsl * 512: d * MACRO + (tsl + 1) * 512],
                        start=(d == 0), stop=(d == ND - 1)),
                        reads=[kw1, kxT], writes=[kpsa], tag=f"{pref}eA{m}_{e}_{hc}")
            p.emit("act", lambda e_, hc=hc, pb=pb, kpsa=kpsa: e_.activation(
                out=ehT_m[:, hc * MACRO:(hc + 1) * MACRO],
                in_=PSA[:, pb * 1024: pb * 1024 + MACRO], func=actfn,
                bias=b1e[:, e * NHC + hc: e * NHC + hc + 1]),
                reads=[kpsa, kb1f], writes=[f"{pref}ehT_{hc}"],
                tag=f"{pref}eAs{m}_{e}_{hc}")
    ehkeys = [f"{pref}ehT_{hc}" for hc in range(NHC)]
    NHG = NHC // 4
    for dhalf in range(2):
        for hg in range(NHG):
            b = hg % 2
            kw2 = f"{pref}w2_{b}"
            p.emit(q_lw2, lambda e_, e=e, hg=hg, b=b, dhalf=dhalf: e_.dma_start(
                out=w2sb[b][:].rearrange("p (hc f) -> p hc f", hc=4),
                in_=io_w2[e][hg * 512:(hg + 1) * 512,
                             dhalf * 512:(dhalf + 1) * 512]
                .rearrange("(hc p) f -> p hc f", p=128)),
                writes=[kw2], tag=f"{pref}lw2_{m}_{e}_{dhalf}_{hg}")
            for hq in range(4):
                hc = hg * 4 + hq
                for i in range(NTM):
                    PSx = PSB if i < 4 else PSA
                    kps = (pref + ("PSB" if i < 4 else "PSA") + str((i % 4) // 2))
                    off = (i % 4) * 512
                    p.emit("pe", lambda e_, hc=hc, hq=hq, i=i, b=b, PSx=PSx, off=off: e_.matmul(
                        PSx[:, off:off + 512],
                        ehT_m[:, hc * MACRO + i * 128: hc * MACRO + (i + 1) * 128],
                        w2sb[b][:, hq * 512:(hq + 1) * 512], start=(hc == 0), stop=False),
                        reads=[kw2, ehkeys[hc]], writes=[kps],
                        tag=f"{pref}eB{m}_{e}_{dhalf}_{i}")
        for i in range(NTM):
            PSx = PSB if i < 4 else PSA
            kps = (pref + ("PSB" if i < 4 else "PSA") + str((i % 4) // 2))
            off = (i % 4) * 512
            p.emit("pe", lambda e_, e=e, i=i, dhalf=dhalf, PSx=PSx, off=off: e_.matmul(
                PSx[:, off:off + 512], cst["ones_2x128b"][:],
                b2e[0:2, e * D + dhalf * 512: e * D + (dhalf + 1) * 512],
                start=False, stop=True),
                reads=[kb2f, "c_ones_2x128b"], writes=[kps],
                tag=f"{pref}eBb{m}_{e}_{dhalf}_{i}")
            p.emit("dve", lambda e_, e=e, i=i, dhalf=dhalf, PSx=PSx, off=off: e_.scalar_tensor_tensor(
                out=y[:, i * D + dhalf * 512: i * D + (dhalf + 1) * 512],
                in0=PSx[:, off:off + 512],
                scalar=wgate[:, i * g.E + e: i * g.E + e + 1],
                in1=y[:, i * D + dhalf * 512: i * D + (dhalf + 1) * 512],
                op0=ALU.mult, op1=ALU.add),
                reads=[kps, kwg, f"y_{i}"], writes=[f"y_{i}"],
                tag=f"{pref}yacc{m}_{e}_{i}")


def phase_l0back(p, nc, es, g, io, cst):
    TH, D, ND, E, F0, MACRO = g.TH, g.D, g.ND, g.E, g.F0, g.MACRO
    NM, NTM = g.NMAC, MACRO // 128
    S = lambda nm, shp, dt=F32: es.enter_context(nc.sbuf_tensor("m0" + nm, shp, dt))
    PS = lambda nm, shp, dt=F32: es.enter_context(nc.psum_tensor("m0" + nm, shp, dt))
    q_lw = p.dma_queue("m0_lw", "gpsimd")
    q_lw1 = p.dma_queue("m0_lw1", "gpsimd")
    q_lw2 = p.dma_queue("m0_lw2", "gpsimd")
    p.dma_queue("m0_lb", "gpsimd")
    q_lx = p.dma_queue("m0_lx", "sync")
    q_lxT = p.dma_queue("m0_lxT", "sync")
    q_st = p.dma_queue("m0_st", "sync")

    wo = S("wo", [128, ND * D], BF16)
    rw1 = S("rw1", [128, ND * D], BF16)
    rw2 = S("rw2", [128, ND * E], BF16)
    for t, nm in ((wo, "Wo"), (rw1, "rw1")):
        p.emit(q_lw, lambda e, t=t, nm=nm: e.dma_start(
            out=t[:].rearrange("p (dc f) -> p dc f", dc=ND),
            in_=io[nm].rearrange("(dc p) f -> p dc f", p=128)),
            writes=[t.name], tag="ld" + nm)
    p.emit(q_lw, lambda e: e.dma_start(
        out=rw2[:].rearrange("p (dc f) -> p dc f", dc=ND),
        in_=io["rw2"].rearrange("(dc p) f -> p dc f", p=128)),
        writes=["m0rw2"], tag="ldrw2")
    rb1f = S("rb1f", [128, ND]); rb2f = S("rb2f", [1, E])
    NHC0 = F0 // 128
    rb1r = S("rb1r", [ND, 128])
    p.emit(q_lw, lambda e: e.dma_start(
        out=rb1r[:], in_=io["rb1"].rearrange("o (f p) -> (o f) p", p=128)),
        writes=["m0rb1r"], tag="ldrb1")
    p.emit(q_lw, lambda e: e.dma_start(out=rb2f[:], in_=io["rb2"]),
           writes=["m0rb2f"], tag="ldrb2")
    b1e = S("b1e", [128, E * NHC0])
    b2e = S("b2e", [2, E * D], BF16)

    xnT_m = S("xnT_m", [128, ND * MACRO], BF16)
    scr = S("scr", [128, ND * MACRO], BF16)
    oT_m = scr; rT_m = scr
    ehT_m = S("ehT_m", [128, (F0 // 128) * MACRO], BF16)
    y = S("y", [128, NTM * D])
    wg = S("wg", [128, NTM * E])
    lg = S("lg", [128, 24])
    xt = [S(f"xt{i}", [128, D]) for i in range(2)]
    xnt = [S(f"xnt{i}", [128, D]) for i in range(2)]
    w1sb = [S(f"w1s{i}", [128, ND * 512], BF16) for i in range(2)]
    w2sb = [S(f"w2s{i}", [128, 2048], BF16) for i in range(2)]
    PSA = PS("PSA", [128, 2048])
    PSB = PS("PSB", [128, 2048])

    b1r = ehT_m[0:NHC0, 0:2 * E * 128].bitcast(F32)
    kb1r = _ehT_keys("m0", 0, 2 * E * 128, MACRO)
    p.emit(q_lw, lambda e: e.dma_start(
        out=b1r[:].rearrange("hc (e p) -> hc e p", e=E),
        in_=io["eb1"].rearrange("o (e hc p) -> (o hc) e p", hc=NHC0, p=128)),
        writes=["m0b1r"] + kb1r, tag="ldb1r")
    for e in range(E):
        p.emit("pe", lambda e_, e=e: e_.transpose(
            PSA[:, 0:NHC0], b1r[:, e * 128:(e + 1) * 128],
            cst["ident128"][0:NHC0, 0:NHC0]),
            reads=["m0b1r", "c_ident128"] + kb1r,
            writes=["m0PSA0"], tag=f"b1t{e}")
        p.emit("act", lambda e_, e=e: e_.copy(
            b1e[:, e * NHC0:(e + 1) * NHC0], PSA[:, 0:NHC0]),
            reads=["m0PSA0"], writes=["m0b1e"], tag=f"b1tc{e}")
    p.emit("pe", lambda e: e.transpose(
        PSA[:, 0:ND], rb1r[:], cst["ident128"][0:ND, 0:ND]),
        reads=["m0rb1r", "c_ident128"], writes=["m0PSA0"], tag="rb1t")
    p.emit("act", lambda e: e.copy(rb1f[:], PSA[:, 0:ND]),
           reads=["m0PSA0"], writes=["m0rb1f"], tag="rb1tc")
    b2st = ehT_m[0:1, 2 * D:4 * D]
    b2r = ehT_m[0:1, 4 * D:6 * D].bitcast(F32)
    b2l32 = ehT_m[0:1, 6 * D:8 * D].bitcast(F32)
    kst = _ehT_keys("m0", 2 * D, 4 * D, MACRO)
    kr = _ehT_keys("m0", 4 * D, 6 * D, MACRO)
    kl = _ehT_keys("m0", 6 * D, 8 * D, MACRO)
    for e in range(E):
        p.emit(q_lw, lambda e_, e=e: e_.dma_start(
            out=b2r[:], in_=io["eb2"][0:1, e * D:(e + 1) * D]),
            writes=["m0b2r"] + kr, tag=f"lb2r{e}")
        p.emit("dve", lambda e_: e_.tensor_copy(b2st[0:1, 0:D], b2r[:]),
               reads=["m0b2r"], writes=["m0b2hi"] + kst, tag=f"b2h{e}")
        p.emit("dve", lambda e_: e_.tensor_tensor(
            out=b2l32[:], in0=b2r[:], in1=b2st[0:1, 0:D], op=ALU.subtract),
            reads=["m0b2hi", "m0b2r"],
            writes=["m0b2l32"] + kl, tag=f"b2s{e}")
        p.emit("dve", lambda e_: e_.tensor_copy(b2st[0:1, D:2 * D], b2l32[:]),
               reads=["m0b2l32"], writes=["m0b2lo"] + kst, tag=f"b2l{e}")
        p.emit(q_lw, lambda e_, e=e: e_.dma_start(
            out=b2e[:, e * D:(e + 1) * D],
            in_=b2st[0:1, :]),
            reads=["m0b2hi", "m0b2lo"], writes=["m0b2e"], tag=f"b2p{e}")

    for m in range(NM):
        p.emit(q_lxT, lambda e, m=m: e.dma_start(
            out=xnT_m[:].rearrange("p (j t) -> p j t", j=ND),
            in_=io["xnT"][:, :, m * MACRO:(m + 1) * MACRO].rearrange("j p t -> p j t")),
            writes=["kxnTm"], tag=f"lxnT{m}")
        p.emit(q_lxT, lambda e, m=m: e.dma_start(
            out=oT_m[:].rearrange("p (j t) -> p j t", j=ND),
            in_=io["outHT"][:, :, m * MACRO:(m + 1) * MACRO].rearrange("j p t -> p j t")),
            writes=["m0scr"], tag=f"loT{m}")
        for i in range(NTM):
            pb = i % 2
            kps = f"m0PSB{pb}"
            pa = i % 2
            p.emit(q_lx, lambda e, m=m, i=i, pa=pa: e.dma_start(
                out=xt[pa][:], in_=io["x"][m * MACRO + i * 128: m * MACRO + (i + 1) * 128, :]),
                writes=[f"xt{pa}"], tag=f"lx{m}_{i}")
            p.emit(q_lx, lambda e, m=m, i=i, pa=pa: e.dma_start(
                out=xnt[pa][:], in_=io["xn32"][m * MACRO + i * 128: m * MACRO + (i + 1) * 128, :]),
                writes=[f"xnt{pa}"], tag=f"lxn{m}_{i}")
            for j in range(ND):
                for half in range(2):
                    p.emit("pe", lambda e, i=i, j=j, half=half, pb=pb: e.matmul(
                        PSB[:, pb * 1024 + half * 512: pb * 1024 + (half + 1) * 512],
                        oT_m[:, j * MACRO + i * 128: j * MACRO + (i + 1) * 128],
                        wo[:, j * D + half * 512: j * D + (half + 1) * 512],
                        start=(j == 0), stop=(j == ND - 1)),
                        reads=["m0scr", "m0wo"], writes=[kps], tag=f"wo{m}_{i}")
            p.emit("dve", lambda e, i=i, pa=pa, pb=pb: e.tensor_tensor(
                out=y[:, i * D:(i + 1) * D], in0=PSB[:, pb * 1024:(pb + 1) * 1024],
                in1=xt[pa][:], op=ALU.add),
                reads=[kps, f"xt{pa}"], writes=[f"y_{i}"], tag=f"yi{m}_{i}")
            p.emit("dve", lambda e, i=i, pa=pa: e.tensor_tensor(
                out=y[:, i * D:(i + 1) * D], in0=y[:, i * D:(i + 1) * D],
                in1=xnt[pa][:], op=ALU.add),
                reads=[f"y_{i}", f"xnt{pa}"], writes=[f"y_{i}"], tag=f"yn{m}_{i}")
        for fout in range(ND):
            pb = fout % 2
            kps = f"m0PSA{pb}"
            for d in range(ND):
                for tsl in range(MACRO // 512):
                    p.emit("pe", lambda e, fout=fout, d=d, tsl=tsl, pb=pb: e.matmul(
                        PSA[:, pb * 1024 + tsl * 512: pb * 1024 + (tsl + 1) * 512],
                        rw1[:, d * D + fout * 128: d * D + (fout + 1) * 128],
                        xnT_m[:, d * MACRO + tsl * 512: d * MACRO + (tsl + 1) * 512],
                        start=(d == 0), stop=(d == ND - 1)),
                        reads=["m0rw1", "kxnTm"], writes=[kps], tag=f"rt{m}_{fout}")
            p.emit("act", lambda e, fout=fout, pb=pb: e.activation(
                out=rT_m[:, fout * MACRO:(fout + 1) * MACRO],
                in_=PSA[:, pb * 1024: pb * 1024 + MACRO], func=AF.Silu,
                bias=rb1f[:, fout:fout + 1]),
                reads=[kps, "m0rb1f"], writes=["m0scr"], tag=f"rts{m}_{fout}")
        for i in range(NTM):
            for j in range(ND):
                p.emit("pe", lambda e, i=i, j=j: e.matmul(
                    PSB[:, 0:E], rT_m[:, j * MACRO + i * 128: j * MACRO + (i + 1) * 128],
                    rw2[:, j * E:(j + 1) * E], start=(j == 0), stop=False),
                    reads=["m0scr", "m0rw2"], writes=["m0PSB0"], tag=f"lg{m}_{i}")
            p.emit("pe", lambda e, i=i: e.matmul(
                PSB[:, 0:E], cst["ones_1x128"][:], rb2f[:],
                start=False, stop=True),
                reads=["m0rb2f", "c_ones_1x128"], writes=["m0PSB0"], tag=f"lgb{m}_{i}")
            p.emit("dve", lambda e: e.tensor_copy(lg[:, 0:E], PSB[:, 0:E]),
                   reads=["m0PSB0"], writes=["lgv"], tag=f"lgc{m}_{i}")
            p.emit("dve", lambda e: e.reduce_max(lg[:, 8:9], lg[:, 0:E], axis=AX.X),
                   reads=["lgv"], writes=["lgm1"], tag=f"gm1{m}_{i}")
            p.emit("dve", lambda e: e.tensor_scalar(
                out=lg[:, 9:10], in0=lg[:, 8:9], scalar1=-1.0, scalar2=None,
                op0=ALU.mult), reads=["lgm1"], writes=["lgnm1"], tag=f"gn{m}_{i}")
            p.emit("dve", lambda e: e.tensor_scalar(
                out=lg[:, 10:10 + E], in0=lg[:, 0:E], scalar1=lg[:, 8:9],
                scalar2=None, op0=ALU.is_ge), reads=["lgv", "lgm1"],
                writes=["lgmask1"], tag=f"gmk{m}_{i}")
            p.emit("dve", lambda e: e.scalar_tensor_tensor(
                out=lg[:, 10:10 + E], in0=lg[:, 10:10 + E], scalar=-1e30,
                in1=lg[:, 0:E], op0=ALU.mult, op1=ALU.add),
                reads=["lgmask1", "lgv"], writes=["lgl2"], tag=f"gl2{m}_{i}")
            p.emit("dve", lambda e: e.reduce_max(lg[:, 9:10], lg[:, 10:10 + E],
                   axis=AX.X), reads=["lgl2"], writes=["lgm2"], tag=f"gm2{m}_{i}")
            p.emit("dve", lambda e, i=i: e.tensor_scalar(
                out=wg[:, i * E:(i + 1) * E], in0=lg[:, 0:E], scalar1=lg[:, 9:10],
                scalar2=None, op0=ALU.is_ge), reads=["lgv", "lgm2"],
                writes=["wmask2"], tag=f"gmk2{m}_{i}")
            p.emit("dve", lambda e: e.tensor_scalar(
                out=lg[:, 10:10 + E], in0=lg[:, 0:E], scalar1=lg[:, 8:9],
                scalar2=None, op0=ALU.subtract), reads=["lgv", "lgm1"],
                writes=["lgsub"], tag=f"gsb{m}_{i}")
            p.emit("act", lambda e: e.activation(out=lg[:, 10:10 + E],
                   in_=lg[:, 10:10 + E], func=AF.Exp), reads=["lgsub"],
                   writes=["lgexp"], tag=f"gex{m}_{i}")
            p.emit("dve", lambda e, i=i: e.tensor_tensor(
                out=wg[:, i * E:(i + 1) * E], in0=wg[:, i * E:(i + 1) * E],
                in1=lg[:, 10:10 + E], op=ALU.mult),
                reads=["wmask2", "lgexp"], writes=["wgew"], tag=f"gew{m}_{i}")
            p.emit("dve", lambda e, i=i: e.reduce_sum(
                lg[:, 8:9], wg[:, i * E:(i + 1) * E], axis=AX.X),
                reads=["wgew"], writes=["lgsum"], tag=f"gsm{m}_{i}")
            p.emit("dve", lambda e: e.reciprocal(lg[:, 9:10], lg[:, 8:9]),
                   reads=["lgsum"], writes=["lgrec"], tag=f"grc{m}_{i}")
            p.emit("dve", lambda e, i=i: e.tensor_scalar(
                out=wg[:, i * E:(i + 1) * E], in0=wg[:, i * E:(i + 1) * E],
                scalar1=lg[:, 9:10], scalar2=None, op0=ALU.mult),
                reads=["wgew", "lgrec"], writes=["kwg"], tag=f"gw{m}_{i}")
        for e in range(E):
            emit_expert_block(p, nc, g, cst, "m0", m, e, F0, AF.Silu,
                              xnT_m, "kxnTm", w1sb, w2sb, q_lw1, q_lw2,
                              io["ew1"], io["ew2"], b1e, io["eb1"],
                              b2e, io["eb2"], ehT_m, PSA, PSB, y, wg, "kwg")
        for i in range(NTM):
            p.emit(q_st, lambda e, m=m, i=i: e.dma_start(
                out=io["x1"][m * MACRO + i * 128: m * MACRO + (i + 1) * 128, :],
                in_=y[:, i * D:(i + 1) * D]),
                reads=[f"y_{i}"], writes=["x1w"], tag=f"sx1_{m}_{i}")


def phase_mid(p, nc, es, g, io, cst):
    TH, D, ND, NT = g.TH, g.D, g.ND, g.NT
    S = lambda nm, shp, dt=F32: es.enter_context(nc.sbuf_tensor("md" + nm, shp, dt))
    PS = lambda nm, shp, dt=F32: es.enter_context(nc.psum_tensor("md" + nm, shp, dt))
    q_lx = p.dma_queue("md_lx", "sync")
    q_st = p.dma_queue("md_st", "sync")
    q_stg = p.dma_queue("md_stg", "sync")

    x1n = S("x1n", [128, NT * D], BF16)
    xt = [S(f"xt{i}", [128, D]) for i in range(2)]
    xc = [S(f"xc{i}", [128, D]) for i in range(2)]
    stat = [S(f"stat{i}", [128, 8]) for i in range(2)]
    ptr = [PS(f"ptrT{i}", [128, 128], BF16) for i in range(2)]
    PG = PS("PG", [128, 2048])
    x1nT_sb = S("x1nT_sb", [128, ND * NT * 128], BF16)
    gev = S("gev", [128, D])

    for i in range(NT):
        pa = i % 2
        kxt, kxc = f"xt{pa}", f"xc{pa}"
        p.emit(q_lx, lambda e, i=i, pa=pa: e.dma_start(
            out=xt[pa][:], in_=io["x1"][i * 128:(i + 1) * 128, :]),
            writes=[kxt], tag=f"lx1_{i}")
        st = stat[pa]
        k0, km, kv, kv2, kl, kr = (f"st0{pa}", f"stm{pa}", f"stv{pa}",
                                   f"stv2{pa}", f"stl{pa}", f"str{pa}")
        p.emit("dve", lambda e, pa=pa, st=st: e.reduce_sum(
            st[:, 0:1], xt[pa][:], axis=AX.X), reads=[kxt], writes=[k0])
        p.emit("dve", lambda e, st=st: e.tensor_scalar(
            out=st[:, 1:2], in0=st[:, 0:1], scalar1=1.0 / D, scalar2=None,
            op0=ALU.mult), reads=[k0], writes=[km])
        p.emit("dve", lambda e, pa=pa, st=st: e.tensor_scalar(
            out=xc[pa][:], in0=xt[pa][:], scalar1=st[:, 1:2], scalar2=None,
            op0=ALU.subtract), reads=[kxt, km], writes=[kxc])
        p.emit("act", lambda e, pa=pa: e.activation(
            out=xt[pa][:], in_=xc[pa][:], func=AF.Square),
            reads=[kxc], writes=[kxt])
        p.emit("dve", lambda e, pa=pa, st=st: e.reduce_sum(
            st[:, 2:3], xt[pa][:], axis=AX.X), reads=[kxt], writes=[kv])
        p.emit("dve", lambda e, st=st: e.tensor_scalar(
            out=st[:, 5:6], in0=st[:, 2:3], scalar1=1.0 / D, scalar2=1e-5,
            op0=ALU.mult, op1=ALU.add), reads=[kv], writes=[kv2])
        p.emit("act", lambda e, st=st: e.activation(
            out=st[:, 3:4], in_=st[:, 5:6], func=AF.Ln),
            reads=[kv2], writes=[kl])
        p.emit("act", lambda e, st=st: e.activation(
            out=st[:, 4:5], in_=st[:, 3:4], func=AF.Exp, scale=-0.5),
            reads=[kl], writes=[kr])
        p.emit("dve", lambda e, i=i, pa=pa, st=st: e.tensor_scalar(
            out=x1n[:, i * D:(i + 1) * D], in0=xc[pa][:], scalar1=st[:, 4:5],
            scalar2=None, op0=ALU.mult), reads=[kxc, kr],
            writes=[f"x1n_{i}"], tag=f"x1n{i}")
        for j in range(ND):
            pb = j % 2
            p.emit("pe", lambda e, i=i, j=j, pb=pb: e.transpose(
                ptr[pb][:], x1n[:, i * D + j * 128: i * D + (j + 1) * 128],
                cst["identb"][:]), reads=[f"x1n_{i}", "c_identb"],
                writes=[f"ptrT{pb}"], tag=f"tr{i}_{j}")
            p.emit("dve", lambda e, i=i, j=j, pb=pb: e.tensor_copy(
                x1nT_sb[:, (j * NT + i) * 128:(j * NT + i + 1) * 128], ptr[pb][:]),
                reads=[f"ptrT{pb}"], writes=[f"x1nT_{j}_{i}"], tag=f"trc{i}_{j}")
    x1nkeys = [f"x1n_{i}" for i in range(NT)]
    for f1 in range(ND):
        pb = f1 % 2
        for i in range(NT):
            for half in range(2):
                p.emit("pe", lambda e, f1=f1, i=i, half=half, pb=pb: e.matmul(
                    PG[:, pb * 1024 + half * 512: pb * 1024 + (half + 1) * 512],
                    x1n[:, i * D + f1 * 128: i * D + (f1 + 1) * 128],
                    x1n[:, i * D + half * 512: i * D + (half + 1) * 512],
                    start=(i == 0), stop=(i == NT - 1)),
                    reads=x1nkeys, writes=["PG"], tag=f"g{f1}_{i}")
        p.emit("dve", lambda e, f1=f1, pb=pb: e.tensor_copy(
            gev[:], PG[:, pb * 1024:(pb + 1) * 1024]),
            reads=["PG"], writes=["gev"], tag=f"gev{f1}")
        p.emit(q_stg, lambda e, f1=f1: e.dma_start(
            out=io["G_half"][:, f1 * D:(f1 + 1) * D], in_=gev[:]),
            reads=["gev"], writes=["Gw"], tag=f"stG{f1}")
    tkeys = [f"x1nT_{j}_{i}" for j in range(ND) for i in range(NT)]
    for j in range(ND):
        p.emit(q_st, lambda e, j=j: e.dma_start(
            out=io["x1nT"][j, :, :],
            in_=x1nT_sb[:, j * NT * 128:(j + 1) * NT * 128]),
            reads=tkeys, writes=["x1nTw"], tag=f"strT{j}")
    if "x1nb" in io:
        p.emit(q_st, lambda e: e.dma_start(
            out=io["x1nb"].rearrange("(i p) d -> p i d", p=128),
            in_=x1n[:].rearrange("p (i d) -> p i d", i=NT)),
            reads=[f"x1n_{i}" for i in range(NT)], writes=["x1nbw"],
            tag="stx1nb")


def phase_l1back(p, nc, es, g, io, cst):
    TH, D, ND, E, F1, MACRO = g.TH, g.D, g.ND, g.E, g.F1, g.MACRO
    NM, NTM = g.NMAC, MACRO // 128
    S = lambda nm, shp, dt=F32: es.enter_context(nc.sbuf_tensor("m1" + nm, shp, dt))
    PS = lambda nm, shp, dt=F32: es.enter_context(nc.psum_tensor("m1" + nm, shp, dt))
    q_lw = p.dma_queue("m1_lw", "gpsimd")
    q_lw1 = p.dma_queue("m1_lw1", "gpsimd")
    q_lw2 = p.dma_queue("m1_lw2", "gpsimd")
    p.dma_queue("m1_lb", "gpsimd")
    q_lx = p.dma_queue("m1_lx", "sync")
    q_lxT = p.dma_queue("m1_lxT", "sync")
    q_st = p.dma_queue("m1_st", "sync")

    PSA = PS("PSA", [128, 2048])
    PSB = PS("PSB", [128, 2048])
    Msb = S("Msb", [128, ND * D], BF16)
    es_m = ExitStack()
    Sm = lambda nm, shp, dt=F32: es_m.enter_context(nc.sbuf_tensor("m1" + nm, shp, dt))
    gta = [Sm(f"gta{i}", [128, D]) for i in range(2)]
    gtb = [Sm(f"gtb{i}", [128, D]) for i in range(2)]
    gb16 = Sm("gb16", [128, ND * D], BF16)
    for f1 in range(ND):
        pa = f1 % 2
        p.emit(q_lx, lambda e, f1=f1, pa=pa: e.dma_start(
            out=gta[pa][:], in_=io["Ga"][:, f1 * D:(f1 + 1) * D]),
            reads=["Gw"], writes=[f"gta{pa}"], tag=f"lGa{f1}")
        p.emit(q_lxT, lambda e, f1=f1, pa=pa: e.dma_start(
            out=gtb[pa][:], in_=io["Gb"][:, f1 * D:(f1 + 1) * D]),
            reads=["Gbw"], writes=[f"gtb{pa}"], tag=f"lGb{f1}")
        p.emit("dve", lambda e, f1=f1, pa=pa: e.tensor_tensor(
            out=gb16[:, f1 * D:(f1 + 1) * D], in0=gta[pa][:], in1=gtb[pa][:],
            op=ALU.add), reads=[f"gta{pa}", f"gtb{pa}"], writes=["gb16"],
            tag=f"gc{f1}")
    wq1 = Sm("wq1", [128, ND * D], BF16)
    p.emit(q_lw, lambda e: e.dma_start(
        out=wq1[:].rearrange("p (dc f) -> p dc f", dc=ND),
        in_=io["W_q"].rearrange("(dc p) f -> p dc f", p=128)),
        writes=["wq1"], tag="ldWq1")
    for hc in range(ND):
        pb = hc % 2
        for oc in range(ND):
            for half in range(2):
                p.emit("pe", lambda e, hc=hc, oc=oc, half=half, pb=pb: e.matmul(
                    PSA[:, pb * 1024 + half * 512: pb * 1024 + (half + 1) * 512],
                    wq1[:, oc * D + hc * 128: oc * D + (hc + 1) * 128],
                    gb16[:, oc * D + half * 512: oc * D + (half + 1) * 512],
                    start=(oc == 0), stop=(oc == ND - 1)),
                    reads=["wq1", "gb16"], writes=[f"m1PSA{hc % 2}"], tag=f"M{hc}_{oc}")
        p.emit("dve", lambda e, hc=hc, pb=pb: e.tensor_copy(
            Msb[:, hc * D:(hc + 1) * D], PSA[:, pb * 1024:(pb + 1) * 1024]),
            reads=[f"m1PSA{pb}"], writes=["Msb"], tag=f"Mev{hc}")
    p.barrier_all("m1_Mdone")
    es_m.close()
    gwsb = S("gwsb", [E, D], BF16)
    gwT = S("gwT", [128, ND * E], BF16)
    NHC1 = F1 // 128
    b1e = S("b1e", [128, E * NHC1])
    b2e = S("b2e", [2, E * D], BF16)
    xT_m = S("xT_m", [128, ND * MACRO], BF16)
    ehT_m = S("ehT_m", [128, (F1 // 128) * MACRO], BF16)
    y = S("y", [128, NTM * D])
    wg = S("wg", [128, NTM * E])
    lg = S("lg", [128, 24])
    xt = [S(f"xt{i}", [128, D]) for i in range(2)]
    w1sb = [S(f"w1s{i}", [128, ND * 512], BF16) for i in range(2)]
    w2sb = [S(f"w2s{i}", [128, 2048], BF16) for i in range(2)]
    gwp = PSA[:, 0:4].bitcast(BF16)
    p.emit(q_lw, lambda e: e.dma_start(out=gwsb[:], in_=io["gate_w"]),
           writes=["gwsb"], tag="ldgw")
    for j in range(ND):
        p.emit("pe", lambda e, j=j: e.transpose(
            gwp, gwsb[:, j * 128:(j + 1) * 128], cst["identb"][0:E, 0:E]),
            reads=["gwsb", "c_identb"], writes=["m1PSA0"], tag=f"gwt{j}")
        p.emit("dve", lambda e, j=j: e.tensor_copy(
            gwT[:, j * E:(j + 1) * E], gwp), reads=["m1PSA0"],
            writes=["gwT"], tag=f"gwc{j}")
    b1r = ehT_m[0:NHC1, 0:2 * E * 128].bitcast(F32)
    kb1r = _ehT_keys("m1", 0, 2 * E * 128, MACRO)
    p.emit(q_lw, lambda e: e.dma_start(
        out=b1r[:].rearrange("hc (e p) -> hc e p", e=E),
        in_=io["b1"].rearrange("o (e hc p) -> (o hc) e p", hc=NHC1, p=128)),
        writes=["m1b1r"] + kb1r, tag="ldb1r")
    for e in range(E):
        p.emit("pe", lambda e_, e=e: e_.transpose(
            PSA[:, 0:NHC1], b1r[:, e * 128:(e + 1) * 128],
            cst["ident128"][0:NHC1, 0:NHC1]),
            reads=["m1b1r", "c_ident128"] + kb1r,
            writes=["m1PSA0"], tag=f"b1t{e}")
        p.emit("act", lambda e_, e=e: e_.copy(
            b1e[:, e * NHC1:(e + 1) * NHC1], PSA[:, 0:NHC1]),
            reads=["m1PSA0"], writes=["m1b1e"], tag=f"b1tc{e}")
    b2st = ehT_m[0:1, 2 * D:4 * D]
    b2r = ehT_m[0:1, 4 * D:6 * D].bitcast(F32)
    b2l32 = ehT_m[0:1, 6 * D:8 * D].bitcast(F32)
    kst = _ehT_keys("m1", 2 * D, 4 * D, MACRO)
    kr = _ehT_keys("m1", 4 * D, 6 * D, MACRO)
    kl = _ehT_keys("m1", 6 * D, 8 * D, MACRO)
    for e in range(E):
        p.emit(q_lw, lambda e_, e=e: e_.dma_start(
            out=b2r[:], in_=io["b2"][0:1, e * D:(e + 1) * D]),
            writes=["m1b2r"] + kr, tag=f"lb2r{e}")
        p.emit("dve", lambda e_: e_.tensor_copy(b2st[0:1, 0:D], b2r[:]),
               reads=["m1b2r"], writes=["m1b2hi"] + kst, tag=f"b2h{e}")
        p.emit("dve", lambda e_: e_.tensor_tensor(
            out=b2l32[:], in0=b2r[:], in1=b2st[0:1, 0:D], op=ALU.subtract),
            reads=["m1b2hi", "m1b2r"],
            writes=["m1b2l32"] + kl, tag=f"b2s{e}")
        p.emit("dve", lambda e_: e_.tensor_copy(b2st[0:1, D:2 * D], b2l32[:]),
               reads=["m1b2l32"], writes=["m1b2lo"] + kst, tag=f"b2l{e}")
        p.emit(q_lw, lambda e_, e=e: e_.dma_start(
            out=b2e[:, e * D:(e + 1) * D],
            in_=b2st[0:1, :]),
            reads=["m1b2hi", "m1b2lo"], writes=["m1b2e"], tag=f"b2p{e}")
    for m in range(NM):
        p.emit(q_lxT, lambda e, m=m: e.dma_start(
            out=xT_m[:].rearrange("p (j t) -> p j t", j=ND),
            in_=io["x1nT"][:, :, m * MACRO:(m + 1) * MACRO].rearrange("j p t -> p j t")),
            writes=["kxTm"], tag=f"lxT{m}")
        for i in range(NTM):
            pb = i % 2
            kpsb = f"m1PSB{pb}"
            pa = i % 2
            p.emit(q_lx, lambda e, m=m, i=i, pa=pa: e.dma_start(
                out=xt[pa][:], in_=io["x1"][m * MACRO + i * 128: m * MACRO + (i + 1) * 128, :]),
                writes=[f"xt{pa}"], tag=f"lx1b{m}_{i}")
            for j in range(ND):
                for half in range(2):
                    p.emit("pe", lambda e, i=i, j=j, half=half, pb=pb: e.matmul(
                        PSB[:, pb * 1024 + half * 512: pb * 1024 + (half + 1) * 512],
                        xT_m[:, j * MACRO + i * 128: j * MACRO + (i + 1) * 128],
                        Msb[:, j * D + half * 512: j * D + (half + 1) * 512],
                        start=(j == 0), stop=(j == ND - 1)),
                        reads=["kxTm", "Msb"], writes=[kpsb], tag=f"at{m}_{i}")
            p.emit("dve", lambda e, i=i, pa=pa, pb=pb: e.tensor_tensor(
                out=y[:, i * D:(i + 1) * D], in0=PSB[:, pb * 1024:(pb + 1) * 1024],
                in1=xt[pa][:], op=ALU.add),
                reads=[kpsb, f"xt{pa}"], writes=[f"y_{i}"], tag=f"yi1{m}_{i}")
        for i in range(NTM):
            for j in range(ND):
                p.emit("pe", lambda e, i=i, j=j: e.matmul(
                    PSA[:, 0:E], xT_m[:, j * MACRO + i * 128: j * MACRO + (i + 1) * 128],
                    gwT[:, j * E:(j + 1) * E], start=(j == 0), stop=(j == ND - 1)),
                    reads=["kxTm", "gwT"], writes=["m1PSA0"], tag=f"gl{m}_{i}")
            p.emit("dve", lambda e: e.tensor_copy(lg[:, 0:E], PSA[:, 0:E]),
                   reads=["m1PSA0"], writes=["lgv"], tag=f"glc{m}_{i}")
            p.emit("dve", lambda e: e.reduce_max(lg[:, 8:9], lg[:, 0:E], axis=AX.X),
                   reads=["lgv"], writes=["lgm1"], tag=f"g1m{m}_{i}")
            p.emit("dve", lambda e, i=i: e.tensor_scalar(
                out=wg[:, i * E:(i + 1) * E], in0=lg[:, 0:E], scalar1=lg[:, 8:9],
                scalar2=None, op0=ALU.is_ge), reads=["lgv", "lgm1"],
                writes=["wmask"], tag=f"g1k{m}_{i}")
            p.emit("dve", lambda e: e.tensor_scalar(
                out=lg[:, 10:10 + E], in0=lg[:, 0:E], scalar1=lg[:, 8:9],
                scalar2=None, op0=ALU.subtract), reads=["lgv", "lgm1"],
                writes=["lgsub"], tag=f"g1s{m}_{i}")
            p.emit("act", lambda e: e.activation(out=lg[:, 10:10 + E],
                   in_=lg[:, 10:10 + E], func=AF.Exp), reads=["lgsub"],
                   writes=["lgexp"], tag=f"g1e{m}_{i}")
            p.emit("dve", lambda e: e.reduce_sum(lg[:, 8:9], lg[:, 10:10 + E],
                   axis=AX.X), reads=["lgexp"], writes=["lgsum"], tag=f"g1t{m}_{i}")
            p.emit("dve", lambda e: e.reciprocal(lg[:, 9:10], lg[:, 8:9]),
                   reads=["lgsum"], writes=["lgrec"], tag=f"g1r{m}_{i}")
            p.emit("dve", lambda e, i=i: e.tensor_scalar(
                out=wg[:, i * E:(i + 1) * E], in0=wg[:, i * E:(i + 1) * E],
                scalar1=lg[:, 9:10], scalar2=None, op0=ALU.mult),
                reads=["wmask", "lgrec"], writes=["kwg"], tag=f"g1w{m}_{i}")
        for e in range(E):
            emit_expert_block(p, nc, g, cst, "m1", m, e, F1, AF.Relu,
                              xT_m, "kxTm", w1sb, w2sb, q_lw1, q_lw2,
                              io["w1"], io["w2"], b1e, io["b1"],
                              b2e, io["b2"], ehT_m, PSA, PSB, y, wg, "kwg")
        for i in range(NTM):
            p.emit(q_st, lambda e, m=m, i=i: e.dma_start(
                out=io["out"][m * MACRO + i * 128: m * MACRO + (i + 1) * 128, :],
                in_=y[:, i * D:(i + 1) * D]),
                reads=[f"y_{i}"], writes=["outw"], tag=f"sout{m}_{i}")



GEO = Geo()
CONST_WANT = ["identb", "ident128", "U128", "ones_1x128", "ones_128x1",
              "ones_1x512", "maskU_rep", "negIexp", "eps_1x512", "delta64",
              "deltaP", "ones_16x64", "ones_128x1b", "ones_2x128b"]
CONST_BASE = ["identb", "ident128", "U128", "ones_1x128", "ones_128x1",
              "ones_1x512", "eps_1x512", "ones_128x1b", "ones_2x128b",
              "U128strict", "eCbase0", "eCbase1", "caprow0", "caprow1"]
CONST_SCAN = ["maskU_rep", "negIexp2b", "delta64b", "deltaP", "ones_16x64",
              "eps_1x512b"]


def _declare(nc, specs):
    io = {}
    for nm, shp, dt, kind in specs:
        io[nm] = nc.dram_tensor(nm, shp, dt, kind=kind).ap()
    return io


def _const_specs(consts):
    return [("c_" + nm, list(a.shape), F32, "ExternalInput")
            for nm, a in consts.items()]


def _build_launch(phase_fns, specs, consts, want=CONST_WANT):
    nc = bass.Bass("TRN2", target_bir_lowering=False)
    io = _declare(nc, specs + _const_specs(consts))
    es = ExitStack()
    with es:
        p = Prog(nc)
        cst = load_consts(p, nc, es, GEO, io, want, consts)
        for fn in phase_fns:
            with ExitStack() as es_p:
                fn(p, nc, es_p, GEO, io, cst)
                p.barrier_all("phase_end")
        block = es.enter_context(nc.Block())
        p.build(block, es=es)
    return nc


def kernel_unfused(L0_attn_x, L0_attn_Wq, L0_attn_Wk, L0_attn_Wv, L0_attn_Wo,
           L0_attn_Wa, L0_attn_ba, L0_ffn_rw1, L0_ffn_rb1, L0_ffn_rw2,
           L0_ffn_rb2, L0_ffn_ew1, L0_ffn_eb1, L0_ffn_ew2, L0_ffn_eb2,
           L1_attn_W_q, L1_ffn_gate_w, L1_ffn_w1, L1_ffn_b1, L1_ffn_w2,
           L1_ffn_b2):
    g = GEO
    TH, D, ND, E, F0, F1, H, NCH = (g.TH, g.D, g.ND, g.E, g.F0, g.F1, g.H,
                                    g.NCH)
    f32 = lambda a: np.ascontiguousarray(np.asarray(a, dtype=np.float32))
    x_full = f32(L0_attn_x)
    consts = build_consts(g)
    shards = []
    for c in range(8):
        b, half = c // 2, c % 2
        shards.append(f32(x_full[b, half * TH:(half + 1) * TH, :]))
    flat = lambda a: f32(a).reshape(1, -1)
    Wq, Wk, Wv, Wo = map(f32, (L0_attn_Wq, L0_attn_Wk, L0_attn_Wv, L0_attn_Wo))
    Wa = f32(L0_attn_Wa)
    ba = flat(L0_attn_ba)
    rw1, rw2 = f32(L0_ffn_rw1), f32(L0_ffn_rw2)
    rb1, rb2 = flat(L0_ffn_rb1), flat(L0_ffn_rb2)
    ew1, ew2 = f32(L0_ffn_ew1), f32(L0_ffn_ew2)
    eb1, eb2 = flat(L0_ffn_eb1), flat(L0_ffn_eb2)
    Wq1 = f32(L1_attn_W_q)
    gate_w = f32(L1_ffn_gate_w)
    w1, w2 = f32(L1_ffn_w1), f32(L1_ffn_w2)
    b1, b2 = flat(L1_ffn_b1), flat(L1_ffn_b2)
    cmaps = {"c_" + nm: a for nm, a in consts.items()}

    EI, EO = "ExternalInput", "ExternalOutput"
    specsA = [("x", [TH, D], F32, EI), ("Wq", [D, D], F32, EI),
              ("Wk", [D, D], F32, EI), ("Wv", [D, D], F32, EI),
              ("Wa", [D, H], F32, EI), ("ba", [1, H], F32, EI),
              ("xn32", [TH, D], F32, EO), ("xnT", [ND, 128, TH], BF16, EO),
              ("QT", [ND, 128, TH], BF16, EO), ("KT", [ND, 128, TH], BF16, EO),
              ("Ktm", [TH, D], BF16, EO), ("V", [TH, D], BF16, EO),
              ("cum_nat", [H, TH], F32, EO),
              ("cumtm", [128, H * NCH], F32, EO),
              ("totals", [1, H * NCH], F32, EO)]
    ncA = _build_launch([phase_front], specsA, consts)
    imA = [dict(x=shards[c], Wq=Wq, Wk=Wk, Wv=Wv, Wa=Wa, ba=ba, **cmaps)
           for c in range(8)]
    resA = run_bass_kernel_spmd(ncA, imA, core_ids=list(range(8))).results

    specsB = [("QT", [ND, 128, TH], BF16, EI), ("KT", [ND, 128, TH], BF16, EI),
              ("Ktm", [TH, D], BF16, EI), ("V", [TH, D], BF16, EI),
              ("cum_nat", [H, TH], F32, EI),
              ("cumtm", [128, H * NCH], F32, EI),
              ("totals", [1, H * NCH], F32, EI),
              ("S0", [128, 520], F32, EI),
              ("outHT", [ND, 128, TH], BF16, EO),
              ("Sfin", [128, 520], F32, EO)]
    ncB = _build_launch([phase_scan], specsB, consts)
    z520 = np.zeros((128, 520), np.float32)
    def imB(c, s0):
        r = resA[c]
        return dict(QT=r["QT"], KT=r["KT"], Ktm=r["Ktm"], V=r["V"],
                    cum_nat=r["cum_nat"], cumtm=r["cumtm"],
                    totals=r["totals"], S0=s0, **cmaps)
    resB1 = run_bass_kernel_spmd(
        ncB, [imB(c, z520) for c in (0, 2, 4, 6)], core_ids=[0, 2, 4, 6]).results
    resB2 = run_bass_kernel_spmd(
        ncB, [imB(c, resB1[i]["Sfin"]) for i, c in enumerate((1, 3, 5, 7))],
        core_ids=[1, 3, 5, 7]).results
    outHT = [None] * 8
    Sfin = [None] * 8
    for i, c in enumerate((0, 2, 4, 6)):
        outHT[c] = resB1[i]["outHT"]
    for i, c in enumerate((1, 3, 5, 7)):
        outHT[c] = resB2[i]["outHT"]

    specsC = [("x", [TH, D], F32, EI), ("xn32", [TH, D], F32, EI),
              ("xnT", [ND, 128, TH], BF16, EI),
              ("outHT", [ND, 128, TH], BF16, EI),
              ("Wo", [D, D], F32, EI), ("rw1", [D, D], F32, EI),
              ("rw2", [D, E], F32, EI), ("rb1", [1, D], F32, EI),
              ("rb2", [1, E], F32, EI), ("ew1", [E, D, F0], F32, EI),
              ("eb1", [1, E * F0], F32, EI), ("ew2", [E, F0, D], F32, EI),
              ("eb2", [1, E * D], F32, EI), ("x1", [TH, D], F32, EO)]
    ncC = _build_launch([phase_l0back], specsC, consts)
    imC = [dict(x=shards[c], xn32=resA[c]["xn32"], xnT=resA[c]["xnT"],
                outHT=outHT[c], Wo=Wo, rw1=rw1, rw2=rw2, rb1=rb1, rb2=rb2,
                ew1=ew1, eb1=eb1, ew2=ew2, eb2=eb2, **cmaps)
           for c in range(8)]
    resC = run_bass_kernel_spmd(ncC, imC, core_ids=list(range(8))).results

    specsD = [("x1", [TH, D], F32, EI),
              ("x1nT", [ND, 128, TH], BF16, EO),
              ("G_half", [128, ND * D], F32, EO)]
    ncD = _build_launch([phase_mid], specsD, consts)
    imD = [dict(x1=resC[c]["x1"], **cmaps) for c in range(8)]
    resD = run_bass_kernel_spmd(ncD, imD, core_ids=list(range(8))).results

    specsE = [("x1", [TH, D], F32, EI), ("x1nT", [ND, 128, TH], BF16, EI),
              ("Ga", [128, ND * D], F32, EI), ("Gb", [128, ND * D], F32, EI),
              ("W_q", [D, D], F32, EI), ("gate_w", [E, D], F32, EI),
              ("w1", [E, D, F1], F32, EI), ("b1", [1, E * F1], F32, EI),
              ("w2", [E, F1, D], F32, EI), ("b2", [1, E * D], F32, EI),
              ("out", [TH, D], F32, EO)]
    ncE = _build_launch([phase_l1back], specsE, consts)
    imE = [dict(x1=resC[c]["x1"], x1nT=resD[c]["x1nT"],
                Ga=resD[c]["G_half"], Gb=resD[c ^ 1]["G_half"],
                W_q=Wq1, gate_w=gate_w, w1=w1, b1=b1, w2=w2, b2=b2, **cmaps)
           for c in range(8)]
    resE = run_bass_kernel_spmd(ncE, imE, core_ids=list(range(8))).results

    out = np.empty((4, 8192, 1024), np.float32)
    for c in range(8):
        b, half = c // 2, c % 2
        out[b, half * TH:(half + 1) * TH, :] = resE[c]["out"]
    return out


def make_exchange(p, nc, es, g, io):
    D, ND = g.D, g.ND
    SH_S = nc.dram_tensor("SH_S", [128, 520], F32, kind="Internal",
                          addr_space="Shared").ap()
    SH_G = nc.dram_tensor("SH_G", [2, 128, ND * D], F32, kind="Internal",
                          addr_space="Shared").ap()
    SH_F = nc.dram_tensor("SH_F", [4, 16], I32, kind="Internal",
                          addr_space="Shared").ap()
    tok = es.enter_context(nc.sbuf_tensor("ex_tok", [1, 16], I32))
    fb = es.enter_context(nc.sbuf_tensor("ex_fb", [1, 16], I32))
    sbounce = es.enter_context(nc.sbuf_tensor("ex_sb", [128, 520], F32))
    gbounce = es.enter_context(nc.sbuf_tensor("ex_gb", [128, 512], F32))
    exd1 = es.enter_context(nc.semaphore("exd1"))
    exp1 = es.enter_context(nc.semaphore("exp1"))
    exd2 = es.enter_context(nc.semaphore("exd2"))
    exd3 = es.enter_context(nc.semaphore("exd3"))
    exp3 = es.enter_context(nc.semaphore("exp3"))

    def poll(e, flag_row, psem, sfx=""):
        with e.register("nd" + sfx) as nd, e.register("f" + sfx) as f, e.register("tk" + sfx) as tk, e.register("cnt" + sfx) as cnt:
            e.reg_load(tk, tok[0:1, 0:1])
            e.reg_mov(nd, 1)
            e.reg_mov(cnt, 0)
            with e.While(nd):
                e.reg_alu(cnt, cnt, 16, RALU.add)
                e.dma_start(out=fb[:], in_=SH_F[flag_row:flag_row + 1, :]).then_inc(psem, 16)
                e.wait_ge(psem, cnt)
                e.reg_load(f, fb[0:1, 0:1])
                e.reg_alu(nd, f, tk, RALU.subtract)

    def m1_fn(e):
        e.dma_start(out=tok[:], in_=io["token"]).then_inc(exd1, 16)
        e.wait_ge(exd1, 16)
        par = e.partition_id() % 2
        with e.If(par == 0):
            e.memset(sbounce[:], 0.0)
            e.dma_start(out=io["S0"], in_=sbounce[:]).then_inc(exd1, 16)
            e.wait_ge(exd1, 32)
        with e.Else():
            poll(e, 0, exp1, "m1")
            e.dma_start(out=sbounce[:], in_=SH_S).then_inc(exd1, 16)
            e.wait_ge(exd1, 32)
            e.dma_start(out=io["S0"], in_=sbounce[:]).then_inc(exd1, 16)
            e.wait_ge(exd1, 48)
        return e.engine_nop()

    def m2_fn(e):
        e.dma_start(out=tok[:], in_=io["token"]).then_inc(exd2, 16)
        e.wait_ge(exd2, 16)
        par = e.partition_id() % 2
        with e.If(par == 0):
            e.dma_start(out=sbounce[:], in_=io["Sfin"]).then_inc(exd2, 16)
            e.wait_ge(exd2, 32)
            e.dma_start(out=SH_S, in_=sbounce[:]).then_inc(exd2, 16)
            e.wait_ge(exd2, 48)
            e.dma_start(out=SH_F[0:1, :], in_=tok[:]).then_inc(exd2, 16)
            e.wait_ge(exd2, 64)
        return e.engine_nop()

    def gx_fn(e):
        par = e.partition_id() % 2
        NCK = (ND * D) // 512
        for slot in range(2):
            with (e.If(par == 0) if slot == 0 else e.Else()):
                v = 0
                for k in range(NCK):
                    e.dma_start(out=gbounce[:],
                                in_=io["G_half"][:, k * 512:(k + 1) * 512]).then_inc(exd3, 16)
                    v += 16
                    e.wait_ge(exd3, v)
                    e.dma_start(out=SH_G[slot, :, k * 512:(k + 1) * 512],
                                in_=gbounce[:]).then_inc(exd3, 16)
                    v += 16
                    e.wait_ge(exd3, v)
                e.dma_start(out=SH_F[1 + slot:2 + slot, :], in_=tok[:]).then_inc(exd3, 16)
                v += 16
                e.wait_ge(exd3, v)
                poll(e, 2 - slot, exp3, f"g{slot}")
                for k in range(NCK):
                    e.dma_start(out=gbounce[:],
                                in_=SH_G[1 - slot, :, k * 512:(k + 1) * 512]).then_inc(exd3, 16)
                    v += 16
                    e.wait_ge(exd3, v)
                    e.dma_start(out=io["Gb"][:, k * 512:(k + 1) * 512],
                                in_=gbounce[:]).then_inc(exd3, 16)
                    v += 16
                    e.wait_ge(exd3, v)
        return e.engine_nop()

    def m1(p_):
        p_.emit("pool", m1_fn, reads=["ex"], writes=["S0w", "ex"], tag="m1x")

    def m2(p_):
        p_.emit("pool", m2_fn, reads=["Sfinw", "ex"], writes=["ex"], tag="m2x")

    def gx(p_):
        p_.emit("pool", gx_fn, reads=["Gw", "ex"], writes=["Gbw", "ex"], tag="gxx")

    return m1, m2, gx


def kernel_fused(inp):
    g = GEO
    TH, D, ND, E, F0, F1, H, NCH = (g.TH, g.D, g.ND, g.E, g.F0, g.F1, g.H,
                                    g.NCH)
    f32 = lambda a: np.ascontiguousarray(np.asarray(a, dtype=np.float32))
    flat = lambda a: f32(a).reshape(1, -1)
    consts = build_consts(g)
    x_full = f32(inp["L0_attn_x"])
    shards = [f32(x_full[c // 2, (c % 2) * TH:(c % 2 + 1) * TH, :])
              for c in range(8)]
    token = np.full((1, 16), secrets.randbelow(2**31 - 2) + 1, np.int32)

    EI, EO, IN = "ExternalInput", "ExternalOutput", "Internal"
    specs = [
        ("x", [TH, D], F32, EI), ("Wq", [D, D], F32, EI),
        ("Wk", [D, D], F32, EI), ("Wv", [D, D], F32, EI),
        ("Wa", [D, H], F32, EI), ("ba", [1, H], F32, EI),
        ("Wo", [D, D], F32, EI), ("rw1", [D, D], F32, EI),
        ("rw2", [D, E], F32, EI), ("rb1", [1, D], F32, EI),
        ("rb2", [1, E], F32, EI), ("ew1", [E, D, F0], F32, EI),
        ("eb1", [1, E * F0], F32, EI), ("ew2", [E, F0, D], F32, EI),
        ("eb2", [1, E * D], F32, EI), ("W_q", [D, D], F32, EI),
        ("gate_w", [E, D], F32, EI), ("w1", [E, D, F1], F32, EI),
        ("b1", [1, E * F1], F32, EI), ("w2", [E, F1, D], F32, EI),
        ("b2", [1, E * D], F32, EI), ("token", [1, 16], I32, EI),
        ("xn32", [TH, D], F32, IN), ("xnT", [ND, 128, TH], BF16, IN),
        ("QT", [ND, 128, TH], BF16, IN), ("KT", [ND, 128, TH], BF16, IN),
        ("Ktm", [TH, D], BF16, IN), ("V", [TH, D], BF16, IN),
        ("cum_nat", [H, TH], F32, IN), ("cumtm", [128, H * NCH], F32, IN),
        ("totals", [1, H * NCH], F32, IN), ("S0", [128, 520], F32, IN),
        ("cumg_nat", [H, TH], F32, IN),
        ("cumhl", [2, H, TH], BF16, IN), ("cumghl", [2, H, TH], BF16, IN),
        ("num_hbm", [NCH, 128, D], F32, IN), ("den_hbm", [NCH, H * 128], F32, IN),
        ("outHT", [ND, 128, TH], BF16, IN), ("Sfin", [128, 520], F32, IN),
        ("x1", [TH, D], F32, IN), ("x1nT", [ND, 128, TH], BF16, IN),
        ("G_half", [128, ND * D], F32, IN), ("Gb", [128, ND * D], F32, IN),
        ("x1nb", [TH, D], BF16, IN),
        ("pX1", [E * g.C1, D], BF16, IN), ("pW1", [E * g.C1, 16], F32, IN),
        ("A1", [TH, D], BF16, IN), ("tot1", [TH // 128, E], F32, IN),
        ("offs1", [TH // 128, E], F32, IN),
        ("out", [TH, D], F32, EO),
    ]
    nc = bass.Bass("TRN2", target_bir_lowering=False)
    io = _declare(nc, specs + _const_specs(consts))
    io["Ga"] = io["G_half"]
    es = ExitStack()
    with es:
        p = Prog(nc)
        cst = load_consts(p, nc, es, g, io, CONST_BASE, consts)
        m1, m2, gx = make_exchange(p, nc, es, g, io)
        for fn, marker in ((phase_front, None), (phase_scan, None),
                           (None, m2), (None, m1), (phase_scan2, None),
                           (phase_l0back, None), (phase_mid, None),
                           (None, gx), (phase_l1back, None)):
            if marker is not None:
                marker(p)
                continue
            with ExitStack() as es_p:
                if fn is phase_scan:
                    cst2 = dict(cst)
                    cst2.update(load_consts(p, nc, es_p, g, io, CONST_SCAN,
                                            consts, qname="q_const_scan"))
                    fn(p, nc, es_p, g, io, cst2)
                else:
                    fn(p, nc, es_p, g, io, cst)
                p.barrier_all("phase_end")
        p.barrier_all("final")
        block = es.enter_context(nc.Block())
        p.build(block, es=es)

    im = []
    for c in range(8):
        im.append(dict(
            x=shards[c], Wq=f32(inp["L0_attn_Wq"]), Wk=f32(inp["L0_attn_Wk"]),
            Wv=f32(inp["L0_attn_Wv"]), Wa=f32(inp["L0_attn_Wa"]),
            ba=flat(inp["L0_attn_ba"]), Wo=f32(inp["L0_attn_Wo"]),
            rw1=f32(inp["L0_ffn_rw1"]), rw2=f32(inp["L0_ffn_rw2"]),
            rb1=flat(inp["L0_ffn_rb1"]), rb2=flat(inp["L0_ffn_rb2"]),
            ew1=f32(inp["L0_ffn_ew1"]), eb1=flat(inp["L0_ffn_eb1"]),
            ew2=f32(inp["L0_ffn_ew2"]), eb2=flat(inp["L0_ffn_eb2"]),
            W_q=f32(inp["L1_attn_W_q"]), gate_w=f32(inp["L1_ffn_gate_w"]),
            w1=f32(inp["L1_ffn_w1"]), b1=flat(inp["L1_ffn_b1"]),
            w2=f32(inp["L1_ffn_w2"]), b2=flat(inp["L1_ffn_b2"]),
            token=token,
            **{"c_" + nm: a for nm, a in consts.items()}))
    res = run_bass_kernel_spmd(nc, im, core_ids=list(range(8))).results
    out = np.empty((4, 8192, 1024), np.float32)
    for c in range(8):
        out[c // 2, (c % 2) * TH:(c % 2 + 1) * TH, :] = res[c]["out"]
    return out


def kernel(**inputs):
    return kernel_fused(inputs)



def emit_pool_init(p, nc, g, pref, io_pW, bigtile, EC, io_pX, zt):
    q = pref + "_pwi"
    NB = (EC * 16 + 128 * 256 - 1) // (128 * 256)
    rows_per = 128 * 256 // 16
    for b in range(NB):
        r0 = b * rows_per
        r1 = min(EC, r0 + rows_per)
        p.emit(q, lambda e, r0=r0, r1=r1: e.dma_start(
            out=io_pW[r0:r1, :],
            in_=bigtile[:].rearrange("p (q w) -> p q w", w=16)[:, 0:(r1 - r0) // 128, :]),
            reads=["bigtile"], writes=["pWi"], tag=f"{pref}pwi{b}")
    for b in range(EC // 128):
        p.emit(q, lambda e, b=b: e.dma_start(
            out=io_pX[b * 128:(b + 1) * 128, :], in_=zt[:]),
            reads=["ztz"], writes=["pXi"], tag=f"{pref}pxi{b}")


def emit_slot_scatter(p, nc, g, cst, pref, i, rank, masks, kmask, wsel, kwsel,
                      offs_flat, koffs, xnb_t, kxnb, iota_sb, PSA, slot_s,
                      wid, ux, io_pX, io_pW, C, EC, layer, id_base=0):
    E, NT = g.E, g.NT
    kps = pref + "PSA0"
    ecb = "eCbase0" if layer == 0 else "eCbase1"
    cap = "caprow0" if layer == 0 else "caprow1"
    p.emit("pe", lambda e, i=i: e.matmul(
        PSA[:, 0:E], cst["U128"][:], masks[:, i * E:(i + 1) * E],
        start=True, stop=False), reads=[kmask, "c_U128"], writes=[kps],
        tag=f"{pref}sp{i}_{rank}")
    p.emit("pe", lambda e, i=i: e.matmul(
        PSA[:, 0:E], cst["ones_1x128"][:], offs_flat[0:1, i * E:(i + 1) * E],
        start=False, stop=False), reads=[koffs, "c_ones_1x128"], writes=[kps],
        tag=f"{pref}so{i}_{rank}")
    p.emit("pe", lambda e: e.matmul(
        PSA[:, 0:E], cst["ones_1x128"][:], cst[ecb][:],
        start=False, stop=True), reads=["c_" + ecb, "c_ones_1x128"],
        writes=[kps], tag=f"{pref}se{i}_{rank}")
    p.emit("dve", lambda e: e.tensor_copy(slot_s[:, 0:E], PSA[:, 0:E]),
           reads=[kps], writes=["slotf"], tag=f"{pref}sc{i}_{rank}")
    p.emit("dve", lambda e: e.tensor_tensor(
        out=slot_s[:, E:2 * E], in0=slot_s[:, 0:E], in1=cst[cap][:],
        op=ALU.is_le),
        reads=["slotf", "c_" + cap], writes=["slotle"], tag=f"{pref}sl{i}_{rank}")
    p.emit("dve", lambda e, i=i: e.tensor_tensor(
        out=slot_s[:, E:2 * E], in0=slot_s[:, E:2 * E],
        in1=masks[:, i * E:(i + 1) * E], op=ALU.mult),
        reads=["slotle", kmask], writes=["slotok"], tag=f"{pref}sm{i}_{rank}")
    p.emit("dve", lambda e: e.tensor_scalar(
        out=slot_s[:, 2 * E:3 * E], in0=slot_s[:, E:2 * E], scalar1=-1e9,
        scalar2=1e9, op0=ALU.mult, op1=ALU.add),
        reads=["slotok"], writes=["slotbig"], tag=f"{pref}sb{i}_{rank}")
    p.emit("dve", lambda e: e.tensor_tensor(
        out=slot_s[:, 0:E], in0=slot_s[:, 0:E], in1=slot_s[:, E:2 * E],
        op=ALU.mult), reads=["slotf", "slotok"], writes=["slotf"],
        tag=f"{pref}sf{i}_{rank}")
    p.emit("dve", lambda e: e.tensor_tensor(
        out=slot_s[:, 0:E], in0=slot_s[:, 0:E], in1=slot_s[:, 2 * E:3 * E],
        op=ALU.add), reads=["slotf", "slotbig"], writes=["slotf"],
        tag=f"{pref}sf2{i}_{rank}")
    p.emit("dve", lambda e: e.tensor_reduce(
        slot_s[:, 3 * E:3 * E + 1], slot_s[:, 0:E], axis=AX.X, op=ALU.min),
        reads=["slotf"], writes=["slotmin"], tag=f"{pref}sr{i}_{rank}")
    p.emit("dve", lambda e: e.tensor_copy(ux[:, 0:1], slot_s[:, 3 * E:3 * E + 1]),
           reads=["slotmin"], writes=["uxw"], tag=f"{pref}su{i}_{rank}")
    p.emit("dve", lambda e, i=i: e.tensor_copy(
        wid[:, 0:1], wsel[:, i:i + 1]), reads=[kwsel], writes=["widw"],
        tag=f"{pref}sw{i}_{rank}")
    p.emit("dve", lambda e, i=i, id_base=id_base: e.tensor_scalar(
        out=wid[:, 1:2], in0=iota_sb[:, i:i + 1], scalar1=float(id_base),
        scalar2=None, op0=ALU.add), reads=["c_iota"], writes=["widw"],
        tag=f"{pref}si{i}_{rank}")
    qx = pref + "_scx"
    qw = pref + "_scw"
    p.emit(qx, lambda e: e.indirect_dma_start(
        out=io_pX, out_offset=bass.IndirectOffsetOnAxis(ap=ux[:, 0:1], axis=0),
        in_=xnb_t[:], in_offset=None, bounds_check=EC - 1, oob_is_err=False),
        reads=["uxw", kxnb, "pXi"], writes=["pX"], tag=f"{pref}scx{i}_{rank}")
    p.emit(qw, lambda e: e.indirect_dma_start(
        out=io_pW, out_offset=bass.IndirectOffsetOnAxis(ap=ux[:, 0:1], axis=0),
        in_=wid[:, 0:16], in_offset=None, bounds_check=EC - 1, oob_is_err=False),
        reads=["uxw", "widw", "pWi"], writes=["pW"], tag=f"{pref}scw{i}_{rank}")


def emit_sparse_expert(p, nc, g, cst, pref, e, NF, C, actfn, io_pX, io_pW,
                       io_w1, io_w2, io_b1, io_b2, io_AB,
                       xg, xgT, pw_sb, ehT_m, ebuf, b1e, b2e, ux2,
                       w1sb, w2sb, PSA, PSB, ptr, D=1024):
    ND, E = g.ND, g.E
    NTS = C // 128
    NHC = NF // 128
    q_lp = pref + "_lp"
    q_lw1 = pref + "_lw1"
    q_lw2 = pref + "_lw2"
    q_lb = pref + "_lb"
    q_sb = pref + "_sb"
    kxg, kxgT, kpw = pref + "xg", pref + "xgT", pref + "pw"
    kb1f, kb2f = pref + "b1e", pref + "b2e"
    p.emit(q_lp, lambda e_, e=e: e_.dma_start(
        out=xg[:].rearrange("p (q d) -> p q d", q=NTS),
        in_=io_pX[e * C:(e + 1) * C, :].rearrange("(q p) d -> p q d", p=128)),
        reads=["pX"], writes=[kxg], tag=f"{pref}lpx{e}")
    p.emit(q_lp, lambda e_, e=e: e_.dma_start(
        out=pw_sb[:].rearrange("p (q w) -> p q w", q=NTS),
        in_=io_pW[e * C:(e + 1) * C, :].rearrange("(q p) w -> p q w", p=128)),
        reads=["pW"], writes=[kpw], tag=f"{pref}lpw{e}")
    p.emit(q_lb, lambda e_, e=e: e_.dma_start(
        out=b1e[:], in_=io_b1[0:1, e * NF:(e + 1) * NF]),
        writes=[kb1f], tag=f"{pref}lb1s{e}")
    p.emit(q_lb, lambda e_, e=e: e_.dma_start(
        out=b2e[:], in_=io_b2[0:1, e * D:(e + 1) * D]),
        writes=[kb2f], tag=f"{pref}lb2s{e}")
    kptr = pref + "PSB0"
    for q in range(NTS):
        for j in range(ND):
            p.emit("pe", lambda e_, q=q, j=j: e_.transpose(
                ptr, xg[:, q * D + j * 128:(q * D) + (j + 1) * 128],
                cst["identb"][:]), reads=[kxg, "c_identb"], writes=[kptr],
                tag=f"{pref}tp{e}_{q}_{j}")
            p.emit("dve", lambda e_, q=q, j=j: e_.tensor_copy(
                xgT[:, j * C + q * 128: j * C + (q + 1) * 128], ptr),
                reads=[kptr], writes=[kxgT], tag=f"{pref}tpc{e}_{q}_{j}")
    tslices = []
    off = 0
    while off < C:
        w = min(512, C - off)
        tslices.append((off, w))
        off += w
    NSL = NF // 512
    for sl in range(NSL):
        b = sl % 2
        kw1 = f"{pref}w1_{b}"
        p.emit(q_lw1, lambda e_, e=e, sl=sl, b=b: e_.dma_start(
            out=w1sb[b][:].rearrange("p (dc h) -> p dc h", dc=ND),
            in_=io_w1[e].rearrange("(dc p) h -> p dc h", p=128)[:, :, sl * 512:(sl + 1) * 512]),
            writes=[kw1], tag=f"{pref}slw1_{e}_{sl}")
        for hq in range(4):
            hc = sl * 4 + hq
            pb = hc % 2
            kpsa = f"{pref}PSA{pb}"
            for d in range(ND):
                for (toff, tw) in tslices:
                    p.emit("pe", lambda e_, d=d, hq=hq, toff=toff, tw=tw, b=b, pb=pb: e_.matmul(
                        PSA[:, pb * 1024 + toff: pb * 1024 + toff + tw],
                        w1sb[b][:, d * 512 + hq * 128: d * 512 + (hq + 1) * 128],
                        xgT[:, d * C + toff: d * C + toff + tw],
                        start=(d == 0), stop=False),
                        reads=[kw1, kxgT], writes=[kpsa], tag=f"{pref}sA{e}_{hc}")
            for (toff, tw) in tslices:
                p.emit("pe", lambda e_, e=e, hc=hc, toff=toff, tw=tw, pb=pb: e_.matmul(
                    PSA[:, pb * 1024 + toff: pb * 1024 + toff + tw],
                    b1e[0:1, hc * 128:(hc + 1) * 128],
                    cst["ones_1x512"][:, 0:tw], start=False, stop=True),
                    reads=[kb1f, "c_ones_1x512"], writes=[kpsa],
                    tag=f"{pref}sAb{e}_{hc}")
            p.emit("act", lambda e_, hc=hc, pb=pb, kpsa=kpsa: e_.activation(
                out=ehT_m[:, hc * C:(hc + 1) * C],
                in_=PSA[:, pb * 1024: pb * 1024 + C], func=actfn),
                reads=[kpsa], writes=[f"{pref}ehT_{hc}"], tag=f"{pref}sAs{e}_{hc}")
    assert NTS <= 8
    ehkeys = [f"{pref}ehT_{hc}" for hc in range(NHC)]
    NHG = NHC // 4
    for dhalf in range(2):
        for hg in range(NHG):
            b = hg % 2
            kw2 = f"{pref}w2_{b}"
            p.emit(q_lw2, lambda e_, e=e, hg=hg, b=b, dhalf=dhalf: e_.dma_start(
                out=w2sb[b][:].rearrange("p (hc f) -> p hc f", hc=4),
                in_=io_w2[e][hg * 512:(hg + 1) * 512,
                             dhalf * 512:(dhalf + 1) * 512]
                .rearrange("(hc p) f -> p hc f", p=128)),
                writes=[kw2], tag=f"{pref}slw2_{e}_{dhalf}_{hg}")
            for hq in range(4):
                hc = hg * 4 + hq
                for i in range(NTS):
                    PSx = PSA if i < 4 else PSB
                    kps = (pref + ("PSA" if i < 4 else "PSB") + str((i % 4) // 2))
                    off = (i % 4) * 512
                    p.emit("pe", lambda e_, hc=hc, hq=hq, i=i, b=b, PSx=PSx, off=off: e_.matmul(
                        PSx[:, off:off + 512],
                        ehT_m[:, hc * C + i * 128: hc * C + (i + 1) * 128],
                        w2sb[b][:, hq * 512:(hq + 1) * 512],
                        start=(hc == 0), stop=False),
                        reads=[kw2, ehkeys[hc]], writes=[kps],
                        tag=f"{pref}sB{e}_{dhalf}_{i}")
        for i in range(NTS):
            PSx = PSA if i < 4 else PSB
            kps = (pref + ("PSA" if i < 4 else "PSB") + str((i % 4) // 2))
            off = (i % 4) * 512
            p.emit("pe", lambda e_, e=e, i=i, dhalf=dhalf, PSx=PSx, off=off: e_.matmul(
                PSx[:, off:off + 512], cst["ones_1x128"][:],
                b2e[0:1, dhalf * 512:(dhalf + 1) * 512],
                start=False, stop=True),
                reads=[kb2f, "c_ones_1x128"], writes=[kps],
                tag=f"{pref}sBb{e}_{dhalf}_{i}")
            p.emit("dve", lambda e_, i=i, dhalf=dhalf, PSx=PSx, off=off: e_.tensor_scalar(
                out=ebuf[:, i * D + dhalf * 512: i * D + (dhalf + 1) * 512],
                in0=PSx[:, off:off + 512], scalar1=pw_sb[:, i * 16:i * 16 + 1],
                scalar2=None, op0=ALU.mult),
                reads=[kps, kpw], writes=[f"{pref}eb_{i}_{dhalf}"],
                tag=f"{pref}sY{e}_{dhalf}_{i}")
    for i in range(NTS):
        p.emit("dve", lambda e_, i=i: e_.tensor_copy(
            ux2[:, 0:1], pw_sb[:, i * 16 + 1: i * 16 + 2]),
            reads=[kpw], writes=["ux2w"], tag=f"{pref}su2{e}_{i}")
        p.emit(q_sb, lambda e_, i=i: e_.indirect_dma_start(
            out=io_AB, out_offset=bass.IndirectOffsetOnAxis(ap=ux2[:, 0:1], axis=0),
            in_=ebuf[:, i * D:(i + 1) * D], in_offset=None,
            bounds_check=io_AB.shape[0] - 1, oob_is_err=False),
            reads=["ux2w", f"{pref}eb_{i}_0", f"{pref}eb_{i}_1"],
            writes=["ABw"], tag=f"{pref}scb{e}_{i}")


def phase_l1back_sparse(p, nc, es, g, io, cst):
    TH, D, ND, E, F1, MACRO = g.TH, g.D, g.ND, g.E, g.F1, g.MACRO
    C, NT = g.C1, g.NT
    EC = E * C
    NM, NTM = g.NMAC, MACRO // 128
    S = lambda nm, shp, dt=F32: es.enter_context(nc.sbuf_tensor("m1" + nm, shp, dt))
    PS = lambda nm, shp, dt=F32: es.enter_context(nc.psum_tensor("m1" + nm, shp, dt))
    q_lw = p.dma_queue("m1_lw", "gpsimd")
    q_lw1 = p.dma_queue("m1_lw1", "gpsimd")
    q_lw2 = p.dma_queue("m1_lw2", "gpsimd")
    p.dma_queue("m1_lb", "gpsimd")
    p.dma_queue("m1_lp", "sync")
    p.dma_queue("m1_pwi", "gpsimd")
    p.dma_queue("m1_scx", "gpsimd")
    p.dma_queue("m1_scw", "gpsimd")
    p.dma_queue("m1_sb", "gpsimd")
    q_lx = p.dma_queue("m1_lx", "sync")
    q_lxT = p.dma_queue("m1_lxT", "sync")
    q_st = p.dma_queue("m1_st", "sync")
    q_tot = p.dma_queue("m1_tot", "sync")

    PSA = PS("PSA", [128, 2048])
    PSB = PS("PSB", [128, 2048])
    ptr = PSB[:, 0:64].bitcast(BF16)
    Msb = S("Msb", [128, ND * D], BF16)
    es_m = ExitStack()
    Sm = lambda nm, shp, dt=F32: es_m.enter_context(nc.sbuf_tensor("m1" + nm, shp, dt))
    gta = [Sm(f"gta{i}", [128, D]) for i in range(2)]
    gtb = [Sm(f"gtb{i}", [128, D]) for i in range(2)]
    gb16 = Sm("gb16", [128, ND * D], BF16)
    for f1 in range(ND):
        pa = f1 % 2
        p.emit(q_lx, lambda e, f1=f1, pa=pa: e.dma_start(
            out=gta[pa][:], in_=io["Ga"][:, f1 * D:(f1 + 1) * D]),
            reads=["Gw"], writes=[f"gta{pa}"], tag=f"lGa{f1}")
        p.emit(q_lxT, lambda e, f1=f1, pa=pa: e.dma_start(
            out=gtb[pa][:], in_=io["Gb"][:, f1 * D:(f1 + 1) * D]),
            reads=["Gbw"], writes=[f"gtb{pa}"], tag=f"lGb{f1}")
        p.emit("dve", lambda e, f1=f1, pa=pa: e.tensor_tensor(
            out=gb16[:, f1 * D:(f1 + 1) * D], in0=gta[pa][:], in1=gtb[pa][:],
            op=ALU.add), reads=[f"gta{pa}", f"gtb{pa}"], writes=["gb16"],
            tag=f"gc{f1}")
    wq1 = Sm("wq1", [128, ND * D], BF16)
    p.emit(q_lw, lambda e: e.dma_start(
        out=wq1[:].rearrange("p (dc f) -> p dc f", dc=ND),
        in_=io["W_q"].rearrange("(dc p) f -> p dc f", p=128)),
        writes=["wq1"], tag="ldWq1")
    for hc in range(ND):
        pb = hc % 2
        for oc in range(ND):
            for half in range(2):
                p.emit("pe", lambda e, hc=hc, oc=oc, half=half, pb=pb: e.matmul(
                    PSA[:, pb * 1024 + half * 512: pb * 1024 + (half + 1) * 512],
                    wq1[:, oc * D + hc * 128: oc * D + (hc + 1) * 128],
                    gb16[:, oc * D + half * 512: oc * D + (half + 1) * 512],
                    start=(oc == 0), stop=(oc == ND - 1)),
                    reads=["wq1", "gb16"], writes=[f"m1PSA{hc % 2}"], tag=f"M{hc}_{oc}")
        p.emit("dve", lambda e, hc=hc, pb=pb: e.tensor_copy(
            Msb[:, hc * D:(hc + 1) * D], PSA[:, pb * 1024:(pb + 1) * 1024]),
            reads=[f"m1PSA{pb}"], writes=["Msb"], tag=f"Mev{hc}")
    p.barrier_all("m1_Mdone")
    es_m.close()
    gwsb = S("gwsb", [E, D], BF16)
    gwT = S("gwT", [128, ND * E], BF16)
    b1e = S("b1e", [1, F1]); b2e = S("b2e", [1, D])
    xT_m = S("xT_m", [128, ND * MACRO], BF16)
    masks = S("masks", [128, NT * E])
    wsel = S("wsel", [128, NT])
    lg = S("lg", [128, 24])
    xt = [S(f"xt{i}", [128, D]) for i in range(2)]
    yt = [S(f"yt{i}", [128, D]) for i in range(2)]
    iota_sb = S("iota_sb", [128, NT])
    totrow = S("totrow", [1, E])
    offsf = S("offsf", [1, NT * E])
    offs32 = S("offs32", [NT, E])
    slot_s = S("slot_s", [128, 32])
    wid = S("wid", [128, 16])
    ux = S("ux", [128, 1], mybir.dt.uint32)
    ux2 = S("ux2", [128, 1], mybir.dt.uint32)
    bigtile = S("bigtile", [128, 256])
    xnb_t = [S(f"xnb{i}", [128, D], BF16) for i in range(2)]
    xg = S("xg", [128, (C // 128) * D], BF16)
    xgT = S("xgT", [128, ND * C], BF16)
    pw_sb = S("pw_sb", [128, (C // 128) * 16])
    ehT_m = S("ehT_m", [128, (F1 // 128) * C], BF16)
    ebuf = S("ebuf", [128, (C // 128) * D], BF16)
    w1sb = [S(f"w1s{i}", [128, ND * 512], BF16) for i in range(2)]
    w2sb = [S(f"w2s{i}", [128, 2048], BF16) for i in range(2)]
    p.emit(q_lw, lambda e: e.dma_start(out=iota_sb[:], in_=io["c_iota_t"]),
           writes=["c_iota"], tag="ldiota")
    p.emit("dve", lambda e: e.memset(bigtile[:], 1e9), writes=["bigtile"],
           tag="bigt")
    p.emit("dve", lambda e: e.memset(wid[:], 0.0), writes=["widw"], tag="widz")
    ztb = S("ztb", [128, D], BF16)
    p.emit("dve", lambda e: e.memset(ztb[:], 0.0), writes=["ztz"], tag="ztz")
    emit_pool_init(p, nc, g, "m1", io["pW1"], bigtile, EC, io["pX1"], ztb)
    gwp = PSA[:, 0:4].bitcast(BF16)
    p.emit(q_lw, lambda e: e.dma_start(out=gwsb[:], in_=io["gate_w"]),
           writes=["gwsb"], tag="ldgw")
    for j in range(ND):
        p.emit("pe", lambda e, j=j: e.transpose(
            gwp, gwsb[:, j * 128:(j + 1) * 128], cst["identb"][0:E, 0:E]),
            reads=["gwsb", "c_identb"], writes=["m1PSA0"], tag=f"gwt{j}")
        p.emit("dve", lambda e, j=j: e.tensor_copy(
            gwT[:, j * E:(j + 1) * E], gwp), reads=["m1PSA0"],
            writes=["gwT"], tag=f"gwc{j}")
    for m in range(NM):
        p.emit(q_lxT, lambda e, m=m: e.dma_start(
            out=xT_m[:].rearrange("p (j t) -> p j t", j=ND),
            in_=io["x1nT"][:, :, m * MACRO:(m + 1) * MACRO].rearrange("j p t -> p j t")),
            writes=["kxTm"], tag=f"lxT{m}")
        for i in range(NTM):
            gi = m * NTM + i
            pb = i % 2
            kpsb = f"m1PSB{pb}"
            pa = i % 2
            p.emit(q_lx, lambda e, m=m, i=i, pa=pa: e.dma_start(
                out=xt[pa][:], in_=io["x1"][m * MACRO + i * 128: m * MACRO + (i + 1) * 128, :]),
                writes=[f"xt{pa}"], tag=f"lx1b{m}_{i}")
            for j in range(ND):
                for half in range(2):
                    p.emit("pe", lambda e, i=i, j=j, half=half, pb=pb: e.matmul(
                        PSB[:, pb * 1024 + half * 512: pb * 1024 + (half + 1) * 512],
                        xT_m[:, j * MACRO + i * 128: j * MACRO + (i + 1) * 128],
                        Msb[:, j * D + half * 512: j * D + (half + 1) * 512],
                        start=(j == 0), stop=(j == ND - 1)),
                        reads=["kxTm", "Msb"], writes=[kpsb], tag=f"at{m}_{i}")
            p.emit("dve", lambda e, i=i, pa=pa, pb=pb: e.tensor_tensor(
                out=yt[pa][:], in0=PSB[:, pb * 1024:(pb + 1) * 1024],
                in1=xt[pa][:], op=ALU.add),
                reads=[kpsb, f"xt{pa}"], writes=[f"yt{pa}"], tag=f"yi1{m}_{i}")
            p.emit(q_st, lambda e, m=m, i=i, pa=pa: e.dma_start(
                out=io["out"][m * MACRO + i * 128: m * MACRO + (i + 1) * 128, :],
                in_=yt[pa][:]),
                reads=[f"yt{pa}"], writes=["outw"], tag=f"sbase{m}_{i}")
            for j in range(ND):
                p.emit("pe", lambda e, i=i, j=j: e.matmul(
                    PSA[:, 0:E], xT_m[:, j * MACRO + i * 128: j * MACRO + (i + 1) * 128],
                    gwT[:, j * E:(j + 1) * E], start=(j == 0), stop=(j == ND - 1)),
                    reads=["kxTm", "gwT"], writes=["m1PSA0"], tag=f"gl{m}_{i}")
            p.emit("dve", lambda e: e.tensor_copy(lg[:, 0:E], PSA[:, 0:E]),
                   reads=["m1PSA0"], writes=["lgv"], tag=f"glc{m}_{i}")
            p.emit("dve", lambda e: e.reduce_max(lg[:, 8:9], lg[:, 0:E], axis=AX.X),
                   reads=["lgv"], writes=["lgm1"], tag=f"g1m{m}_{i}")
            p.emit("dve", lambda e, gi=gi: e.tensor_scalar(
                out=masks[:, gi * E:(gi + 1) * E], in0=lg[:, 0:E], scalar1=lg[:, 8:9],
                scalar2=None, op0=ALU.is_ge), reads=["lgv", "lgm1"],
                writes=["kmask"], tag=f"g1k{m}_{i}")
            p.emit("dve", lambda e: e.tensor_scalar(
                out=lg[:, 10:10 + E], in0=lg[:, 0:E], scalar1=lg[:, 8:9],
                scalar2=None, op0=ALU.subtract), reads=["lgv", "lgm1"],
                writes=["lgsub"], tag=f"g1s{m}_{i}")
            p.emit("act", lambda e: e.activation(out=lg[:, 10:10 + E],
                   in_=lg[:, 10:10 + E], func=AF.Exp), reads=["lgsub"],
                   writes=["lgexp"], tag=f"g1e{m}_{i}")
            p.emit("dve", lambda e: e.reduce_sum(lg[:, 8:9], lg[:, 10:10 + E],
                   axis=AX.X), reads=["lgexp"], writes=["lgsum"], tag=f"g1t{m}_{i}")
            p.emit("dve", lambda e, gi=gi: e.reciprocal(wsel[:, gi:gi + 1], lg[:, 8:9]),
                   reads=["lgsum"], writes=["kwsel"], tag=f"g1r{m}_{i}")
            p.emit("pe", lambda e, gi=gi: e.matmul(
                PSA[0:1, 16:16 + E], cst["ones_128x1"][:],
                masks[:, gi * E:(gi + 1) * E], start=True, stop=True),
                reads=["kmask", "c_ones_128x1"], writes=["m1PSA0"],
                tag=f"tot{m}_{i}")
            p.emit("dve", lambda e: e.tensor_copy(totrow[:], PSA[0:1, 16:16 + E]),
                   reads=["m1PSA0"], writes=["totrow"], tag=f"totc{m}_{i}")
            p.emit(q_tot, lambda e, gi=gi: e.dma_start(
                out=io["tot1"][gi:gi + 1, :], in_=totrow[:]),
                reads=["totrow"], writes=["totw"], tag=f"tots{m}_{i}")
    p.emit(q_tot, lambda e: e.dma_start(out=offs32[:], in_=io["tot1"]),
           reads=["totw"], writes=["offs32"], tag="ldtot")
    p.emit("pe", lambda e: e.matmul(
        PSA[0:NT, 0:E], cst["U128strict"][0:NT, 0:NT], offs32[:],
        start=True, stop=True), reads=["offs32", "c_U128strict"],
        writes=["m1PSA0"], tag="offmm")
    p.emit("dve", lambda e: e.tensor_copy(offs32[:], PSA[0:NT, 0:E]),
           reads=["m1PSA0"], writes=["offs32"], tag="offev")
    p.emit(q_tot, lambda e: e.dma_start(out=io["offs1"], in_=offs32[:]),
           reads=["offs32"], writes=["offsw"], tag="stoffs")
    p.emit(q_tot, lambda e: e.dma_start(
        out=offsf[:], in_=io["offs1"].rearrange("a b -> (a b)").unsqueeze(0)),
        reads=["offsw"], writes=["koffs"], tag="ldoffs")
    for i in range(NT):
        pa = i % 2
        p.emit(q_lx, lambda e, i=i, pa=pa: e.dma_start(
            out=xnb_t[pa][:], in_=io["x1nb"][i * 128:(i + 1) * 128, :]),
            reads=["x1nbw"], writes=[f"kxnb{pa}"], tag=f"lxnb{i}")
        emit_slot_scatter(p, nc, g, cst, "m1", i, 0, masks, "kmask", wsel,
                          "kwsel", offsf, "koffs", xnb_t[pa], f"kxnb{pa}",
                          iota_sb, PSA, slot_s, wid, ux, io["pX1"], io["pW1"],
                          C, EC, layer=1)
    for e in range(E):
        emit_sparse_expert(p, nc, g, cst, "m1", e, F1, C, AF.Relu,
                           io["pX1"], io["pW1"], io["w1"], io["w2"],
                           io["b1"], io["b2"], io["A1"],
                           xg, xgT, pw_sb, ehT_m, ebuf, b1e, b2e, ux2,
                           w1sb, w2sb, PSA, PSB, ptr)
    for i in range(NT):
        pa = i % 2
        p.emit(q_lx, lambda e, i=i, pa=pa: e.dma_start(
            out=xnb_t[pa][:], in_=io["A1"][i * 128:(i + 1) * 128, :]),
            reads=["ABw", f"kxnb{pa}"], writes=[f"kxnb{pa}"], tag=f"lA{i}")
        p.emit("dve", lambda e, pa=pa: e.tensor_copy(yt[pa][:], xnb_t[pa][:]),
               reads=[f"kxnb{pa}"], writes=[f"yt{pa}"], tag=f"cA{i}")
        p.emit(q_lxT, lambda e, i=i, pa=pa: e.dma_start(
            out=xt[pa][:], in_=io["out"][i * 128:(i + 1) * 128, :]),
            reads=["outw", f"xt{pa}"], writes=[f"xt{pa}"], tag=f"lO{i}")
        p.emit("dve", lambda e, pa=pa: e.tensor_tensor(
            out=xt[pa][:], in0=xt[pa][:], in1=yt[pa][:], op=ALU.add),
            reads=[f"yt{pa}", f"xt{pa}"], writes=[f"xt{pa}"], tag=f"fadd{i}")
        p.emit(q_st, lambda e, i=i, pa=pa: e.dma_start(
            out=io["out"][i * 128:(i + 1) * 128, :], in_=xt[pa][:]),
            reads=[f"xt{pa}"], writes=["outw2"], tag=f"sfin{i}")


def phase_scan2(p, nc, es, g, io, cst):
    TH, D, ND, H, CH, NCH, DH = g.TH, g.D, g.ND, g.H, g.CH, g.NCH, g.DH
    SCOLS = (H // 2) * DH + H // 2
    S = lambda nm, shp, dt=F32: es.enter_context(nc.sbuf_tensor("s2" + nm, shp, dt))
    PS = lambda nm, shp, dt=F32: es.enter_context(nc.psum_tensor("s2" + nm, shp, dt))
    q_lq = p.dma_queue("s2_lq", "sync")
    q_ln = p.dma_queue("s2_ln", "sync")
    q_ld = p.dma_queue("s2_ld", "gpsimd")
    q_lcf = p.dma_queue("s2_lcf", "gpsimd")
    q_st = p.dma_queue("s2_st", "sync")

    PB = PS("PB", [128, 2048])
    PP = PS("PP", [128, 2048])
    POv = PB[:, 0:1024]

    S32 = S("S32", [128, SCOLS])
    Sb = S("Sb", [128, SCOLS], BF16)
    p.emit(q_ld, lambda e: e.dma_start(out=S32[:], in_=io["S0"]),
           reads=["S0w"], writes=["s2S32"])
    p.emit("dve", lambda e: e.tensor_copy(Sb[:], S32[:]), reads=["s2S32"],
           writes=["s2Sb"])

    qt = [S(f"qt{i}", [128, ND * CH], BF16) for i in range(2)]
    num = [S(f"num{i}", [128, ND * CH]) for i in range(2)]
    den = [S(f"den{i}", [1, H * CH]) for i in range(2)]
    cfl = [S(f"cfl{i}", [2, H * CH], BF16) for i in range(2)]
    ecum = S("ecum", [128, H * CH], BF16)
    qte = S("qte", [128, ND * CH], BF16)
    dent = S("dent", [1, H * CH])
    recl = S("recl", [128, H * CH])
    recd = S("recd", [128, H * CH])
    ntot = S("ntot", [128, ND * CH])
    outc = [S(f"outc{i}", [128, ND * CH], BF16) for i in range(2)]
    cumg_flat = io["cumghl"].rearrange("w (a b) t -> w b a t", b=2)

    for c in range(NCH):
        pa = c % 2
        kq, kn, kd, kcf = f"s2qt{pa}", f"s2num{pa}", f"s2den{pa}", f"s2cfl{pa}"
        p.emit(q_lq, lambda e, c=c, pa=pa: e.dma_start(
            out=qt[pa][:].rearrange("p (j t) -> p j t", j=ND),
            in_=io["QT"][:, :, c * CH:(c + 1) * CH].rearrange("j p t -> p j t")),
            writes=[kq], tag=f"s2lq{c}")
        p.emit(q_ln, lambda e, c=c, pa=pa: e.dma_start(
            out=num[pa][:], in_=io["num_hbm"][c, :, :]),
            reads=["numw"], writes=[kn], tag=f"s2ln{c}")
        p.emit(q_ld, lambda e, c=c, pa=pa: e.dma_start(
            out=den[pa][:], in_=io["den_hbm"][c:c + 1, :]),
            reads=["denw"], writes=[kd], tag=f"s2ld{c}")
        for w in range(2):
            p.emit(q_lcf, lambda e, c=c, pa=pa, w=w: e.dma_start(
                out=cfl[pa][w:w + 1, :].rearrange("o (b a t) -> o b a t", b=2, a=ND),
                in_=cumg_flat[w:w + 1, :, :, c * CH:(c + 1) * CH].squeeze(0).unsqueeze(0)),
                reads=["cumghlw"], writes=[kcf], tag=f"s2lcf{c}_{w}")
        for sl in range(4):
            p.emit("pe", lambda e, sl=sl, pa=pa: e.matmul(
                PB[:, sl * 512:(sl + 1) * 512], cst["ones_2x128b"][:],
                cfl[pa][0:2, sl * 512:(sl + 1) * 512], start=True, stop=True),
                reads=[kcf, "c_ones_2x128b"], writes=["s2PB"], tag=f"s2b1_{c}_{sl}")
        p.emit("act", lambda e: e.activation(out=ecum[:], in_=PB[:],
               func=AF.Exp), reads=["s2PB"], writes=["s2ecum"], tag=f"s2ec{c}")
        for b in range(2):
            p.emit("dve", lambda e, b=b, pa=pa: e.tensor_tensor(
                out=qte[b * 64:(b + 1) * 64, :],
                in0=qt[pa][b * 64:(b + 1) * 64, :],
                in1=ecum[b * 64:(b + 1) * 64, b * 1024:(b + 1) * 1024],
                op=ALU.mult), reads=[kq, "s2ecum"], writes=["s2qte"],
                tag=f"s2qte{c}_{b}")
        for h in range(H):
            b, j = h % 2, h // 2
            p.emit("pe", lambda e, b=b, j=j, h=h: e.matmul(
                POv[b * 64:(b + 1) * 64, j * CH:(j + 1) * CH],
                Sb[b * 64:(b + 1) * 64, j * DH:(j + 1) * DH],
                qte[b * 64:(b + 1) * 64, j * CH:(j + 1) * CH],
                start=True, stop=True, tile_position=(b * 64, b * 64)),
                reads=["s2Sb", "s2qte"], writes=["s2PB"], tag=f"s2ox{c}_{h}")
        by_bank = {}
        for h in range(H):
            by_bank.setdefault(g.col(h) // 512, []).append(h)
        for bank, hs in by_bank.items():
            for k, h in enumerate(hs):
                b, j = h % 2, h // 2
                p.emit("pe", lambda e, b=b, j=j, h=h, st=(k == 0),
                       sp=(k == len(hs) - 1): e.matmul(
                    PP[0:1, g.col(h):g.col(h) + CH],
                    Sb[b * 64:(b + 1) * 64, 512 + j:513 + j],
                    qte[b * 64:(b + 1) * 64, j * CH:(j + 1) * CH],
                    start=st, stop=sp),
                    reads=["s2Sb", "s2qte"], writes=["s2PP"], tag=f"s2dz{c}_{h}")
        p.emit("dve", lambda e, pa=pa: e.tensor_tensor(
            out=dent[:], in0=PP[0:1, :], in1=den[pa][:], op=ALU.add),
            reads=["s2PP", kd], writes=["s2dent"], tag=f"s2dt{c}")
        for sl in range(4):
            p.emit("pe", lambda e, sl=sl: e.matmul(
                PP[:, sl * 512:(sl + 1) * 512], cst["ones_1x128"][:],
                dent[:, sl * 512:(sl + 1) * 512], start=True, stop=True),
                reads=["s2dent", "c_ones_1x128"], writes=["s2PP"],
                tag=f"s2rb{c}_{sl}")
        p.emit("act", lambda e: e.activation(out=recl[:], in_=PP[:],
               func=AF.Ln), reads=["s2PP"], writes=["s2recl"], tag=f"s2ln{c}")
        p.emit("act", lambda e: e.activation(out=recd[:], in_=recl[:],
               func=AF.Exp, scale=-1.0), reads=["s2recl"], writes=["s2recd"],
               tag=f"s2rc{c}")
        p.emit("dve", lambda e, pa=pa: e.tensor_tensor(
            out=ntot[:], in0=POv, in1=num[pa][:], op=ALU.add),
            reads=["s2PB", kn], writes=["s2ntot"], tag=f"s2nt{c}")
        for b in range(2):
            p.emit("dve", lambda e, b=b, pa=pa: e.tensor_tensor(
                out=outc[pa][b * 64:(b + 1) * 64, :],
                in0=ntot[b * 64:(b + 1) * 64, :],
                in1=recd[b * 64:(b + 1) * 64, b * 1024:(b + 1) * 1024],
                op=ALU.mult), reads=["s2ntot", "s2recd"],
                writes=[f"s2outc{pa}"], tag=f"s2dv{c}_{b}")
        p.emit(q_st, lambda e, c=c, pa=pa: e.dma_start(
            out=io["outHT"][:, :, c * CH:(c + 1) * CH].rearrange("j p t -> p j t"),
            in_=outc[pa][:].rearrange("p (j t) -> p j t", j=ND)),
            reads=[f"s2outc{pa}"], writes=["outHTw"], tag=f"s2st{c}")
